# version 01
# baseline (speedup 1.0000x reference)
import math
import contextlib
import numpy as np
import ml_dtypes
import concourse.bass as bass
import concourse.mybir as mybir
from concourse.bass_utils import run_bass_kernel_spmd

F32 = mybir.dt.float32
BF16 = mybir.dt.bfloat16
I32 = mybir.dt.int32
U8 = mybir.dt.uint8
AF = mybir.ActivationFunctionType
ALU = mybir.AluOpType
AX = mybir.AxisListType

BF = ml_dtypes.bfloat16

D = 1024
T = 2048
NDT = 8
NTT = 16
H = 16
DH = 64

ENGINES = ("tensor", "vector", "scalar", "gpsimd", "sync")


def _coalesce(tokens):
    best = {}
    for t in tokens:
        if t is None:
            continue
        sn, sv = t
        if sv > best.get(sn, -1):
            best[sn] = sv
    return tuple(best.items())


class Prog:
    AUTO_ENG = {"vector": "v", "scalar": "s", "gpsimd": "g"}

    def __init__(self, nc):
        self.nc = nc
        self.ops = {e: [] for e in ENGINES}
        self.cnt = {}

    def use_sems(self, *names):
        for n in names:
            self.cnt.setdefault(n, 0)

    def op(self, eng, fn, waits=(), inc=None, inc_n=1, dma=False):
        auto = (not dma) and eng in self.AUTO_ENG
        if auto:
            sem = self.AUTO_ENG[eng]
            assert inc in (None, sem)
            self.cnt.setdefault(sem, 0)
            prev = (sem, self.cnt[sem]) if self.cnt[sem] > 0 else None
            self.cnt[sem] += 1
            tok = (sem, self.cnt[sem])
            waits = tuple(waits) + ((prev,) if prev else ())
            inc, inc_n = sem, 1
        else:
            tok = None
            if inc is not None:
                self.cnt.setdefault(inc, 0)
        waits = _coalesce(self._bump(waits))
        if not auto and inc is not None:
            self.cnt[inc] += inc_n
            tok = (inc, self.cnt[inc])

        def closure(e, sems, fn=fn, waits=waits, tok=tok, inc=inc, inc_n=inc_n):
            for (sn, sv) in waits:
                e.wait_ge(sems[sn], sv)
            instr = fn(e, sems)
            if tok is not None:
                if instr is None:
                    instr = e.nop()
                instr.then_inc(sems[inc], inc_n)
            return instr

        self.ops[eng].append(closure)
        return tok

    def _bump(self, tokens):
        out = []
        for t in tokens:
            if t is None:
                continue
            sn, sv = t
            if sn.startswith("d_"):
                sv = self.cnt.get(sn, sv)
            out.append((sn, sv))
        return out

    def wait(self, eng, *tokens):
        toks = _coalesce(self._bump(tokens))
        if not toks:
            return

        def closure(e, sems, toks=toks):
            for (sn, sv) in toks:
                e.wait_ge(sems[sn], sv)

        self.ops[eng].append(closure)

    def dma(self, queue, out, in_, waits=(), sem=None, **kw):
        if sem is None:
            sem = "d_" + queue
        return self.op(queue, lambda e, s: e.dma_start(out=out, in_=in_, **kw),
                       waits=waits, inc=sem, inc_n=16, dma=True)


class Arena:

    def __init__(self, B):
        self.B = B
        self.off = 0
        self.stack = []
        self.total = B.AR.shape[1]

    def mark(self):
        self.stack.append(self.off)

    def release(self):
        self.off = self.stack.pop()

    def alloc(self, cols):
        o = (self.off + 3) & ~3
        self.off = o + cols
        assert self.off <= self.total, f"arena overflow {self.off} > {self.total}"
        return self.B.AR[:, o:o + cols]

    def alloc3(self, a, b):
        ap = self.alloc(a * b)
        return ap.rearrange("p (a b) -> p a b", b=b)

    def alloc4(self, a, b, c):
        ap = self.alloc(a * b * c)
        return ap.rearrange("p (a b c) -> p a b c", b=b, c=c)

    def rowf(self, n):
        ap = self.alloc(2 * n)
        return ap.bitcast(F32)[0:1, 0:n]

    def rowb(self, n):
        ap = self.alloc(n)
        return ap[0:1, :]


def _alibi_slopes(num_heads):
    closest_pow2 = 2 ** math.floor(math.log2(num_heads))
    base = 2.0 ** (-2.0 ** (-(math.log2(closest_pow2) - 3)))
    powers = np.arange(1, closest_pow2 + 1, dtype=np.float32)
    slopes = base ** powers
    if num_heads != closest_pow2:
        start = 2.0 ** (-(2.0 ** (-(math.log2(closest_pow2) - 3)) - 1))
        extra = np.linspace(start, base, num_heads - closest_pow2, dtype=np.float32)
        slopes = np.concatenate([slopes, extra])
    return slopes.astype(np.float32)


CONST_SPECS = {}
CV_VALS = {}


def build_consts():
    cols = []
    specs = {}

    def add(name, arr):
        specs[name] = (sum(c.shape[1] for c in cols), arr.shape[1])
        cols.append(arr.astype(np.float32))

    i = np.arange(128)[:, None]
    jq = np.arange(128)[None, :]

    add("eye", np.eye(128, dtype=np.float32))
    add("ones_col", np.ones((128, 1), dtype=np.float32))
    onesrow = np.zeros((128, 128), dtype=np.float32)
    onesrow[0, :] = 1.0
    add("ones_row", onesrow)
    s2t = np.zeros((128, 128), dtype=np.float32)
    s2t[0, 0:64] = 1.0
    s2t[1, 64:128] = 1.0
    add("sel2", s2t)
    s16 = np.zeros((128, 128), dtype=np.float32)
    for j in range(8):
        s16[j, 16 * j:16 * (j + 1)] = 1.0
    add("sel16", s16)
    s6 = np.zeros((128, 768), dtype=np.float32)
    for j in range(6):
        s6[j, 128 * j:128 * (j + 1)] = 1.0
    add("sel6", s6)
    add("triuIncl", (i <= jq).astype(np.float32))
    for lvl in range(4):
        blk = 2 ** (lvl + 1)
        start = (i // blk) * blk
        Mm = ((jq >= start) & (jq <= i)).astype(np.float32) / (i - start + 1)
        add(f"haarM_T_{lvl}", np.ascontiguousarray(Mm.T))
    R = np.zeros((128, 128), dtype=np.float32)
    for b0 in range(0, 128, 64):
        for ii in range(32):
            R[b0 + ii, b0 + 32 + ii] = -1.0
            R[b0 + 32 + ii, b0 + ii] = 1.0
    add("rope_rotT", np.ascontiguousarray(R.T))
    GLOBAL_CON_END = sum(c.shape[1] for c in cols)

    CV_VALS.clear()
    cv_list = [("ln_eps", 1e-5), ("rms_eps", float(np.finfo(np.float32).eps)),
               ("tiny", 1e-12), ("one", 1.0), ("half", 0.5), ("zero", 0.0),
               ("linattn_eps", 1e-6)]
    for idx, (nm, val) in enumerate(cv_list):
        CV_VALS[nm] = idx
    add("cvals", np.tile(np.array([[v for _, v in cv_list]], dtype=np.float32),
                         (128, 1)))

    slopes = _alibi_slopes(H)
    for h in range(H):
        tiles = []
        for off in (-2, -1, 0):
            rel = (i + 128 * off) - jq
            keep = (rel <= 0) & (rel > -256)
            tiles.append(np.where(keep, np.exp(slopes[h] * rel.astype(np.float64)),
                                  0.0).astype(np.float32))
        add(f"l0_strip_{h}", np.concatenate(tiles, axis=1))

    tiles = []
    for off in (-1, 0):
        rel = (i + 128 * off) - jq
        keep = (rel <= 0) & (rel > -64)
        tiles.append(keep.astype(np.float32))
    add("l1_lstrip", np.concatenate(tiles, axis=1))

    gamma = 1.0 - np.exp2(-5.0 - np.arange(H, dtype=np.float64))
    u = np.arange(128)
    for h in range(H):
        dm = np.where(i <= jq, float(gamma[h] ** -128.0), 0.0)
        add(f"l2_decmv_{h}", dm.astype(np.float32))
    for hp in range(8):
        tile = np.zeros((128, 128), dtype=np.float64)
        tile[:64, :] = gamma[2 * hp] ** u[None, :] / 8.0
        tile[64:, :] = gamma[2 * hp + 1] ** u[None, :] / 8.0
        add(f"l2_qg_{hp}", tile.astype(np.float32))
        tile2 = np.zeros((128, 128), dtype=np.float64)
        tile2[:64, :] = gamma[2 * hp] ** (128 - u[None, :])
        tile2[64:, :] = gamma[2 * hp + 1] ** (128 - u[None, :])
        add(f"l2_kg_{hp}", tile2.astype(np.float32))
    g128 = np.zeros((128, 8), dtype=np.float32)
    for hp in range(8):
        g128[:64, hp] = gamma[2 * hp] ** 128
        g128[64:, hp] = gamma[2 * hp + 1] ** 128
    add("l2_g128", g128)

    inv_freq = 1.0 / (10000.0 ** (np.arange(0, DH, 2, dtype=np.float32) / DH))
    t = np.arange(T, dtype=np.float32)
    freqs = np.einsum("i,j->ij", t, inv_freq)
    emb = np.concatenate([freqs, freqs], axis=1)
    add("rope_cos", np.ascontiguousarray(np.tile(np.cos(emb).T, (2, 1))))
    add("rope_sin", np.ascontiguousarray(np.tile(np.sin(emb).T, (2, 1))))

    CONST_SPECS.clear()
    CONST_SPECS.update(specs)
    arr = np.ascontiguousarray(np.concatenate(cols, axis=1))
    return arr, GLOBAL_CON_END


CONSTS_ARR, GLOBAL_CON_END = build_consts()
CONST_SHAPE = CONSTS_ARR.shape


def _bcol(v):
    k = v.shape[0]
    return np.ascontiguousarray(v.reshape(k // 128, 128).T)


def host_prep(inputs):
    g = {k: np.asarray(v, dtype=np.float32) for k, v in inputs.items()}
    d = {}
    tr = lambda a: np.ascontiguousarray(a.T)

    d["L0_qkvT"] = tr(g["L0_attn_qkv_w"])
    d["L0_qkv_b_col"] = _bcol(g["L0_attn_qkv_b"])
    d["L0_vb_row"] = np.ascontiguousarray(g["L0_attn_qkv_b"][2048:].reshape(1, 1024))
    d["L0_projT"] = tr(g["L0_attn_proj_w"])
    d["L0_proj_b_col"] = _bcol(g["L0_attn_proj_b"])
    d["L0_WrT"] = tr(g["L0_ffn_W_r"])
    d["L0_WkT"] = tr(g["L0_ffn_W_k"])
    d["L0_WvT"] = tr(g["L0_ffn_W_v"])
    d["L0_mur_col"] = _bcol(g["L0_ffn_mu_r"])
    d["L0_muk_col"] = _bcol(g["L0_ffn_mu_k"])

    for w in ("Wq", "Wk", "Wv", "Wkl", "Wvl", "Wo"):
        d[f"L1_{w}T"] = tr(g[f"L1_attn_{w}"])
    d["L1_WgT"] = tr(g["L1_attn_Wg"])
    d["L1_bg_col"] = _bcol(g["L1_attn_bg"])
    d["L1_WgoT_col"] = _bcol(g["L1_attn_Wgo"].reshape(1024))
    d["L1_bgo"] = np.ascontiguousarray(g["L1_attn_bgo"].reshape(1, 1))
    d["L1_bo_col"] = _bcol(g["L1_attn_bo"])
    d["L1_haar_scale"] = np.ascontiguousarray(g["L1_attn_haar_scale"].reshape(1, 4))
    bd = np.zeros((4, 2, 128, 128), dtype=np.float32)
    for lvl in range(4):
        for idx, nm in enumerate(("haar_Wk", "haar_Wv")):
            wt = g[f"L1_attn_{nm}"][lvl].T
            bd[lvl, idx, :64, :64] = wt
            bd[lvl, idx, 64:, 64:] = wt
    d["L1_haarBD"] = np.ascontiguousarray(bd.reshape(8, 128, 128))

    d["L1_gateT"] = tr(g["L1_ffn_gate_w"])
    d["L1_gate_b_bc"] = np.ascontiguousarray(
        np.tile(g["L1_ffn_gate_b"].reshape(1, 6), (128, 1)))
    for w in ("ew1", "ew2", "ew3"):
        d[f"L1_{w}T"] = np.ascontiguousarray(g[f"L1_ffn_{w}"].transpose(0, 2, 1))
    for w in ("fc1", "fc2", "fc3"):
        d[f"L1_{w}T"] = tr(g[f"L1_ffn_{w}"])

    for w in ("wq", "wk", "wv", "wg", "wo"):
        d[f"L2_{w}T"] = tr(g[f"L2_attn_{w}"])
    d["L2_W1"] = np.ascontiguousarray(g["L2_ffn_W1"])
    d["L2_b1_col"] = _bcol(g["L2_ffn_b1"])
    d["L2_theta_col"] = _bcol(g["L2_ffn_quantum_phases"])
    d["L2_wr_col"] = _bcol(g["L2_ffn_quantum_weights_real"])
    d["L2_wi_col"] = _bcol(g["L2_ffn_quantum_weights_imag"])
    d["L2_W2"] = np.ascontiguousarray(g["L2_ffn_W2"])
    d["L2_b2_col"] = _bcol(g["L2_ffn_b2"])

    d["L3_normw_col"] = _bcol(g["L3_attn_norm_w"])
    d["L3_Wq"] = np.ascontiguousarray(g["L3_attn_Wq"])
    d["L3_Wkv"] = np.ascontiguousarray(g["L3_attn_Wkv"])
    d["L3_Wout"] = np.ascontiguousarray(g["L3_attn_Wout"])
    d["L3_Wg"] = np.ascontiguousarray(g["L3_attn_Wg"])
    d["L3_wgateT"] = tr(g["L3_ffn_w_gate"])
    d["L3_wupT"] = tr(g["L3_ffn_w_up"])
    d["L3_wdownT"] = tr(g["L3_ffn_w_down"])
    d["L3_wgate_nat"] = np.ascontiguousarray(g["L3_ffn_w_gate"])
    d["L3_wup_nat"] = np.ascontiguousarray(g["L3_ffn_w_up"])
    d["L3_wdown_nat"] = np.ascontiguousarray(g["L3_ffn_w_down"])

    d["CONSTS"] = CONSTS_ARR

    x = np.asarray(inputs["L0_attn_x"], dtype=np.float32)
    xs = [np.ascontiguousarray(x[b].T) for b in range(8)]
    return d, xs


class Bufs:
    pass


def build_program(blocks, taps=(), weight_names=None):
    nc = bass.Bass("TRN2", target_bir_lowering=False, debug=False)
    P = Prog(nc)
    es = contextlib.ExitStack()

    x_in = nc.dram_tensor("x_in", [D, T], F32, kind="ExternalInput").ap()
    consts_h = nc.dram_tensor("CONSTS", list(CONST_SHAPE), F32, kind="ExternalInput").ap()
    wh = {}
    for name, shape in (weight_names or {}).items():
        wh[name] = nc.dram_tensor(name, list(shape), F32, kind="ExternalInput").ap()
    x_out = nc.dram_tensor("x_out", [D, T], F32, kind="ExternalOutput").ap()
    tap_out = {}
    for name, shape, dt_, getter in taps:
        tap_out[name] = nc.dram_tensor("tap_" + name, list(shape), dt_,
                                       kind="ExternalOutput").ap()

    with es:
        B = Bufs()
        B.nc = nc
        B.wh = wh
        B.consts_h = consts_h
        sb = lambda n, sh, dt_: es.enter_context(nc.sbuf_tensor(n, sh, dt_))
        B.XT = sb("XT", [128, NDT, T], F32)
        B.XN = sb("XN", [128, NDT, T], BF16)
        B.ST = sb("ST", [128, NDT, T], BF16)
        B.AR = sb("AR", [128, 36864], BF16)
        B.SMALL = sb("SMALL", [128, 256], F32)
        B.CON = sb("CON", [128, GLOBAL_CON_END], BF16)
        B.CONF = sb("CONF", [128, 16], F32)
        B.PSA = es.enter_context(nc.psum_tensor("PSA", [128, 2048], F32))
        B.PSB = es.enter_context(nc.psum_tensor("PSB", [128, 2048], F32))
        B.tok_x_final = None
        B.ar = Arena(B)

        P.use_sems("t", "v", "s", "g", "d_sync", "d_gpsimd", "d_scalar", "d_out",
                   "d_w0", "d_w1", "d_w2", "d_w3")

        emit_main(P, B, blocks, x_in, x_out, tap_out, taps)

        sems = {n: es.enter_context(nc.semaphore(n)) for n in sorted(P.cnt.keys())}
        block = es.enter_context(nc.Block())
        for eng in ENGINES:
            deco = getattr(block, eng)
            opslist = P.ops[eng]

            def make_section(opslist):
                def section(e):
                    for closure in opslist:
                        closure(e, sems)
                return section
            deco(make_section(opslist))

    return nc


def cvf(B, name, rows=(0, 128)):
    idx = CV_VALS[name]
    return B.CONF[rows[0]:rows[1], idx:idx + 1]


def cbf(B, name, rows=None):
    off, n = CONST_SPECS[name]
    assert off < GLOBAL_CON_END
    if rows is None:
        return B.CON[:, off:off + n]
    return B.CON[rows[0]:rows[1], off:off + n]


def chbm(B, name):
    off, n = CONST_SPECS[name]
    return B.consts_h[:, off:off + n]


def emit_main(P, B, blocks, x_in, x_out, tap_out, taps):
    xsrc = x_in.rearrange("(dt p) t -> p dt t", p=128)
    tok_x = P.dma("sync", B.XT[:], xsrc, sem="d_boot")
    ctok = P.dma("gpsimd", B.CON[:], B.consts_h[:, 0:GLOBAL_CON_END])
    o_cv, n_cv = CONST_SPECS["cvals"]
    ctok2 = P.dma("sync", B.CONF[:, 0:n_cv], B.consts_h[:, o_cv:o_cv + n_cv], sem="d_boot")
    for e in ("tensor", "vector", "scalar", "gpsimd"):
        P.wait(e, tok_x, ctok, ctok2)

    for b in blocks:
        emit_layernorm(P, B)
        EMITTERS[b](P, B)

    for name, shape, dt_, getter in taps:
        ap, waits = getter(B)
        P.dma("sync", tap_out[name], ap, waits=waits, sem="d_out")

    done = P.dma("sync", x_out.rearrange("(dt p) t -> p dt t", p=128), B.XT[:],
                 waits=(B.tok_x_final,) if B.tok_x_final else (), sem="d_out")
    P.wait("sync", done)


def emit_layernorm(P, B):
    ar = B.ar
    ar.mark()
    ones_bf = cbf(B, "ones_col")
    ones_row = cbf(B, "ones_row", rows=(0, 1))
    SQ = ar.alloc3(NDT, 512)
    AB = ar.alloc(512)
    BB = ar.alloc(512)
    R_m = ar.rowf(512)
    R_ss = ar.rowf(512)
    R_t = ar.rowf(512)
    R_inv = ar.rowf(512)
    RB_a = ar.rowb(512)
    RB_b = ar.rowb(512)
    stats = B.PSA[0:1, 0:512]
    bcA = B.PSB[:, 0:512]
    bcB = B.PSB[:, 512:1024]
    xfin = B.tok_x_final
    prev_ab = None
    prev_bb = None
    prev_xn = None
    for cc in range(4):
        tsl = slice(512 * cc, 512 * (cc + 1))
        cast_toks = []
        for dt_ in range(NDT):
            cast_toks.append(P.op("scalar",
                lambda e, s, dt_=dt_, tsl=tsl: e.copy(out=B.XN[:, dt_, tsl],
                                                      in_=B.XT[:, dt_, tsl]),
                waits=(xfin,) if (cc == 0 and dt_ == 0) else ()))
        for dt_ in range(NDT):
            w = [cast_toks[dt_]]
            if dt_ == 0:
                w += [prev_ab, prev_bb, xfin]
            mm_mean = P.op("tensor",
                lambda e, s, dt_=dt_, tsl=tsl: e.matmul(
                    out=stats, lhsT=ones_bf, rhs=B.XN[:, dt_, tsl],
                    start=(dt_ == 0), stop=(dt_ == NDT - 1)),
                waits=w, inc="t" if dt_ == NDT - 1 else None)
        t_m = P.op("scalar", lambda e, s: e.mul(out=R_m, in_=stats, mul=1.0 / 1024),
                   waits=(mm_mean,))
        sq_toks = []
        for dt_ in range(NDT):
            sq_toks.append(P.op("vector",
                lambda e, s, dt_=dt_, tsl=tsl: e.tensor_tensor(
                    out=SQ[:, dt_, :], in0=B.XN[:, dt_, tsl],
                    in1=B.XN[:, dt_, tsl], op=ALU.mult),
                waits=(cast_toks[dt_],)))
        for dt_ in range(NDT):
            mm_sq = P.op("tensor",
                lambda e, s, dt_=dt_: e.matmul(
                    out=stats, lhsT=ones_bf, rhs=SQ[:, dt_, :],
                    start=(dt_ == 0), stop=(dt_ == NDT - 1)),
                waits=(sq_toks[dt_], t_m) if dt_ == 0 else (sq_toks[dt_],),
                inc="t" if dt_ == NDT - 1 else None)
        t_v1 = P.op("scalar", lambda e, s: e.mul(out=R_ss, in_=stats, mul=1.0 / 1024),
                    waits=(mm_sq,))
        t_v2 = P.op("vector", lambda e, s: e.tensor_tensor(
            out=R_t, in0=R_m, in1=R_m, op=ALU.mult), waits=(t_m,))
        t_v3 = P.op("vector", lambda e, s: e.tensor_tensor(
            out=R_ss, in0=R_ss, in1=R_t, op=ALU.subtract), waits=(t_v1, t_v2))
        t_v4 = P.op("scalar", lambda e, s: e.activation(
            out=R_t, in_=R_ss, func=AF.Ln, bias=cvf(B, "ln_eps", (0, 1))),
            waits=(t_v3,))
        t_v5 = P.op("scalar", lambda e, s: e.activation(
            out=R_inv, in_=R_t, func=AF.Exp, scale=-0.5), waits=(t_v4,))
        t_a = P.op("vector", lambda e, s: e.tensor_copy(out=RB_a, in_=R_inv),
                   waits=(t_v5,))
        t_b = P.op("vector", lambda e, s: e.scalar_tensor_tensor(
            out=RB_b, in0=R_m, scalar=-1.0, in1=R_inv, op0=ALU.mult, op1=ALU.mult),
            waits=(t_v5,))
        mm_a = P.op("tensor", lambda e, s: e.matmul(
            out=bcA, lhsT=ones_row, rhs=RB_a, start=True, stop=True),
            waits=(t_a,), inc="t")
        mm_b = P.op("tensor", lambda e, s: e.matmul(
            out=bcB, lhsT=ones_row, rhs=RB_b, start=True, stop=True),
            waits=(t_b,), inc="t")
        t_ab = P.op("scalar", lambda e, s: e.copy(out=AB, in_=bcA),
                    waits=(mm_a, mm_sq))
        t_bb = P.op("scalar", lambda e, s: e.copy(out=BB, in_=bcB),
                    waits=(mm_b,))
        for dt_ in range(NDT):
            P.op("vector", lambda e, s, dt_=dt_, tsl=tsl: e.tensor_tensor(
                out=SQ[:, dt_, :], in0=B.XN[:, dt_, tsl], in1=AB, op=ALU.mult),
                waits=(t_ab,) if dt_ == 0 else ())
            prev_xn = P.op("vector", lambda e, s, dt_=dt_, tsl=tsl: e.tensor_tensor(
                out=B.XN[:, dt_, tsl], in0=SQ[:, dt_, :], in1=BB, op=ALU.add),
                waits=(t_bb,) if dt_ == 0 else ())
        prev_ab, prev_bb = t_ab, t_bb
    B.tok_xn = prev_xn
    P.wait("tensor", B.tok_xn)
    ar.release()


def wslice_src(hap, c0, c1):
    return hap[:, c0:c1].rearrange("(it p) c -> p it c", p=128)


def emit_block0(P, B):
    ar = B.ar
    ar.mark()
    SCALE = 0.125

    tok_b = P.dma("sync", B.SMALL[:, 0:24], B.wh["L0_qkv_b_col"], sem="d_small")
    tok_pb = P.dma("sync", B.SMALL[:, 24:32], B.wh["L0_proj_b_col"], sem="d_small")
    tok_mu = P.dma("sync", B.SMALL[:, 32:40], B.wh["L0_mur_col"], sem="d_small")
    tok_mk = P.dma("sync", B.SMALL[:, 40:48], B.wh["L0_muk_col"], sem="d_small")
    t_qb = P.op("vector", lambda e, s: e.tensor_scalar(
        out=B.SMALL[:, 48:56], in0=B.SMALL[:, 0:8], scalar1=SCALE, scalar2=None,
        op0=ALU.mult), waits=(tok_b,))

    A_qw = ar.alloc3(NDT, 256)
    A_kw = ar.alloc3(NDT, 256)
    A_vw = ar.alloc3(NDT, 256)
    A_pw = ar.alloc3(2, 1024)
    A_qT = ar.alloc3(2, T)
    A_kT = ar.alloc3(2, T)
    A_at = ar.alloc3(2, T)
    A_vx = ar.alloc4(NTT, 4, 65)
    A_pt = ar.alloc3(4, 384)
    A_al = ar.alloc3(4, 384)
    A_vb = ar.alloc(256)
    A_rec = ar.alloc(512)
    R_rec = ar.rowf(512)
    RB_rec = ar.rowb(512)

    prev = {}
    p_done = None
    for hg in range(4):
        qc0 = hg * 256
        kc0 = 1024 + hg * 256
        vc0 = 2048 + hg * 256
        wsem = f"d_w{hg % 2}"
        ln0 = B.tok_xn if hg == 0 else None
        w_q = P.dma("gpsimd", A_qw[:], wslice_src(B.wh["L0_qkvT"], qc0, qc0 + 256),
                    waits=(prev.get("q_done"), ln0), sem=wsem)
        w_k = P.dma("gpsimd", A_kw[:], wslice_src(B.wh["L0_qkvT"], kc0, kc0 + 256),
                    waits=(prev.get("k_done"), ln0), sem=wsem)
        w_v = P.dma("gpsimd", A_vw[:], wslice_src(B.wh["L0_qkvT"], vc0, vc0 + 256),
                    waits=(prev.get("v_done"), ln0), sem=wsem)
        w_p = P.dma("gpsimd", A_pw[:],
                    B.wh["L0_projT"][hg * 256:(hg + 1) * 256, :]
                    .rearrange("(it p) c -> p it c", p=128),
                    waits=(p_done, ln0), sem=wsem)
        w_vb = P.dma("gpsimd", A_vb[0:1, :],
                     B.wh["L0_vb_row"][0:1, hg * 256:(hg + 1) * 256],
                     waits=(prev.get("vb_done"), ln0), sem=wsem)
        al_tok = None
        for hs in range(4):
            al_tok = P.dma("gpsimd", A_al[:, hs, :], chbm(B, f"l0_strip_{4 * hg + hs}"),
                           waits=(prev.get("al_done"), ln0), sem=wsem)

        def proj_fm(wview, dst, bias_cols, extra_waits, scale=None):
            last = None
            for oc in range(2):
                for ch in range(4):
                    psum = B.PSA[:, 512 * ch:512 * (ch + 1)]
                    for it in range(NDT):
                        mm = P.op("tensor", lambda e, s, oc=oc, ch=ch, it=it, wview=wview, psum=psum: e.matmul(
                            out=psum, lhsT=wview[:, it, 128 * oc:128 * (oc + 1)],
                            rhs=B.XN[:, it, 512 * ch:512 * (ch + 1)],
                            start=(it == 0), stop=(it == NDT - 1)),
                            waits=(tuple(extra_waits) + (B.tok_xn, prev.get(("evac", ch)),)) if it == 0 else (),
                            inc="t" if it == NDT - 1 else None)
                    ev = P.op("scalar", lambda e, s, oc=oc, ch=ch, dst=dst, psum=psum, bias_cols=bias_cols, scale=scale: e.activation(
                        out=dst[:, oc, 512 * ch:512 * (ch + 1)], in_=psum,
                        func=AF.Identity, bias=bias_cols[:, oc:oc + 1],
                        **({"scale": scale} if scale is not None else {})),
                        waits=(mm,))
                    prev[("evac", ch)] = ev
                    last = ev
            return last

        q_done = proj_fm(A_qw, A_qT, B.SMALL[:, 48 + 2 * hg:50 + 2 * hg],
                         (w_q, t_qb), scale=SCALE)
        k_done = proj_fm(A_kw, A_kT, B.SMALL[:, 8 + 2 * hg:10 + 2 * hg], (w_k, tok_b))

        mm_vb = P.op("tensor", lambda e, s: e.matmul(
            out=B.PSB[:, 0:256], lhsT=cbf(B, "ones_row", rows=(0, 1)),
            rhs=A_vb[0:1, :], start=True, stop=True),
            waits=(w_vb, prev.get(("pv", 0))), inc="t")
        t_vbb = P.op("scalar", lambda e, s: e.copy(out=A_vb, in_=B.PSB[:, 0:256]),
                     waits=(mm_vb,))
        t_ones = P.op("vector", lambda e, s: e.memset(A_vx[:, :, :, 64:65], 1.0),
                      waits=(prev.get("at_done"),))
        v_done = None
        for tt in range(NTT):
            psum = B.PSB[:, 1024 + 512 * (tt % 2):1024 + 512 * (tt % 2) + 256]
            for it in range(NDT):
                mm = P.op("tensor", lambda e, s, tt=tt, it=it, psum=psum: e.matmul(
                    out=psum, lhsT=B.XN[:, it, 128 * tt:128 * (tt + 1)],
                    rhs=A_vw[:, it, :], start=(it == 0), stop=(it == NDT - 1)),
                    waits=(w_v, B.tok_xn, prev.get(("vevac", tt % 2))) if it == 0 else (),
                    inc="t" if it == NDT - 1 else None)
            ev = P.op("vector", lambda e, s, tt=tt, psum=psum: e.scalar_tensor_tensor(
                out=A_vx[:, tt, :, 0:64],
                in0=psum.rearrange("p (a b) -> p a b", b=64), scalar=1.0,
                in1=A_vb.rearrange("p (a b) -> p a b", b=64),
                op0=ALU.mult, op1=ALU.add),
                waits=(mm, t_vbb, t_ones))
            prev[("vevac", tt % 2)] = ev
            v_done = ev

        attn_done = None
        for qt in range(NTT):
            kts = [kt for kt in (qt - 2, qt - 1, qt) if kt >= 0]
            for hs in range(4):
                rb = (hs % 2) * 64
                for kt in kts:
                    slot = 3 - (qt - kt)
                    mm_s = P.op("tensor", lambda e, s, hs=hs, kt=kt, qt=qt, rb=rb, slot=slot: e.matmul(
                        out=B.PSA[:, 512 * hs + 128 * slot:512 * hs + 128 * (slot + 1)],
                        lhsT=A_kT[rb:rb + 64, hs // 2, 128 * kt:128 * (kt + 1)],
                        rhs=A_qT[rb:rb + 64, hs // 2, 128 * qt:128 * (qt + 1)],
                        start=True, stop=True),
                        waits=(q_done, k_done, prev.get("psa_free")) if (hs == 0 and kt == kts[0]) else (),
                        inc="t" if (hs == 3 and kt == kts[-1]) else None)
            t_exp = P.op("scalar", lambda e, s: e.activation(
                out=A_pt[:], func=AF.Exp,
                in_=B.PSA[:].rearrange("p (h c) -> p h c", c=512)[:, :, 128:512]),
                waits=(mm_s,))
            t_msk = P.op("vector", lambda e, s: e.tensor_tensor(
                out=A_pt[:], in0=A_pt[:], in1=A_al[:], op=ALU.mult),
                waits=(t_exp, al_tok))
            pvbank = B.PSB[:, 512 * (qt % 2):512 * (qt % 2) + 512]
            for hs in range(4):
                for ki, kt in enumerate(kts):
                    slot = 3 - (qt - kt)
                    mm_pv = P.op("tensor", lambda e, s, hs=hs, kt=kt, slot=slot, pvbank=pvbank, ki=ki, nk=len(kts): e.matmul(
                        out=pvbank[0:65, 128 * hs:128 * (hs + 1)],
                        lhsT=A_vx[:, kt, hs, :],
                        rhs=A_pt[:, hs, 128 * (slot - 1):128 * slot],
                        start=(ki == 0), stop=(ki == nk - 1)),
                        waits=(t_msk, v_done, prev.get(("pv", qt % 2))) if (hs == 0 and ki == 0) else (),
                        inc="t" if (hs == 3 and ki == len(kts) - 1) else None)
            t_rc = P.op("vector", lambda e, s, pvbank=pvbank: e.reciprocal(
                out=R_rec, in_=pvbank[64:65, :]), waits=(mm_pv,))
            t_rb = P.op("vector", lambda e, s: e.tensor_copy(out=RB_rec, in_=R_rec),
                        waits=(t_rc,))
            mm_rb = P.op("tensor", lambda e, s: e.matmul(
                out=B.PSA[0:64, 0:512], lhsT=cbf(B, "ones_row", rows=(0, 1))[:, 0:64],
                rhs=RB_rec, start=True, stop=True), waits=(t_rb, t_exp), inc="t")
            t_rcb = P.op("scalar", lambda e, s: e.copy(out=A_rec[0:64, :],
                                                       in_=B.PSA[0:64, 0:512]),
                         waits=(mm_rb, attn_done))
            prev["psa_free"] = t_rcb
            for par in range(2):
                ev = P.op("vector", lambda e, s, par=par, qt=qt, pvbank=pvbank: e.tensor_tensor(
                    out=A_at[64 * par:64 * par + 64, :, 128 * qt:128 * (qt + 1)],
                    in0=pvbank[0:64, :].rearrange("p (h c) -> p h c", c=128)[:, par::2, :],
                    in1=A_rec[0:64, :].rearrange("p (h c) -> p h c", c=128)[:, par::2, :],
                    op=ALU.mult), waits=(t_rcb,))
                prev[("pv", qt % 2)] = ev
                attn_done = ev
        prev["at_done"] = attn_done

        for oc in range(NDT):
            for ch in range(4):
                psum = B.PSA[:, 512 * ch:512 * (ch + 1)]
                for it2 in range(2):
                    mm = P.op("tensor", lambda e, s, oc=oc, ch=ch, it2=it2, psum=psum: e.matmul(
                        out=psum, lhsT=A_pw[:, it2, 128 * oc:128 * (oc + 1)],
                        rhs=A_at[:, it2, 512 * ch:512 * (ch + 1)],
                        start=(it2 == 0), stop=(it2 == 1)),
                        waits=(w_p, attn_done, prev.get(("evac", ch)), prev.get("psa_free")) if it2 == 0 else (),
                        inc="t" if it2 == 1 else None)
                if hg == 0:
                    ev = P.op("scalar", lambda e, s, oc=oc, ch=ch, psum=psum: e.activation(
                        out=B.ST[:, oc, 512 * ch:512 * (ch + 1)], in_=psum,
                        func=AF.Identity, bias=B.SMALL[:, 24 + oc:25 + oc]),
                        waits=(mm, tok_pb))
                else:
                    ev = P.op("vector", lambda e, s, oc=oc, ch=ch, psum=psum: e.tensor_tensor(
                        out=B.ST[:, oc, 512 * ch:512 * (ch + 1)],
                        in0=B.ST[:, oc, 512 * ch:512 * (ch + 1)], in1=psum,
                        op=ALU.add), waits=(mm,))
                prev[("evac", ch)] = ev
                p_done = ev

        prev["q_done"] = q_done
        prev["k_done"] = k_done
        prev["v_done"] = v_done
        prev["vb_done"] = t_vbb
        prev["al_done"] = t_msk
    B.tok_s0 = p_done
    ar.release()

    ar.mark()
    A_w1 = ar.alloc3(NDT, 1024)
    A_w2 = ar.alloc3(NDT, 1024)
    A_xr = ar.alloc3(NDT, 512)
    A_rr = ar.alloc3(NDT, 512)
    A_ka = ar.alloc3(NDT, 512)
    A_t1 = ar.alloc3(NDT, 512)
    A_kaf = ar.alloc(1024).bitcast(F32)
    A_vof = ar.alloc(1024).bitcast(F32)
    mur = B.SMALL[:, 32:40]
    muk = B.SMALL[:, 40:48]

    wload = {}
    for ch in range(4):
        wload[("fevac", ch)] = prev.get(("evac", ch))
    ffn_done = None
    barrier = B.tok_s0
    for qtr in range(4):
        q0 = 512 * qtr
        tq = slice(q0, q0 + 512)
        w_r = P.dma("gpsimd", A_w1[:], wslice_src(B.wh["L0_WrT"], 0, 1024),
                    waits=(wload.get("w1free"), B.tok_s0), sem="d_w2")
        w_k = P.dma("gpsimd", A_w2[:], wslice_src(B.wh["L0_WkT"], 0, 1024),
                    waits=(wload.get("w2free"), B.tok_s0), sem="d_w3")

        def tshift(dst, mu, waits):
            toks = []
            for dt_ in range(NDT):
                if qtr == 0:
                    P.op("vector", lambda e, s, dt_=dt_: e.tensor_tensor(
                        out=A_t1[:, dt_, 1:512], in0=B.XN[:, dt_, 1:512],
                        in1=B.XN[:, dt_, 0:511], op=ALU.subtract), waits=waits)
                    P.op("vector", lambda e, s, dt_=dt_, mu=mu, dst=dst: e.scalar_tensor_tensor(
                        out=dst[:, dt_, 1:512], in0=A_t1[:, dt_, 1:512],
                        scalar=mu[:, dt_:dt_ + 1], in1=B.XN[:, dt_, 0:511],
                        op0=ALU.mult, op1=ALU.add))
                    toks.append(P.op("vector", lambda e, s, dt_=dt_, mu=mu, dst=dst: e.tensor_scalar(
                        out=dst[:, dt_, 0:1], in0=B.XN[:, dt_, 0:1],
                        scalar1=mu[:, dt_:dt_ + 1], scalar2=None, op0=ALU.mult)))
                else:
                    P.op("vector", lambda e, s, dt_=dt_, tq=tq, q0=q0: e.tensor_tensor(
                        out=A_t1[:, dt_, :], in0=B.XN[:, dt_, tq],
                        in1=B.XN[:, dt_, q0 - 1:q0 + 511], op=ALU.subtract),
                        waits=waits)
                    toks.append(P.op("vector", lambda e, s, dt_=dt_, mu=mu, dst=dst, q0=q0: e.scalar_tensor_tensor(
                        out=dst[:, dt_, :], in0=A_t1[:, dt_, :],
                        scalar=mu[:, dt_:dt_ + 1], in1=B.XN[:, dt_, q0 - 1:q0 + 511],
                        op0=ALU.mult, op1=ALU.add)))
            return toks[-1]

        def ffn_proj(wview, wtok, src, srctok, evac):
            last = None
            for oc in range(NDT):
                psum = B.PSA[:, 512 * (oc % 4):512 * (oc % 4 + 1)]
                for it in range(NDT):
                    mm = P.op("tensor", lambda e, s, oc=oc, it=it, psum=psum, wview=wview, src=src: e.matmul(
                        out=psum, lhsT=wview[:, it, 128 * oc:128 * (oc + 1)],
                        rhs=src[:, it, :], start=(it == 0), stop=(it == NDT - 1)),
                        waits=(wtok, srctok, wload.get(("fevac", oc % 4))) if it == 0 else (),
                        inc="t" if it == NDT - 1 else None)
                ev = evac(oc, psum, mm)
                wload[("fevac", oc % 4)] = ev
                last = ev
            return last

        t_xr = tshift(A_xr, mur, (tok_mu, barrier))
        r_done = ffn_proj(A_w1, w_r, A_xr, t_xr,
                          lambda oc, psum, mm: P.op("scalar",
                              lambda e, s, oc=oc, psum=psum: e.activation(
                                  out=A_rr[:, oc, :], in_=psum, func=AF.Sigmoid),
                              waits=(mm,)))
        w_v = P.dma("gpsimd", A_w1[:], wslice_src(B.wh["L0_WvT"], 0, 1024),
                    waits=(r_done,), sem="d_w2")
        t_xk = tshift(A_xr, muk, (tok_mk, r_done))

        def ka_evac(oc, psum, mm):
            P.op("vector", lambda e, s, oc=oc, psum=psum: e.tensor_scalar(
                out=A_kaf, in0=psum, scalar1=0.0, scalar2=None,
                op0=ALU.max), waits=(mm,))
            return P.op("vector", lambda e, s, oc=oc: e.tensor_tensor(
                out=A_ka[:, oc, :], in0=A_kaf, in1=A_kaf, op=ALU.mult))

        ka_done = ffn_proj(A_w2, w_k, A_xr, t_xk, ka_evac)

        def vout_evac(oc, psum, mm):
            P.op("vector", lambda e, s, oc=oc, psum=psum: e.tensor_tensor(
                out=A_vof, in0=psum, in1=A_rr[:, oc, :], op=ALU.mult),
                waits=(mm,))
            P.op("scalar", lambda e, s, oc=oc: e.activation(
                out=A_vof, in_=A_vof, func=AF.Silu),
                waits=(("v", P.cnt["v"]),))
            P.op("vector", lambda e, s, oc=oc, tq=tq: e.tensor_tensor(
                out=A_vof, in0=A_vof, in1=B.ST[:, oc, tq],
                op=ALU.mult), waits=(B.tok_s0, ("s", P.cnt["s"])))
            return P.op("vector", lambda e, s, oc=oc, tq=tq: e.tensor_tensor(
                out=B.XT[:, oc, tq], in0=B.XT[:, oc, tq], in1=A_vof,
                op=ALU.add))

        ffn_done = ffn_proj(A_w1, w_v, A_ka, ka_done, vout_evac)
        wload["w1free"] = ffn_done
        wload["w2free"] = ka_done
        barrier = ffn_done
    B.tok_x_final = ffn_done
    ar.release()


L1_PARTS = 9
L1_HP = 8
L1_NOLOC = False
L1_NOGLOB = False
L1_NOPASS = False
L1_SEC = 9
L1_NOPHI = False


def emit_block1(P, B):
    emit_block1_attn(P, B)
    if L1_PARTS >= 2:
        emit_block1_ffn(P, B)
    else:
        B.tok_x_final = B.tok_s1


def emit_block1_attn(P, B):
    ar = B.ar
    ar.mark()
    nc = B.nc
    B.l1_diff = nc.dram_tensor("l1_diff", [D, T], BF16, kind="Internal").ap()
    B.l1_glob = nc.dram_tensor("l1_glob", [D, T], BF16, kind="Internal").ap()
    ones_row = cbf(B, "ones_row", rows=(0, 1))
    eye = cbf(B, "eye")
    triu = cbf(B, "triuIncl")

    tok_bg = P.dma("sync", B.SMALL[:, 0:8], B.wh["L1_bg_col"], sem="d_small",
                   waits=(B.tok_xn,))
    tok_bo = P.dma("sync", B.SMALL[:, 8:16], B.wh["L1_bo_col"], sem="d_small")
    tok_go = P.dma("sync", B.SMALL[:, 16:24], B.wh["L1_WgoT_col"], sem="d_small")
    tok_gb = P.dma("sync", B.SMALL[:, 40:46], B.wh["L1_gate_b_bc"], sem="d_small")
    tok_bgo = P.dma("sync", B.SMALL[0:1, 25:26], B.wh["L1_bgo"], sem="d_small")
    tok_hs = P.dma("sync", B.SMALL[0:1, 28:32], B.wh["L1_haar_scale"],
                   sem="d_small")

    RB_w = ar.rowb(4)
    A_w4 = ar.alloc(8).bitcast(F32)
    t_e = P.op("scalar", lambda e, s: e.activation(
        out=B.SMALL[0:1, 32:36], in_=B.SMALL[0:1, 28:32], func=AF.Exp),
        waits=(tok_hs,))
    P.op("vector", lambda e, s: e.tensor_reduce(
        out=B.SMALL[0:1, 36:37], in_=B.SMALL[0:1, 32:36], axis=AX.X,
        op=ALU.add), waits=(t_e,))
    P.op("vector", lambda e, s: e.reciprocal(
        out=B.SMALL[0:1, 37:38], in_=B.SMALL[0:1, 36:37]))
    P.op("vector", lambda e, s: e.tensor_scalar(
        out=B.SMALL[0:1, 32:36], in0=B.SMALL[0:1, 32:36],
        scalar1=B.SMALL[0:1, 37:38], scalar2=None, op0=ALU.mult))
    t_wb = P.op("vector", lambda e, s: e.tensor_copy(
        out=RB_w, in_=B.SMALL[0:1, 32:36]))
    mm_w = P.op("tensor", lambda e, s: e.matmul(
        out=B.PSA[:, 0:4], lhsT=ones_row, rhs=RB_w, start=True, stop=True),
        waits=(t_wb, B.tok_xn), inc="t")
    t_w4 = P.op("scalar", lambda e, s: e.copy(out=A_w4, in_=B.PSA[:, 0:4]),
                waits=(mm_w,))

    A_bd = ar.alloc3(8, 128)
    tok_bd = P.dma("gpsimd", A_bd[:],
                   B.wh["L1_haarBD"].rearrange("a p c -> p a c"),
                   waits=(B.tok_xn,), sem="d_w3")
    bd_folded = None
    for lvl in range(4):
        bd_folded = P.op("vector", lambda e, s, lvl=lvl: e.tensor_scalar(
            out=A_bd[:, 2 * lvl + 1, :], in0=A_bd[:, 2 * lvl + 1, :],
            scalar1=A_w4[:, lvl:lvl + 1], scalar2=None, op0=ALU.mult),
            waits=(tok_bd, t_w4))

    A_wa = ar.alloc3(NDT, 128)
    A_wb = ar.alloc3(NDT, 128)
    A_qT = ar.alloc(T)
    A_qpT = ar.alloc(T)
    A_kT = ar.alloc(T)
    A_kpT = ar.alloc(T)
    A_klT = ar.alloc(T)
    A_vT = ar.alloc(T)
    A_kpN = ar.alloc4(2, NTT, 64)
    A_vN = ar.alloc4(2, NTT, 65)
    A_vlN = ar.alloc4(2, NTT, 65)
    A_locT = ar.alloc(T)
    A_gT = ar.alloc(T)
    A_phit = ar.alloc(512)
    A_t2k = ar.alloc(512)
    A_t2v = ar.alloc(512)
    A_kpLN = ar.alloc4(4, 2, 64)
    A_mvN = ar.alloc4(4, 2, 65)
    A_kpLT = ar.alloc3(4, 128)
    A_am = ar.alloc3(2, 512)
    A_ls = ar.alloc(256)

    A_stF = ar.alloc(650).bitcast(F32)
    A_stB = ar.alloc(328)
    A_gtmp = ar.alloc3(2, 512)
    A_rc = ar.alloc(128)
    A_rcl = ar.alloc(512)
    A_rc0 = ar.alloc(128)
    R_n = ar.rowf(640)
    RB_n = ar.rowb(640)

    tok_ls = P.dma("gpsimd", A_ls[:], chbm(B, "l1_lstrip"),
                   waits=(B.tok_xn,), sem="d_w2")
    fp = {("pevac", 0): t_w4}
    diff_done = None
    glob_dma = None
    for hp in range(L1_HP):
        csl0 = slice(128 * hp, 128 * (hp + 1))

        def proj(wname, wbuf, wsem, evac_fn, extra_w=()):
            wtok = P.dma("gpsimd", wbuf[:], wslice_src(B.wh[wname], 128 * hp,
                                                       128 * (hp + 1)),
                         waits=(fp.get(("wfree", wsem)),) + tuple(extra_w),
                         sem=wsem)
            last = None
            for ch in range(4):
                psum = B.PSA[:, 512 * ch:512 * (ch + 1)]
                for it in range(NDT):
                    mm = P.op("tensor", lambda e, s, ch=ch, it=it, wbuf=wbuf, psum=psum: e.matmul(
                        out=psum, lhsT=wbuf[:, it, :],
                        rhs=B.XN[:, it, 512 * ch:512 * (ch + 1)],
                        start=(it == 0), stop=(it == NDT - 1)),
                        waits=(wtok, B.tok_xn, fp.get(("pevac", ch))) if it == 0 else (),
                        inc="t" if it == NDT - 1 else None)
                last = evac_fn(ch, psum, mm)
                fp[("pevac", ch)] = last
            fp[("wfree", wsem)] = last
            return last

        def phi_chain(psum, mm, dstphi, csl, after=None):
            t1 = P.op("vector", lambda e, s, psum=psum: e.tensor_scalar(
                out=A_phit[:], in0=psum, scalar1=0.0, scalar2=None,
                op0=ALU.min), waits=(mm, after, fp.get("phit")))
            t2 = P.op("scalar", lambda e, s, csl=csl, dstphi=dstphi: e.activation(
                out=dstphi[:, csl], in_=A_phit[:], func=AF.Exp), waits=(t1,))
            t3 = P.op("vector", lambda e, s, psum=psum: e.tensor_scalar(
                out=A_phit[:], in0=psum, scalar1=0.0, scalar2=None,
                op0=ALU.max), waits=(t2,))
            t4 = P.op("vector", lambda e, s, csl=csl, dstphi=dstphi: e.tensor_tensor(
                out=dstphi[:, csl], in0=dstphi[:, csl], in1=A_phit[:],
                op=ALU.add), waits=(t2,))
            fp["phit"] = t4
            return t4

        def ev_scale(dst, scale):
            def f(ch, psum, mm):
                return P.op("scalar", lambda e, s, ch=ch, psum=psum, scale=scale: e.activation(
                    out=dst[:, 512 * ch:512 * (ch + 1)], in_=psum,
                    func=AF.Identity, scale=scale), waits=(mm,))
            return f

        def ev_scale_phi(dstq, dstphi):
            def f(ch, psum, mm):
                csl = slice(512 * ch, 512 * (ch + 1))
                t0 = P.op("scalar", lambda e, s, psum=psum, csl=csl: e.activation(
                    out=dstq[:, csl], in_=psum, func=AF.Identity, scale=0.125),
                    waits=(mm,))
                return phi_chain(psum, mm, dstphi, csl, after=t0)
            return f

        def ev_copy_phi(dstraw, dstphi):
            def f(ch, psum, mm):
                csl = slice(512 * ch, 512 * (ch + 1))
                t0 = P.op("scalar", lambda e, s, psum=psum, csl=csl: e.copy(
                    out=dstraw[:, csl], in_=psum), waits=(mm,))
                return phi_chain(psum, mm, dstphi, csl, after=t0)
            return f

        def ev_copy(dst):
            def f(ch, psum, mm):
                return P.op("scalar", lambda e, s, ch=ch, psum=psum: e.copy(
                    out=dst[:, 512 * ch:512 * (ch + 1)], in_=psum), waits=(mm,))
            return f

        if L1_NOPHI:
            q_done = proj("L1_WqT", A_wa, "d_w0", ev_scale(A_qT, 0.125),
                          extra_w=(fp.get("qT_free"), fp.get("qpT_free")))
            qp_done = q_done
            k_done = proj("L1_WkT", A_wb, "d_w1", ev_copy(A_kT),
                          extra_w=(fp.get("kT_free"), fp.get("kpT_free")))
        else:
            q_done = proj("L1_WqT", A_wa, "d_w0", ev_scale_phi(A_qT, A_qpT),
                          extra_w=(fp.get("qT_free"), fp.get("qpT_free")))
            qp_done = q_done
            k_done = proj("L1_WkT", A_wb, "d_w1", ev_copy_phi(A_kT, A_kpT),
                          extra_w=(fp.get("kT_free"), fp.get("kpT_free")))
        kl_done = proj("L1_WklT", A_wa, "d_w0", ev_copy(A_klT),
                       extra_w=(fp.get("klT_free"),))
        v_done = proj("L1_WvT", A_wb, "d_w1", ev_copy(A_vT),
                      extra_w=(fp.get("vT_free"),))

        if L1_SEC < 2:
            dtoks = [v_done]
            dma_d = P.dma("sync", B.l1_diff[csl0, :], A_locT[:],
                          waits=(v_done,), sem="d_st0")
            dma_g = P.dma("sync", B.l1_glob[csl0, :], A_gT[:],
                          waits=(v_done,), sem="d_st1")
            for ch in range(4):
                fp[("pevac", ch)] = v_done
            fp["locT_free"] = dma_d
            fp["gT_free"] = dma_g
            diff_done = dma_d
            glob_dma = dma_g
            continue
        t_ones = P.op("vector", lambda e, s: e.memset(A_vN[:, :, :, 64:65], 1.0),
                      waits=(fp.get("vN_free"),))
        t_ones2 = P.op("vector", lambda e, s: e.memset(A_vlN[:, :, :, 64:65], 1.0),
                       waits=(fp.get("vlN_free"),))
        tr_done = {}
        kpn_done = None
        for c in range(NTT):
            for si, (src, wsrc) in enumerate(((A_kpT, k_done), (A_vT, v_done))):
                bsel = 2 * si + (c % 2)
                bank = B.PSB[:, 512 * bsel:512 * bsel + 64].bitcast(BF16)
                mm_t = P.op("tensor", lambda e, s, src=src, c=c, bank=bank: e.transpose(
                    out=bank[:, 0:128], in_=src[:, 128 * c:128 * (c + 1)],
                    identity=eye),
                    waits=(wsrc, tr_done.get(("ev", bsel)),
                           fp.get("psb_free")), inc="t")
                if si == 0:
                    ev = P.op("vector", lambda e, s, c=c, bank=bank: e.tensor_copy(
                        out=A_kpN[:, :, c, :],
                        in_=bank[:, 0:128].rearrange("p (a b) -> p a b", b=64)),
                        waits=(mm_t,))
                else:
                    ev = P.op("vector", lambda e, s, c=c, bank=bank: e.tensor_copy(
                        out=A_vN[:, :, c, 0:64],
                        in_=bank[:, 0:128].rearrange("p (a b) -> p a b", b=64)),
                        waits=(mm_t, t_ones))
                tr_done[("ev", bsel)] = ev
                kpn_done = ev

        if L1_SEC < 3:
            dtoks = [kpn_done]
            dma_d = P.dma("sync", B.l1_diff[csl0, :], A_locT[:],
                          waits=(kpn_done,), sem="d_st0")
            dma_g = P.dma("sync", B.l1_glob[csl0, :], A_gT[:],
                          waits=(kpn_done,), sem="d_st1")
            for ch in range(4):
                fp[("pevac", ch)] = kpn_done
            fp["psb_free"] = kpn_done
            fp["locT_free"] = dma_d
            fp["gT_free"] = dma_g
            diff_done = dma_d
            glob_dma = dma_g
            continue
        w_vl = P.dma("gpsimd", A_wb[:], wslice_src(B.wh["L1_WvlT"], 128 * hp,
                                                   128 * (hp + 1)),
                     waits=(v_done,), sem="d_w1")
        vln_done = None
        for c in range(NTT):
            psum = B.PSB[:, 1024 + 512 * (c % 2):1024 + 512 * (c % 2) + 128]
            for it in range(NDT):
                mm = P.op("tensor", lambda e, s, c=c, it=it, psum=psum: e.matmul(
                    out=psum, lhsT=B.XN[:, it, 128 * c:128 * (c + 1)],
                    rhs=A_wb[:, it, :], start=(it == 0), stop=(it == NDT - 1)),
                    waits=((w_vl, B.tok_xn, fp.get(("vlev", c % 2)))
                           + ((kpn_done,) if c < 2 else ())) if it == 0 else (),
                    inc="t" if it == NDT - 1 else None)
            ev = P.op("vector", lambda e, s, c=c, psum=psum: e.tensor_copy(
                out=A_vlN[:, :, c, 0:64],
                in_=psum.rearrange("p (a b) -> p a b", b=64)),
                waits=(mm, t_ones2))
            fp[("vlev", c % 2)] = ev
            vln_done = ev
        fp[("wfree", "d_w1")] = vln_done

        loc_done = vln_done
        for qt in range(0 if L1_NOLOC else NTT):
            sbank = B.PSA[:, 512 * (qt % 2):512 * (qt % 2) + 512]
            pvb = B.PSA[:, 1024 + 512 * (qt % 2):1024 + 512 * (qt % 2) + 128]
            kts = [kt for kt in (qt - 1, qt) if kt >= 0]
            c0 = 128 * (2 - len(kts))
            for h2 in (0, 64):
                mm_s = None
                for ki, kt in enumerate(kts):
                    sl2 = 2 - len(kts) + ki
                    mm_s = P.op("tensor", lambda e, s, h2=h2, kt=kt, qt=qt, sl2=sl2, sbank=sbank: e.matmul(
                        out=sbank[:, 128 * sl2:128 * (sl2 + 1)],
                        lhsT=A_klT[h2:h2 + 64, 128 * kt:128 * (kt + 1)],
                        rhs=A_qT[h2:h2 + 64, 128 * qt:128 * (qt + 1)],
                        start=True, stop=True),
                        waits=(q_done, kl_done, fp.get(("locs", qt % 2))) if ki == 0 else (),
                        inc="t" if ki == len(kts) - 1 else None)
                t_exp = P.op("scalar", lambda e, s, sbank=sbank, c0=c0: e.activation(
                    out=A_am[:, 0, c0:256], in_=sbank[:, c0:256],
                    func=AF.Exp), waits=(mm_s, fp.get("am0_free")))
                t_msk = P.op("vector", lambda e, s, c0=c0: e.tensor_tensor(
                    out=A_am[:, 0, c0:256], in0=A_am[:, 0, c0:256],
                    in1=A_ls[:, c0:256], op=ALU.mult),
                    waits=(t_exp, tok_ls))
                mm_pv = None
                for ki, kt in enumerate(kts):
                    mm_pv = P.op("tensor", lambda e, s, h2=h2, kt=kt, ki=ki, nk=len(kts), pvb=pvb: e.matmul(
                        out=pvb[0:65, :], lhsT=A_vlN[:, h2 // 64, kt, :],
                        rhs=A_am[:, 0, 128 * (2 - nk + ki):128 * (2 - nk + ki) + 128],
                        start=(ki == 0), stop=(ki == nk - 1)),
                        waits=(t_msk, vln_done, fp.get(("locpv", qt % 2))) if ki == 0 else (),
                        inc="t" if ki == len(kts) - 1 else None)
                t_rc = P.op("vector", lambda e, s, pvb=pvb: e.reciprocal(
                    out=R_n[0:1, 0:128], in_=pvb[64:65, :]), waits=(mm_pv,))
                t_rb = P.op("vector", lambda e, s: e.tensor_copy(
                    out=RB_n[0:1, 0:128], in_=R_n[0:1, 0:128]))
                mm_rb = P.op("tensor", lambda e, s, sbank=sbank: e.matmul(
                    out=sbank[0:64, 256:384], lhsT=ones_row[:, 0:64],
                    rhs=RB_n[0:1, 0:128], start=True, stop=True),
                    waits=(t_rb, t_msk), inc="t")
                t_rcb = P.op("scalar", lambda e, s, sbank=sbank: e.copy(
                    out=A_rc[0:64, :], in_=sbank[0:64, 256:384]),
                    waits=(mm_rb, fp.get("rc_free")))
                ev = P.op("vector", lambda e, s, h2=h2, qt=qt, pvb=pvb: e.tensor_tensor(
                    out=A_locT[h2:h2 + 64, 128 * qt:128 * (qt + 1)],
                    in0=pvb[0:64, :], in1=A_rc[0:64, :], op=ALU.mult),
                    waits=(t_rcb, fp.get("locT_free")))
                fp[("locs", qt % 2)] = t_rcb
                fp[("locpv", qt % 2)] = ev
                fp["rc_free"] = ev
                fp["am0_free"] = mm_pv
                loc_done = ev

        st_z1 = P.op("vector", lambda e, s: e.memset(A_stF[:, :], 0.0),
                     waits=(fp.get("st_free"),))
        st_z2 = P.op("vector", lambda e, s: e.memset(A_stB[:, 0:325], 0.0))
        glob_done = s4 = mm_pl = mm_a0 = mm_p0 = kplt_done = mm_al = st_z2
        for c in range(0 if L1_NOGLOB else NTT):
            ctile = slice(128 * c, 128 * (c + 1))
            lw0 = (loc_done,) if c == 0 else ()
            mm = None
            for lvl in range(4):
                mm = P.op("tensor", lambda e, s, lvl=lvl, ctile=ctile: e.matmul(
                    out=B.PSA[:, 128 * lvl:128 * (lvl + 1)],
                    lhsT=A_kT[:, ctile], rhs=A_bd[:, 2 * lvl, :],
                    start=True, stop=True),
                    waits=(k_done, bd_folded, fp.get("t2k_free")) + lw0 if lvl == 0 else (),
                    inc="t" if lvl == 3 else None)
            ev_t2k = P.op("scalar", lambda e, s: e.copy(out=A_t2k[:], in_=B.PSA[:, 0:512]),
                          waits=(mm,))
            mm = None
            for lvl in range(4):
                mm = P.op("tensor", lambda e, s, lvl=lvl: e.matmul(
                    out=B.PSA[:, 512 + 128 * lvl:512 + 128 * (lvl + 1)],
                    lhsT=cbf(B, f"haarM_T_{lvl}"), rhs=A_t2k[:, 128 * lvl:128 * (lvl + 1)],
                    start=True, stop=True),
                    waits=(ev_t2k, fp.get("mk_free")) + lw0 if lvl == 0 else (),
                    inc="t" if lvl == 3 else None)
            p1 = P.op("vector", lambda e, s: e.tensor_scalar(
                out=A_phit[:], in0=B.PSA[:, 512:1024], scalar1=0.0,
                scalar2=None, op0=ALU.min), waits=(mm, fp.get("phit")))
            p2 = P.op("scalar", lambda e, s: e.activation(
                out=A_kpLN[:].rearrange("p a b c -> p (a b c)"), in_=A_phit[:],
                func=AF.Exp), waits=(p1, fp.get("kpLN_free"), fp.get("kpLN_free2")))
            p3 = P.op("vector", lambda e, s: e.tensor_scalar(
                out=A_phit[:], in0=B.PSA[:, 512:1024], scalar1=0.0,
                scalar2=None, op0=ALU.max), waits=(p2,))
            p4 = P.op("vector", lambda e, s: e.tensor_tensor(
                out=A_kpLN[:].rearrange("p a b c -> p (a b c)"),
                in0=A_kpLN[:].rearrange("p a b c -> p (a b c)"),
                in1=A_phit[:], op=ALU.add), waits=(p2,))
            fp["phit"] = p4
            mm = None
            for lvl in range(4):
                mm = P.op("tensor", lambda e, s, lvl=lvl, ctile=ctile: e.matmul(
                    out=B.PSA[:, 1024 + 128 * lvl:1024 + 128 * (lvl + 1)],
                    lhsT=A_vT[:, ctile], rhs=A_bd[:, 2 * lvl + 1, :],
                    start=True, stop=True),
                    waits=(v_done, bd_folded, fp.get("t2v_free")) + lw0 if lvl == 0 else (),
                    inc="t" if lvl == 3 else None)
            ev_t2v = P.op("scalar", lambda e, s: e.copy(out=A_t2v[:], in_=B.PSA[:, 1024:1536]),
                          waits=(mm,))
            t_mvo = P.op("vector", lambda e, s: e.memset(A_mvN[:, :, :, 64:65], 1.0),
                         waits=(fp.get("mvN_free"),))
            mm = None
            for lvl in range(4):
                mm = P.op("tensor", lambda e, s, lvl=lvl: e.matmul(
                    out=B.PSA[:, 1536 + 128 * lvl:1536 + 128 * (lvl + 1)],
                    lhsT=cbf(B, f"haarM_T_{lvl}"), rhs=A_t2v[:, 128 * lvl:128 * (lvl + 1)],
                    start=True, stop=True),
                    waits=(ev_t2v, fp.get("mv_free")) + lw0 if lvl == 0 else (),
                    inc="t" if lvl == 3 else None)
            ev_mv = P.op("vector", lambda e, s: e.tensor_copy(
                out=A_mvN[:, :, :, 0:64],
                in_=B.PSA[:, 1536:2048].rearrange("p (a b c) -> p a b c", b=2, c=64)),
                waits=(mm, t_mvo))
            trtok = {}
            evt = None
            for j in range(8):
                lvl, h2i = j // 2, j % 2
                bank = B.PSA[:, 1024 * (j % 2):1024 * (j % 2) + 64].bitcast(BF16)
                mm_t = P.op("tensor", lambda e, s, lvl=lvl, h2i=h2i, bank=bank: e.transpose(
                    out=bank[0:64, 0:128], in_=A_kpLN[:, lvl, h2i, :],
                    identity=eye),
                    waits=(p4, trtok.get(("ev", j % 2)),
                           (ev_t2k if j % 2 == 0 else ev_t2v)), inc="t")
                evt = P.op("scalar", lambda e, s, lvl=lvl, h2i=h2i, bank=bank: e.copy(
                    out=A_kpLT[64 * h2i:64 * h2i + 64, lvl, :],
                    in_=bank[0:64, 0:128]),
                    waits=(mm_t, fp.get("kpLT_free")))
                trtok[("ev", j % 2)] = evt
            kplt_done = evt
            fp["t2k_free"] = trtok[("ev", 0)]
            fp["t2v_free"] = trtok[("ev", 1)]

            if L1_NOPASS:
                glob_done = s4 = kplt_done
                continue
            for h2 in (0, 1):
                rb = 64 * h2
                g0 = (kpn_done, vln_done) if (c == 0 and h2 == 0) else ()
                mm_a0 = P.op("tensor", lambda e, s, rb=rb, ctile=ctile: e.matmul(
                    out=B.PSB[:, 512:640], lhsT=A_kpT[rb:rb + 64, ctile],
                    rhs=A_qpT[rb:rb + 64, ctile], start=True, stop=True),
                    waits=(k_done, qp_done, fp.get("a0_free")) + g0, inc="t")
                mm_al = None
                for lvl in range(4):
                    mm_al = P.op("tensor", lambda e, s, rb=rb, lvl=lvl, ctile=ctile: e.matmul(
                        out=B.PSB[:, 128 * lvl:128 * (lvl + 1)],
                        lhsT=A_kpLT[rb:rb + 64, lvl, :],
                        rhs=A_qpT[rb:rb + 64, ctile], start=True, stop=True),
                        waits=(kplt_done, qp_done, fp.get("al_free")) + g0 if lvl == 0 else (),
                        inc="t" if lvl == 3 else None)
                t_m0 = P.op("vector", lambda e, s: e.tensor_tensor(
                    out=A_am[:, 0, 0:128], in0=B.PSB[:, 512:640], in1=triu,
                    op=ALU.mult), waits=(mm_a0, fp.get("am0p_free")))
                t_ml = P.op("vector", lambda e, s: e.tensor_tensor(
                    out=A_am[:, 1, :].rearrange("p (a b) -> p a b", b=128),
                    in0=B.PSB[:, 0:512].rearrange("p (a b) -> p a b", b=128),
                    in1=triu[:, None, :].broadcast_to([128, 4, 128]),
                    op=ALU.mult), waits=(mm_al, fp.get("aml_free")))
                mm_s0 = P.op("tensor", lambda e, s, h2=h2, c=c: e.matmul(
                    out=B.PSB[0:64, 1536 + 272:1536 + 337],
                    lhsT=A_kpN[:, h2, c, :], rhs=A_vN[:, h2, c, :],
                    start=True, stop=True),
                    waits=(kpn_done, fp.get("s0_free")) + g0, inc="t")
                mm_sl = None
                for lvl in range(4):
                    mm_sl = P.op("tensor", lambda e, s, h2=h2, lvl=lvl: e.matmul(
                        out=B.PSB[0:64, 1536 + 66 * lvl:1536 + 66 * lvl + 65],
                        lhsT=A_kpLN[:, lvl, h2, :], rhs=A_mvN[:, lvl, h2, :],
                        start=True, stop=True),
                        waits=(p4, ev_mv, fp.get("sl_free")) + g0 if lvl == 0 else (),
                        inc="t" if lvl == 3 else None)
                mm_p0 = P.op("tensor", lambda e, s, h2=h2, c=c: e.matmul(
                    out=B.PSB[0:65, 640:768], lhsT=A_vN[:, h2, c, :],
                    rhs=A_am[:, 0, 0:128], start=True, stop=(c == 0)),
                    waits=(t_m0, kpn_done, fp.get("pv0_free")),
                    inc="t" if c == 0 else None)
                if c > 0:
                    mm_p0 = P.op("tensor", lambda e, s, rb=rb, ctile=ctile: e.matmul(
                        out=B.PSB[0:65, 640:768],
                        lhsT=A_stB[rb:rb + 64, 0:65],
                        rhs=A_qpT[rb:rb + 64, ctile], start=False, stop=True),
                        waits=(fp.get("stB_tok"),), inc="t")
                mm_pl = None
                for lvl in range(4):
                    mm_pl = P.op("tensor", lambda e, s, h2=h2, lvl=lvl, c=c: e.matmul(
                        out=B.PSB[0:65, 1024 + 128 * lvl:1024 + 128 * (lvl + 1)],
                        lhsT=A_mvN[:, lvl, h2, :],
                        rhs=A_am[:, 1, 128 * lvl:128 * (lvl + 1)],
                        start=True, stop=(c == 0)),
                        waits=(t_ml, ev_mv, fp.get("pvl_free")) if lvl == 0 else (),
                        inc="t" if (c == 0 and lvl == 3) else None)
                    if c > 0:
                        mm_pl = P.op("tensor", lambda e, s, rb=rb, lvl=lvl, ctile=ctile: e.matmul(
                            out=B.PSB[0:65, 1024 + 128 * lvl:1024 + 128 * (lvl + 1)],
                            lhsT=A_stB[rb:rb + 64, 65 * (lvl + 1):65 * (lvl + 2)],
                            rhs=A_qpT[rb:rb + 64, ctile], start=False,
                            stop=True),
                            waits=(fp.get("stB_tok"),) if lvl == 0 else (),
                            inc="t" if lvl == 3 else None)
                t_n0 = P.op("vector", lambda e, s: e.tensor_scalar(
                    out=R_n[0:1, 512:640], in0=B.PSB[64:65, 640:768],
                    scalar1=cvf(B, "linattn_eps", (0, 1)), scalar2=None,
                    op0=ALU.max), waits=(mm_p0,))
                t_nl = P.op("vector", lambda e, s: e.tensor_scalar(
                    out=R_n[0:1, 0:512], in0=B.PSB[64:65, 1024:1536],
                    scalar1=cvf(B, "linattn_eps", (0, 1)), scalar2=None,
                    op0=ALU.max), waits=(mm_pl,))
                P.op("vector", lambda e, s: e.reciprocal(
                    out=R_n[0:1, 0:640], in_=R_n[0:1, 0:640]))
                t_rb2 = P.op("vector", lambda e, s: e.tensor_copy(
                    out=RB_n[0:1, 0:640], in_=R_n[0:1, 0:640]))
                mm_bl = P.op("tensor", lambda e, s: e.matmul(
                    out=B.PSA[0:64, 512:1024], lhsT=ones_row[:, 0:64],
                    rhs=RB_n[0:1, 0:512], start=True, stop=True),
                    waits=(t_rb2, fp.get("bl_free")), inc="t")
                mm_b0 = P.op("tensor", lambda e, s: e.matmul(
                    out=B.PSB[0:64, 768:896], lhsT=ones_row[:, 0:64],
                    rhs=RB_n[0:1, 512:640], start=True, stop=True),
                    waits=(t_rb2,), inc="t")
                rc_l = P.op("scalar", lambda e, s: e.copy(
                    out=A_rcl[0:64, :], in_=B.PSA[0:64, 512:1024]),
                    waits=(mm_bl, fp.get("rcl_free")))
                rc_0 = P.op("scalar", lambda e, s: e.copy(
                    out=A_rc0[0:64, :], in_=B.PSB[0:64, 768:896]),
                    waits=(mm_b0, fp.get("rc0_free")))
                g1 = P.op("vector", lambda e, s: e.tensor_tensor(
                    out=A_gtmp[0:64, 1, :], in0=B.PSB[0:64, 1024:1536],
                    in1=A_rcl[0:64, :], op=ALU.mult),
                    waits=(rc_l, fp.get("gtmp_free")))
                g2 = P.op("vector", lambda e, s: e.tensor_tensor(
                    out=A_gtmp[0:64, 0, 0:256], in0=A_gtmp[0:64, 1, 0:256],
                    in1=A_gtmp[0:64, 1, 256:512], op=ALU.add))
                g3 = P.op("vector", lambda e, s: e.tensor_tensor(
                    out=A_gtmp[0:64, 0, 256:384], in0=A_gtmp[0:64, 0, 0:128],
                    in1=A_gtmp[0:64, 0, 128:256], op=ALU.add))
                g4 = P.op("vector", lambda e, s: e.tensor_tensor(
                    out=A_gtmp[0:64, 0, 384:512], in0=B.PSB[0:64, 640:768],
                    in1=A_rc0[0:64, :], op=ALU.mult), waits=(rc_0,))
                g5 = P.op("vector", lambda e, s, rb=rb, ctile=ctile: e.tensor_tensor(
                    out=A_gT[rb:rb + 64, ctile], in0=A_gtmp[0:64, 0, 256:384],
                    in1=A_gtmp[0:64, 0, 384:512], op=ALU.add),
                    waits=(fp.get("gT_free"),))
                s1 = P.op("vector", lambda e, s, rb=rb: e.tensor_tensor(
                    out=A_stF[rb:rb + 64, 260:325], in0=A_stF[rb:rb + 64, 260:325],
                    in1=B.PSB[0:64, 1536 + 272:1536 + 337], op=ALU.add),
                    waits=(mm_s0, st_z1))
                s2 = P.op("vector", lambda e, s, rb=rb: e.tensor_tensor(
                    out=A_stF[rb:rb + 64, 0:260].rearrange("p (a b) -> p a b", b=65),
                    in0=A_stF[rb:rb + 64, 0:260].rearrange("p (a b) -> p a b", b=65),
                    in1=B.PSB[0:64, 1536:1536 + 264].rearrange(
                        "p (a b) -> p a b", b=66)[:, :, 0:65], op=ALU.add),
                    waits=(mm_sl,))
                s3 = P.op("vector", lambda e, s, rb=rb: e.tensor_copy(
                    out=A_stB[rb:rb + 64, 260:325], in_=A_stF[rb:rb + 64, 260:325]),
                    waits=(mm_p0, st_z2))
                s4 = P.op("vector", lambda e, s, rb=rb: e.tensor_copy(
                    out=A_stB[rb:rb + 64, 0:260], in_=A_stF[rb:rb + 64, 0:260]),
                    waits=(mm_pl,))
                fp["gtmp_free"] = g5
                fp["bl_free"] = rc_l
                fp["rcl_free"] = g1
                fp["rc0_free"] = g4
                fp["pvl_free"] = g1
                fp["pv0_free"] = g4
                fp["a0_free"] = g4
                fp["al_free"] = t_ml
                fp["am0p_free"] = mm_p0
                fp["aml_free"] = mm_pl
                fp["mk_free"] = rc_l
                fp["mv_free"] = g4
                fp["s0_free"] = s2
                fp["sl_free"] = s2
                if h2 == 1:
                    fp["stB_tok"] = s4
                    fp["kpLN_free2"] = mm_sl
                    fp["mvN_free"] = mm_pl
                    fp["kpLT_free"] = mm_al
                glob_done = g5
        fp["st_free"] = s4
        fp["kpLN_free"] = kplt_done
        fp["psb_free"] = s4
        for ch in range(4):
            fp[("pevac", ch)] = s4

        dtoks = []
        for cc in range(4):
            csl = slice(512 * cc, 512 * (cc + 1))
            dtoks.append(P.op("vector", lambda e, s, csl=csl: e.tensor_tensor(
                out=A_locT[:, csl], in0=A_locT[:, csl], in1=A_gT[:, csl],
                op=ALU.subtract), waits=(loc_done, glob_done)))
        dma_d = P.dma("sync", B.l1_diff[csl0, :], A_locT[:], waits=tuple(dtoks),
                      sem="d_st0")
        dma_g = P.dma("sync", B.l1_glob[csl0, :], A_gT[:], waits=(glob_done,) + tuple(dtoks),
                      sem="d_st1")
        fp["qT_free"] = loc_done
        fp["qpT_free"] = mm_pl
        fp["kT_free"] = fp.get("t2k_free", loc_done)
        fp["kpT_free"] = mm_a0
        fp["klT_free"] = loc_done
        fp["vT_free"] = fp.get("t2v_free", loc_done)
        fp["vN_free"] = mm_p0
        fp["vlN_free"] = loc_done
        fp["locT_free"] = dma_d
        fp["gT_free"] = dma_g
        diff_done = dma_d
        glob_dma = dma_g

    B.l1_diff_done = diff_done
    B.l1_glob_done = glob_dma
    ar.release()
    if L1_PARTS >= 1:
        emit_block1_gate(P, B)
    else:
        B.tok_s1 = diff_done


def emit_block1_gate(P, B):
    ar = B.ar
    ar.mark()
    ones_row = cbf(B, "ones_row", rows=(0, 1))
    A_wg = ar.alloc3(NTT, 1024)
    A_wo = [ar.alloc3(NDT, 128), ar.alloc3(NDT, 128)]
    A_df = ar.alloc3(NDT, 512)
    A_gl = ar.alloc3(NDT, 512)
    A_gh = ar.alloc(512)
    A_alB = ar.alloc(512)
    A_wgo = ar.alloc(8)
    R_al = ar.rowf(512)
    RB_al = ar.rowb(512)

    wg_tok = P.dma("gpsimd", A_wg[:],
                   B.wh["L1_WgT"].rearrange("(it p) c -> p it c", p=128),
                   waits=(B.tok_xn, B.l1_diff_done, B.l1_glob_done),
                   sem="d_w2")
    t_wgo = P.op("vector", lambda e, s: e.tensor_copy(
        out=A_wgo[:], in_=B.SMALL[:, 16:24]),
        waits=(("d_small", P.cnt["d_small"]),))
    fp = {}
    s1_done = None
    for tcol in range(4):
        tsl = slice(512 * tcol, 512 * (tcol + 1))
        d_df = P.dma("sync", A_df[:],
                     B.l1_diff.rearrange("(dt p) t -> p dt t", p=128)[:, :, tsl],
                     waits=(B.l1_diff_done, fp.get("df_free")), sem="d_st0")
        d_gl = P.dma("sync", A_gl[:],
                     B.l1_glob.rearrange("(dt p) t -> p dt t", p=128)[:, :, tsl],
                     waits=(B.l1_glob_done, fp.get("gl_free")), sem="d_st1")
        mm_al = None
        for od in range(8):
            psum = B.PSA[:, 512 * (od % 2):512 * (od % 2) + 512]
            for it in range(16):
                rhs_is_x = it < 8
                mm = P.op("tensor", lambda e, s, od=od, it=it, psum=psum, tsl=tsl, rhs_is_x=rhs_is_x: e.matmul(
                    out=psum, lhsT=A_wg[:, it, 128 * od:128 * (od + 1)],
                    rhs=(B.XN[:, it, tsl] if rhs_is_x else A_df[:, it - 8, :]),
                    start=(it == 0), stop=(it == 15)),
                    waits=(wg_tok, d_df, B.tok_xn, fp.get(("gev", od % 2))) if it == 0 else (),
                    inc="t" if it == 15 else None)
            ev = P.op("scalar", lambda e, s, od=od, psum=psum: e.activation(
                out=A_gh[:], in_=psum, func=AF.Silu,
                bias=B.SMALL[:, od:od + 1]),
                waits=(mm, ("d_small", P.cnt["d_small"]), fp.get("gh_free")))
            fp[("gev", od % 2)] = ev
            apb = B.PSB[0:1, 512 * tcol:512 * tcol + 512]
            mm_al = P.op("tensor", lambda e, s, od=od, apb=apb: e.matmul(
                out=apb, lhsT=A_wgo[:, od:od + 1], rhs=A_gh[:],
                start=(od == 0), stop=(od == 7)),
                waits=(ev, t_wgo), inc="t" if od == 7 else None)
            fp["gh_free"] = mm_al
        t_al = P.op("scalar", lambda e, s, tcol=tcol: e.activation(
            out=R_al, in_=B.PSB[0:1, 512 * tcol:512 * tcol + 512],
            func=AF.Sigmoid, bias=B.SMALL[0:1, 25:26]), waits=(mm_al,))
        t_ab = P.op("vector", lambda e, s: e.tensor_copy(out=RB_al, in_=R_al),
                    waits=(t_al,))
        mm_ab = P.op("tensor", lambda e, s: e.matmul(
            out=B.PSA[:, 1024:1536], lhsT=ones_row, rhs=RB_al,
            start=True, stop=True),
            waits=(t_ab, fp.get("ab_free"), fp.get(("sev", 0))), inc="t")
        t_alB = P.op("scalar", lambda e, s: e.copy(out=A_alB[:], in_=B.PSA[:, 1024:1536]),
                     waits=(mm_ab, fp.get("alB_free")))
        mix_done = []
        for dt_ in range(NDT):
            P.op("vector", lambda e, s, dt_=dt_: e.tensor_tensor(
                out=A_df[:, dt_, :], in0=A_df[:, dt_, :], in1=A_alB[:],
                op=ALU.mult), waits=(t_alB,) if dt_ == 0 else ())
            mix_done.append(P.op("vector", lambda e, s, dt_=dt_: e.tensor_tensor(
                out=A_df[:, dt_, :], in0=A_df[:, dt_, :], in1=A_gl[:, dt_, :],
                op=ALU.add), waits=(d_gl,) if dt_ == 0 else ()))
        fp["gl_free"] = mix_done[-1]
        for od in range(NDT):
            wo_tok = P.dma("gpsimd", A_wo[od % 2][:],
                           wslice_src(B.wh["L1_WoT"], 128 * od, 128 * (od + 1)),
                           waits=(fp.get(("wo_free", od % 2)), B.l1_glob_done),
                           sem="d_w" + str(od % 2))
            psum = B.PSA[:, 1024 + 512 * (od % 2):1024 + 512 * (od % 2) + 512]
            for it in range(NDT):
                mm = P.op("tensor", lambda e, s, od=od, it=it, psum=psum: e.matmul(
                    out=psum, lhsT=A_wo[od % 2][:, it, :], rhs=A_df[:, it, :],
                    start=(it == 0), stop=(it == NDT - 1)),
                    waits=(wo_tok, mix_done[-1], t_alB, fp.get(("sev", od % 2))) if it == 0 else (),
                    inc="t" if it == NDT - 1 else None)
            sev = P.op("scalar", lambda e, s, od=od, psum=psum, tsl=tsl: e.activation(
                out=B.ST[:, od, tsl], in_=psum, func=AF.Identity,
                bias=B.SMALL[:, 8 + od:9 + od]), waits=(mm,))
            fp[("sev", od % 2)] = sev
            fp[("wo_free", od % 2)] = mm
            s1_done = sev
        fp["df_free"] = mm
        fp["ab_free"] = t_alB
        fp["alB_free"] = mm
    B.tok_s1 = s1_done
    ar.release()


def esl_src(hap, ei, c0, c1):
    return hap[ei, :, c0:c1].rearrange("(a p) c -> p a c", p=128)


def emit_block1_ffn(P, B):
    ar = B.ar
    ar.mark()
    eye = cbf(B, "eye")
    A_gw = ar.alloc3(NDT, 6)
    P_nat = ar.alloc(192).bitcast(F32).rearrange("p (a b) -> p a b", b=6)
    CF = ar.alloc(192).bitcast(F32).rearrange("p (a b) -> p a b", b=6)
    M1 = ar.alloc(32).bitcast(F32)
    M2 = ar.alloc(32).bitcast(F32)
    DEN = ar.alloc(32).bitcast(F32)
    GE_bf = ar.alloc(96).rearrange("p (a b) -> p a b", b=6)
    A_geT = ar.alloc(T)
    A_geB = ar.alloc(1024)
    A_h = ar.alloc3(NDT, 1024)
    A_y = ar.alloc(16384).bitcast(F32).rearrange("p (a b) -> p a b", b=1024)
    A_we1 = [ar.alloc3(NDT, 128), ar.alloc3(NDT, 128)]
    A_we3 = [ar.alloc3(NDT, 128), ar.alloc3(NDT, 128)]
    A_w2 = [ar.alloc3(NDT, 128), ar.alloc3(NDT, 128)]
    A_sil = ar.alloc(512)
    A_gf = ar.alloc(512)

    g_tok = P.dma("gpsimd", A_gw[:],
                  B.wh["L1_gateT"].rearrange("(a p) c -> p a c", p=128),
                  waits=(B.tok_xn, B.tok_s1), sem="d_w3")
    fp = {}
    rt_toks = []
    for tt in range(NTT):
        psum = B.PSB[:, 512 * (tt % 2):512 * (tt % 2) + 6]
        for it in range(NDT):
            mm = P.op("tensor", lambda e, s, tt=tt, it=it, psum=psum: e.matmul(
                out=psum, lhsT=B.XN[:, it, 128 * tt:128 * (tt + 1)],
                rhs=A_gw[:, it, :], start=(it == 0), stop=(it == NDT - 1)),
                waits=(g_tok, B.tok_xn, B.tok_s1, fp.get(("lg", tt % 2))) if it == 0 else (),
                inc="t" if it == NDT - 1 else None)
        t1 = P.op("vector", lambda e, s, tt=tt, psum=psum: e.tensor_tensor(
            out=P_nat[:, tt, :], in0=psum, in1=B.SMALL[:, 40:46], op=ALU.add),
            waits=(mm, ("d_small", P.cnt["d_small"])))
        t2 = P.op("scalar", lambda e, s, tt=tt: e.activation(
            out=P_nat[:, tt, :], in_=P_nat[:, tt, :], func=AF.Sigmoid),
            waits=(t1,))
        fp[("lg", tt % 2)] = t2
        c1 = P.op("vector", lambda e, s, tt=tt: e.tensor_reduce(
            out=M1[:, tt:tt + 1], in_=P_nat[:, tt, :], axis=AX.X, op=ALU.max),
            waits=(t2,))
        P.op("vector", lambda e, s, tt=tt: e.tensor_scalar(
            out=CF[:, tt, :], in0=P_nat[:, tt, :], scalar1=M1[:, tt:tt + 1],
            scalar2=None, op0=ALU.is_equal))
        P.op("vector", lambda e, s, tt=tt: e.scalar_tensor_tensor(
            out=CF[:, tt, :], in0=CF[:, tt, :], scalar=-1e30,
            in1=P_nat[:, tt, :], op0=ALU.mult, op1=ALU.add))
        P.op("vector", lambda e, s, tt=tt: e.tensor_reduce(
            out=M2[:, tt:tt + 1], in_=CF[:, tt, :], axis=AX.X, op=ALU.max))
        P.op("vector", lambda e, s, tt=tt: e.tensor_scalar(
            out=CF[:, tt, :], in0=P_nat[:, tt, :], scalar1=M2[:, tt:tt + 1],
            scalar2=None, op0=ALU.is_ge))
        P.op("vector", lambda e, s, tt=tt: e.tensor_scalar(
            out=DEN[:, tt:tt + 1], in0=M1[:, tt:tt + 1],
            scalar1=M2[:, tt:tt + 1], scalar2=1e-8, op0=ALU.add, op1=ALU.add))
        P.op("vector", lambda e, s, tt=tt: e.reciprocal(
            out=DEN[:, tt:tt + 1], in_=DEN[:, tt:tt + 1]))
        P.op("vector", lambda e, s, tt=tt: e.tensor_tensor(
            out=CF[:, tt, :], in0=CF[:, tt, :], in1=P_nat[:, tt, :],
            op=ALU.mult))
        rt_toks.append(P.op("vector", lambda e, s, tt=tt: e.tensor_scalar(
            out=GE_bf[:, tt, :], in0=CF[:, tt, :], scalar1=DEN[:, tt:tt + 1],
            scalar2=None, op0=ALU.mult)))
    geT_done = None
    for tt in range(NTT):
        bank = B.PSB[:, 512 * (tt % 2) + 256:512 * (tt % 2) + 320].bitcast(BF16)
        mm_t = P.op("tensor", lambda e, s, tt=tt, bank=bank: e.transpose(
            out=bank[0:6, 0:128], in_=GE_bf[:, tt, :], identity=eye),
            waits=(rt_toks[-1], fp.get(("get", tt % 2))), inc="t")
        geT_done = P.op("scalar", lambda e, s, tt=tt, bank=bank: e.copy(
            out=A_geT[0:6, 128 * tt:128 * (tt + 1)], in_=bank[0:6, 0:128]),
            waits=(mm_t,))
        fp[("get", tt % 2)] = geT_done

    WN = {"w1": ("L1_ew1T", "L1_fc1T"), "w3": ("L1_ew3T", "L1_fc2T"),
          "w2": ("L1_ew2T", "L1_fc3T")}

    def wsrc(kind, ei, c0, c1):
        en, sn = WN[kind]
        if ei == 6:
            return wslice_src(B.wh[sn], c0, c1)
        return esl_src(B.wh[en], ei, c0, c1)

    x_done = None
    for tch in range(2):
        t0 = 1024 * tch
        for ei in range(7):
            shared = ei == 6
            geb_ev = []
            if not shared:
                for hh in range(2):
                    gps = B.PSB[:, 1024 + 512 * hh:1024 + 512 * (hh + 1)]
                    mm_g = P.op("tensor", lambda e, s, ei=ei, hh=hh, t0=t0, gps=gps: e.matmul(
                        out=gps, lhsT=cbf(B, "sel6")[0:6, 128 * ei:128 * (ei + 1)],
                        rhs=A_geT[0:6, t0 + 512 * hh:t0 + 512 * (hh + 1)],
                        start=True, stop=True),
                        waits=(geT_done, fp.get(("gebf", hh))), inc="t")
                    ev = P.op("scalar", lambda e, s, hh=hh, gps=gps: e.copy(
                        out=A_geB[:, 512 * hh:512 * (hh + 1)], in_=gps),
                        waits=(mm_g, fp.get(("gebu", hh))))
                    geb_ev.append(ev)
                    fp[("gebf", hh)] = ev
            h_done = None
            for it in range(NDT):
                bi = it % 2
                w1t = P.dma("gpsimd", A_we1[bi][:], wsrc("w1", ei, 128 * it, 128 * (it + 1)),
                            waits=(fp.get(("w1f", bi)), B.tok_s1),
                            sem=("d_w0", "d_w4")[bi])
                w3t = P.dma("gpsimd", A_we3[bi][:], wsrc("w3", ei, 128 * it, 128 * (it + 1)),
                            waits=(fp.get(("w3f", bi)), B.tok_s1),
                            sem=("d_w1", "d_w5")[bi])
                for hh in range(2):
                    ps1 = B.PSA[:, 512 * hh:512 * (hh + 1)]
                    ps3 = B.PSA[:, 1024 + 512 * hh:1024 + 512 * (hh + 1)]
                    for kt in range(NDT):
                        mm1 = P.op("tensor", lambda e, s, bi=bi, kt=kt, hh=hh, t0=t0, ps1=ps1: e.matmul(
                            out=ps1, lhsT=A_we1[bi][:, kt, :],
                            rhs=B.XN[:, kt, t0 + 512 * hh:t0 + 512 * (hh + 1)],
                            start=(kt == 0), stop=(kt == NDT - 1)),
                            waits=(w1t, B.tok_s1, fp.get(("h1f", hh))) if kt == 0 else (),
                            inc="t" if kt == NDT - 1 else None)
                    for kt in range(NDT):
                        mm3 = P.op("tensor", lambda e, s, bi=bi, kt=kt, hh=hh, t0=t0, ps3=ps3: e.matmul(
                            out=ps3, lhsT=A_we3[bi][:, kt, :],
                            rhs=B.XN[:, kt, t0 + 512 * hh:t0 + 512 * (hh + 1)],
                            start=(kt == 0), stop=(kt == NDT - 1)),
                            waits=(w3t, fp.get(("h3f", hh))) if kt == 0 else (),
                            inc="t" if kt == NDT - 1 else None)
                    sil = P.op("scalar", lambda e, s, ps1=ps1: e.activation(
                        out=A_sil[:], in_=ps1, func=AF.Silu),
                        waits=(mm1, fp.get("silf")))
                    hv = P.op("vector", lambda e, s, it=it, hh=hh, ps3=ps3: e.tensor_tensor(
                        out=A_h[:, it, 512 * hh:512 * (hh + 1)], in0=A_sil[:],
                        in1=ps3, op=ALU.mult),
                        waits=(sil, mm3, fp.get(("hfree", it, hh))))
                    fp[("h1f", hh)] = hv
                    fp[("h3f", hh)] = hv
                    fp["silf"] = hv
                    if not shared:
                        hv = P.op("vector", lambda e, s, it=it, hh=hh: e.tensor_tensor(
                            out=A_h[:, it, 512 * hh:512 * (hh + 1)],
                            in0=A_h[:, it, 512 * hh:512 * (hh + 1)],
                            in1=A_geB[:, 512 * hh:512 * (hh + 1)],
                            op=ALU.mult), waits=(geb_ev[hh],))
                    h_done = hv
                fp[("w1f", bi)] = mm1
                fp[("w3f", bi)] = mm3
            for dt_ in range(NDT):
                bi = dt_ % 2
                w2t = P.dma("gpsimd", A_w2[bi][:], wsrc("w2", ei, 128 * dt_, 128 * (dt_ + 1)),
                            waits=(fp.get(("w2f", bi)), B.tok_s1),
                            sem=("d_w2", "d_w6")[bi])
                for hh in range(2):
                    psy = B.PSB[:, 512 * hh:512 * (hh + 1)]
                    for it in range(NDT):
                        mm = P.op("tensor", lambda e, s, bi=bi, it=it, hh=hh, psy=psy: e.matmul(
                            out=psy, lhsT=A_w2[bi][:, it, :],
                            rhs=A_h[:, it, 512 * hh:512 * (hh + 1)],
                            start=(it == 0), stop=(it == NDT - 1)),
                            waits=(w2t, h_done, geT_done, fp.get(("yev", hh))) if it == 0 else (),
                            inc="t" if it == NDT - 1 else None)
                    if ei == 0:
                        yev = P.op("scalar", lambda e, s, dt_=dt_, hh=hh, psy=psy: e.copy(
                            out=A_y[:, dt_, 512 * hh:512 * (hh + 1)], in_=psy),
                            waits=(mm, fp.get(("yw", dt_, hh))))
                    else:
                        yev = P.op("vector", lambda e, s, dt_=dt_, hh=hh, psy=psy: e.tensor_tensor(
                            out=A_y[:, dt_, 512 * hh:512 * (hh + 1)],
                            in0=A_y[:, dt_, 512 * hh:512 * (hh + 1)], in1=psy,
                            op=ALU.add), waits=(mm,))
                    fp[("yev", hh)] = yev
                    fp[("yv", dt_, hh)] = yev
                fp[("w2f", bi)] = mm
            for it in range(NDT):
                for hh in range(2):
                    fp[("hfree", it, hh)] = mm
        for dt_ in range(NDT):
            for hh in range(2):
                tq = slice(t0 + 512 * hh, t0 + 512 * (hh + 1))
                g1 = P.op("scalar", lambda e, s, dt_=dt_, hh=hh: e.activation(
                    out=A_gf[:], in_=A_y[:, dt_, 512 * hh:512 * (hh + 1)],
                    func=AF.Silu),
                    waits=(fp[("yv", dt_, hh)], fp.get("gff")))
                g2 = P.op("vector", lambda e, s, dt_=dt_, tq=tq: e.tensor_tensor(
                    out=A_gf[:], in0=A_gf[:], in1=B.ST[:, dt_, tq],
                    op=ALU.mult), waits=(g1, B.tok_s1))
                x_done = P.op("vector", lambda e, s, dt_=dt_, tq=tq: e.tensor_tensor(
                    out=B.XT[:, dt_, tq], in0=B.XT[:, dt_, tq], in1=A_gf[:],
                    op=ALU.add))
                fp["gff"] = x_done
                fp[("yw", dt_, hh)] = x_done
    B.tok_x_final = x_done
    ar.release()


def emit_block2(P, B):
    ar = B.ar
    ar.mark()
    gamma = [1.0 - 2.0 ** (-5.0 - h) for h in range(H)]

    tok_b1 = P.dma("sync", B.SMALL[:, 0:32], B.wh["L2_b1_col"], sem="d_small",
                   waits=(B.tok_xn,))
    tok_th = P.dma("sync", B.SMALL[:, 32:64], B.wh["L2_theta_col"], sem="d_small")
    tok_wr = P.dma("sync", B.SMALL[:, 64:96], B.wh["L2_wr_col"], sem="d_small")
    tok_wi = P.dma("sync", B.SMALL[:, 96:128], B.wh["L2_wi_col"], sem="d_small")
    tok_b2 = P.dma("sync", B.SMALL[:, 128:136], B.wh["L2_b2_col"], sem="d_small")
    B.tok_l2small = tok_b2

    A_qw = ar.alloc3(NDT, 256)
    A_kw = ar.alloc3(NDT, 256)
    A_vw = ar.alloc3(NDT, 256)
    A_gw = ar.alloc3(NDT, 256)
    A_ow = ar.alloc3(2, 1024)
    A_qT = ar.alloc3(2, T)
    A_kT = ar.alloc3(2, T)
    A_gT = ar.alloc3(2, T)
    A_vx = ar.alloc4(NTT, 4, 64)
    A_kn = ar.alloc(256)
    A_pm = ar.alloc3(4, 128)
    A_S = ar.alloc(256).bitcast(F32)
    A_Sb = ar.alloc(256)
    A_rt = ar.alloc(512).bitcast(F32)
    A_cos = ar.alloc(T)
    A_sin = ar.alloc(T)
    A_qg = ar.alloc3(2, 128)
    A_kg = ar.alloc3(2, 128)
    A_dmv = ar.alloc3(4, 128)
    tok_cos = P.dma("gpsimd", A_cos[:], chbm(B, "rope_cos"), sem="d_gr",
                    waits=(B.tok_xn,))
    tok_sin = P.dma("gpsimd", A_sin[:], chbm(B, "rope_sin"), sem="d_gr")

    prev = {}
    p_done = None
    for hg in range(4):
        c0 = hg * 256
        wsem = f"d_w{hg % 2}"
        w_q = P.dma("gpsimd", A_qw[:], wslice_src(B.wh["L2_wqT"], c0, c0 + 256),
                    waits=(prev.get("q_done"), B.tok_xn), sem=wsem)
        w_k = P.dma("gpsimd", A_kw[:], wslice_src(B.wh["L2_wkT"], c0, c0 + 256),
                    waits=(prev.get("k_done"),), sem=wsem)
        w_v = P.dma("gpsimd", A_vw[:], wslice_src(B.wh["L2_wvT"], c0, c0 + 256),
                    waits=(prev.get("v_done"),), sem=wsem)
        w_g = P.dma("gpsimd", A_gw[:], wslice_src(B.wh["L2_wgT"], c0, c0 + 256),
                    waits=(prev.get("g_done"),), sem=wsem)
        w_o = P.dma("gpsimd", A_ow[:],
                    B.wh["L2_woT"][c0:c0 + 256, :].rearrange("(it p) c -> p it c", p=128),
                    waits=(p_done,), sem=wsem)
        tok_con = None
        for oc in range(2):
            tok_con = P.dma("gpsimd", A_qg[:, oc, :], chbm(B, f"l2_qg_{2 * hg + oc}"),
                            sem="d_gr", waits=(prev.get("con_used"),))
            tok_con = P.dma("gpsimd", A_kg[:, oc, :], chbm(B, f"l2_kg_{2 * hg + oc}"),
                            sem="d_gr")
        for hs in range(4):
            h_ = 4 * hg + 2 * (hs // 2) + (hs % 2)
            tok_con = P.dma("gpsimd", A_dmv[:, hs, :], chbm(B, f"l2_decmv_{h_}"),
                            sem="d_gr")

        def proj2(wview, dst, wtok, act=None):
            last = None
            for oc in range(2):
                for ch in range(4):
                    psum = B.PSA[:, 512 * ch:512 * (ch + 1)]
                    for it in range(NDT):
                        mm = P.op("tensor", lambda e, s, oc=oc, ch=ch, it=it, wview=wview, psum=psum: e.matmul(
                            out=psum, lhsT=wview[:, it, 128 * oc:128 * (oc + 1)],
                            rhs=B.XN[:, it, 512 * ch:512 * (ch + 1)],
                            start=(it == 0), stop=(it == NDT - 1)),
                            waits=(wtok, B.tok_xn, prev.get(("evac", ch))) if it == 0 else (),
                            inc="t" if it == NDT - 1 else None)
                    if act is not None:
                        ev = P.op("scalar", lambda e, s, oc=oc, ch=ch, dst=dst, psum=psum, act=act: e.activation(
                            out=dst[:, oc, 512 * ch:512 * (ch + 1)], in_=psum, func=act),
                            waits=(mm,))
                    else:
                        ev = P.op("scalar", lambda e, s, oc=oc, ch=ch, dst=dst, psum=psum: e.copy(
                            out=dst[:, oc, 512 * ch:512 * (ch + 1)], in_=psum),
                            waits=(mm,))
                    prev[("evac", ch)] = ev
                    last = ev
            return last

        q_done = proj2(A_qw, A_qT, w_q)
        k_done = proj2(A_kw, A_kT, w_k)
        g_done = proj2(A_gw, A_gT, w_g, act=AF.Silu)

        v_done = None
        for tt in range(NTT):
            psum = B.PSB[:, 1024 + 512 * (tt % 2):1024 + 512 * (tt % 2) + 256]
            for it in range(NDT):
                mm = P.op("tensor", lambda e, s, tt=tt, it=it, psum=psum: e.matmul(
                    out=psum, lhsT=B.XN[:, it, 128 * tt:128 * (tt + 1)],
                    rhs=A_vw[:, it, :], start=(it == 0), stop=(it == NDT - 1)),
                    waits=(w_v, B.tok_xn, prev.get(("vevac", tt % 2)), prev.get("ret_done")) if it == 0 else (),
                    inc="t" if it == NDT - 1 else None)
            ev = P.op("vector", lambda e, s, tt=tt, psum=psum: e.tensor_copy(
                out=A_vx[:, tt, :, :],
                in_=psum.rearrange("p (a b) -> p a b", b=64)), waits=(mm,))
            prev[("vevac", tt % 2)] = ev
            v_done = ev

        rope_done = {}
        for (srcT, src_done, tag) in ((A_qT, q_done, "q"), (A_kT, k_done, "k")):
            lastt = None
            for oc in range(2):
                for ch in range(8):
                    tsl = slice(256 * ch, 256 * (ch + 1))
                    mm = P.op("tensor", lambda e, s, oc=oc, tsl=tsl, srcT=srcT: e.matmul(
                        out=B.PSA[:, 1536:1792], lhsT=cbf(B, "rope_rotT"),
                        rhs=srcT[:, oc, tsl], start=True, stop=True),
                        waits=(src_done, g_done, prev.get("rope_ps")), inc="t")
                    t1 = P.op("vector", lambda e, s, tsl=tsl: e.tensor_tensor(
                        out=A_rt[:, 0:256], in0=B.PSA[:, 1536:1792],
                        in1=A_sin[:, tsl], op=ALU.mult), waits=(mm, tok_sin))
                    prev["rope_ps"] = t1
                    P.op("vector", lambda e, s, oc=oc, tsl=tsl, srcT=srcT: e.tensor_tensor(
                        out=srcT[:, oc, tsl], in0=srcT[:, oc, tsl],
                        in1=A_cos[:, tsl], op=ALU.mult), waits=(tok_cos,))
                    lastt = P.op("vector", lambda e, s, oc=oc, tsl=tsl, srcT=srcT: e.tensor_tensor(
                        out=srcT[:, oc, tsl], in0=srcT[:, oc, tsl],
                        in1=A_rt[:, 0:256], op=ALU.add))
            rope_done[tag] = lastt

        fold_done = {}
        for (srcT, ctab, dep, tag) in ((A_qT, A_qg, rope_done["q"], "q"),
                                       (A_kT, A_kg, rope_done["k"], "k")):
            lastt = None
            for oc in range(2):
                cc_ = ctab[:, oc, :]
                lastt = P.op("vector", lambda e, s, oc=oc, srcT=srcT, cc_=cc_: e.tensor_tensor(
                    out=srcT[:, oc, :].rearrange("p (c t) -> p c t", t=128),
                    in0=srcT[:, oc, :].rearrange("p (c t) -> p c t", t=128),
                    in1=cc_[:, None, :].broadcast_to([128, 16, 128]),
                    op=ALU.mult), waits=(dep, tok_con))
            fold_done[tag] = lastt

        t_s0 = P.op("vector", lambda e, s: e.memset(A_S[:, 0:128], 0.0),
                    waits=(prev.get("ret_done"),))
        t_s0b = P.op("vector", lambda e, s: e.memset(A_Sb[:, 0:256], 0.0))

        ret_done = None
        sttok = {}
        for c in range(NTT):
            csl = slice(128 * c, 128 * (c + 1))
            for hs in range(4):
                rb = 64 * (hs % 2)
                oc = hs // 2
                h = 4 * hg + 2 * oc + (hs % 2)
                bank = B.PSA[:, 512 * hs:512 * (hs + 1)]
                obank = B.PSB[0:64, 512 * hs + 128 * (c % 4):512 * hs + 128 * (c % 4) + 128]
                Sf = A_S[rb:rb + 64, 64 * oc:64 * oc + 64]
                Sb = A_Sb[rb:rb + 64, 64 * oc:64 * oc + 64]
                mm_a = P.op("tensor", lambda e, s, hs=hs, rb=rb, oc=oc, csl=csl, bank=bank: e.matmul(
                    out=bank[:, 0:128], lhsT=A_kT[rb:rb + 64, oc, csl],
                    rhs=A_qT[rb:rb + 64, oc, csl], start=True, stop=True),
                    waits=(fold_done["q"], fold_done["k"],
                           sttok.get(("stt", hs)), t_s0b, v_done), inc="t")
                t_pm = P.op("vector", lambda e, s, hs=hs, bank=bank: e.tensor_tensor(
                    out=A_pm[:, hs, :], in0=bank[:, 0:128],
                    in1=A_dmv[:, hs, :], op=ALU.mult), waits=(mm_a, tok_con))
                mm_i1 = P.op("tensor", lambda e, s, hs=hs, c=c, obank=obank: e.matmul(
                    out=obank, lhsT=A_vx[:, c, hs, :], rhs=A_pm[:, hs, :],
                    start=True, stop=False),
                    waits=(t_pm, sttok.get(("oev", hs))), inc="t")
                mm_i2 = P.op("tensor", lambda e, s, hs=hs, rb=rb, oc=oc, csl=csl, obank=obank, Sb=Sb: e.matmul(
                    out=obank, lhsT=Sb, rhs=A_qT[rb:rb + 64, oc, csl],
                    start=False, stop=True), inc="t")
                mm_t = P.op("tensor", lambda e, s, hs=hs, rb=rb, oc=oc, csl=csl, bank=bank: e.transpose(
                    out=bank[:, 256:288].bitcast(BF16)[:, 0:64],
                    in_=A_kT[rb:rb + 64, oc, csl], identity=cbf(B, "eye")[rb:rb + 64, rb:rb + 64]),
                    waits=(fold_done["k"], sttok.get(("knev", hs))), inc="t")
                knev = P.op("vector", lambda e, s, hs=hs, bank=bank: e.tensor_copy(
                    out=A_kn[:, 64 * hs:64 * hs + 64],
                    in_=bank[:, 256:288].bitcast(BF16)[:, 0:64]), waits=(mm_t,))
                sttok[("knev", hs)] = knev
                mm_s = P.op("tensor", lambda e, s, hs=hs, c=c, bank=bank: e.matmul(
                    out=bank[0:64, 384:448], lhsT=A_kn[:, 64 * hs:64 * hs + 64],
                    rhs=A_vx[:, c, hs, :], start=True, stop=True),
                    waits=(knev, v_done), inc="t")
                oev = P.op("vector", lambda e, s, hs=hs, rb=rb, oc=oc, csl=csl, obank=obank: e.tensor_tensor(
                    out=A_kT[rb:rb + 64, oc, csl], in0=obank,
                    in1=A_gT[rb:rb + 64, oc, csl], op=ALU.mult),
                    waits=(mm_i2, g_done))
                sttok[("oev", hs)] = oev
                stt_t = P.op("vector", lambda e, s, h=h, bank=bank, Sf=Sf: e.scalar_tensor_tensor(
                    out=Sf, in0=Sf, scalar=float(gamma[h] ** 128),
                    in1=bank[0:64, 384:448], op0=ALU.mult, op1=ALU.add),
                    waits=(mm_s, t_s0))
                sttok[("stt", hs)] = P.op("vector", lambda e, s, Sf=Sf, Sb=Sb: e.tensor_copy(
                    out=Sb, in_=Sf))
                ret_done = sttok[("stt", hs)]
                prev[("retoev", hs)] = oev
        last_oev = prev[("retoev", 3)]
        prev["ret_done"] = ret_done

        for oc2 in range(NDT):
            for ch in range(4):
                psum = B.PSA[:, 512 * ch:512 * (ch + 1)]
                for it2 in range(2):
                    mm = P.op("tensor", lambda e, s, oc2=oc2, ch=ch, it2=it2, psum=psum: e.matmul(
                        out=psum, lhsT=A_ow[:, it2, 128 * oc2:128 * (oc2 + 1)],
                        rhs=A_kT[:, it2, 512 * ch:512 * (ch + 1)],
                        start=(it2 == 0), stop=(it2 == 1)),
                        waits=(w_o, last_oev, prev.get(("evac", ch)), sttok.get(("stt", 3))) if it2 == 0 else (),
                        inc="t" if it2 == 1 else None)
                if hg == 0:
                    ev = P.op("scalar", lambda e, s, oc2=oc2, ch=ch, psum=psum: e.copy(
                        out=B.ST[:, oc2, 512 * ch:512 * (ch + 1)], in_=psum),
                        waits=(mm,))
                else:
                    ev = P.op("vector", lambda e, s, oc2=oc2, ch=ch, psum=psum: e.tensor_tensor(
                        out=B.ST[:, oc2, 512 * ch:512 * (ch + 1)],
                        in0=B.ST[:, oc2, 512 * ch:512 * (ch + 1)], in1=psum,
                        op=ALU.add), waits=(mm,))
                prev[("evac", ch)] = ev
                p_done = ev
        prev["q_done"] = q_done
        prev["k_done"] = k_done
        prev["v_done"] = v_done
        prev["g_done"] = g_done
        prev["con_used"] = ret_done
    B.tok_s2 = p_done
    ar.release()
    emit_block2_ffn(P, B)


def emit_block2_ffn(P, B):
    ar = B.ar
    ar.mark()
    PI_LO = 3.1415925025939941
    TWO_PI_LO = 6.2831850051879883
    DC = ar.alloc(256).bitcast(F32)

    def _reduce_passes(dst, tmp, npass):
        tok = None
        for _ in range(npass):
            P.op("vector", lambda e, s, dst=dst, tmp=tmp: e.tensor_scalar(
                out=tmp, in0=dst, scalar1=PI_LO, scalar2=None, op0=ALU.is_gt))
            P.op("vector", lambda e, s, dst=dst, tmp=tmp: e.scalar_tensor_tensor(
                out=dst, in0=tmp, scalar=-TWO_PI_LO, in1=dst,
                op0=ALU.mult, op1=ALU.add))
            P.op("vector", lambda e, s, dst=dst, tmp=tmp: e.tensor_scalar(
                out=tmp, in0=dst, scalar1=-PI_LO, scalar2=None, op0=ALU.is_lt))
            tok = P.op("vector", lambda e, s, dst=dst, tmp=tmp: e.scalar_tensor_tensor(
                out=dst, in0=tmp, scalar=TWO_PI_LO, in1=dst,
                op0=ALU.mult, op1=ALU.add))
        return tok

    P.op("vector", lambda e, s: e.scalar_tensor_tensor(
        out=DC[:, 0:32], in0=B.SMALL[:, 0:32], scalar=0.1,
        in1=B.SMALL[:, 32:64], op0=ALU.mult, op1=ALU.add),
        waits=(B.tok_s2, ("d_small", P.cnt["d_small"])))
    _reduce_passes(DC[:, 0:32], DC[:, 96:128], 6)
    P.op("vector", lambda e, s: e.tensor_copy(
        out=DC[:, 32:64], in_=B.SMALL[:, 32:64]))
    _reduce_passes(DC[:, 32:64], DC[:, 96:128], 6)
    P.op("vector", lambda e, s: e.tensor_scalar(
        out=DC[:, 96:128], in0=DC[:, 32:64], scalar1=math.pi / 2.0,
        scalar2=None, op0=ALU.add))
    P.op("vector", lambda e, s: e.tensor_scalar(
        out=DC[:, 64:96], in0=DC[:, 96:128], scalar1=PI_LO, scalar2=None,
        op0=ALU.is_gt))
    t_c = P.op("vector", lambda e, s: e.scalar_tensor_tensor(
        out=DC[:, 96:128], in0=DC[:, 64:96], scalar=-TWO_PI_LO,
        in1=DC[:, 96:128], op0=ALU.mult, op1=ALU.add))
    t_sin = P.op("scalar", lambda e, s: e.activation(
        out=DC[:, 32:64], in_=DC[:, 32:64], func=AF.Sin), waits=(t_c,))
    t_cos = P.op("scalar", lambda e, s: e.activation(
        out=DC[:, 96:128], in_=DC[:, 96:128], func=AF.Sin))
    P.op("vector", lambda e, s: e.tensor_tensor(
        out=DC[:, 64:96], in0=DC[:, 96:128], in1=B.SMALL[:, 64:96],
        op=ALU.mult), waits=(t_cos,))
    P.op("vector", lambda e, s: e.tensor_tensor(
        out=DC[:, 96:128], in0=DC[:, 32:64], in1=B.SMALL[:, 96:128],
        op=ALU.mult), waits=(t_sin,))
    P.op("vector", lambda e, s: e.tensor_tensor(
        out=DC[:, 64:96], in0=DC[:, 64:96], in1=DC[:, 96:128], op=ALU.add))
    P.op("vector", lambda e, s: e.tensor_tensor(
        out=DC[:, 64:96], in0=DC[:, 64:96], in1=B.SMALL[:, 96:128],
        op=ALU.mult))
    tC1 = P.op("vector", lambda e, s: e.tensor_scalar(
        out=DC[:, 64:96], in0=DC[:, 64:96], scalar1=0.1, scalar2=None,
        op0=ALU.mult))

    A_w1 = ar.alloc3(NDT, 128)
    A_w2 = ar.alloc3(4, 1024)
    A_ht = ar.alloc3(32, 512)
    A_tf = ar.alloc(1024).bitcast(F32)
    A_cf = ar.alloc(1024).bitcast(F32)
    A_gf = ar.alloc(1024).bitcast(F32)
    fp = {}
    mm_done = None
    for tch in range(4):
        t0 = 512 * tch
        for ot in range(32):
            lw = P.dma("gpsimd", A_w1[:],
                       B.wh["L2_W1"].rearrange("(it p) c -> p it c", p=128)[:, :, 128 * ot:128 * (ot + 1)],
                       sem="d_w" + str(2 + ot % 2),
                       waits=(fp.get("w1used"), B.tok_s2))
            psum = B.PSA[:, 512 * (ot % 2):512 * (ot % 2) + 512]
            for it in range(NDT):
                mm = P.op("tensor", lambda e, s, it=it, ot=ot, t0=t0, psum=psum: e.matmul(
                    out=psum, lhsT=A_w1[:, it, :], rhs=B.XN[:, it, t0:t0 + 512],
                    start=(it == 0), stop=(it == NDT - 1)),
                    waits=(lw, B.tok_xn, fp.get(("hq", ot % 2)),
                           fp.get(("yev", ot % 2))) if it == 0 else (),
                    inc="t" if it == NDT - 1 else None)
            fp["w1used"] = mm
            m1 = P.op("scalar", lambda e, s, ot=ot, psum=psum: e.activation(
                out=A_tf[:, 0:512], in_=psum, func=AF.Identity, scale=0.1,
                bias=DC[:, ot:ot + 1]),
                waits=(mm, tC1, fp.get("tffree")))
            m2 = P.op("vector", lambda e, s: e.tensor_scalar(
                out=A_cf[:, 0:512], in0=A_tf[:, 0:512], scalar1=PI_LO,
                scalar2=None, op0=ALU.is_gt), waits=(m1,))
            P.op("vector", lambda e, s: e.scalar_tensor_tensor(
                out=A_tf[:, 0:512], in0=A_cf[:, 0:512], scalar=-TWO_PI_LO,
                in1=A_tf[:, 0:512], op0=ALU.mult, op1=ALU.add))
            P.op("vector", lambda e, s: e.tensor_scalar(
                out=A_cf[:, 0:512], in0=A_tf[:, 0:512], scalar1=-PI_LO,
                scalar2=None, op0=ALU.is_lt))
            m5 = P.op("vector", lambda e, s: e.scalar_tensor_tensor(
                out=A_tf[:, 0:512], in0=A_cf[:, 0:512], scalar=TWO_PI_LO,
                in1=A_tf[:, 0:512], op0=ALU.mult, op1=ALU.add))
            k3 = P.op("scalar", lambda e, s: e.activation(
                out=A_tf[:, 0:512], in_=A_tf[:, 0:512], func=AF.Sin),
                waits=(m5,))
            k4 = P.op("vector", lambda e, s, ot=ot, psum=psum: e.scalar_tensor_tensor(
                out=A_tf[:, 0:512], in0=A_tf[:, 0:512], scalar=DC[:, 64 + ot:65 + ot],
                in1=psum, op0=ALU.mult, op1=ALU.add), waits=(k3,))
            hq = P.op("vector", lambda e, s, ot=ot: e.tensor_scalar(
                out=A_ht[:, ot, :], in0=A_tf[:, 0:512],
                scalar1=B.SMALL[:, ot:ot + 1], scalar2=0.0, op0=ALU.add,
                op1=ALU.max), waits=(k4, B.tok_l2small))
            fp[("hq", ot % 2)] = hq
            fp["tffree"] = hq
        h_done = hq
        for ktg in range(8):
            lw2 = P.dma("gpsimd", A_w2[:],
                        B.wh["L2_W2"][512 * ktg:512 * (ktg + 1), :]
                        .rearrange("(g p) c -> p g c", p=128),
                        sem="d_w" + str(2 + ktg % 2),
                        waits=(fp.get("w2used"), h_done))
            mm = None
            for oc in range(NDT):
                psum = (B.PSA if oc < 4 else B.PSB)[:, 512 * (oc % 4):512 * (oc % 4) + 512]
                for k4_ in range(4):
                    kt = 4 * ktg + k4_
                    mm = P.op("tensor", lambda e, s, kt=kt, k4_=k4_, oc=oc, psum=psum, lw2=lw2, ktg=ktg: e.matmul(
                        out=psum, lhsT=A_w2[:, k4_, 128 * oc:128 * (oc + 1)],
                        rhs=A_ht[:, kt, :], start=(kt == 0), stop=(kt == 31)),
                        waits=(lw2, h_done, fp.get(("yev", oc % 4))) if (k4_ == 0 and ktg == 0) else ((lw2,) if k4_ == 0 else ()),
                        inc="t" if (k4_ == 3) else None)
            fp["w2used"] = mm
        for oc in range(NDT):
            psum = (B.PSA if oc < 4 else B.PSB)[:, 512 * (oc % 4):512 * (oc % 4) + 512]
            tq = slice(t0, t0 + 512)
            g1 = P.op("scalar", lambda e, s, oc=oc, psum=psum: e.activation(
                out=A_gf[:, 0:512], in_=psum, func=AF.Silu,
                bias=B.SMALL[:, 128 + oc:129 + oc]),
                waits=(mm, B.tok_l2small, ("v", P.cnt["v"])))
            g2 = P.op("vector", lambda e, s, oc=oc, tq=tq: e.tensor_tensor(
                out=A_gf[:, 0:512], in0=A_gf[:, 0:512], in1=B.ST[:, oc, tq],
                op=ALU.mult), waits=(g1, B.tok_s2))
            mm_done = P.op("vector", lambda e, s, oc=oc, tq=tq: e.tensor_tensor(
                out=B.XT[:, oc, tq], in0=B.XT[:, oc, tq], in1=A_gf[:, 0:512],
                op=ALU.add))
            fp[("yev", oc % 4)] = mm_done
    B.tok_x_final = mm_done
    ar.release()


def emit_fp4_quant(P, B, tmp, src, dst, n, waits=()):
    nb = n // 16
    U = tmp["U"][:, 0:n]
    R1 = tmp["R1"][:, 0:n]
    R2 = tmp["R2"][:, 0:n]
    CF = tmp["CF"][:, 0:n]
    CU = tmp["CU"][:, 0:n]
    BM = tmp["BM"][:, 0:nb]
    SC = tmp["SC"][:, 0:nb]
    SCH = tmp["SCH"][:, 0:nb]
    rs_bcast = BM[:, :, None].broadcast_to([128, nb, 16])
    sch_bcast = SCH[:, :, None].broadcast_to([128, nb, 16])
    src3 = src.rearrange("p (nb b) -> p nb b", b=16)
    dst3 = dst.rearrange("p (nb b) -> p nb b", b=16)
    U3 = U.rearrange("p (nb b) -> p nb b", b=16)
    R23 = R2.rearrange("p (nb b) -> p nb b", b=16)
    waits = tuple(waits) + (("v", P.cnt["v"]),)
    P.op("vector", lambda e, s: e.tensor_reduce(
        out=BM[:, :, None], in_=src3, axis=AX.X, op=ALU.max,
        apply_absolute_value=True), waits=waits)
    P.op("vector", lambda e, s: e.tensor_scalar(
        out=SC, in0=BM, scalar1=1.0 / 6.0, scalar2=1e-12, op0=ALU.mult,
        op1=ALU.max))
    P.op("vector", lambda e, s: e.reciprocal(out=BM, in_=SC))
    P.op("vector", lambda e, s: e.tensor_scalar(
        out=SCH, in0=SC, scalar1=0.5, scalar2=None, op0=ALU.mult))
    tabs = P.op("scalar", lambda e, s: e.activation(out=U, in_=src, func=AF.Abs),
                waits=waits)
    P.op("vector", lambda e, s: e.tensor_tensor(
        out=U3, in0=U3, in1=rs_bcast, op=ALU.mult), waits=(tabs,))
    P.op("vector", lambda e, s: e.tensor_scalar(
        out=R2, in0=U, scalar1=0.25, scalar2=None, op0=ALU.is_gt))
    for thr, w in ((0.75, 1.0), (1.25, 1.0), (1.75, 1.0), (2.5, 2.0),
                   (3.5, 2.0), (5.0, 4.0)):
        P.op("vector", lambda e, s, thr=thr: e.tensor_scalar(
            out=CF, in0=U, scalar1=thr, scalar2=None, op0=ALU.is_gt))
        if w == 1.0:
            P.op("vector", lambda e, s: e.tensor_tensor(
                out=R2, in0=R2, in1=CF, op=ALU.add))
        else:
            P.op("vector", lambda e, s, w=w: e.scalar_tensor_tensor(
                out=R2, in0=CF, scalar=w, in1=R2, op0=ALU.mult, op1=ALU.add))
    P.op("vector", lambda e, s: e.tensor_scalar(
        out=CU, in0=src, scalar1=0.0, scalar2=None, op0=ALU.is_lt))
    P.op("vector", lambda e, s: e.tensor_scalar(
        out=R1, in0=R2, scalar1=-1.0, scalar2=None, op0=ALU.mult))
    P.op("vector", lambda e, s: e.copy_predicated(R2, CU, R1))
    return P.op("vector", lambda e, s: e.tensor_tensor(
        out=dst3, in0=R23, in1=sch_bcast, op=ALU.mult))


def mk_fp4_tmps(ar, n=1408):
    return {
        "U": ar.alloc(2 * n).bitcast(F32),
        "R1": ar.alloc(2 * n).bitcast(F32),
        "R2": ar.alloc(2 * n).bitcast(F32),
        "CF": ar.alloc(2 * n).bitcast(F32),
        "CU": ar.alloc((n + 1) // 2 * 2).bitcast(U8)[:, 0:n],
        "BM": ar.alloc(2 * (n // 16)).bitcast(F32),
        "SC": ar.alloc(2 * (n // 16)).bitcast(F32),
        "SCH": ar.alloc(2 * (n // 16)).bitcast(F32),
    }


def emit_block3(P, B):
    ar = B.ar
    ar.mark()
    SCALE = 0.125
    ones_bf = cbf(B, "ones_col")
    ones_row = cbf(B, "ones_row", rows=(0, 1))

    tok_nw = P.dma("sync", B.SMALL[:, 0:8], B.wh["L3_normw_col"], sem="d_small",
                   waits=(B.tok_xn,))

    A_invB = ar.alloc(T)
    A_invC = ar.alloc(32).bitcast(F32)
    ar.mark()
    SQ = ar.alloc3(NDT, 512)
    R_ss = ar.rowf(512)
    R_inv = ar.rowf(512)
    RB_i = ar.rowb(512)
    stats = B.PSA[0:1, 0:512]
    inv_done = None
    invc_tok = None
    for cc in range(4):
        tsl = slice(512 * cc, 512 * (cc + 1))
        sq_toks = []
        for dt_ in range(NDT):
            sq_toks.append(P.op("vector",
                lambda e, s, dt_=dt_, tsl=tsl: e.tensor_tensor(
                    out=SQ[:, dt_, :], in0=B.XN[:, dt_, tsl],
                    in1=B.XN[:, dt_, tsl], op=ALU.mult), waits=(B.tok_xn,)))
        for dt_ in range(NDT):
            mm_sq = P.op("tensor",
                lambda e, s, dt_=dt_: e.matmul(
                    out=stats, lhsT=ones_bf, rhs=SQ[:, dt_, :],
                    start=(dt_ == 0), stop=(dt_ == NDT - 1)),
                waits=(sq_toks[dt_], inv_done, B.tok_xn) if dt_ == 0 else (sq_toks[dt_],),
                inc="t" if dt_ == NDT - 1 else None)
        t1 = P.op("scalar", lambda e, s: e.mul(out=R_ss, in_=stats, mul=1.0 / 1024),
                  waits=(mm_sq,))
        P.op("scalar", lambda e, s: e.activation(
            out=R_ss, in_=R_ss, func=AF.Ln, bias=cvf(B, "rms_eps", (0, 1))))
        t3 = P.op("scalar", lambda e, s: e.activation(
            out=R_inv, in_=R_ss, func=AF.Exp, scale=-0.5))
        t4 = P.op("vector", lambda e, s: e.tensor_copy(out=RB_i, in_=R_inv),
                  waits=(t3, invc_tok))
        mm_i = P.op("tensor", lambda e, s: e.matmul(
            out=B.PSB[:, 0:512], lhsT=ones_row, rhs=RB_i, start=True, stop=True),
            waits=(t4,), inc="t")
        inv_done = P.op("scalar", lambda e, s, tsl=tsl: e.copy(
            out=A_invB[:, tsl], in_=B.PSB[:, 0:512]), waits=(mm_i,))
        for c4 in range(4):
            mm_c = P.op("tensor", lambda e, s, c4=c4: e.transpose(
                out=B.PSB[:, 512 + 1024 * (c4 % 2):512 + 1024 * (c4 % 2) + 256].bitcast(BF16)[:, 0:1],
                in_=RB_i[:, 128 * c4:128 * (c4 + 1)], identity=cbf(B, "eye")[0:1, 0:1]),
                waits=(t4, invc_tok), inc="t")
            invc_tok = P.op("scalar", lambda e, s, cc=cc, c4=c4: e.copy(
                out=A_invC[:, 4 * cc + c4:4 * cc + c4 + 1],
                in_=B.PSB[:, 512 + 1024 * (c4 % 2):512 + 1024 * (c4 % 2) + 256].bitcast(BF16)[:, 0:1]),
                waits=(mm_c,))
    ar.release()

    A_gfm = ar.alloc(T)
    A_wg = ar.alloc3(NDT, 16)
    w_g = P.dma("gpsimd", A_wg[:], B.wh["L3_Wg"].rearrange("(it p) c -> p it c", p=128),
                waits=(B.tok_xn, inv_done), sem="d_w1")
    t_wg = None
    for it in range(NDT):
        t_wg = P.op("vector", lambda e, s, it=it: e.tensor_scalar(
            out=A_wg[:, it, :], in0=A_wg[:, it, :],
            scalar1=B.SMALL[:, it:it + 1], scalar2=None, op0=ALU.mult),
            waits=(w_g, tok_nw))
    gfm_done = None
    for ch in range(4):
        tsl = slice(512 * ch, 512 * (ch + 1))
        for it in range(NDT):
            mm = P.op("tensor", lambda e, s, it=it, tsl=tsl: e.matmul(
                out=B.PSB[0:16, 1024:1536], lhsT=A_wg[:, it, :],
                rhs=B.XN[:, it, tsl], start=(it == 0), stop=(it == NDT - 1)),
                waits=(t_wg, gfm_done) if it == 0 else (),
                inc="t" if it == NDT - 1 else None)
        t1 = P.op("vector", lambda e, s, tsl=tsl: e.tensor_tensor(
            out=A_gfm[0:16, tsl], in0=B.PSB[0:16, 1024:1536],
            in1=A_invB[0:16, tsl], op=ALU.mult), waits=(mm, inv_done))
        gfm_done = P.op("scalar", lambda e, s, tsl=tsl: e.activation(
            out=A_gfm[0:16, tsl], in_=A_gfm[0:16, tsl], func=AF.Sigmoid),
            waits=(t1,))

    A_qw = ar.alloc3(NDT, 256)
    A_kw = ar.alloc3(NDT, 256)
    A_vw = ar.alloc3(NDT, 256)
    A_ow = ar.alloc3(2, 1024)
    A_qT = ar.alloc3(2, T)
    A_kT = ar.alloc3(2, T)
    A_at = ar.alloc3(2, T)
    A_vx = ar.alloc4(NTT, 4, 65)
    A_pt = ar.alloc3(4, 512)
    A_gr = ar.rowb(512)
    A_rec = ar.alloc(512)
    R_rec = ar.rowf(512)
    RB_rec = ar.rowb(512)

    prev = {}
    p_done = None
    for hg in range(4):
        c0 = hg * 256
        wsem = f"d_w{hg % 2}"
        first = invc_tok if hg == 0 else None
        w_q = P.dma("gpsimd", A_qw[:], wslice_src(B.wh["L3_Wq"], c0, c0 + 256),
                    waits=(prev.get("q_done"), B.tok_xn, inv_done), sem=wsem)
        w_k = P.dma("gpsimd", A_kw[:], wslice_src(B.wh["L3_Wkv"], c0, c0 + 256),
                    waits=(prev.get("k_done"), first), sem=wsem)
        w_v = P.dma("gpsimd", A_vw[:], wslice_src(B.wh["L3_Wkv"], 1024 + c0, 1024 + c0 + 256),
                    waits=(prev.get("v_done"), first), sem=wsem)
        w_o = P.dma("gpsimd", A_ow[:],
                    B.wh["L3_Wout"][c0:c0 + 256, :].rearrange("(it p) c -> p it c", p=128),
                    waits=(p_done, first), sem=wsem)
        nw_done = None
        for it in range(NDT):
            for wv, wt in ((A_qw, w_q), (A_kw, w_k), (A_vw, w_v)):
                nw_done = P.op("vector", lambda e, s, it=it, wv=wv: e.tensor_scalar(
                    out=wv[:, it, :], in0=wv[:, it, :],
                    scalar1=B.SMALL[:, it:it + 1], scalar2=None, op0=ALU.mult),
                    waits=(wt, tok_nw))

        def proj_fm3(wview, dst, scale=None):
            last = None
            for oc in range(2):
                for ch in range(4):
                    psum = B.PSA[:, 512 * ch:512 * (ch + 1)]
                    for it in range(NDT):
                        mm = P.op("tensor", lambda e, s, oc=oc, ch=ch, it=it, wview=wview, psum=psum: e.matmul(
                            out=psum, lhsT=wview[:, it, 128 * oc:128 * (oc + 1)],
                            rhs=B.XN[:, it, 512 * ch:512 * (ch + 1)],
                            start=(it == 0), stop=(it == NDT - 1)),
                            waits=(nw_done, inv_done, prev.get(("evac", ch))) if it == 0 else (),
                            inc="t" if it == NDT - 1 else None)
                    if scale is not None:
                        ev = P.op("vector", lambda e, s, oc=oc, ch=ch, dst=dst, psum=psum, scale=scale: e.scalar_tensor_tensor(
                            out=dst[:, oc, 512 * ch:512 * (ch + 1)], in0=psum,
                            scalar=scale, in1=A_invB[:, 512 * ch:512 * (ch + 1)],
                            op0=ALU.mult, op1=ALU.mult), waits=(mm,))
                    else:
                        ev = P.op("vector", lambda e, s, oc=oc, ch=ch, dst=dst, psum=psum: e.tensor_tensor(
                            out=dst[:, oc, 512 * ch:512 * (ch + 1)], in0=psum,
                            in1=A_invB[:, 512 * ch:512 * (ch + 1)], op=ALU.mult),
                            waits=(mm,))
                    prev[("evac", ch)] = ev
                    last = ev
            return last

        q_done = proj_fm3(A_qw, A_qT, scale=SCALE)
        k_done = proj_fm3(A_kw, A_kT)

        t_ones = P.op("vector", lambda e, s: e.memset(A_vx[:, :, :, 64:65], 1.0),
                      waits=(prev.get("at_done"),))
        v_done = None
        for tt in range(NTT):
            psum = B.PSB[:, 1024 + 512 * (tt % 2):1024 + 512 * (tt % 2) + 256]
            for it in range(NDT):
                mm = P.op("tensor", lambda e, s, tt=tt, it=it, psum=psum: e.matmul(
                    out=psum, lhsT=B.XN[:, it, 128 * tt:128 * (tt + 1)],
                    rhs=A_vw[:, it, :], start=(it == 0), stop=(it == NDT - 1)),
                    waits=(nw_done, gfm_done, prev.get(("vevac", tt % 2)), prev.get("pv_evac")) if it == 0 else (),
                    inc="t" if it == NDT - 1 else None)
            ev = P.op("vector", lambda e, s, tt=tt, psum=psum: e.tensor_scalar(
                out=A_vx[:, tt, :, 0:64],
                in0=psum.rearrange("p (a b) -> p a b", b=64),
                scalar1=A_invC[:, tt:tt + 1], scalar2=None, op0=ALU.mult),
                waits=(mm, t_ones, invc_tok))
            prev[("vevac", tt % 2)] = ev
            v_done = ev

        attn_done = prev.get("at_done")
        for qt in range(NTT):
            nkt = qt + 1
            G = (nkt + 3) // 4
            qo = 128 * (qt % 4)
            pvs = [B.PSB[0:65, 512 * hs + qo:512 * hs + qo + 128] for hs in range(4)]
            g_dma = P.dma("gpsimd", A_gr.rearrange("o (h c) -> o h c", c=128),
                          A_gfm[4 * hg:4 * hg + 4, 128 * qt:128 * (qt + 1)],
                          sem="d_gr",
                          waits=(gfm_done, prev.get("gr_used")))
            last_pv = None
            t_exp_prev = prev.get("psa_free")
            for g in range(G):
                n = min(4, nkt - 4 * g)
                for hs in range(4):
                    rb = (hs % 2) * 64
                    for ki in range(n):
                        kt = 4 * g + ki
                        mm_s = P.op("tensor", lambda e, s, hs=hs, kt=kt, qt=qt, rb=rb, ki=ki: e.matmul(
                            out=B.PSA[:, 512 * hs + 128 * ki:512 * hs + 128 * (ki + 1)],
                            lhsT=A_kT[rb:rb + 64, hs // 2, 128 * kt:128 * (kt + 1)],
                            rhs=A_qT[rb:rb + 64, hs // 2, 128 * qt:128 * (qt + 1)],
                            start=True, stop=True),
                            waits=(q_done, k_done, t_exp_prev) if (hs == 0 and ki == 0) else (),
                            inc="t" if (hs == 3 and ki == n - 1) else None)
                t_exp = P.op("scalar", lambda e, s, n=n: e.activation(
                    out=A_pt[:, :, 0:128 * n], func=AF.Exp,
                    in_=B.PSA[:].rearrange("p (h c) -> p h c", c=512)[:, :, 0:128 * n]),
                    waits=(mm_s, ("v", P.cnt["v"]),))
                t_exp_prev = t_exp
                tok_after_exp = t_exp
                if 4 * g <= qt < 4 * (g + 1):
                    ki_d = qt - 4 * g
                    tok_after_exp = P.op("vector", lambda e, s, ki_d=ki_d: e.tensor_tensor(
                        out=A_pt[:, :, 128 * ki_d:128 * (ki_d + 1)],
                        in0=A_pt[:, :, 128 * ki_d:128 * (ki_d + 1)],
                        in1=cbf(B, "triuIncl")[:, None, :].broadcast_to([128, 4, 128]),
                        op=ALU.mult), waits=(t_exp,))
                for hs in range(4):
                    for ki in range(n):
                        kt = 4 * g + ki
                        last_pv = P.op("tensor", lambda e, s, hs=hs, kt=kt, ki=ki, g=g, n=n, G=G, pv=pvs[hs]: e.matmul(
                            out=pv,
                            lhsT=A_vx[:, kt, hs, :],
                            rhs=A_pt[:, hs, 128 * ki:128 * (ki + 1)],
                            start=(g == 0 and ki == 0), stop=(g == G - 1 and ki == n - 1)),
                            waits=(tok_after_exp, v_done, prev.get("pv_evac")) if (hs == 0 and ki == 0) else (),
                            inc="t" if (hs == 3 and ki == n - 1) else None)
            sums_v = B.PSB[64:65, 0:2048].rearrange("o (h c) -> o h c", c=512)[:, :, qo:qo + 128]
            t_rc = P.op("vector", lambda e, s, sums_v=sums_v: e.reciprocal(
                out=R_rec.rearrange("o (h c) -> o h c", c=128), in_=sums_v),
                waits=(last_pv,))
            t_rg = P.op("vector", lambda e, s: e.tensor_tensor(
                out=RB_rec, in0=R_rec, in1=A_gr, op=ALU.mult), waits=(g_dma, t_rc))
            prev["gr_used"] = t_rg
            mm_rb = P.op("tensor", lambda e, s: e.matmul(
                out=B.PSA[0:64, 0:512], lhsT=ones_row[:, 0:64],
                rhs=RB_rec, start=True, stop=True), waits=(t_rg, t_exp_prev), inc="t")
            t_rcb = P.op("scalar", lambda e, s: e.copy(out=A_rec[0:64, :],
                                                       in_=B.PSA[0:64, 0:512]),
                         waits=(mm_rb, attn_done))
            prev["psa_free"] = t_rcb
            for par in range(2):
                ev = P.op("vector", lambda e, s, par=par, qt=qt, qo=qo: e.tensor_tensor(
                    out=A_at[64 * par:64 * par + 64, :, 128 * qt:128 * (qt + 1)],
                    in0=B.PSB[0:64, 0:2048].rearrange("p (h c) -> p h c", c=512)[:, par::2, qo:qo + 128],
                    in1=A_rec[0:64, :].rearrange("p (h c) -> p h c", c=128)[:, par::2, :],
                    op=ALU.mult), waits=(t_rcb,))
                prev["pv_evac"] = ev
                attn_done = ev
        prev["at_done"] = attn_done

        for oc in range(NDT):
            for ch in range(4):
                psum = B.PSA[:, 512 * ch:512 * (ch + 1)]
                for it2 in range(2):
                    mm = P.op("tensor", lambda e, s, oc=oc, ch=ch, it2=it2, psum=psum: e.matmul(
                        out=psum, lhsT=A_ow[:, it2, 128 * oc:128 * (oc + 1)],
                        rhs=A_at[:, it2, 512 * ch:512 * (ch + 1)],
                        start=(it2 == 0), stop=(it2 == 1)),
                        waits=(w_o, attn_done, prev.get(("evac", ch)), prev.get("psa_free")) if it2 == 0 else (),
                        inc="t" if it2 == 1 else None)
                if hg == 0:
                    ev = P.op("scalar", lambda e, s, oc=oc, ch=ch, psum=psum: e.copy(
                        out=B.ST[:, oc, 512 * ch:512 * (ch + 1)], in_=psum),
                        waits=(mm,))
                else:
                    ev = P.op("vector", lambda e, s, oc=oc, ch=ch, psum=psum: e.tensor_tensor(
                        out=B.ST[:, oc, 512 * ch:512 * (ch + 1)],
                        in0=B.ST[:, oc, 512 * ch:512 * (ch + 1)], in1=psum,
                        op=ALU.add), waits=(mm,))
                prev[("evac", ch)] = ev
                p_done = ev
        prev["q_done"] = q_done
        prev["k_done"] = k_done
        prev["v_done"] = v_done
    B.tok_s3 = p_done
    ar.release()
    if L3_PARTS >= 1:
        emit_block3_ffn(P, B)
    else:
        B.tok_x_final = p_done


L3_PARTS = 9


def emit_block3_ffn(P, B):
    ar = B.ar
    nc = B.nc
    if not hasattr(B, "wq_g"):
        B.wq_g = nc.dram_tensor("wq_g", [1024, 2816], BF16, kind="Internal").ap()
        B.wq_u = nc.dram_tensor("wq_u", [1024, 2816], BF16, kind="Internal").ap()
        B.wq_d = nc.dram_tensor("wq_d", [2816, 1024], BF16, kind="Internal").ap()
    eye_bf = cbf(B, "eye")

    ar.mark()
    tmps = mk_fp4_tmps(ar, 1408)
    WN = ar.alloc(5632).bitcast(F32)
    WQo = ar.alloc(2816)
    WTs = ar.alloc(2816)
    pq = {}
    wq_done = None
    P.wait("tensor", ("s", P.cnt["s"]), ("v", P.cnt["v"]))
    for (nat, scr, nrt, ncol) in (("L3_wgate_nat", B.wq_g, 22, 1024),
                                  ("L3_wup_nat", B.wq_u, 22, 1024),
                                  ("L3_wdown_nat", B.wq_d, 8, 2816)):
        ncs = ncol // 128
        for c in range(ncs):
            ld = P.dma("sync", WN[:, 0:128 * nrt].rearrange("p (r c) -> p r c", c=128),
                       B.wh[nat][:, 128 * c:128 * (c + 1)]
                       .rearrange("(r p) c -> p r c", p=128),
                       sem="d_sw2", waits=(pq.get("wn_used"), B.tok_s3))
            qd = None
            for q0 in range(0, 128 * nrt, 1408):
                qw = min(1408, 128 * nrt - q0)
                qd = emit_fp4_quant(P, B, tmps, WN[:, q0:q0 + qw],
                                    WQo[:, q0:q0 + qw], qw, waits=(ld,))
            pq["wn_used"] = qd
            tev = None
            for r in range(nrt):
                mm = P.op("tensor", lambda e, s, r=r: e.transpose(
                    out=B.PSA[:, 512 * (r % 4):512 * (r % 4) + 256].bitcast(BF16)[:, 0:128],
                    in_=WQo[:, 128 * r:128 * (r + 1)], identity=eye_bf),
                    waits=(qd, pq.get(("trev", r % 4))), inc="t")
                tev = P.op("vector", lambda e, s, r=r: e.tensor_copy(
                    out=WTs[:, 128 * r:128 * (r + 1)],
                    in_=B.PSA[:, 512 * (r % 4):512 * (r % 4) + 256].bitcast(BF16)[:, 0:128]),
                    waits=(mm, pq.get("wts_used")))
                pq[("trev", r % 4)] = tev
            wq_done = P.dma("sync", scr[128 * c:128 * (c + 1), :],
                            WTs[:, 0:128 * nrt], sem="d_sw3", waits=(tev,))
            pq["wts_used"] = wq_done
    ar.release()

    if L3_PARTS < 2:
        B.tok_x_final = wq_done
        return
    ar.mark()
    tmps = mk_fp4_tmps(ar, 1408)
    XNAT = ar.alloc3(8, 1024)
    pq2 = {}
    xq_done = None
    P.wait("tensor", ("s", P.cnt["s"]), ("v", P.cnt["v"]))
    for half in range(2):
        tb = 8 * half
        for tt in range(8):
            for dt_ in range(NDT):
                mm = P.op("tensor", lambda e, s, tt=tt, dt_=dt_, tb=tb: e.transpose(
                    out=B.PSA[:, 512 * (dt_ % 4):512 * (dt_ % 4) + 256].bitcast(BF16)[:, 0:128],
                    in_=B.XN[:, dt_, 128 * (tb + tt):128 * (tb + tt) + 128],
                    identity=eye_bf),
                    waits=(pq2.get(("tev", dt_ % 4)), xq_done), inc="t")
                ev = P.op("vector", lambda e, s, tt=tt, dt_=dt_: e.tensor_copy(
                    out=XNAT[:, tt, 128 * dt_:128 * (dt_ + 1)],
                    in_=B.PSA[:, 512 * (dt_ % 4):512 * (dt_ % 4) + 256].bitcast(BF16)[:, 0:128]),
                    waits=(mm,))
                pq2[("tev", dt_ % 4)] = ev
        for tt in range(8):
            qd = emit_fp4_quant(P, B, tmps, XNAT[:, tt, :], XNAT[:, tt, :], 1024)
            for dt_ in range(NDT):
                mm = P.op("tensor", lambda e, s, tt=tt, dt_=dt_: e.transpose(
                    out=B.PSB[:, 512 * (dt_ % 4):512 * (dt_ % 4) + 256].bitcast(BF16)[:, 0:128],
                    in_=XNAT[:, tt, 128 * dt_:128 * (dt_ + 1)],
                    identity=eye_bf),
                    waits=(qd, pq2.get(("tev2", dt_ % 4))), inc="t")
                ev = P.op("vector", lambda e, s, tt=tt, dt_=dt_, tb=tb: e.tensor_copy(
                    out=B.XN[:, dt_, 128 * (tb + tt):128 * (tb + tt) + 128],
                    in_=B.PSB[:, 512 * (dt_ % 4):512 * (dt_ % 4) + 256].bitcast(BF16)[:, 0:128]),
                    waits=(mm,))
                pq2[("tev2", dt_ % 4)] = ev
                xq_done = ev
    ar.release()

    if L3_PARTS < 3:
        B.tok_x_final = xq_done
        return
    ar.mark()
    A_wsl = ar.alloc4(2, NDT, 512)
    A_dk = ar.alloc3(1, 1024)
    A_h = ar.alloc3(4, 2816)
    A_sil = ar.alloc(512)
    A_gf = ar.alloc(1024).bitcast(F32)
    mm_done = None
    fp = {}
    for tch in range(4):
        t0 = 512 * tch
        for ici, ic in enumerate(range(0, 2816, 512)):
            icw = min(512, 2816 - ic)
            lg = P.dma("sync", A_wsl[:, 0, :, 0:icw],
                       B.wq_g.rearrange("(it p) i -> p it i", p=128)[:, :, ic:ic + icw],
                       sem="d_sw0", waits=(fp.get("wused"), wq_done, xq_done))
            lu = P.dma("sync", A_wsl[:, 1, :, 0:icw],
                       B.wq_u.rearrange("(it p) i -> p it i", p=128)[:, :, ic:ic + icw],
                       sem="d_sw1", waits=(fp.get("wused"), xq_done))
            mmu = None
            for tt in range(4):
                gps = B.PSA[:, 0:512]
                ups = B.PSA[:, 512:1024]
                for it in range(NDT):
                    mmg = P.op("tensor", lambda e, s, it=it, tt=tt, icw=icw, t0=t0, gps=gps: e.matmul(
                        out=gps[:, 0:icw], lhsT=B.XN[:, it, t0 + 128 * tt:t0 + 128 * (tt + 1)],
                        rhs=A_wsl[:, 0, it, 0:icw], start=(it == 0), stop=(it == NDT - 1)),
                        waits=(lg, xq_done, fp.get("hev")) if it == 0 else (),
                        inc="t" if it == NDT - 1 else None)
                for it in range(NDT):
                    mmu = P.op("tensor", lambda e, s, it=it, tt=tt, icw=icw, t0=t0, ups=ups: e.matmul(
                        out=ups[:, 0:icw], lhsT=B.XN[:, it, t0 + 128 * tt:t0 + 128 * (tt + 1)],
                        rhs=A_wsl[:, 1, it, 0:icw], start=(it == 0), stop=(it == NDT - 1)),
                        waits=(lu,) if it == 0 else (),
                        inc="t" if it == NDT - 1 else None)
                t_sil = P.op("scalar", lambda e, s, icw=icw, gps=gps: e.activation(
                    out=A_sil[:, 0:icw], in_=gps[:, 0:icw], func=AF.Silu),
                    waits=(mmg, ("v", P.cnt["v"])))
                hev = P.op("vector", lambda e, s, tt=tt, ic=ic, icw=icw, ups=ups: e.tensor_tensor(
                    out=A_h[:, tt, ic:ic + icw], in0=ups[:, 0:icw],
                    in1=A_sil[:, 0:icw], op=ALU.mult), waits=(mmu, t_sil))
                fp["hev"] = hev
            fp["wused"] = mmu
        if L3_PARTS < 4:
            mm_done = fp["hev"]
            continue
        ar.mark()
        tmps = mk_fp4_tmps(ar, 704)
        hq_done = None
        for tt in range(4):
            for c0 in range(0, 2816, 704):
                cw = min(704, 2816 - c0)
                hq_done = emit_fp4_quant(P, B, tmps, A_h[:, tt, c0:c0 + cw],
                                         A_h[:, tt, c0:c0 + cw], cw,
                                         waits=(fp.get("hev"),))
        ar.release()
        if L3_PARTS < 5:
            mm_done = hq_done
            continue
        ar.mark()
        A_hqt = ar.alloc3(22, 256)
        for hf in range(2):
            tev = None
            for kt in range(22):
                for t2 in range(2):
                    mm = P.op("tensor", lambda e, s, kt=kt, t2=t2, hf=hf: e.transpose(
                        out=B.PSA[:, 512 * (kt % 4):512 * (kt % 4) + 256].bitcast(BF16)[:, 128 * t2:128 * (t2 + 1)],
                        in_=A_h[:, 2 * hf + t2, 128 * kt:128 * (kt + 1)],
                        identity=eye_bf),
                        waits=(hq_done, fp.get(("hqtev", kt % 4)), fp.get("yevac")) if t2 == 0 else (),
                        inc="t" if t2 == 1 else None)
                tev = P.op("vector", lambda e, s, kt=kt: e.tensor_copy(
                    out=A_hqt[:, kt, :],
                    in_=B.PSA[:, 512 * (kt % 4):512 * (kt % 4) + 256].bitcast(BF16)[:, 0:256]),
                    waits=(mm, fp.get("hqt_used")))
                fp[("hqtev", kt % 4)] = tev
            for oh in range(2):
                mmd = None
                for kt in range(22):
                    ldk = P.dma("sync", A_dk[:],
                                B.wq_d.rearrange("(kt p) o -> p kt o", p=128)[:, kt:kt + 1, :],
                                sem="d_sw2", waits=(fp.get("dused"), wq_done))
                    for o2 in range(4):
                        ot = 4 * oh + o2
                        psum = B.PSB[:, 512 * o2:512 * o2 + 256]
                        mmd = P.op("tensor", lambda e, s, kt=kt, ot=ot, o2=o2, oh=oh, psum=psum, ldk=ldk: e.matmul(
                            out=psum, lhsT=A_dk[:, 0, 128 * ot:128 * (ot + 1)],
                            rhs=A_hqt[:, kt, :], start=(kt == 0), stop=(kt == 21)),
                            waits=(tev, ldk, fp.get("yevac")) if (o2 == 0 and kt == 0) else ((ldk,) if o2 == 0 else ()),
                            inc="t" if o2 == 3 else None)
                    fp["dused"] = mmd
                for o2 in range(4):
                    ot = 4 * oh + o2
                    psum = B.PSB[:, 512 * o2:512 * o2 + 256]
                    tq = slice(t0 + 256 * hf, t0 + 256 * hf + 256)
                    t1 = P.op("scalar", lambda e, s, psum=psum: e.activation(
                        out=A_gf[:, 0:256], in_=psum, func=AF.Silu),
                        waits=(mmd, ("v", P.cnt["v"])))
                    P.op("vector", lambda e, s, ot=ot, tq=tq: e.tensor_tensor(
                        out=A_gf[:, 0:256], in0=A_gf[:, 0:256], in1=B.ST[:, ot, tq],
                        op=ALU.mult), waits=(t1, B.tok_s3))
                    mm_done = P.op("vector", lambda e, s, ot=ot, tq=tq: e.tensor_tensor(
                        out=B.XT[:, ot, tq], in0=B.XT[:, ot, tq], in1=A_gf[:, 0:256],
                        op=ALU.add))
                fp["yevac"] = mm_done
            fp["hqt_used"] = mmd
        ar.release()
        fp["hev"] = mm_done
    B.tok_x_final = mm_done
    ar.release()


EMITTERS = {0: emit_block0, 1: emit_block1, 2: emit_block2, 3: emit_block3}


def kernel(**inputs):
    dev, xs = host_prep(inputs)
    weight_names = {k: v.shape for k, v in dev.items() if k != "CONSTS"}
    nc = build_program([0, 1, 2, 3], weight_names=weight_names)
    in_maps = []
    for b in range(8):
        m = {"x_in": xs[b]}
        m.update(dev)
        in_maps.append(m)
    res = run_bass_kernel_spmd(nc, in_maps, core_ids=list(range(8)))
    out = np.stack([np.asarray(r["x_out"]).reshape(D, T).T for r in res.results])
    return out.astype(np.float32)


# version 02
# speedup vs baseline: 1.0600x; 1.0600x over previous
import math
import contextlib
import numpy as np
import ml_dtypes
import concourse.bass as bass
import concourse.mybir as mybir
from concourse.bass_utils import run_bass_kernel_spmd

F32 = mybir.dt.float32
BF16 = mybir.dt.bfloat16
I32 = mybir.dt.int32
U8 = mybir.dt.uint8
AF = mybir.ActivationFunctionType
ALU = mybir.AluOpType
AX = mybir.AxisListType

BF = ml_dtypes.bfloat16

D = 1024
T = 2048
NDT = 8
NTT = 16
H = 16
DH = 64

ENGINES = ("tensor", "vector", "scalar", "gpsimd", "sync")


def _coalesce(tokens):
    best = {}
    for t in tokens:
        if t is None:
            continue
        sn, sv = t
        if sv > best.get(sn, -1):
            best[sn] = sv
    return tuple(best.items())


class Prog:
    AUTO_ENG = {"vector": "v", "scalar": "s", "gpsimd": "g"}

    def __init__(self, nc):
        self.nc = nc
        self.ops = {e: [] for e in ENGINES}
        self.cnt = {}
        self.scope = None

    def use_sems(self, *names):
        for n in names:
            self.cnt.setdefault(n, 0)

    def op(self, eng, fn, waits=(), inc=None, inc_n=1, dma=False):
        auto = (not dma) and eng in self.AUTO_ENG
        if auto:
            sem = self.AUTO_ENG[eng]
            assert inc in (None, sem)
            self.cnt.setdefault(sem, 0)
            prev = (sem, self.cnt[sem]) if self.cnt[sem] > 0 else None
            self.cnt[sem] += 1
            tok = (sem, self.cnt[sem])
            waits = tuple(waits) + ((prev,) if prev else ())
            inc, inc_n = sem, 1
        else:
            tok = None
            if inc is not None:
                self.cnt.setdefault(inc, 0)
        waits = _coalesce(self._bump(waits))
        if not auto and inc is not None:
            self.cnt[inc] += inc_n
            tok = (inc, self.cnt[inc])

        def closure(e, sems, fn=fn, waits=waits, tok=tok, inc=inc, inc_n=inc_n):
            for (sn, sv) in waits:
                e.wait_ge(sems[sn], sv)
            instr = fn(e, sems)
            if tok is not None:
                if instr is None:
                    instr = e.nop()
                instr.then_inc(sems[inc], inc_n)
            return instr

        self.ops[eng].append((self.scope, closure))
        return tok

    def _bump(self, tokens):
        out = []
        for t in tokens:
            if t is None:
                continue
            sn, sv = t
            if sn.startswith("d_"):
                sv = self.cnt.get(sn, sv)
            out.append((sn, sv))
        return out

    def wait(self, eng, *tokens):
        toks = _coalesce(self._bump(tokens))
        if not toks:
            return

        def closure(e, sems, toks=toks):
            for (sn, sv) in toks:
                e.wait_ge(sems[sn], sv)

        self.ops[eng].append((self.scope, closure))

    def dma(self, queue, out, in_, waits=(), sem=None, **kw):
        if sem is None:
            sem = "d_" + queue
        return self.op(queue, lambda e, s: e.dma_start(out=out, in_=in_, **kw),
                       waits=waits, inc=sem, inc_n=16, dma=True)


class Arena:

    def __init__(self, B):
        self.B = B
        self.off = 0
        self.stack = []
        self.total = B.AR.shape[1]

    def mark(self):
        self.stack.append(self.off)

    def release(self):
        self.off = self.stack.pop()

    def alloc(self, cols):
        o = (self.off + 3) & ~3
        self.off = o + cols
        assert self.off <= self.total, f"arena overflow {self.off} > {self.total}"
        return self.B.AR[:, o:o + cols]

    def alloc3(self, a, b):
        ap = self.alloc(a * b)
        return ap.rearrange("p (a b) -> p a b", b=b)

    def alloc4(self, a, b, c):
        ap = self.alloc(a * b * c)
        return ap.rearrange("p (a b c) -> p a b c", b=b, c=c)

    def rowf(self, n):
        ap = self.alloc(2 * n)
        return ap.bitcast(F32)[0:1, 0:n]

    def rowb(self, n):
        ap = self.alloc(n)
        return ap[0:1, :]


def _alibi_slopes(num_heads):
    closest_pow2 = 2 ** math.floor(math.log2(num_heads))
    base = 2.0 ** (-2.0 ** (-(math.log2(closest_pow2) - 3)))
    powers = np.arange(1, closest_pow2 + 1, dtype=np.float32)
    slopes = base ** powers
    if num_heads != closest_pow2:
        start = 2.0 ** (-(2.0 ** (-(math.log2(closest_pow2) - 3)) - 1))
        extra = np.linspace(start, base, num_heads - closest_pow2, dtype=np.float32)
        slopes = np.concatenate([slopes, extra])
    return slopes.astype(np.float32)


CONST_SPECS = {}
CV_VALS = {}


def build_consts():
    cols = []
    specs = {}

    def add(name, arr):
        specs[name] = (sum(c.shape[1] for c in cols), arr.shape[1])
        cols.append(arr.astype(np.float32))

    i = np.arange(128)[:, None]
    jq = np.arange(128)[None, :]

    add("eye", np.eye(128, dtype=np.float32))
    add("ones_col", np.ones((128, 1), dtype=np.float32))
    onesrow = np.zeros((128, 128), dtype=np.float32)
    onesrow[0, :] = 1.0
    add("ones_row", onesrow)
    s2t = np.zeros((128, 128), dtype=np.float32)
    s2t[0, 0:64] = 1.0
    s2t[1, 64:128] = 1.0
    add("sel2", s2t)
    s16 = np.zeros((128, 128), dtype=np.float32)
    for j in range(8):
        s16[j, 16 * j:16 * (j + 1)] = 1.0
    add("sel16", s16)
    s6 = np.zeros((128, 768), dtype=np.float32)
    for j in range(6):
        s6[j, 128 * j:128 * (j + 1)] = 1.0
    add("sel6", s6)
    add("triuIncl", (i <= jq).astype(np.float32))
    for lvl in range(4):
        blk = 2 ** (lvl + 1)
        start = (i // blk) * blk
        Mm = ((jq >= start) & (jq <= i)).astype(np.float32) / (i - start + 1)
        add(f"haarM_T_{lvl}", np.ascontiguousarray(Mm.T))
    R = np.zeros((128, 128), dtype=np.float32)
    for b0 in range(0, 128, 64):
        for ii in range(32):
            R[b0 + ii, b0 + 32 + ii] = -1.0
            R[b0 + 32 + ii, b0 + ii] = 1.0
    add("rope_rotT", np.ascontiguousarray(R.T))
    GLOBAL_CON_END = sum(c.shape[1] for c in cols)

    CV_VALS.clear()
    cv_list = [("ln_eps", 1e-5), ("rms_eps", float(np.finfo(np.float32).eps)),
               ("tiny", 1e-12), ("one", 1.0), ("half", 0.5), ("zero", 0.0),
               ("linattn_eps", 1e-6)]
    for idx, (nm, val) in enumerate(cv_list):
        CV_VALS[nm] = idx
    add("cvals", np.tile(np.array([[v for _, v in cv_list]], dtype=np.float32),
                         (128, 1)))

    slopes = _alibi_slopes(H)
    for h in range(H):
        tiles = []
        for off in (-2, -1, 0):
            rel = (i + 128 * off) - jq
            keep = (rel <= 0) & (rel > -256)
            tiles.append(np.where(keep, np.exp(slopes[h] * rel.astype(np.float64)),
                                  0.0).astype(np.float32))
        add(f"l0_strip_{h}", np.concatenate(tiles, axis=1))

    tiles = []
    for off in (-1, 0):
        rel = (i + 128 * off) - jq
        keep = (rel <= 0) & (rel > -64)
        tiles.append(keep.astype(np.float32))
    add("l1_lstrip", np.concatenate(tiles, axis=1))

    gamma = 1.0 - np.exp2(-5.0 - np.arange(H, dtype=np.float64))
    u = np.arange(128)
    for h in range(H):
        dm = np.where(i <= jq, float(gamma[h] ** -128.0), 0.0)
        add(f"l2_decmv_{h}", dm.astype(np.float32))
    for hp in range(8):
        tile = np.zeros((128, 128), dtype=np.float64)
        tile[:64, :] = gamma[2 * hp] ** u[None, :] / 8.0
        tile[64:, :] = gamma[2 * hp + 1] ** u[None, :] / 8.0
        add(f"l2_qg_{hp}", tile.astype(np.float32))
        tile2 = np.zeros((128, 128), dtype=np.float64)
        tile2[:64, :] = gamma[2 * hp] ** (128 - u[None, :])
        tile2[64:, :] = gamma[2 * hp + 1] ** (128 - u[None, :])
        add(f"l2_kg_{hp}", tile2.astype(np.float32))
    g128 = np.zeros((128, 8), dtype=np.float32)
    for hp in range(8):
        g128[:64, hp] = gamma[2 * hp] ** 128
        g128[64:, hp] = gamma[2 * hp + 1] ** 128
    add("l2_g128", g128)

    inv_freq = 1.0 / (10000.0 ** (np.arange(0, DH, 2, dtype=np.float32) / DH))
    t = np.arange(T, dtype=np.float32)
    freqs = np.einsum("i,j->ij", t, inv_freq)
    emb = np.concatenate([freqs, freqs], axis=1)
    add("rope_cos", np.ascontiguousarray(np.tile(np.cos(emb).T, (2, 1))))
    add("rope_sin", np.ascontiguousarray(np.tile(np.sin(emb).T, (2, 1))))

    CONST_SPECS.clear()
    CONST_SPECS.update(specs)
    arr = np.ascontiguousarray(np.concatenate(cols, axis=1))
    return arr, GLOBAL_CON_END


CONSTS_ARR, GLOBAL_CON_END = build_consts()
CONST_SHAPE = CONSTS_ARR.shape


def _bcol(v):
    k = v.shape[0]
    return np.ascontiguousarray(v.reshape(k // 128, 128).T)


def host_prep(inputs):
    g = {k: np.asarray(v, dtype=np.float32) for k, v in inputs.items()}
    d = {}
    tr = lambda a: np.ascontiguousarray(a.T)

    d["L0_qkvT"] = tr(g["L0_attn_qkv_w"])
    d["L0_qkv_b_col"] = _bcol(g["L0_attn_qkv_b"])
    d["L0_vb_row"] = np.ascontiguousarray(g["L0_attn_qkv_b"][2048:].reshape(1, 1024))
    d["L0_projT"] = tr(g["L0_attn_proj_w"])
    d["L0_proj_b_col"] = _bcol(g["L0_attn_proj_b"])
    d["L0_WrT"] = tr(g["L0_ffn_W_r"])
    d["L0_WkT"] = tr(g["L0_ffn_W_k"])
    d["L0_WvT"] = tr(g["L0_ffn_W_v"])
    d["L0_mur_col"] = _bcol(g["L0_ffn_mu_r"])
    d["L0_muk_col"] = _bcol(g["L0_ffn_mu_k"])

    for w in ("Wq", "Wk", "Wv", "Wkl", "Wvl", "Wo"):
        d[f"L1_{w}T"] = tr(g[f"L1_attn_{w}"])
    d["L1_WgT"] = tr(g["L1_attn_Wg"])
    d["L1_bg_col"] = _bcol(g["L1_attn_bg"])
    d["L1_WgoT_col"] = _bcol(g["L1_attn_Wgo"].reshape(1024))
    d["L1_bgo"] = np.ascontiguousarray(g["L1_attn_bgo"].reshape(1, 1))
    d["L1_bo_col"] = _bcol(g["L1_attn_bo"])
    d["L1_haar_scale"] = np.ascontiguousarray(g["L1_attn_haar_scale"].reshape(1, 4))
    bd = np.zeros((4, 2, 128, 128), dtype=np.float32)
    for lvl in range(4):
        for idx, nm in enumerate(("haar_Wk", "haar_Wv")):
            wt = g[f"L1_attn_{nm}"][lvl].T
            bd[lvl, idx, :64, :64] = wt
            bd[lvl, idx, 64:, 64:] = wt
    d["L1_haarBD"] = np.ascontiguousarray(bd.reshape(8, 128, 128))

    d["L1_gateT"] = tr(g["L1_ffn_gate_w"])
    d["L1_gate_b_bc"] = np.ascontiguousarray(
        np.tile(g["L1_ffn_gate_b"].reshape(1, 6), (128, 1)))
    for w in ("ew1", "ew2", "ew3"):
        d[f"L1_{w}T"] = np.ascontiguousarray(g[f"L1_ffn_{w}"].transpose(0, 2, 1))
    for w in ("fc1", "fc2", "fc3"):
        d[f"L1_{w}T"] = tr(g[f"L1_ffn_{w}"])

    for w in ("wq", "wk", "wv", "wg", "wo"):
        d[f"L2_{w}T"] = tr(g[f"L2_attn_{w}"])
    d["L2_W1"] = np.ascontiguousarray(g["L2_ffn_W1"])
    d["L2_b1_col"] = _bcol(g["L2_ffn_b1"])
    d["L2_theta_col"] = _bcol(g["L2_ffn_quantum_phases"])
    d["L2_wr_col"] = _bcol(g["L2_ffn_quantum_weights_real"])
    d["L2_wi_col"] = _bcol(g["L2_ffn_quantum_weights_imag"])
    d["L2_W2"] = np.ascontiguousarray(g["L2_ffn_W2"])
    d["L2_b2_col"] = _bcol(g["L2_ffn_b2"])

    d["L3_normw_col"] = _bcol(g["L3_attn_norm_w"])
    d["L3_Wq"] = np.ascontiguousarray(g["L3_attn_Wq"])
    d["L3_Wkv"] = np.ascontiguousarray(g["L3_attn_Wkv"])
    d["L3_Wout"] = np.ascontiguousarray(g["L3_attn_Wout"])
    d["L3_Wg"] = np.ascontiguousarray(g["L3_attn_Wg"])
    d["L3_wgateT"] = tr(g["L3_ffn_w_gate"])
    d["L3_wupT"] = tr(g["L3_ffn_w_up"])
    d["L3_wdownT"] = tr(g["L3_ffn_w_down"])
    d["L3_wgate_nat"] = np.ascontiguousarray(g["L3_ffn_w_gate"])
    d["L3_wup_nat"] = np.ascontiguousarray(g["L3_ffn_w_up"])
    d["L3_wdown_nat"] = np.ascontiguousarray(g["L3_ffn_w_down"])

    d["CONSTS"] = CONSTS_ARR

    x = np.asarray(inputs["L0_attn_x"], dtype=np.float32)
    xs = [np.ascontiguousarray(x[b].T) for b in range(8)]
    return d, xs


class Bufs:
    pass


def build_program(blocks, taps=(), weight_names=None):
    nc = bass.Bass("TRN2", target_bir_lowering=False, debug=False)
    P = Prog(nc)
    es = contextlib.ExitStack()

    x_in = nc.dram_tensor("x_in", [D, T], F32, kind="ExternalInput").ap()
    consts_h = nc.dram_tensor("CONSTS", list(CONST_SHAPE), F32, kind="ExternalInput").ap()
    wh = {}
    for name, shape in (weight_names or {}).items():
        wh[name] = nc.dram_tensor(name, list(shape), F32, kind="ExternalInput").ap()
    x_out = nc.dram_tensor("x_out", [D, T], F32, kind="ExternalOutput").ap()
    tap_out = {}
    for name, shape, dt_, getter in taps:
        tap_out[name] = nc.dram_tensor("tap_" + name, list(shape), dt_,
                                       kind="ExternalOutput").ap()

    with es:
        B = Bufs()
        B.nc = nc
        B.wh = wh
        B.consts_h = consts_h
        sb = lambda n, sh, dt_: es.enter_context(nc.sbuf_tensor(n, sh, dt_))
        B.XT = sb("XT", [128, NDT, T], F32)
        B.XN = sb("XN", [128, NDT, T], BF16)
        B.ST = sb("ST", [128, NDT, T], BF16)
        B.AR = sb("AR", [128, 36864], BF16)
        B.SMALL = sb("SMALL", [128, 256], F32)
        B.CON = sb("CON", [128, GLOBAL_CON_END], BF16)
        B.CONF = sb("CONF", [128, 16], F32)
        B.PSA = es.enter_context(nc.psum_tensor("PSA", [128, 2048], F32))
        B.PSB = es.enter_context(nc.psum_tensor("PSB", [128, 2048], F32))
        B.tok_x_final = None
        B.ar = Arena(B)

        P.use_sems("t", "v", "s", "g", "d_sync", "d_gpsimd", "d_scalar", "d_out",
                   "d_w0", "d_w1", "d_w2", "d_w3")

        emit_main(P, B, blocks, x_in, x_out, tap_out, taps)

        sems = {n: es.enter_context(nc.semaphore(n)) for n in sorted(P.cnt.keys())}
        block = es.enter_context(nc.Block())
        for eng in ENGINES:
            deco = getattr(block, eng)
            opslist = P.ops[eng]

            def make_section(opslist):
                def section(e):
                    cur = None
                    sid = None
                    for label, closure in opslist:
                        if label != cur:
                            if cur is not None:
                                nc.leave_named_scope(cur, sid, False)
                            sid = None
                            if label is not None:
                                sid, _ = nc.enter_named_scope(label, False)
                            cur = label
                        closure(e, sems)
                    if cur is not None:
                        nc.leave_named_scope(cur, sid, False)
                return section
            deco(make_section(opslist))

    return nc


def cvf(B, name, rows=(0, 128)):
    idx = CV_VALS[name]
    return B.CONF[rows[0]:rows[1], idx:idx + 1]


def cbf(B, name, rows=None):
    off, n = CONST_SPECS[name]
    assert off < GLOBAL_CON_END
    if rows is None:
        return B.CON[:, off:off + n]
    return B.CON[rows[0]:rows[1], off:off + n]


def chbm(B, name):
    off, n = CONST_SPECS[name]
    return B.consts_h[:, off:off + n]


def emit_main(P, B, blocks, x_in, x_out, tap_out, taps):
    xsrc = x_in.rearrange("(dt p) t -> p dt t", p=128)
    tok_x = P.dma("sync", B.XT[:], xsrc, sem="d_boot")
    ctok = P.dma("gpsimd", B.CON[:], B.consts_h[:, 0:GLOBAL_CON_END])
    o_cv, n_cv = CONST_SPECS["cvals"]
    ctok2 = P.dma("sync", B.CONF[:, 0:n_cv], B.consts_h[:, o_cv:o_cv + n_cv], sem="d_boot")
    for e in ("tensor", "vector", "scalar", "gpsimd"):
        P.wait(e, tok_x, ctok, ctok2)

    for b in blocks:
        P.scope = f"ln{b}"
        emit_layernorm(P, B)
        P.scope = f"b{b}"
        EMITTERS[b](P, B)
    P.scope = None

    for name, shape, dt_, getter in taps:
        ap, waits = getter(B)
        P.dma("sync", tap_out[name], ap, waits=waits, sem="d_out")

    done = P.dma("sync", x_out.rearrange("(dt p) t -> p dt t", p=128), B.XT[:],
                 waits=(B.tok_x_final,) if B.tok_x_final else (), sem="d_out")
    P.wait("sync", done)


def emit_layernorm(P, B):
    ar = B.ar
    ar.mark()
    ones_bf = cbf(B, "ones_col")
    ones_row = cbf(B, "ones_row", rows=(0, 1))
    SQ = ar.alloc3(NDT, 512)
    AB = ar.alloc(512)
    BB = ar.alloc(512)
    R_m = ar.rowf(512)
    R_ss = ar.rowf(512)
    R_t = ar.rowf(512)
    R_inv = ar.rowf(512)
    RB_a = ar.rowb(512)
    RB_b = ar.rowb(512)
    stats = B.PSA[0:1, 0:512]
    bcA = B.PSB[:, 0:512]
    bcB = B.PSB[:, 512:1024]
    xfin = B.tok_x_final
    prev_ab = None
    prev_bb = None
    prev_xn = None
    for cc in range(4):
        tsl = slice(512 * cc, 512 * (cc + 1))
        cast_toks = []
        for dt_ in range(NDT):
            cast_toks.append(P.op("scalar",
                lambda e, s, dt_=dt_, tsl=tsl: e.copy(out=B.XN[:, dt_, tsl],
                                                      in_=B.XT[:, dt_, tsl]),
                waits=(xfin,) if (cc == 0 and dt_ == 0) else ()))
        for dt_ in range(NDT):
            w = [cast_toks[dt_]]
            if dt_ == 0:
                w += [prev_ab, prev_bb, xfin]
            mm_mean = P.op("tensor",
                lambda e, s, dt_=dt_, tsl=tsl: e.matmul(
                    out=stats, lhsT=ones_bf, rhs=B.XN[:, dt_, tsl],
                    start=(dt_ == 0), stop=(dt_ == NDT - 1)),
                waits=w, inc="t" if dt_ == NDT - 1 else None)
        t_m = P.op("scalar", lambda e, s: e.mul(out=R_m, in_=stats, mul=1.0 / 1024),
                   waits=(mm_mean,))
        sq_toks = []
        for dt_ in range(NDT):
            sq_toks.append(P.op("vector",
                lambda e, s, dt_=dt_, tsl=tsl: e.tensor_tensor(
                    out=SQ[:, dt_, :], in0=B.XN[:, dt_, tsl],
                    in1=B.XN[:, dt_, tsl], op=ALU.mult),
                waits=(cast_toks[dt_],)))
        for dt_ in range(NDT):
            mm_sq = P.op("tensor",
                lambda e, s, dt_=dt_: e.matmul(
                    out=stats, lhsT=ones_bf, rhs=SQ[:, dt_, :],
                    start=(dt_ == 0), stop=(dt_ == NDT - 1)),
                waits=(sq_toks[dt_], t_m) if dt_ == 0 else (sq_toks[dt_],),
                inc="t" if dt_ == NDT - 1 else None)
        t_v1 = P.op("scalar", lambda e, s: e.mul(out=R_ss, in_=stats, mul=1.0 / 1024),
                    waits=(mm_sq,))
        t_v2 = P.op("vector", lambda e, s: e.tensor_tensor(
            out=R_t, in0=R_m, in1=R_m, op=ALU.mult), waits=(t_m,))
        t_v3 = P.op("vector", lambda e, s: e.tensor_tensor(
            out=R_ss, in0=R_ss, in1=R_t, op=ALU.subtract), waits=(t_v1, t_v2))
        t_v4 = P.op("scalar", lambda e, s: e.activation(
            out=R_t, in_=R_ss, func=AF.Ln, bias=cvf(B, "ln_eps", (0, 1))),
            waits=(t_v3,))
        t_v5 = P.op("scalar", lambda e, s: e.activation(
            out=R_inv, in_=R_t, func=AF.Exp, scale=-0.5), waits=(t_v4,))
        t_a = P.op("vector", lambda e, s: e.tensor_copy(out=RB_a, in_=R_inv),
                   waits=(t_v5,))
        t_b = P.op("vector", lambda e, s: e.scalar_tensor_tensor(
            out=RB_b, in0=R_m, scalar=-1.0, in1=R_inv, op0=ALU.mult, op1=ALU.mult),
            waits=(t_v5,))
        mm_a = P.op("tensor", lambda e, s: e.matmul(
            out=bcA, lhsT=ones_row, rhs=RB_a, start=True, stop=True),
            waits=(t_a,), inc="t")
        mm_b = P.op("tensor", lambda e, s: e.matmul(
            out=bcB, lhsT=ones_row, rhs=RB_b, start=True, stop=True),
            waits=(t_b,), inc="t")
        t_ab = P.op("scalar", lambda e, s: e.copy(out=AB, in_=bcA),
                    waits=(mm_a, mm_sq))
        t_bb = P.op("scalar", lambda e, s: e.copy(out=BB, in_=bcB),
                    waits=(mm_b,))
        for dt_ in range(NDT):
            P.op("vector", lambda e, s, dt_=dt_, tsl=tsl: e.tensor_tensor(
                out=SQ[:, dt_, :], in0=B.XN[:, dt_, tsl], in1=AB, op=ALU.mult),
                waits=(t_ab,) if dt_ == 0 else ())
            prev_xn = P.op("vector", lambda e, s, dt_=dt_, tsl=tsl: e.tensor_tensor(
                out=B.XN[:, dt_, tsl], in0=SQ[:, dt_, :], in1=BB, op=ALU.add),
                waits=(t_bb,) if dt_ == 0 else ())
        prev_ab, prev_bb = t_ab, t_bb
    B.tok_xn = prev_xn
    P.wait("tensor", B.tok_xn)
    ar.release()


def wslice_src(hap, c0, c1):
    return hap[:, c0:c1].rearrange("(it p) c -> p it c", p=128)


def emit_block0(P, B):
    P.scope = "b0:attn"
    ar = B.ar
    ar.mark()
    SCALE = 0.125

    tok_b = P.dma("sync", B.SMALL[:, 0:24], B.wh["L0_qkv_b_col"], sem="d_small")
    tok_pb = P.dma("sync", B.SMALL[:, 24:32], B.wh["L0_proj_b_col"], sem="d_small")
    tok_mu = P.dma("sync", B.SMALL[:, 32:40], B.wh["L0_mur_col"], sem="d_small")
    tok_mk = P.dma("sync", B.SMALL[:, 40:48], B.wh["L0_muk_col"], sem="d_small")
    t_qb = P.op("vector", lambda e, s: e.tensor_scalar(
        out=B.SMALL[:, 48:56], in0=B.SMALL[:, 0:8], scalar1=SCALE, scalar2=None,
        op0=ALU.mult), waits=(tok_b,))

    A_qw = ar.alloc3(NDT, 256)
    A_kw = ar.alloc3(NDT, 256)
    A_vw = ar.alloc3(NDT, 256)
    A_pw = ar.alloc3(2, 1024)
    A_qT = ar.alloc3(2, T)
    A_kT = ar.alloc3(2, T)
    A_at = ar.alloc3(2, T)
    A_vx = ar.alloc4(NTT, 4, 65)
    A_pt = ar.alloc3(4, 384)
    A_al = ar.alloc3(4, 384)
    A_vb = ar.alloc(256)
    A_rec = ar.alloc(512)
    R_rec = ar.rowf(512)
    RB_rec = ar.rowb(512)

    prev = {}
    p_done = None
    for hg in range(4):
        qc0 = hg * 256
        kc0 = 1024 + hg * 256
        vc0 = 2048 + hg * 256
        wsem = f"d_w{hg % 2}"
        ln0 = B.tok_xn if hg == 0 else None
        w_q = P.dma("gpsimd", A_qw[:], wslice_src(B.wh["L0_qkvT"], qc0, qc0 + 256),
                    waits=(prev.get("q_done"), ln0), sem=wsem)
        w_k = P.dma("gpsimd", A_kw[:], wslice_src(B.wh["L0_qkvT"], kc0, kc0 + 256),
                    waits=(prev.get("k_done"), ln0), sem=wsem)
        w_v = P.dma("gpsimd", A_vw[:], wslice_src(B.wh["L0_qkvT"], vc0, vc0 + 256),
                    waits=(prev.get("v_done"), ln0), sem=wsem)
        w_p = P.dma("gpsimd", A_pw[:],
                    B.wh["L0_projT"][hg * 256:(hg + 1) * 256, :]
                    .rearrange("(it p) c -> p it c", p=128),
                    waits=(p_done, ln0), sem=wsem)
        w_vb = P.dma("gpsimd", A_vb[0:1, :],
                     B.wh["L0_vb_row"][0:1, hg * 256:(hg + 1) * 256],
                     waits=(prev.get("vb_done"), ln0), sem=wsem)
        al_tok = None
        for hs in range(4):
            al_tok = P.dma("gpsimd", A_al[:, hs, :], chbm(B, f"l0_strip_{4 * hg + hs}"),
                           waits=(prev.get("al_done"), ln0), sem=wsem)

        def proj_fm(wview, dst, bias_cols, extra_waits, scale=None):
            last = None
            for oc in range(2):
                for ch in range(4):
                    psum = B.PSA[:, 512 * ch:512 * (ch + 1)]
                    for it in range(NDT):
                        mm = P.op("tensor", lambda e, s, oc=oc, ch=ch, it=it, wview=wview, psum=psum: e.matmul(
                            out=psum, lhsT=wview[:, it, 128 * oc:128 * (oc + 1)],
                            rhs=B.XN[:, it, 512 * ch:512 * (ch + 1)],
                            start=(it == 0), stop=(it == NDT - 1)),
                            waits=(tuple(extra_waits) + (B.tok_xn, prev.get(("evac", ch)),)) if it == 0 else (),
                            inc="t" if it == NDT - 1 else None)
                    ev = P.op("scalar", lambda e, s, oc=oc, ch=ch, dst=dst, psum=psum, bias_cols=bias_cols, scale=scale: e.activation(
                        out=dst[:, oc, 512 * ch:512 * (ch + 1)], in_=psum,
                        func=AF.Identity, bias=bias_cols[:, oc:oc + 1],
                        **({"scale": scale} if scale is not None else {})),
                        waits=(mm,))
                    prev[("evac", ch)] = ev
                    last = ev
            return last

        q_done = proj_fm(A_qw, A_qT, B.SMALL[:, 48 + 2 * hg:50 + 2 * hg],
                         (w_q, t_qb), scale=SCALE)
        k_done = proj_fm(A_kw, A_kT, B.SMALL[:, 8 + 2 * hg:10 + 2 * hg], (w_k, tok_b))

        mm_vb = P.op("tensor", lambda e, s: e.matmul(
            out=B.PSB[:, 0:256], lhsT=cbf(B, "ones_row", rows=(0, 1)),
            rhs=A_vb[0:1, :], start=True, stop=True),
            waits=(w_vb, prev.get(("pv", 0))), inc="t")
        t_vbb = P.op("scalar", lambda e, s: e.copy(out=A_vb, in_=B.PSB[:, 0:256]),
                     waits=(mm_vb,))
        t_ones = P.op("vector", lambda e, s: e.memset(A_vx[:, :, :, 64:65], 1.0),
                      waits=(prev.get("at_done"),))
        v_done = None
        for tt in range(NTT):
            psum = B.PSB[:, 1024 + 512 * (tt % 2):1024 + 512 * (tt % 2) + 256]
            for it in range(NDT):
                mm = P.op("tensor", lambda e, s, tt=tt, it=it, psum=psum: e.matmul(
                    out=psum, lhsT=B.XN[:, it, 128 * tt:128 * (tt + 1)],
                    rhs=A_vw[:, it, :], start=(it == 0), stop=(it == NDT - 1)),
                    waits=(w_v, B.tok_xn, prev.get(("vevac", tt % 2))) if it == 0 else (),
                    inc="t" if it == NDT - 1 else None)
            ev = P.op("vector", lambda e, s, tt=tt, psum=psum: e.scalar_tensor_tensor(
                out=A_vx[:, tt, :, 0:64],
                in0=psum.rearrange("p (a b) -> p a b", b=64), scalar=1.0,
                in1=A_vb.rearrange("p (a b) -> p a b", b=64),
                op0=ALU.mult, op1=ALU.add),
                waits=(mm, t_vbb, t_ones))
            prev[("vevac", tt % 2)] = ev
            v_done = ev

        attn_done = None
        for qt in range(NTT):
            kts = [kt for kt in (qt - 2, qt - 1, qt) if kt >= 0]
            for hs in range(4):
                rb = (hs % 2) * 64
                for kt in kts:
                    slot = 3 - (qt - kt)
                    mm_s = P.op("tensor", lambda e, s, hs=hs, kt=kt, qt=qt, rb=rb, slot=slot: e.matmul(
                        out=B.PSA[:, 512 * hs + 128 * slot:512 * hs + 128 * (slot + 1)],
                        lhsT=A_kT[rb:rb + 64, hs // 2, 128 * kt:128 * (kt + 1)],
                        rhs=A_qT[rb:rb + 64, hs // 2, 128 * qt:128 * (qt + 1)],
                        start=True, stop=True),
                        waits=(q_done, k_done, prev.get("psa_free")) if (hs == 0 and kt == kts[0]) else (),
                        inc="t" if (hs == 3 and kt == kts[-1]) else None)
            t_exp = P.op("scalar", lambda e, s: e.activation(
                out=A_pt[:], func=AF.Exp,
                in_=B.PSA[:].rearrange("p (h c) -> p h c", c=512)[:, :, 128:512]),
                waits=(mm_s,))
            t_msk = P.op("vector", lambda e, s: e.tensor_tensor(
                out=A_pt[:], in0=A_pt[:], in1=A_al[:], op=ALU.mult),
                waits=(t_exp, al_tok))
            pvbank = B.PSB[:, 512 * (qt % 2):512 * (qt % 2) + 512]
            for hs in range(4):
                for ki, kt in enumerate(kts):
                    slot = 3 - (qt - kt)
                    mm_pv = P.op("tensor", lambda e, s, hs=hs, kt=kt, slot=slot, pvbank=pvbank, ki=ki, nk=len(kts): e.matmul(
                        out=pvbank[0:65, 128 * hs:128 * (hs + 1)],
                        lhsT=A_vx[:, kt, hs, :],
                        rhs=A_pt[:, hs, 128 * (slot - 1):128 * slot],
                        start=(ki == 0), stop=(ki == nk - 1)),
                        waits=(t_msk, v_done, prev.get(("pv", qt % 2))) if (hs == 0 and ki == 0) else (),
                        inc="t" if (hs == 3 and ki == len(kts) - 1) else None)
            t_rc = P.op("vector", lambda e, s, pvbank=pvbank: e.reciprocal(
                out=R_rec, in_=pvbank[64:65, :]), waits=(mm_pv,))
            t_rb = P.op("vector", lambda e, s: e.tensor_copy(out=RB_rec, in_=R_rec),
                        waits=(t_rc,))
            mm_rb = P.op("tensor", lambda e, s: e.matmul(
                out=B.PSA[0:64, 0:512], lhsT=cbf(B, "ones_row", rows=(0, 1))[:, 0:64],
                rhs=RB_rec, start=True, stop=True), waits=(t_rb, t_exp), inc="t")
            t_rcb = P.op("scalar", lambda e, s: e.copy(out=A_rec[0:64, :],
                                                       in_=B.PSA[0:64, 0:512]),
                         waits=(mm_rb, attn_done))
            prev["psa_free"] = t_rcb
            for par in range(2):
                ev = P.op("vector", lambda e, s, par=par, qt=qt, pvbank=pvbank: e.tensor_tensor(
                    out=A_at[64 * par:64 * par + 64, :, 128 * qt:128 * (qt + 1)],
                    in0=pvbank[0:64, :].rearrange("p (h c) -> p h c", c=128)[:, par::2, :],
                    in1=A_rec[0:64, :].rearrange("p (h c) -> p h c", c=128)[:, par::2, :],
                    op=ALU.mult), waits=(t_rcb,))
                prev[("pv", qt % 2)] = ev
                attn_done = ev
        prev["at_done"] = attn_done

        for oc in range(NDT):
            for ch in range(4):
                psum = B.PSA[:, 512 * ch:512 * (ch + 1)]
                for it2 in range(2):
                    mm = P.op("tensor", lambda e, s, oc=oc, ch=ch, it2=it2, psum=psum: e.matmul(
                        out=psum, lhsT=A_pw[:, it2, 128 * oc:128 * (oc + 1)],
                        rhs=A_at[:, it2, 512 * ch:512 * (ch + 1)],
                        start=(it2 == 0), stop=(it2 == 1)),
                        waits=(w_p, attn_done, prev.get(("evac", ch)), prev.get("psa_free")) if it2 == 0 else (),
                        inc="t" if it2 == 1 else None)
                if hg == 0:
                    ev = P.op("scalar", lambda e, s, oc=oc, ch=ch, psum=psum: e.activation(
                        out=B.ST[:, oc, 512 * ch:512 * (ch + 1)], in_=psum,
                        func=AF.Identity, bias=B.SMALL[:, 24 + oc:25 + oc]),
                        waits=(mm, tok_pb))
                else:
                    ev = P.op("vector", lambda e, s, oc=oc, ch=ch, psum=psum: e.tensor_tensor(
                        out=B.ST[:, oc, 512 * ch:512 * (ch + 1)],
                        in0=B.ST[:, oc, 512 * ch:512 * (ch + 1)], in1=psum,
                        op=ALU.add), waits=(mm,))
                prev[("evac", ch)] = ev
                p_done = ev

        prev["q_done"] = q_done
        prev["k_done"] = k_done
        prev["v_done"] = v_done
        prev["vb_done"] = t_vbb
        prev["al_done"] = t_msk
    B.tok_s0 = p_done
    ar.release()

    P.scope = "b0:ffn"
    ar.mark()
    A_w1 = ar.alloc3(NDT, 1024)
    A_w2 = ar.alloc3(NDT, 1024)
    A_xr = ar.alloc3(NDT, 512)
    A_rr = ar.alloc3(NDT, 512)
    A_ka = ar.alloc3(NDT, 512)
    A_t1 = ar.alloc3(NDT, 512)
    A_kaf = ar.alloc(1024).bitcast(F32)
    A_vof = ar.alloc(1024).bitcast(F32)
    mur = B.SMALL[:, 32:40]
    muk = B.SMALL[:, 40:48]

    wload = {}
    for ch in range(4):
        wload[("fevac", ch)] = prev.get(("evac", ch))
    ffn_done = None
    barrier = B.tok_s0
    for qtr in range(4):
        q0 = 512 * qtr
        tq = slice(q0, q0 + 512)
        w_r = P.dma("gpsimd", A_w1[:], wslice_src(B.wh["L0_WrT"], 0, 1024),
                    waits=(wload.get("w1free"), B.tok_s0), sem="d_w2")
        w_k = P.dma("gpsimd", A_w2[:], wslice_src(B.wh["L0_WkT"], 0, 1024),
                    waits=(wload.get("w2free"), B.tok_s0), sem="d_w3")

        def tshift(dst, mu, waits):
            toks = []
            for dt_ in range(NDT):
                if qtr == 0:
                    P.op("vector", lambda e, s, dt_=dt_: e.tensor_tensor(
                        out=A_t1[:, dt_, 1:512], in0=B.XN[:, dt_, 1:512],
                        in1=B.XN[:, dt_, 0:511], op=ALU.subtract), waits=waits)
                    P.op("vector", lambda e, s, dt_=dt_, mu=mu, dst=dst: e.scalar_tensor_tensor(
                        out=dst[:, dt_, 1:512], in0=A_t1[:, dt_, 1:512],
                        scalar=mu[:, dt_:dt_ + 1], in1=B.XN[:, dt_, 0:511],
                        op0=ALU.mult, op1=ALU.add))
                    toks.append(P.op("vector", lambda e, s, dt_=dt_, mu=mu, dst=dst: e.tensor_scalar(
                        out=dst[:, dt_, 0:1], in0=B.XN[:, dt_, 0:1],
                        scalar1=mu[:, dt_:dt_ + 1], scalar2=None, op0=ALU.mult)))
                else:
                    P.op("vector", lambda e, s, dt_=dt_, tq=tq, q0=q0: e.tensor_tensor(
                        out=A_t1[:, dt_, :], in0=B.XN[:, dt_, tq],
                        in1=B.XN[:, dt_, q0 - 1:q0 + 511], op=ALU.subtract),
                        waits=waits)
                    toks.append(P.op("vector", lambda e, s, dt_=dt_, mu=mu, dst=dst, q0=q0: e.scalar_tensor_tensor(
                        out=dst[:, dt_, :], in0=A_t1[:, dt_, :],
                        scalar=mu[:, dt_:dt_ + 1], in1=B.XN[:, dt_, q0 - 1:q0 + 511],
                        op0=ALU.mult, op1=ALU.add)))
            return toks[-1]

        def ffn_proj(wview, wtok, src, srctok, evac):
            last = None
            for oc in range(NDT):
                psum = B.PSA[:, 512 * (oc % 4):512 * (oc % 4 + 1)]
                for it in range(NDT):
                    mm = P.op("tensor", lambda e, s, oc=oc, it=it, psum=psum, wview=wview, src=src: e.matmul(
                        out=psum, lhsT=wview[:, it, 128 * oc:128 * (oc + 1)],
                        rhs=src[:, it, :], start=(it == 0), stop=(it == NDT - 1)),
                        waits=(wtok, srctok, wload.get(("fevac", oc % 4))) if it == 0 else (),
                        inc="t" if it == NDT - 1 else None)
                ev = evac(oc, psum, mm)
                wload[("fevac", oc % 4)] = ev
                last = ev
            return last

        t_xr = tshift(A_xr, mur, (tok_mu, barrier))
        r_done = ffn_proj(A_w1, w_r, A_xr, t_xr,
                          lambda oc, psum, mm: P.op("scalar",
                              lambda e, s, oc=oc, psum=psum: e.activation(
                                  out=A_rr[:, oc, :], in_=psum, func=AF.Sigmoid),
                              waits=(mm,)))
        w_v = P.dma("gpsimd", A_w1[:], wslice_src(B.wh["L0_WvT"], 0, 1024),
                    waits=(r_done,), sem="d_w2")
        t_xk = tshift(A_xr, muk, (tok_mk, r_done))

        def ka_evac(oc, psum, mm):
            P.op("vector", lambda e, s, oc=oc, psum=psum: e.tensor_scalar(
                out=A_kaf, in0=psum, scalar1=0.0, scalar2=None,
                op0=ALU.max), waits=(mm,))
            return P.op("vector", lambda e, s, oc=oc: e.tensor_tensor(
                out=A_ka[:, oc, :], in0=A_kaf, in1=A_kaf, op=ALU.mult))

        ka_done = ffn_proj(A_w2, w_k, A_xr, t_xk, ka_evac)

        def vout_evac(oc, psum, mm):
            P.op("vector", lambda e, s, oc=oc, psum=psum: e.tensor_tensor(
                out=A_vof, in0=psum, in1=A_rr[:, oc, :], op=ALU.mult),
                waits=(mm,))
            P.op("scalar", lambda e, s, oc=oc: e.activation(
                out=A_vof, in_=A_vof, func=AF.Silu),
                waits=(("v", P.cnt["v"]),))
            P.op("vector", lambda e, s, oc=oc, tq=tq: e.tensor_tensor(
                out=A_vof, in0=A_vof, in1=B.ST[:, oc, tq],
                op=ALU.mult), waits=(B.tok_s0, ("s", P.cnt["s"])))
            return P.op("vector", lambda e, s, oc=oc, tq=tq: e.tensor_tensor(
                out=B.XT[:, oc, tq], in0=B.XT[:, oc, tq], in1=A_vof,
                op=ALU.add))

        ffn_done = ffn_proj(A_w1, w_v, A_ka, ka_done, vout_evac)
        wload["w1free"] = ffn_done
        wload["w2free"] = ka_done
        barrier = ffn_done
    B.tok_x_final = ffn_done
    ar.release()


L1_PARTS = 9
L1_HP = 8
L1_NOLOC = False
L1_NOGLOB = False
L1_NOPASS = False
L1_SEC = 9
L1_NOPHI = False


def emit_block1(P, B):
    emit_block1_attn(P, B)
    if L1_PARTS >= 2:
        emit_block1_ffn(P, B)
    else:
        B.tok_x_final = B.tok_s1


def emit_block1_attn(P, B):
    ar = B.ar
    ar.mark()
    nc = B.nc
    B.l1_diff = nc.dram_tensor("l1_diff", [D, T], BF16, kind="Internal").ap()
    B.l1_glob = nc.dram_tensor("l1_glob", [D, T], BF16, kind="Internal").ap()
    ones_row = cbf(B, "ones_row", rows=(0, 1))
    eye = cbf(B, "eye")
    triu = cbf(B, "triuIncl")

    tok_bg = P.dma("sync", B.SMALL[:, 0:8], B.wh["L1_bg_col"], sem="d_small",
                   waits=(B.tok_xn,))
    tok_bo = P.dma("sync", B.SMALL[:, 8:16], B.wh["L1_bo_col"], sem="d_small")
    tok_go = P.dma("sync", B.SMALL[:, 16:24], B.wh["L1_WgoT_col"], sem="d_small")
    tok_gb = P.dma("sync", B.SMALL[:, 40:46], B.wh["L1_gate_b_bc"], sem="d_small")
    tok_bgo = P.dma("sync", B.SMALL[0:1, 25:26], B.wh["L1_bgo"], sem="d_small")
    tok_hs = P.dma("sync", B.SMALL[0:1, 28:32], B.wh["L1_haar_scale"],
                   sem="d_small")

    RB_w = ar.rowb(4)
    A_w4 = ar.alloc(8).bitcast(F32)
    t_e = P.op("scalar", lambda e, s: e.activation(
        out=B.SMALL[0:1, 32:36], in_=B.SMALL[0:1, 28:32], func=AF.Exp),
        waits=(tok_hs,))
    P.op("vector", lambda e, s: e.tensor_reduce(
        out=B.SMALL[0:1, 36:37], in_=B.SMALL[0:1, 32:36], axis=AX.X,
        op=ALU.add), waits=(t_e,))
    P.op("vector", lambda e, s: e.reciprocal(
        out=B.SMALL[0:1, 37:38], in_=B.SMALL[0:1, 36:37]))
    P.op("vector", lambda e, s: e.tensor_scalar(
        out=B.SMALL[0:1, 32:36], in0=B.SMALL[0:1, 32:36],
        scalar1=B.SMALL[0:1, 37:38], scalar2=None, op0=ALU.mult))
    t_wb = P.op("vector", lambda e, s: e.tensor_copy(
        out=RB_w, in_=B.SMALL[0:1, 32:36]))
    mm_w = P.op("tensor", lambda e, s: e.matmul(
        out=B.PSA[:, 0:4], lhsT=ones_row, rhs=RB_w, start=True, stop=True),
        waits=(t_wb, B.tok_xn), inc="t")
    t_w4 = P.op("scalar", lambda e, s: e.copy(out=A_w4, in_=B.PSA[:, 0:4]),
                waits=(mm_w,))

    A_bd = ar.alloc3(8, 128)
    tok_bd = P.dma("gpsimd", A_bd[:],
                   B.wh["L1_haarBD"].rearrange("a p c -> p a c"),
                   waits=(B.tok_xn,), sem="d_w3")
    bd_folded = None
    for lvl in range(4):
        bd_folded = P.op("vector", lambda e, s, lvl=lvl: e.tensor_scalar(
            out=A_bd[:, 2 * lvl + 1, :], in0=A_bd[:, 2 * lvl + 1, :],
            scalar1=A_w4[:, lvl:lvl + 1], scalar2=None, op0=ALU.mult),
            waits=(tok_bd, t_w4))

    A_wa = ar.alloc3(NDT, 128)
    A_wb = ar.alloc3(NDT, 128)
    A_qT = ar.alloc(T)
    A_qpT = ar.alloc(T)
    A_kT = ar.alloc(T)
    A_kpT = ar.alloc(T)
    A_klT = ar.alloc(T)
    A_vT = ar.alloc(T)
    A_kpN = ar.alloc4(2, NTT, 64)
    A_vN = ar.alloc4(2, NTT, 65)
    A_vlN = ar.alloc4(2, NTT, 65)
    A_locT = ar.alloc(T)
    A_gT = ar.alloc(T)
    A_phit = ar.alloc(512)
    A_t2k = ar.alloc(512)
    A_t2v = ar.alloc(512)
    A_kpLN = ar.alloc4(4, 2, 64)
    A_mvN = ar.alloc4(4, 2, 65)
    A_kpLT = ar.alloc3(4, 128)
    A_am = ar.alloc3(2, 512)
    A_ls = ar.alloc(256)

    A_stF = ar.alloc(650).bitcast(F32)
    A_stB = ar.alloc(328)
    A_gtmp = ar.alloc3(2, 512)
    A_rc = ar.alloc(128)
    A_rcl = ar.alloc(512)
    A_rc0 = ar.alloc(128)
    R_n = ar.rowf(640)
    RB_n = ar.rowb(640)

    tok_ls = P.dma("gpsimd", A_ls[:], chbm(B, "l1_lstrip"),
                   waits=(B.tok_xn,), sem="d_w2")
    fp = {("pevac", 0): t_w4}
    diff_done = None
    glob_dma = None
    for hp in range(L1_HP):
        P.scope = f"b1:proj.{hp}"
        csl0 = slice(128 * hp, 128 * (hp + 1))

        def proj(wname, wbuf, wsem, evac_fn, extra_w=()):
            wtok = P.dma("gpsimd", wbuf[:], wslice_src(B.wh[wname], 128 * hp,
                                                       128 * (hp + 1)),
                         waits=(fp.get(("wfree", wsem)),) + tuple(extra_w),
                         sem=wsem)
            last = None
            for ch in range(4):
                psum = B.PSA[:, 512 * ch:512 * (ch + 1)]
                for it in range(NDT):
                    mm = P.op("tensor", lambda e, s, ch=ch, it=it, wbuf=wbuf, psum=psum: e.matmul(
                        out=psum, lhsT=wbuf[:, it, :],
                        rhs=B.XN[:, it, 512 * ch:512 * (ch + 1)],
                        start=(it == 0), stop=(it == NDT - 1)),
                        waits=(wtok, B.tok_xn, fp.get(("pevac", ch))) if it == 0 else (),
                        inc="t" if it == NDT - 1 else None)
                last = evac_fn(ch, psum, mm)
                fp[("pevac", ch)] = last
            fp[("wfree", wsem)] = last
            return last

        def phi_chain(psum, mm, dstphi, csl, after=None):
            t1 = P.op("vector", lambda e, s, psum=psum: e.tensor_scalar(
                out=A_phit[:], in0=psum, scalar1=0.0, scalar2=None,
                op0=ALU.min), waits=(mm, after, fp.get("phit")))
            t2 = P.op("scalar", lambda e, s, csl=csl, dstphi=dstphi: e.activation(
                out=dstphi[:, csl], in_=A_phit[:], func=AF.Exp), waits=(t1,))
            t3 = P.op("vector", lambda e, s, psum=psum: e.tensor_scalar(
                out=A_phit[:], in0=psum, scalar1=0.0, scalar2=None,
                op0=ALU.max), waits=(t2,))
            t4 = P.op("vector", lambda e, s, csl=csl, dstphi=dstphi: e.tensor_tensor(
                out=dstphi[:, csl], in0=dstphi[:, csl], in1=A_phit[:],
                op=ALU.add), waits=(t2,))
            fp["phit"] = t4
            return t4

        def ev_scale(dst, scale):
            def f(ch, psum, mm):
                return P.op("scalar", lambda e, s, ch=ch, psum=psum, scale=scale: e.activation(
                    out=dst[:, 512 * ch:512 * (ch + 1)], in_=psum,
                    func=AF.Identity, scale=scale), waits=(mm,))
            return f

        def ev_scale_phi(dstq, dstphi):
            def f(ch, psum, mm):
                csl = slice(512 * ch, 512 * (ch + 1))
                t0 = P.op("scalar", lambda e, s, psum=psum, csl=csl: e.activation(
                    out=dstq[:, csl], in_=psum, func=AF.Identity, scale=0.125),
                    waits=(mm,))
                return phi_chain(psum, mm, dstphi, csl, after=t0)
            return f

        def ev_copy_phi(dstraw, dstphi):
            def f(ch, psum, mm):
                csl = slice(512 * ch, 512 * (ch + 1))
                t0 = P.op("scalar", lambda e, s, psum=psum, csl=csl: e.copy(
                    out=dstraw[:, csl], in_=psum), waits=(mm,))
                return phi_chain(psum, mm, dstphi, csl, after=t0)
            return f

        def ev_copy(dst):
            def f(ch, psum, mm):
                return P.op("scalar", lambda e, s, ch=ch, psum=psum: e.copy(
                    out=dst[:, 512 * ch:512 * (ch + 1)], in_=psum), waits=(mm,))
            return f

        if L1_NOPHI:
            q_done = proj("L1_WqT", A_wa, "d_w0", ev_scale(A_qT, 0.125),
                          extra_w=(fp.get("qT_free"), fp.get("qpT_free")))
            qp_done = q_done
            k_done = proj("L1_WkT", A_wb, "d_w1", ev_copy(A_kT),
                          extra_w=(fp.get("kT_free"), fp.get("kpT_free")))
        else:
            q_done = proj("L1_WqT", A_wa, "d_w0", ev_scale_phi(A_qT, A_qpT),
                          extra_w=(fp.get("qT_free"), fp.get("qpT_free")))
            qp_done = q_done
            k_done = proj("L1_WkT", A_wb, "d_w1", ev_copy_phi(A_kT, A_kpT),
                          extra_w=(fp.get("kT_free"), fp.get("kpT_free")))
        kl_done = proj("L1_WklT", A_wa, "d_w0", ev_copy(A_klT),
                       extra_w=(fp.get("klT_free"),))
        v_done = proj("L1_WvT", A_wb, "d_w1", ev_copy(A_vT),
                      extra_w=(fp.get("vT_free"),))

        P.scope = f"b1:tr.{hp}"
        if L1_SEC < 2:
            dtoks = [v_done]
            dma_d = P.dma("sync", B.l1_diff[csl0, :], A_locT[:],
                          waits=(v_done,), sem="d_st0")
            dma_g = P.dma("sync", B.l1_glob[csl0, :], A_gT[:],
                          waits=(v_done,), sem="d_st1")
            for ch in range(4):
                fp[("pevac", ch)] = v_done
            fp["locT_free"] = dma_d
            fp["gT_free"] = dma_g
            diff_done = dma_d
            glob_dma = dma_g
            continue
        t_ones = P.op("vector", lambda e, s: e.memset(A_vN[:, :, :, 64:65], 1.0),
                      waits=(fp.get("vN_free"),))
        t_ones2 = P.op("vector", lambda e, s: e.memset(A_vlN[:, :, :, 64:65], 1.0),
                       waits=(fp.get("vlN_free"),))
        tr_done = {}
        kpn_done = None
        for c in range(NTT):
            for si, (src, wsrc) in enumerate(((A_kpT, k_done), (A_vT, v_done))):
                bsel = 2 * si + (c % 2)
                bank = B.PSB[:, 512 * bsel:512 * bsel + 64].bitcast(BF16)
                mm_t = P.op("tensor", lambda e, s, src=src, c=c, bank=bank: e.transpose(
                    out=bank[:, 0:128], in_=src[:, 128 * c:128 * (c + 1)],
                    identity=eye),
                    waits=(wsrc, tr_done.get(("ev", bsel)),
                           fp.get("psb_free")), inc="t")
                if si == 0:
                    ev = P.op("vector", lambda e, s, c=c, bank=bank: e.tensor_copy(
                        out=A_kpN[:, :, c, :],
                        in_=bank[:, 0:128].rearrange("p (a b) -> p a b", b=64)),
                        waits=(mm_t,))
                else:
                    ev = P.op("vector", lambda e, s, c=c, bank=bank: e.tensor_copy(
                        out=A_vN[:, :, c, 0:64],
                        in_=bank[:, 0:128].rearrange("p (a b) -> p a b", b=64)),
                        waits=(mm_t, t_ones))
                tr_done[("ev", bsel)] = ev
                kpn_done = ev

        if L1_SEC < 3:
            dtoks = [kpn_done]
            dma_d = P.dma("sync", B.l1_diff[csl0, :], A_locT[:],
                          waits=(kpn_done,), sem="d_st0")
            dma_g = P.dma("sync", B.l1_glob[csl0, :], A_gT[:],
                          waits=(kpn_done,), sem="d_st1")
            for ch in range(4):
                fp[("pevac", ch)] = kpn_done
            fp["psb_free"] = kpn_done
            fp["locT_free"] = dma_d
            fp["gT_free"] = dma_g
            diff_done = dma_d
            glob_dma = dma_g
            continue
        w_vl = P.dma("gpsimd", A_wb[:], wslice_src(B.wh["L1_WvlT"], 128 * hp,
                                                   128 * (hp + 1)),
                     waits=(v_done,), sem="d_w1")
        vln_done = None
        for c in range(NTT):
            psum = B.PSB[:, 1024 + 512 * (c % 2):1024 + 512 * (c % 2) + 128]
            for it in range(NDT):
                mm = P.op("tensor", lambda e, s, c=c, it=it, psum=psum: e.matmul(
                    out=psum, lhsT=B.XN[:, it, 128 * c:128 * (c + 1)],
                    rhs=A_wb[:, it, :], start=(it == 0), stop=(it == NDT - 1)),
                    waits=((w_vl, B.tok_xn, fp.get(("vlev", c % 2)))
                           + ((kpn_done,) if c < 2 else ())) if it == 0 else (),
                    inc="t" if it == NDT - 1 else None)
            ev = P.op("vector", lambda e, s, c=c, psum=psum: e.tensor_copy(
                out=A_vlN[:, :, c, 0:64],
                in_=psum.rearrange("p (a b) -> p a b", b=64)),
                waits=(mm, t_ones2))
            fp[("vlev", c % 2)] = ev
            vln_done = ev
        fp[("wfree", "d_w1")] = vln_done

        P.scope = f"b1:loc.{hp}"
        loc_done = vln_done
        for qt in range(0 if L1_NOLOC else NTT):
            sbank = B.PSA[:, 512 * (qt % 2):512 * (qt % 2) + 512]
            pvb = B.PSA[:, 1024 + 512 * (qt % 2):1024 + 512 * (qt % 2) + 128]
            kts = [kt for kt in (qt - 1, qt) if kt >= 0]
            c0 = 128 * (2 - len(kts))
            for h2 in (0, 64):
                mm_s = None
                for ki, kt in enumerate(kts):
                    sl2 = 2 - len(kts) + ki
                    mm_s = P.op("tensor", lambda e, s, h2=h2, kt=kt, qt=qt, sl2=sl2, sbank=sbank: e.matmul(
                        out=sbank[:, 128 * sl2:128 * (sl2 + 1)],
                        lhsT=A_klT[h2:h2 + 64, 128 * kt:128 * (kt + 1)],
                        rhs=A_qT[h2:h2 + 64, 128 * qt:128 * (qt + 1)],
                        start=True, stop=True),
                        waits=(q_done, kl_done, fp.get(("locs", qt % 2))) if ki == 0 else (),
                        inc="t" if ki == len(kts) - 1 else None)
                t_exp = P.op("scalar", lambda e, s, sbank=sbank, c0=c0: e.activation(
                    out=A_am[:, 0, c0:256], in_=sbank[:, c0:256],
                    func=AF.Exp), waits=(mm_s, fp.get("am0_free")))
                t_msk = P.op("vector", lambda e, s, c0=c0: e.tensor_tensor(
                    out=A_am[:, 0, c0:256], in0=A_am[:, 0, c0:256],
                    in1=A_ls[:, c0:256], op=ALU.mult),
                    waits=(t_exp, tok_ls))
                mm_pv = None
                for ki, kt in enumerate(kts):
                    mm_pv = P.op("tensor", lambda e, s, h2=h2, kt=kt, ki=ki, nk=len(kts), pvb=pvb: e.matmul(
                        out=pvb[0:65, :], lhsT=A_vlN[:, h2 // 64, kt, :],
                        rhs=A_am[:, 0, 128 * (2 - nk + ki):128 * (2 - nk + ki) + 128],
                        start=(ki == 0), stop=(ki == nk - 1)),
                        waits=(t_msk, vln_done, fp.get(("locpv", qt % 2))) if ki == 0 else (),
                        inc="t" if ki == len(kts) - 1 else None)
                t_rc = P.op("vector", lambda e, s, pvb=pvb: e.reciprocal(
                    out=R_n[0:1, 0:128], in_=pvb[64:65, :]), waits=(mm_pv,))
                t_rb = P.op("vector", lambda e, s: e.tensor_copy(
                    out=RB_n[0:1, 0:128], in_=R_n[0:1, 0:128]))
                mm_rb = P.op("tensor", lambda e, s, sbank=sbank: e.matmul(
                    out=sbank[0:64, 256:384], lhsT=ones_row[:, 0:64],
                    rhs=RB_n[0:1, 0:128], start=True, stop=True),
                    waits=(t_rb, t_msk), inc="t")
                t_rcb = P.op("scalar", lambda e, s, sbank=sbank: e.copy(
                    out=A_rc[0:64, :], in_=sbank[0:64, 256:384]),
                    waits=(mm_rb, fp.get("rc_free")))
                ev = P.op("vector", lambda e, s, h2=h2, qt=qt, pvb=pvb: e.tensor_tensor(
                    out=A_locT[h2:h2 + 64, 128 * qt:128 * (qt + 1)],
                    in0=pvb[0:64, :], in1=A_rc[0:64, :], op=ALU.mult),
                    waits=(t_rcb, fp.get("locT_free")))
                fp[("locs", qt % 2)] = t_rcb
                fp[("locpv", qt % 2)] = ev
                fp["rc_free"] = ev
                fp["am0_free"] = mm_pv
                loc_done = ev

        P.scope = f"b1:glob.{hp}"
        st_z1 = P.op("vector", lambda e, s: e.memset(A_stF[:, :], 0.0),
                     waits=(fp.get("st_free"),))
        st_z2 = P.op("vector", lambda e, s: e.memset(A_stB[:, 0:325], 0.0))
        glob_done = s4 = mm_pl = mm_a0 = mm_p0 = kplt_done = mm_al = st_z2
        for c in range(0 if L1_NOGLOB else NTT):
            ctile = slice(128 * c, 128 * (c + 1))
            lw0 = (loc_done,) if c == 0 else ()
            mm = None
            for lvl in range(4):
                mm = P.op("tensor", lambda e, s, lvl=lvl, ctile=ctile: e.matmul(
                    out=B.PSA[:, 128 * lvl:128 * (lvl + 1)],
                    lhsT=A_kT[:, ctile], rhs=A_bd[:, 2 * lvl, :],
                    start=True, stop=True),
                    waits=(k_done, bd_folded, fp.get("t2k_free")) + lw0 if lvl == 0 else (),
                    inc="t" if lvl == 3 else None)
            ev_t2k = P.op("scalar", lambda e, s: e.copy(out=A_t2k[:], in_=B.PSA[:, 0:512]),
                          waits=(mm,))
            mm = None
            for lvl in range(4):
                mm = P.op("tensor", lambda e, s, lvl=lvl: e.matmul(
                    out=B.PSA[:, 512 + 128 * lvl:512 + 128 * (lvl + 1)],
                    lhsT=cbf(B, f"haarM_T_{lvl}"), rhs=A_t2k[:, 128 * lvl:128 * (lvl + 1)],
                    start=True, stop=True),
                    waits=(ev_t2k, fp.get("mk_free")) + lw0 if lvl == 0 else (),
                    inc="t" if lvl == 3 else None)
            p1 = P.op("vector", lambda e, s: e.tensor_scalar(
                out=A_phit[:], in0=B.PSA[:, 512:1024], scalar1=0.0,
                scalar2=None, op0=ALU.min), waits=(mm, fp.get("phit")))
            p2 = P.op("scalar", lambda e, s: e.activation(
                out=A_kpLN[:].rearrange("p a b c -> p (a b c)"), in_=A_phit[:],
                func=AF.Exp), waits=(p1, fp.get("kpLN_free"), fp.get("kpLN_free2")))
            p3 = P.op("vector", lambda e, s: e.tensor_scalar(
                out=A_phit[:], in0=B.PSA[:, 512:1024], scalar1=0.0,
                scalar2=None, op0=ALU.max), waits=(p2,))
            p4 = P.op("vector", lambda e, s: e.tensor_tensor(
                out=A_kpLN[:].rearrange("p a b c -> p (a b c)"),
                in0=A_kpLN[:].rearrange("p a b c -> p (a b c)"),
                in1=A_phit[:], op=ALU.add), waits=(p2,))
            fp["phit"] = p4
            mm = None
            for lvl in range(4):
                mm = P.op("tensor", lambda e, s, lvl=lvl, ctile=ctile: e.matmul(
                    out=B.PSA[:, 1024 + 128 * lvl:1024 + 128 * (lvl + 1)],
                    lhsT=A_vT[:, ctile], rhs=A_bd[:, 2 * lvl + 1, :],
                    start=True, stop=True),
                    waits=(v_done, bd_folded, fp.get("t2v_free")) + lw0 if lvl == 0 else (),
                    inc="t" if lvl == 3 else None)
            ev_t2v = P.op("scalar", lambda e, s: e.copy(out=A_t2v[:], in_=B.PSA[:, 1024:1536]),
                          waits=(mm,))
            t_mvo = P.op("vector", lambda e, s: e.memset(A_mvN[:, :, :, 64:65], 1.0),
                         waits=(fp.get("mvN_free"),))
            mm = None
            for lvl in range(4):
                mm = P.op("tensor", lambda e, s, lvl=lvl: e.matmul(
                    out=B.PSA[:, 1536 + 128 * lvl:1536 + 128 * (lvl + 1)],
                    lhsT=cbf(B, f"haarM_T_{lvl}"), rhs=A_t2v[:, 128 * lvl:128 * (lvl + 1)],
                    start=True, stop=True),
                    waits=(ev_t2v, fp.get("mv_free")) + lw0 if lvl == 0 else (),
                    inc="t" if lvl == 3 else None)
            ev_mv = P.op("vector", lambda e, s: e.tensor_copy(
                out=A_mvN[:, :, :, 0:64],
                in_=B.PSA[:, 1536:2048].rearrange("p (a b c) -> p a b c", b=2, c=64)),
                waits=(mm, t_mvo))
            trtok = {}
            evt = None
            for j in range(8):
                lvl, h2i = j // 2, j % 2
                bank = B.PSA[:, 1024 * (j % 2):1024 * (j % 2) + 64].bitcast(BF16)
                mm_t = P.op("tensor", lambda e, s, lvl=lvl, h2i=h2i, bank=bank: e.transpose(
                    out=bank[0:64, 0:128], in_=A_kpLN[:, lvl, h2i, :],
                    identity=eye),
                    waits=(p4, trtok.get(("ev", j % 2)),
                           (ev_t2k if j % 2 == 0 else ev_t2v)), inc="t")
                evt = P.op("scalar", lambda e, s, lvl=lvl, h2i=h2i, bank=bank: e.copy(
                    out=A_kpLT[64 * h2i:64 * h2i + 64, lvl, :],
                    in_=bank[0:64, 0:128]),
                    waits=(mm_t, fp.get("kpLT_free")))
                trtok[("ev", j % 2)] = evt
            kplt_done = evt
            fp["t2k_free"] = trtok[("ev", 0)]
            fp["t2v_free"] = trtok[("ev", 1)]

            if L1_NOPASS:
                glob_done = s4 = kplt_done
                continue
            for h2 in (0, 1):
                rb = 64 * h2
                g0 = (kpn_done, vln_done) if (c == 0 and h2 == 0) else ()
                mm_a0 = P.op("tensor", lambda e, s, rb=rb, ctile=ctile: e.matmul(
                    out=B.PSB[:, 512:640], lhsT=A_kpT[rb:rb + 64, ctile],
                    rhs=A_qpT[rb:rb + 64, ctile], start=True, stop=True),
                    waits=(k_done, qp_done, fp.get("a0_free")) + g0, inc="t")
                mm_al = None
                for lvl in range(4):
                    mm_al = P.op("tensor", lambda e, s, rb=rb, lvl=lvl, ctile=ctile: e.matmul(
                        out=B.PSB[:, 128 * lvl:128 * (lvl + 1)],
                        lhsT=A_kpLT[rb:rb + 64, lvl, :],
                        rhs=A_qpT[rb:rb + 64, ctile], start=True, stop=True),
                        waits=(kplt_done, qp_done, fp.get("al_free")) + g0 if lvl == 0 else (),
                        inc="t" if lvl == 3 else None)
                t_m0 = P.op("vector", lambda e, s: e.tensor_tensor(
                    out=A_am[:, 0, 0:128], in0=B.PSB[:, 512:640], in1=triu,
                    op=ALU.mult), waits=(mm_a0, fp.get("am0p_free")))
                t_ml = P.op("vector", lambda e, s: e.tensor_tensor(
                    out=A_am[:, 1, :].rearrange("p (a b) -> p a b", b=128),
                    in0=B.PSB[:, 0:512].rearrange("p (a b) -> p a b", b=128),
                    in1=triu[:, None, :].broadcast_to([128, 4, 128]),
                    op=ALU.mult), waits=(mm_al, fp.get("aml_free")))
                mm_s0 = P.op("tensor", lambda e, s, h2=h2, c=c: e.matmul(
                    out=B.PSB[0:64, 1536 + 272:1536 + 337],
                    lhsT=A_kpN[:, h2, c, :], rhs=A_vN[:, h2, c, :],
                    start=True, stop=True),
                    waits=(kpn_done, fp.get("s0_free")) + g0, inc="t")
                mm_sl = None
                for lvl in range(4):
                    mm_sl = P.op("tensor", lambda e, s, h2=h2, lvl=lvl: e.matmul(
                        out=B.PSB[0:64, 1536 + 66 * lvl:1536 + 66 * lvl + 65],
                        lhsT=A_kpLN[:, lvl, h2, :], rhs=A_mvN[:, lvl, h2, :],
                        start=True, stop=True),
                        waits=(p4, ev_mv, fp.get("sl_free")) + g0 if lvl == 0 else (),
                        inc="t" if lvl == 3 else None)
                mm_p0 = P.op("tensor", lambda e, s, h2=h2, c=c: e.matmul(
                    out=B.PSB[0:65, 640:768], lhsT=A_vN[:, h2, c, :],
                    rhs=A_am[:, 0, 0:128], start=True, stop=(c == 0)),
                    waits=(t_m0, kpn_done, fp.get("pv0_free")),
                    inc="t" if c == 0 else None)
                if c > 0:
                    mm_p0 = P.op("tensor", lambda e, s, rb=rb, ctile=ctile: e.matmul(
                        out=B.PSB[0:65, 640:768],
                        lhsT=A_stB[rb:rb + 64, 0:65],
                        rhs=A_qpT[rb:rb + 64, ctile], start=False, stop=True),
                        waits=(fp.get("stB_tok"),), inc="t")
                mm_pl = None
                for lvl in range(4):
                    mm_pl = P.op("tensor", lambda e, s, h2=h2, lvl=lvl, c=c: e.matmul(
                        out=B.PSB[0:65, 1024 + 128 * lvl:1024 + 128 * (lvl + 1)],
                        lhsT=A_mvN[:, lvl, h2, :],
                        rhs=A_am[:, 1, 128 * lvl:128 * (lvl + 1)],
                        start=True, stop=(c == 0)),
                        waits=(t_ml, ev_mv, fp.get("pvl_free")) if lvl == 0 else (),
                        inc="t" if (c == 0 and lvl == 3) else None)
                    if c > 0:
                        mm_pl = P.op("tensor", lambda e, s, rb=rb, lvl=lvl, ctile=ctile: e.matmul(
                            out=B.PSB[0:65, 1024 + 128 * lvl:1024 + 128 * (lvl + 1)],
                            lhsT=A_stB[rb:rb + 64, 65 * (lvl + 1):65 * (lvl + 2)],
                            rhs=A_qpT[rb:rb + 64, ctile], start=False,
                            stop=True),
                            waits=(fp.get("stB_tok"),) if lvl == 0 else (),
                            inc="t" if lvl == 3 else None)
                t_n0 = P.op("vector", lambda e, s: e.tensor_scalar(
                    out=R_n[0:1, 512:640], in0=B.PSB[64:65, 640:768],
                    scalar1=cvf(B, "linattn_eps", (0, 1)), scalar2=None,
                    op0=ALU.max), waits=(mm_p0,))
                t_nl = P.op("vector", lambda e, s: e.tensor_scalar(
                    out=R_n[0:1, 0:512], in0=B.PSB[64:65, 1024:1536],
                    scalar1=cvf(B, "linattn_eps", (0, 1)), scalar2=None,
                    op0=ALU.max), waits=(mm_pl,))
                P.op("vector", lambda e, s: e.reciprocal(
                    out=R_n[0:1, 0:640], in_=R_n[0:1, 0:640]))
                t_rb2 = P.op("vector", lambda e, s: e.tensor_copy(
                    out=RB_n[0:1, 0:640], in_=R_n[0:1, 0:640]))
                mm_bl = P.op("tensor", lambda e, s: e.matmul(
                    out=B.PSA[0:64, 512:1024], lhsT=ones_row[:, 0:64],
                    rhs=RB_n[0:1, 0:512], start=True, stop=True),
                    waits=(t_rb2, fp.get("bl_free")), inc="t")
                mm_b0 = P.op("tensor", lambda e, s: e.matmul(
                    out=B.PSB[0:64, 768:896], lhsT=ones_row[:, 0:64],
                    rhs=RB_n[0:1, 512:640], start=True, stop=True),
                    waits=(t_rb2,), inc="t")
                rc_l = P.op("scalar", lambda e, s: e.copy(
                    out=A_rcl[0:64, :], in_=B.PSA[0:64, 512:1024]),
                    waits=(mm_bl, fp.get("rcl_free")))
                rc_0 = P.op("scalar", lambda e, s: e.copy(
                    out=A_rc0[0:64, :], in_=B.PSB[0:64, 768:896]),
                    waits=(mm_b0, fp.get("rc0_free")))
                g1 = P.op("vector", lambda e, s: e.tensor_tensor(
                    out=A_gtmp[0:64, 1, :], in0=B.PSB[0:64, 1024:1536],
                    in1=A_rcl[0:64, :], op=ALU.mult),
                    waits=(rc_l, fp.get("gtmp_free")))
                g2 = P.op("vector", lambda e, s: e.tensor_tensor(
                    out=A_gtmp[0:64, 0, 0:256], in0=A_gtmp[0:64, 1, 0:256],
                    in1=A_gtmp[0:64, 1, 256:512], op=ALU.add))
                g3 = P.op("vector", lambda e, s: e.tensor_tensor(
                    out=A_gtmp[0:64, 0, 256:384], in0=A_gtmp[0:64, 0, 0:128],
                    in1=A_gtmp[0:64, 0, 128:256], op=ALU.add))
                g4 = P.op("vector", lambda e, s: e.tensor_tensor(
                    out=A_gtmp[0:64, 0, 384:512], in0=B.PSB[0:64, 640:768],
                    in1=A_rc0[0:64, :], op=ALU.mult), waits=(rc_0,))
                g5 = P.op("vector", lambda e, s, rb=rb, ctile=ctile: e.tensor_tensor(
                    out=A_gT[rb:rb + 64, ctile], in0=A_gtmp[0:64, 0, 256:384],
                    in1=A_gtmp[0:64, 0, 384:512], op=ALU.add),
                    waits=(fp.get("gT_free"),))
                s1 = P.op("vector", lambda e, s, rb=rb: e.tensor_tensor(
                    out=A_stF[rb:rb + 64, 260:325], in0=A_stF[rb:rb + 64, 260:325],
                    in1=B.PSB[0:64, 1536 + 272:1536 + 337], op=ALU.add),
                    waits=(mm_s0, st_z1))
                s2 = P.op("vector", lambda e, s, rb=rb: e.tensor_tensor(
                    out=A_stF[rb:rb + 64, 0:260].rearrange("p (a b) -> p a b", b=65),
                    in0=A_stF[rb:rb + 64, 0:260].rearrange("p (a b) -> p a b", b=65),
                    in1=B.PSB[0:64, 1536:1536 + 264].rearrange(
                        "p (a b) -> p a b", b=66)[:, :, 0:65], op=ALU.add),
                    waits=(mm_sl,))
                s3 = P.op("vector", lambda e, s, rb=rb: e.tensor_copy(
                    out=A_stB[rb:rb + 64, 260:325], in_=A_stF[rb:rb + 64, 260:325]),
                    waits=(mm_p0, st_z2))
                s4 = P.op("vector", lambda e, s, rb=rb: e.tensor_copy(
                    out=A_stB[rb:rb + 64, 0:260], in_=A_stF[rb:rb + 64, 0:260]),
                    waits=(mm_pl,))
                fp["gtmp_free"] = g5
                fp["bl_free"] = rc_l
                fp["rcl_free"] = g1
                fp["rc0_free"] = g4
                fp["pvl_free"] = g1
                fp["pv0_free"] = g4
                fp["a0_free"] = g4
                fp["al_free"] = t_ml
                fp["am0p_free"] = mm_p0
                fp["aml_free"] = mm_pl
                fp["mk_free"] = rc_l
                fp["mv_free"] = g4
                fp["s0_free"] = s2
                fp["sl_free"] = s2
                if h2 == 1:
                    fp["stB_tok"] = s4
                    fp["kpLN_free2"] = mm_sl
                    fp["mvN_free"] = mm_pl
                    fp["kpLT_free"] = mm_al
                glob_done = g5
        fp["st_free"] = s4
        fp["kpLN_free"] = kplt_done
        fp["psb_free"] = s4
        for ch in range(4):
            fp[("pevac", ch)] = s4

        P.scope = f"b1:st.{hp}"
        dtoks = []
        for cc in range(4):
            csl = slice(512 * cc, 512 * (cc + 1))
            dtoks.append(P.op("vector", lambda e, s, csl=csl: e.tensor_tensor(
                out=A_locT[:, csl], in0=A_locT[:, csl], in1=A_gT[:, csl],
                op=ALU.subtract), waits=(loc_done, glob_done)))
        dma_d = P.dma("sync", B.l1_diff[csl0, :], A_locT[:], waits=tuple(dtoks),
                      sem="d_st0")
        dma_g = P.dma("sync", B.l1_glob[csl0, :], A_gT[:], waits=(glob_done,) + tuple(dtoks),
                      sem="d_st1")
        fp["qT_free"] = loc_done
        fp["qpT_free"] = mm_pl
        fp["kT_free"] = fp.get("t2k_free", loc_done)
        fp["kpT_free"] = mm_a0
        fp["klT_free"] = loc_done
        fp["vT_free"] = fp.get("t2v_free", loc_done)
        fp["vN_free"] = mm_p0
        fp["vlN_free"] = loc_done
        fp["locT_free"] = dma_d
        fp["gT_free"] = dma_g
        diff_done = dma_d
        glob_dma = dma_g

    B.l1_diff_done = diff_done
    B.l1_glob_done = glob_dma
    ar.release()
    if L1_PARTS >= 1:
        emit_block1_gate(P, B)
    else:
        B.tok_s1 = diff_done


def emit_block1_gate(P, B):
    P.scope = "b1:gate"
    """gate_h = silu([x, diff] @ Wg.T + bg); alpha = sigmoid(gate_h @ Wgo.T +
    bgo) (scalar per token); mixed = glob + alpha*diff; s1 = mixed @ Wo.T + bo
    -> ST. diff/glob tiles streamed back from Internal DRAM."""
    ar = B.ar
    ar.mark()
    ones_row = cbf(B, "ones_row", rows=(0, 1))
    A_wg = ar.alloc3(NTT, 1024)
    A_wo = [ar.alloc3(NDT, 128), ar.alloc3(NDT, 128)]
    A_df = ar.alloc3(NDT, 512)
    A_gl = ar.alloc3(NDT, 512)
    A_gh = ar.alloc3(NDT, 512)
    A_alB = ar.alloc(512)
    A_wgo = ar.alloc(8)
    R_al = ar.rowf(512)
    RB_al = ar.rowb(512)

    wg_tok = P.dma("gpsimd", A_wg[:],
                   B.wh["L1_WgT"].rearrange("(it p) c -> p it c", p=128),
                   waits=(B.tok_xn, B.l1_diff_done, B.l1_glob_done),
                   sem="d_w2")
    t_wgo = P.op("vector", lambda e, s: e.tensor_copy(
        out=A_wgo[:], in_=B.SMALL[:, 16:24]),
        waits=(("d_small", P.cnt["d_small"]),))
    fp = {}
    s1_done = None
    for tcol in range(4):
        tsl = slice(512 * tcol, 512 * (tcol + 1))
        d_df = P.dma("sync", A_df[:],
                     B.l1_diff.rearrange("(dt p) t -> p dt t", p=128)[:, :, tsl],
                     waits=(B.l1_diff_done, fp.get("df_free")), sem="d_st0")
        d_gl = P.dma("sync", A_gl[:],
                     B.l1_glob.rearrange("(dt p) t -> p dt t", p=128)[:, :, tsl],
                     waits=(B.l1_glob_done, fp.get("gl_free")), sem="d_st1")
        mm_al = None
        evs = []
        for od in range(8):
            psum = B.PSA[:, 512 * (od % 2):512 * (od % 2) + 512]
            for it in range(16):
                rhs_is_x = it < 8
                mm = P.op("tensor", lambda e, s, od=od, it=it, psum=psum, tsl=tsl, rhs_is_x=rhs_is_x: e.matmul(
                    out=psum, lhsT=A_wg[:, it, 128 * od:128 * (od + 1)],
                    rhs=(B.XN[:, it, tsl] if rhs_is_x else A_df[:, it - 8, :]),
                    start=(it == 0), stop=(it == 15)),
                    waits=(wg_tok, d_df, B.tok_xn, fp.get(("gev", od % 2))) if it == 0 else (),
                    inc="t" if it == 15 else None)
            ev = P.op("scalar", lambda e, s, od=od, psum=psum: e.activation(
                out=A_gh[:, od, :], in_=psum, func=AF.Silu,
                bias=B.SMALL[:, od:od + 1]),
                waits=(mm, ("d_small", P.cnt["d_small"]), fp.get(("ghf", od))))
            fp[("gev", od % 2)] = ev
            evs.append(ev)
        apb = B.PSB[0:1, 512 * tcol:512 * tcol + 512]
        for od in range(8):
            mm_al = P.op("tensor", lambda e, s, od=od, apb=apb: e.matmul(
                out=apb, lhsT=A_wgo[:, od:od + 1], rhs=A_gh[:, od, :],
                start=(od == 0), stop=(od == 7)),
                waits=(evs[od], t_wgo), inc="t" if od == 7 else None)
            fp[("ghf", od)] = mm_al
        t_al = P.op("scalar", lambda e, s, tcol=tcol: e.activation(
            out=R_al, in_=B.PSB[0:1, 512 * tcol:512 * tcol + 512],
            func=AF.Sigmoid, bias=B.SMALL[0:1, 25:26]), waits=(mm_al,))
        t_ab = P.op("vector", lambda e, s: e.tensor_copy(out=RB_al, in_=R_al),
                    waits=(t_al,))
        mm_ab = P.op("tensor", lambda e, s: e.matmul(
            out=B.PSA[:, 1024:1536], lhsT=ones_row, rhs=RB_al,
            start=True, stop=True),
            waits=(t_ab, fp.get("ab_free"), fp.get(("sev", 0))), inc="t")
        t_alB = P.op("scalar", lambda e, s: e.copy(out=A_alB[:], in_=B.PSA[:, 1024:1536]),
                     waits=(mm_ab, fp.get("alB_free")))
        mix_done = []
        for dt_ in range(NDT):
            P.op("vector", lambda e, s, dt_=dt_: e.tensor_tensor(
                out=A_df[:, dt_, :], in0=A_df[:, dt_, :], in1=A_alB[:],
                op=ALU.mult), waits=(t_alB,) if dt_ == 0 else ())
            mix_done.append(P.op("vector", lambda e, s, dt_=dt_: e.tensor_tensor(
                out=A_df[:, dt_, :], in0=A_df[:, dt_, :], in1=A_gl[:, dt_, :],
                op=ALU.add), waits=(d_gl,) if dt_ == 0 else ()))
        fp["gl_free"] = mix_done[-1]
        for od in range(NDT):
            wo_tok = P.dma("gpsimd", A_wo[od % 2][:],
                           wslice_src(B.wh["L1_WoT"], 128 * od, 128 * (od + 1)),
                           waits=(fp.get(("wo_free", od % 2)), B.l1_glob_done),
                           sem="d_w" + str(od % 2))
            psum = B.PSA[:, 1024 + 512 * (od % 2):1024 + 512 * (od % 2) + 512]
            for it in range(NDT):
                mm = P.op("tensor", lambda e, s, od=od, it=it, psum=psum: e.matmul(
                    out=psum, lhsT=A_wo[od % 2][:, it, :], rhs=A_df[:, it, :],
                    start=(it == 0), stop=(it == NDT - 1)),
                    waits=(wo_tok, mix_done[-1], t_alB, fp.get(("sev", od % 2))) if it == 0 else (),
                    inc="t" if it == NDT - 1 else None)
            sev = P.op("scalar", lambda e, s, od=od, psum=psum, tsl=tsl: e.activation(
                out=B.ST[:, od, tsl], in_=psum, func=AF.Identity,
                bias=B.SMALL[:, 8 + od:9 + od]), waits=(mm,))
            fp[("sev", od % 2)] = sev
            fp[("wo_free", od % 2)] = mm
            s1_done = sev
        fp["df_free"] = mm
        fp["ab_free"] = t_alB
        fp["alB_free"] = mm
    B.tok_s1 = s1_done
    ar.release()


def esl_src(hap, ei, c0, c1):
    return hap[ei, :, c0:c1].rearrange("(a p) c -> p a c", p=128)


def emit_block1_ffn(P, B):
    P.scope = "b1:route"
    """Dense top-2-of-6 MoE (expert SwiGLU-like: silu(x@ew1.T)*(x@ew3.T) @
    ew2.T, weighted by normalized top-2 sigmoid router probs) + shared
    expert, + glue x += s*silu(f). Router math kept in fp32 (routing
    comparisons must not be perturbed by bf16); ge weights folded into h
    before the down-projection so experts accumulate in one SBUF fp32 y."""
    ar = B.ar
    ar.mark()
    eye = cbf(B, "eye")
    A_gw = ar.alloc3(NDT, 6)
    P_nat = ar.alloc(192).bitcast(F32).rearrange("p (a b) -> p a b", b=6)
    CF = ar.alloc(192).bitcast(F32).rearrange("p (a b) -> p a b", b=6)
    M1 = ar.alloc(32).bitcast(F32)
    M2 = ar.alloc(32).bitcast(F32)
    DEN = ar.alloc(32).bitcast(F32)
    GE_bf = ar.alloc(96).rearrange("p (a b) -> p a b", b=6)
    A_geT = ar.alloc(T)
    A_geB = ar.alloc(1024)
    A_h = ar.alloc3(NDT, 1024)
    A_y = ar.alloc(16384).bitcast(F32).rearrange("p (a b) -> p a b", b=1024)
    A_we1 = [ar.alloc3(NDT, 128), ar.alloc3(NDT, 128)]
    A_we3 = [ar.alloc3(NDT, 128), ar.alloc3(NDT, 128)]
    A_w2 = [ar.alloc3(NDT, 128), ar.alloc3(NDT, 128)]
    A_sil = ar.alloc(512)
    A_gf = ar.alloc(512)

    g_tok = P.dma("gpsimd", A_gw[:],
                  B.wh["L1_gateT"].rearrange("(a p) c -> p a c", p=128),
                  waits=(B.tok_xn, B.tok_s1), sem="d_w3")
    fp = {}
    rt_toks = []
    for tt in range(NTT):
        psum = B.PSB[:, 512 * (tt % 2):512 * (tt % 2) + 6]
        for it in range(NDT):
            mm = P.op("tensor", lambda e, s, tt=tt, it=it, psum=psum: e.matmul(
                out=psum, lhsT=B.XN[:, it, 128 * tt:128 * (tt + 1)],
                rhs=A_gw[:, it, :], start=(it == 0), stop=(it == NDT - 1)),
                waits=(g_tok, B.tok_xn, B.tok_s1, fp.get(("lg", tt % 2))) if it == 0 else (),
                inc="t" if it == NDT - 1 else None)
        t1 = P.op("vector", lambda e, s, tt=tt, psum=psum: e.tensor_tensor(
            out=P_nat[:, tt, :], in0=psum, in1=B.SMALL[:, 40:46], op=ALU.add),
            waits=(mm, ("d_small", P.cnt["d_small"])))
        t2 = P.op("scalar", lambda e, s, tt=tt: e.activation(
            out=P_nat[:, tt, :], in_=P_nat[:, tt, :], func=AF.Sigmoid),
            waits=(t1,))
        fp[("lg", tt % 2)] = t2
        c1 = P.op("vector", lambda e, s, tt=tt: e.tensor_reduce(
            out=M1[:, tt:tt + 1], in_=P_nat[:, tt, :], axis=AX.X, op=ALU.max),
            waits=(t2,))
        P.op("vector", lambda e, s, tt=tt: e.tensor_scalar(
            out=CF[:, tt, :], in0=P_nat[:, tt, :], scalar1=M1[:, tt:tt + 1],
            scalar2=None, op0=ALU.is_equal))
        P.op("vector", lambda e, s, tt=tt: e.scalar_tensor_tensor(
            out=CF[:, tt, :], in0=CF[:, tt, :], scalar=-1e30,
            in1=P_nat[:, tt, :], op0=ALU.mult, op1=ALU.add))
        P.op("vector", lambda e, s, tt=tt: e.tensor_reduce(
            out=M2[:, tt:tt + 1], in_=CF[:, tt, :], axis=AX.X, op=ALU.max))
        P.op("vector", lambda e, s, tt=tt: e.tensor_scalar(
            out=CF[:, tt, :], in0=P_nat[:, tt, :], scalar1=M2[:, tt:tt + 1],
            scalar2=None, op0=ALU.is_ge))
        P.op("vector", lambda e, s, tt=tt: e.tensor_scalar(
            out=DEN[:, tt:tt + 1], in0=M1[:, tt:tt + 1],
            scalar1=M2[:, tt:tt + 1], scalar2=1e-8, op0=ALU.add, op1=ALU.add))
        P.op("vector", lambda e, s, tt=tt: e.reciprocal(
            out=DEN[:, tt:tt + 1], in_=DEN[:, tt:tt + 1]))
        P.op("vector", lambda e, s, tt=tt: e.tensor_tensor(
            out=CF[:, tt, :], in0=CF[:, tt, :], in1=P_nat[:, tt, :],
            op=ALU.mult))
        rt_toks.append(P.op("vector", lambda e, s, tt=tt: e.tensor_scalar(
            out=GE_bf[:, tt, :], in0=CF[:, tt, :], scalar1=DEN[:, tt:tt + 1],
            scalar2=None, op0=ALU.mult)))
    geT_done = None
    for tt in range(NTT):
        bank = B.PSB[:, 512 * (tt % 2) + 256:512 * (tt % 2) + 320].bitcast(BF16)
        mm_t = P.op("tensor", lambda e, s, tt=tt, bank=bank: e.transpose(
            out=bank[0:6, 0:128], in_=GE_bf[:, tt, :], identity=eye),
            waits=(rt_toks[-1], fp.get(("get", tt % 2))), inc="t")
        geT_done = P.op("scalar", lambda e, s, tt=tt, bank=bank: e.copy(
            out=A_geT[0:6, 128 * tt:128 * (tt + 1)], in_=bank[0:6, 0:128]),
            waits=(mm_t,))
        fp[("get", tt % 2)] = geT_done

    WN = {"w1": ("L1_ew1T", "L1_fc1T"), "w3": ("L1_ew3T", "L1_fc2T"),
          "w2": ("L1_ew2T", "L1_fc3T")}

    def wsrc(kind, ei, c0, c1):
        en, sn = WN[kind]
        if ei == 6:
            return wslice_src(B.wh[sn], c0, c1)
        return esl_src(B.wh[en], ei, c0, c1)

    x_done = None
    for tch in range(2):
        P.scope = f"b1:moe.{tch}"
        t0 = 1024 * tch
        for ei in range(7):
            shared = ei == 6
            geb_ev = []
            if not shared:
                for hh in range(2):
                    gps = B.PSB[:, 1024 + 512 * hh:1024 + 512 * (hh + 1)]
                    mm_g = P.op("tensor", lambda e, s, ei=ei, hh=hh, t0=t0, gps=gps: e.matmul(
                        out=gps, lhsT=cbf(B, "sel6")[0:6, 128 * ei:128 * (ei + 1)],
                        rhs=A_geT[0:6, t0 + 512 * hh:t0 + 512 * (hh + 1)],
                        start=True, stop=True),
                        waits=(geT_done, fp.get(("gebf", hh))), inc="t")
                    ev = P.op("scalar", lambda e, s, hh=hh, gps=gps: e.copy(
                        out=A_geB[:, 512 * hh:512 * (hh + 1)], in_=gps),
                        waits=(mm_g, fp.get(("gebu", hh))))
                    geb_ev.append(ev)
                    fp[("gebf", hh)] = ev
            h_done = None
            for it in range(NDT):
                bi = it % 2
                w1t = P.dma("gpsimd", A_we1[bi][:], wsrc("w1", ei, 128 * it, 128 * (it + 1)),
                            waits=(fp.get(("w1f", bi)), B.tok_s1),
                            sem=("d_w0", "d_w4")[bi])
                w3t = P.dma("gpsimd", A_we3[bi][:], wsrc("w3", ei, 128 * it, 128 * (it + 1)),
                            waits=(fp.get(("w3f", bi)), B.tok_s1),
                            sem=("d_w1", "d_w5")[bi])
                for hh in range(2):
                    ps1 = B.PSA[:, 512 * hh:512 * (hh + 1)]
                    ps3 = B.PSA[:, 1024 + 512 * hh:1024 + 512 * (hh + 1)]
                    for kt in range(NDT):
                        mm1 = P.op("tensor", lambda e, s, bi=bi, kt=kt, hh=hh, t0=t0, ps1=ps1: e.matmul(
                            out=ps1, lhsT=A_we1[bi][:, kt, :],
                            rhs=B.XN[:, kt, t0 + 512 * hh:t0 + 512 * (hh + 1)],
                            start=(kt == 0), stop=(kt == NDT - 1)),
                            waits=(w1t, B.tok_s1, fp.get(("h1f", hh))) if kt == 0 else (),
                            inc="t" if kt == NDT - 1 else None)
                    for kt in range(NDT):
                        mm3 = P.op("tensor", lambda e, s, bi=bi, kt=kt, hh=hh, t0=t0, ps3=ps3: e.matmul(
                            out=ps3, lhsT=A_we3[bi][:, kt, :],
                            rhs=B.XN[:, kt, t0 + 512 * hh:t0 + 512 * (hh + 1)],
                            start=(kt == 0), stop=(kt == NDT - 1)),
                            waits=(w3t, fp.get(("h3f", hh))) if kt == 0 else (),
                            inc="t" if kt == NDT - 1 else None)
                    sil = P.op("scalar", lambda e, s, ps1=ps1: e.activation(
                        out=A_sil[:], in_=ps1, func=AF.Silu),
                        waits=(mm1, fp.get("silf")))
                    hv = P.op("vector", lambda e, s, it=it, hh=hh, ps3=ps3: e.tensor_tensor(
                        out=A_h[:, it, 512 * hh:512 * (hh + 1)], in0=A_sil[:],
                        in1=ps3, op=ALU.mult),
                        waits=(sil, mm3, fp.get(("hfree", it, hh))))
                    fp[("h1f", hh)] = hv
                    fp[("h3f", hh)] = hv
                    fp["silf"] = hv
                    if not shared:
                        hv = P.op("vector", lambda e, s, it=it, hh=hh: e.tensor_tensor(
                            out=A_h[:, it, 512 * hh:512 * (hh + 1)],
                            in0=A_h[:, it, 512 * hh:512 * (hh + 1)],
                            in1=A_geB[:, 512 * hh:512 * (hh + 1)],
                            op=ALU.mult), waits=(geb_ev[hh],))
                    h_done = hv
                fp[("w1f", bi)] = mm1
                fp[("w3f", bi)] = mm3
            for dt_ in range(NDT):
                bi = dt_ % 2
                w2t = P.dma("gpsimd", A_w2[bi][:], wsrc("w2", ei, 128 * dt_, 128 * (dt_ + 1)),
                            waits=(fp.get(("w2f", bi)), B.tok_s1),
                            sem=("d_w2", "d_w6")[bi])
                for hh in range(2):
                    psy = B.PSB[:, 512 * hh:512 * (hh + 1)]
                    for it in range(NDT):
                        mm = P.op("tensor", lambda e, s, bi=bi, it=it, hh=hh, psy=psy: e.matmul(
                            out=psy, lhsT=A_w2[bi][:, it, :],
                            rhs=A_h[:, it, 512 * hh:512 * (hh + 1)],
                            start=(it == 0), stop=(it == NDT - 1)),
                            waits=(w2t, h_done, geT_done, fp.get(("yev", hh))) if it == 0 else (),
                            inc="t" if it == NDT - 1 else None)
                    if ei == 0:
                        yev = P.op("scalar", lambda e, s, dt_=dt_, hh=hh, psy=psy: e.copy(
                            out=A_y[:, dt_, 512 * hh:512 * (hh + 1)], in_=psy),
                            waits=(mm, fp.get(("yw", dt_, hh))))
                    else:
                        yev = P.op("vector", lambda e, s, dt_=dt_, hh=hh, psy=psy: e.tensor_tensor(
                            out=A_y[:, dt_, 512 * hh:512 * (hh + 1)],
                            in0=A_y[:, dt_, 512 * hh:512 * (hh + 1)], in1=psy,
                            op=ALU.add), waits=(mm,))
                    fp[("yev", hh)] = yev
                    fp[("yv", dt_, hh)] = yev
                fp[("w2f", bi)] = mm
            for it in range(NDT):
                for hh in range(2):
                    fp[("hfree", it, hh)] = mm
        for dt_ in range(NDT):
            for hh in range(2):
                tq = slice(t0 + 512 * hh, t0 + 512 * (hh + 1))
                g1 = P.op("scalar", lambda e, s, dt_=dt_, hh=hh: e.activation(
                    out=A_gf[:], in_=A_y[:, dt_, 512 * hh:512 * (hh + 1)],
                    func=AF.Silu),
                    waits=(fp[("yv", dt_, hh)], fp.get("gff")))
                g2 = P.op("vector", lambda e, s, dt_=dt_, tq=tq: e.tensor_tensor(
                    out=A_gf[:], in0=A_gf[:], in1=B.ST[:, dt_, tq],
                    op=ALU.mult), waits=(g1, B.tok_s1))
                x_done = P.op("vector", lambda e, s, dt_=dt_, tq=tq: e.tensor_tensor(
                    out=B.XT[:, dt_, tq], in0=B.XT[:, dt_, tq], in1=A_gf[:],
                    op=ALU.add))
                fp["gff"] = x_done
                fp[("yw", dt_, hh)] = x_done
    B.tok_x_final = x_done
    ar.release()


def emit_block2(P, B):
    P.scope = "b2:attn"
    """L2: RoPE retention (chunked recurrence) * silu(gate) @ wo ; quantum FFN."""
    ar = B.ar
    ar.mark()
    gamma = [1.0 - 2.0 ** (-5.0 - h) for h in range(H)]

    tok_b1 = P.dma("sync", B.SMALL[:, 0:32], B.wh["L2_b1_col"], sem="d_small",
                   waits=(B.tok_xn,))
    tok_th = P.dma("sync", B.SMALL[:, 32:64], B.wh["L2_theta_col"], sem="d_small")
    tok_wr = P.dma("sync", B.SMALL[:, 64:96], B.wh["L2_wr_col"], sem="d_small")
    tok_wi = P.dma("sync", B.SMALL[:, 96:128], B.wh["L2_wi_col"], sem="d_small")
    tok_b2 = P.dma("sync", B.SMALL[:, 128:136], B.wh["L2_b2_col"], sem="d_small")
    B.tok_l2small = tok_b2

    A_qw = ar.alloc3(NDT, 256)
    A_kw = ar.alloc3(NDT, 256)
    A_vw = ar.alloc3(NDT, 256)
    A_gw = ar.alloc3(NDT, 256)
    A_ow = ar.alloc3(2, 1024)
    A_qT = ar.alloc3(2, T)
    A_kT = ar.alloc3(2, T)
    A_gT = ar.alloc3(2, T)
    A_vx = ar.alloc4(NTT, 4, 64)
    A_kn = ar.alloc(256)
    A_pm = ar.alloc3(4, 128)
    A_S = ar.alloc(256).bitcast(F32)
    A_Sb = ar.alloc(256)
    A_rt = ar.alloc(512).bitcast(F32)
    A_cos = ar.alloc(T)
    A_sin = ar.alloc(T)
    A_qg = ar.alloc3(2, 128)
    A_kg = ar.alloc3(2, 128)
    A_dmv = ar.alloc3(4, 128)
    tok_cos = P.dma("gpsimd", A_cos[:], chbm(B, "rope_cos"), sem="d_gr",
                    waits=(B.tok_xn,))
    tok_sin = P.dma("gpsimd", A_sin[:], chbm(B, "rope_sin"), sem="d_gr")

    prev = {}
    p_done = None
    for hg in range(4):
        c0 = hg * 256
        wsem = f"d_w{hg % 2}"
        w_q = P.dma("gpsimd", A_qw[:], wslice_src(B.wh["L2_wqT"], c0, c0 + 256),
                    waits=(prev.get("q_done"), B.tok_xn), sem=wsem)
        w_k = P.dma("gpsimd", A_kw[:], wslice_src(B.wh["L2_wkT"], c0, c0 + 256),
                    waits=(prev.get("k_done"),), sem=wsem)
        w_v = P.dma("gpsimd", A_vw[:], wslice_src(B.wh["L2_wvT"], c0, c0 + 256),
                    waits=(prev.get("v_done"),), sem=wsem)
        w_g = P.dma("gpsimd", A_gw[:], wslice_src(B.wh["L2_wgT"], c0, c0 + 256),
                    waits=(prev.get("g_done"),), sem=wsem)
        w_o = P.dma("gpsimd", A_ow[:],
                    B.wh["L2_woT"][c0:c0 + 256, :].rearrange("(it p) c -> p it c", p=128),
                    waits=(p_done,), sem=wsem)
        tok_con = None
        for oc in range(2):
            tok_con = P.dma("gpsimd", A_qg[:, oc, :], chbm(B, f"l2_qg_{2 * hg + oc}"),
                            sem="d_gr", waits=(prev.get("con_used"),))
            tok_con = P.dma("gpsimd", A_kg[:, oc, :], chbm(B, f"l2_kg_{2 * hg + oc}"),
                            sem="d_gr")
        for hs in range(4):
            h_ = 4 * hg + 2 * (hs // 2) + (hs % 2)
            tok_con = P.dma("gpsimd", A_dmv[:, hs, :], chbm(B, f"l2_decmv_{h_}"),
                            sem="d_gr")

        def proj2(wview, dst, wtok, act=None):
            last = None
            for oc in range(2):
                for ch in range(4):
                    psum = B.PSA[:, 512 * ch:512 * (ch + 1)]
                    for it in range(NDT):
                        mm = P.op("tensor", lambda e, s, oc=oc, ch=ch, it=it, wview=wview, psum=psum: e.matmul(
                            out=psum, lhsT=wview[:, it, 128 * oc:128 * (oc + 1)],
                            rhs=B.XN[:, it, 512 * ch:512 * (ch + 1)],
                            start=(it == 0), stop=(it == NDT - 1)),
                            waits=(wtok, B.tok_xn, prev.get(("evac", ch))) if it == 0 else (),
                            inc="t" if it == NDT - 1 else None)
                    if act is not None:
                        ev = P.op("scalar", lambda e, s, oc=oc, ch=ch, dst=dst, psum=psum, act=act: e.activation(
                            out=dst[:, oc, 512 * ch:512 * (ch + 1)], in_=psum, func=act),
                            waits=(mm,))
                    else:
                        ev = P.op("scalar", lambda e, s, oc=oc, ch=ch, dst=dst, psum=psum: e.copy(
                            out=dst[:, oc, 512 * ch:512 * (ch + 1)], in_=psum),
                            waits=(mm,))
                    prev[("evac", ch)] = ev
                    last = ev
            return last

        q_done = proj2(A_qw, A_qT, w_q)
        k_done = proj2(A_kw, A_kT, w_k)
        g_done = proj2(A_gw, A_gT, w_g, act=AF.Silu)

        v_done = None
        for tt in range(NTT):
            psum = B.PSB[:, 1024 + 512 * (tt % 2):1024 + 512 * (tt % 2) + 256]
            for it in range(NDT):
                mm = P.op("tensor", lambda e, s, tt=tt, it=it, psum=psum: e.matmul(
                    out=psum, lhsT=B.XN[:, it, 128 * tt:128 * (tt + 1)],
                    rhs=A_vw[:, it, :], start=(it == 0), stop=(it == NDT - 1)),
                    waits=(w_v, B.tok_xn, prev.get(("vevac", tt % 2)), prev.get("ret_done")) if it == 0 else (),
                    inc="t" if it == NDT - 1 else None)
            ev = P.op("vector", lambda e, s, tt=tt, psum=psum: e.tensor_copy(
                out=A_vx[:, tt, :, :],
                in_=psum.rearrange("p (a b) -> p a b", b=64)), waits=(mm,))
            prev[("vevac", tt % 2)] = ev
            v_done = ev

        rope_done = {}
        for (srcT, src_done, tag) in ((A_qT, q_done, "q"), (A_kT, k_done, "k")):
            lastt = None
            for oc in range(2):
                for ch in range(8):
                    tsl = slice(256 * ch, 256 * (ch + 1))
                    mm = P.op("tensor", lambda e, s, oc=oc, tsl=tsl, srcT=srcT: e.matmul(
                        out=B.PSA[:, 1536:1792], lhsT=cbf(B, "rope_rotT"),
                        rhs=srcT[:, oc, tsl], start=True, stop=True),
                        waits=(src_done, g_done, prev.get("rope_ps")), inc="t")
                    t1 = P.op("vector", lambda e, s, tsl=tsl: e.tensor_tensor(
                        out=A_rt[:, 0:256], in0=B.PSA[:, 1536:1792],
                        in1=A_sin[:, tsl], op=ALU.mult), waits=(mm, tok_sin))
                    prev["rope_ps"] = t1
                    P.op("vector", lambda e, s, oc=oc, tsl=tsl, srcT=srcT: e.tensor_tensor(
                        out=srcT[:, oc, tsl], in0=srcT[:, oc, tsl],
                        in1=A_cos[:, tsl], op=ALU.mult), waits=(tok_cos,))
                    lastt = P.op("vector", lambda e, s, oc=oc, tsl=tsl, srcT=srcT: e.tensor_tensor(
                        out=srcT[:, oc, tsl], in0=srcT[:, oc, tsl],
                        in1=A_rt[:, 0:256], op=ALU.add))
            rope_done[tag] = lastt

        fold_done = {}
        for (srcT, ctab, dep, tag) in ((A_qT, A_qg, rope_done["q"], "q"),
                                       (A_kT, A_kg, rope_done["k"], "k")):
            lastt = None
            for oc in range(2):
                cc_ = ctab[:, oc, :]
                lastt = P.op("vector", lambda e, s, oc=oc, srcT=srcT, cc_=cc_: e.tensor_tensor(
                    out=srcT[:, oc, :].rearrange("p (c t) -> p c t", t=128),
                    in0=srcT[:, oc, :].rearrange("p (c t) -> p c t", t=128),
                    in1=cc_[:, None, :].broadcast_to([128, 16, 128]),
                    op=ALU.mult), waits=(dep, tok_con))
            fold_done[tag] = lastt

        t_s0 = P.op("vector", lambda e, s: e.memset(A_S[:, 0:128], 0.0),
                    waits=(prev.get("ret_done"),))
        t_s0b = P.op("vector", lambda e, s: e.memset(A_Sb[:, 0:256], 0.0))

        ret_done = None
        sttok = {}
        for c in range(NTT):
            csl = slice(128 * c, 128 * (c + 1))
            for hs in range(4):
                rb = 64 * (hs % 2)
                oc = hs // 2
                h = 4 * hg + 2 * oc + (hs % 2)
                bank = B.PSA[:, 512 * hs:512 * (hs + 1)]
                obank = B.PSB[0:64, 512 * hs + 128 * (c % 4):512 * hs + 128 * (c % 4) + 128]
                Sf = A_S[rb:rb + 64, 64 * oc:64 * oc + 64]
                Sb = A_Sb[rb:rb + 64, 64 * oc:64 * oc + 64]
                mm_a = P.op("tensor", lambda e, s, hs=hs, rb=rb, oc=oc, csl=csl, bank=bank: e.matmul(
                    out=bank[:, 0:128], lhsT=A_kT[rb:rb + 64, oc, csl],
                    rhs=A_qT[rb:rb + 64, oc, csl], start=True, stop=True),
                    waits=(fold_done["q"], fold_done["k"],
                           sttok.get(("stt", hs)), t_s0b, v_done), inc="t")
                t_pm = P.op("vector", lambda e, s, hs=hs, bank=bank: e.tensor_tensor(
                    out=A_pm[:, hs, :], in0=bank[:, 0:128],
                    in1=A_dmv[:, hs, :], op=ALU.mult), waits=(mm_a, tok_con))
                mm_i1 = P.op("tensor", lambda e, s, hs=hs, c=c, obank=obank: e.matmul(
                    out=obank, lhsT=A_vx[:, c, hs, :], rhs=A_pm[:, hs, :],
                    start=True, stop=False),
                    waits=(t_pm, sttok.get(("oev", hs))), inc="t")
                mm_i2 = P.op("tensor", lambda e, s, hs=hs, rb=rb, oc=oc, csl=csl, obank=obank, Sb=Sb: e.matmul(
                    out=obank, lhsT=Sb, rhs=A_qT[rb:rb + 64, oc, csl],
                    start=False, stop=True), inc="t")
                mm_t = P.op("tensor", lambda e, s, hs=hs, rb=rb, oc=oc, csl=csl, bank=bank: e.transpose(
                    out=bank[:, 256:288].bitcast(BF16)[:, 0:64],
                    in_=A_kT[rb:rb + 64, oc, csl], identity=cbf(B, "eye")[rb:rb + 64, rb:rb + 64]),
                    waits=(fold_done["k"], sttok.get(("knev", hs))), inc="t")
                knev = P.op("vector", lambda e, s, hs=hs, bank=bank: e.tensor_copy(
                    out=A_kn[:, 64 * hs:64 * hs + 64],
                    in_=bank[:, 256:288].bitcast(BF16)[:, 0:64]), waits=(mm_t,))
                sttok[("knev", hs)] = knev
                mm_s = P.op("tensor", lambda e, s, hs=hs, c=c, bank=bank: e.matmul(
                    out=bank[0:64, 384:448], lhsT=A_kn[:, 64 * hs:64 * hs + 64],
                    rhs=A_vx[:, c, hs, :], start=True, stop=True),
                    waits=(knev, v_done), inc="t")
                oev = P.op("vector", lambda e, s, hs=hs, rb=rb, oc=oc, csl=csl, obank=obank: e.tensor_tensor(
                    out=A_kT[rb:rb + 64, oc, csl], in0=obank,
                    in1=A_gT[rb:rb + 64, oc, csl], op=ALU.mult),
                    waits=(mm_i2, g_done))
                sttok[("oev", hs)] = oev
                stt_t = P.op("vector", lambda e, s, h=h, bank=bank, Sf=Sf: e.scalar_tensor_tensor(
                    out=Sf, in0=Sf, scalar=float(gamma[h] ** 128),
                    in1=bank[0:64, 384:448], op0=ALU.mult, op1=ALU.add),
                    waits=(mm_s, t_s0))
                sttok[("stt", hs)] = P.op("vector", lambda e, s, Sf=Sf, Sb=Sb: e.tensor_copy(
                    out=Sb, in_=Sf))
                ret_done = sttok[("stt", hs)]
                prev[("retoev", hs)] = oev
        last_oev = prev[("retoev", 3)]
        prev["ret_done"] = ret_done

        for oc2 in range(NDT):
            for ch in range(4):
                psum = B.PSA[:, 512 * ch:512 * (ch + 1)]
                for it2 in range(2):
                    mm = P.op("tensor", lambda e, s, oc2=oc2, ch=ch, it2=it2, psum=psum: e.matmul(
                        out=psum, lhsT=A_ow[:, it2, 128 * oc2:128 * (oc2 + 1)],
                        rhs=A_kT[:, it2, 512 * ch:512 * (ch + 1)],
                        start=(it2 == 0), stop=(it2 == 1)),
                        waits=(w_o, last_oev, prev.get(("evac", ch)), sttok.get(("stt", 3))) if it2 == 0 else (),
                        inc="t" if it2 == 1 else None)
                if hg == 0:
                    ev = P.op("scalar", lambda e, s, oc2=oc2, ch=ch, psum=psum: e.copy(
                        out=B.ST[:, oc2, 512 * ch:512 * (ch + 1)], in_=psum),
                        waits=(mm,))
                else:
                    ev = P.op("vector", lambda e, s, oc2=oc2, ch=ch, psum=psum: e.tensor_tensor(
                        out=B.ST[:, oc2, 512 * ch:512 * (ch + 1)],
                        in0=B.ST[:, oc2, 512 * ch:512 * (ch + 1)], in1=psum,
                        op=ALU.add), waits=(mm,))
                prev[("evac", ch)] = ev
                p_done = ev
        prev["q_done"] = q_done
        prev["k_done"] = k_done
        prev["v_done"] = v_done
        prev["g_done"] = g_done
        prev["con_used"] = ret_done
    B.tok_s2 = p_done
    ar.release()
    emit_block2_ffn(P, B)


def emit_block2_ffn(P, B):
    P.scope = "b2:ffn"
    """hidden = xn@W1+b1; quantum transform; relu; y = act@W2 + b2; glue.

    sin() on ScalarE needs args in [-pi,pi]. Range reduction is done with
    strict compare-and-shift passes (thresholds +-PI_LO just below pi, shift
    TWO_PI_LO just below 2pi) so reduced values stay strictly inside
    [-pi,pi] under device fp32, sim fp32 and sim fp64 arithmetic alike
    (float->int cast rounding differs between device and sim, so casts are
    avoided entirely). theta_b = theta+0.1*b1 is per-feature (|theta| <=
    ~22 here, 6 passes cover |x|<=40.8 = 6.5 sigma of the theta init
    randn*2pi); after that the per-element argument theta_b_red + 0.1*h has
    |arg| <= pi + 0.1*|h|max << 3pi, so ONE shift pass suffices in the hot
    loop (|0.1*h| <= ~0.6 for layernormed x and W1 ~ randn/sqrt(d))."""
    ar = B.ar
    ar.mark()
    PI_LO = 3.1415925025939941
    TWO_PI_LO = 6.2831850051879883
    DC = ar.alloc(256).bitcast(F32)

    def _reduce_passes(dst, tmp, npass):
        tok = None
        for _ in range(npass):
            P.op("vector", lambda e, s, dst=dst, tmp=tmp: e.tensor_scalar(
                out=tmp, in0=dst, scalar1=PI_LO, scalar2=None, op0=ALU.is_gt))
            P.op("vector", lambda e, s, dst=dst, tmp=tmp: e.scalar_tensor_tensor(
                out=dst, in0=tmp, scalar=-TWO_PI_LO, in1=dst,
                op0=ALU.mult, op1=ALU.add))
            P.op("vector", lambda e, s, dst=dst, tmp=tmp: e.tensor_scalar(
                out=tmp, in0=dst, scalar1=-PI_LO, scalar2=None, op0=ALU.is_lt))
            tok = P.op("vector", lambda e, s, dst=dst, tmp=tmp: e.scalar_tensor_tensor(
                out=dst, in0=tmp, scalar=TWO_PI_LO, in1=dst,
                op0=ALU.mult, op1=ALU.add))
        return tok

    P.op("vector", lambda e, s: e.scalar_tensor_tensor(
        out=DC[:, 0:32], in0=B.SMALL[:, 0:32], scalar=0.1,
        in1=B.SMALL[:, 32:64], op0=ALU.mult, op1=ALU.add),
        waits=(B.tok_s2, ("d_small", P.cnt["d_small"])))
    _reduce_passes(DC[:, 0:32], DC[:, 96:128], 6)
    P.op("vector", lambda e, s: e.tensor_copy(
        out=DC[:, 32:64], in_=B.SMALL[:, 32:64]))
    _reduce_passes(DC[:, 32:64], DC[:, 96:128], 6)
    P.op("vector", lambda e, s: e.tensor_scalar(
        out=DC[:, 96:128], in0=DC[:, 32:64], scalar1=math.pi / 2.0,
        scalar2=None, op0=ALU.add))
    P.op("vector", lambda e, s: e.tensor_scalar(
        out=DC[:, 64:96], in0=DC[:, 96:128], scalar1=PI_LO, scalar2=None,
        op0=ALU.is_gt))
    t_c = P.op("vector", lambda e, s: e.scalar_tensor_tensor(
        out=DC[:, 96:128], in0=DC[:, 64:96], scalar=-TWO_PI_LO,
        in1=DC[:, 96:128], op0=ALU.mult, op1=ALU.add))
    t_sin = P.op("scalar", lambda e, s: e.activation(
        out=DC[:, 32:64], in_=DC[:, 32:64], func=AF.Sin), waits=(t_c,))
    t_cos = P.op("scalar", lambda e, s: e.activation(
        out=DC[:, 96:128], in_=DC[:, 96:128], func=AF.Sin))
    P.op("vector", lambda e, s: e.tensor_tensor(
        out=DC[:, 64:96], in0=DC[:, 96:128], in1=B.SMALL[:, 64:96],
        op=ALU.mult), waits=(t_cos,))
    P.op("vector", lambda e, s: e.tensor_tensor(
        out=DC[:, 96:128], in0=DC[:, 32:64], in1=B.SMALL[:, 96:128],
        op=ALU.mult), waits=(t_sin,))
    P.op("vector", lambda e, s: e.tensor_tensor(
        out=DC[:, 64:96], in0=DC[:, 64:96], in1=DC[:, 96:128], op=ALU.add))
    P.op("vector", lambda e, s: e.tensor_tensor(
        out=DC[:, 64:96], in0=DC[:, 64:96], in1=B.SMALL[:, 96:128],
        op=ALU.mult))
    tC1 = P.op("vector", lambda e, s: e.tensor_scalar(
        out=DC[:, 64:96], in0=DC[:, 64:96], scalar1=0.1, scalar2=None,
        op0=ALU.mult))

    A_w1 = ar.alloc3(NDT, 128)
    A_w2 = ar.alloc3(4, 1024)
    A_ht = ar.alloc3(32, 512)
    A_tf = [ar.alloc(1024).bitcast(F32), ar.alloc(1024).bitcast(F32)]
    A_cf = [ar.alloc(1024).bitcast(F32), ar.alloc(1024).bitcast(F32)]
    A_gf = ar.alloc(1024).bitcast(F32)
    fp = {}
    mm_done = None
    for tch in range(4):
        t0 = 512 * tch
        for ot in range(32):
            lw = P.dma("gpsimd", A_w1[:],
                       B.wh["L2_W1"].rearrange("(it p) c -> p it c", p=128)[:, :, 128 * ot:128 * (ot + 1)],
                       sem="d_w" + str(2 + ot % 2),
                       waits=(fp.get("w1used"), B.tok_s2))
            psum = B.PSA[:, 512 * (ot % 2):512 * (ot % 2) + 512]
            for it in range(NDT):
                mm = P.op("tensor", lambda e, s, it=it, ot=ot, t0=t0, psum=psum: e.matmul(
                    out=psum, lhsT=A_w1[:, it, :], rhs=B.XN[:, it, t0:t0 + 512],
                    start=(it == 0), stop=(it == NDT - 1)),
                    waits=(lw, B.tok_xn, fp.get(("hq", ot % 2)),
                           fp.get(("yev", ot % 2))) if it == 0 else (),
                    inc="t" if it == NDT - 1 else None)
            fp["w1used"] = mm
            tf = A_tf[ot % 2]
            cf = A_cf[ot % 2]
            m1 = P.op("scalar", lambda e, s, ot=ot, psum=psum, tf=tf: e.activation(
                out=tf[:, 0:512], in_=psum, func=AF.Identity, scale=0.1,
                bias=DC[:, ot:ot + 1]),
                waits=(mm, tC1, fp.get(("tffree", ot % 2))))
            m2 = P.op("vector", lambda e, s, tf=tf, cf=cf: e.tensor_scalar(
                out=cf[:, 0:512], in0=tf[:, 0:512], scalar1=PI_LO,
                scalar2=None, op0=ALU.is_gt), waits=(m1,))
            P.op("vector", lambda e, s, tf=tf, cf=cf: e.scalar_tensor_tensor(
                out=tf[:, 0:512], in0=cf[:, 0:512], scalar=-TWO_PI_LO,
                in1=tf[:, 0:512], op0=ALU.mult, op1=ALU.add))
            P.op("vector", lambda e, s, tf=tf, cf=cf: e.tensor_scalar(
                out=cf[:, 0:512], in0=tf[:, 0:512], scalar1=-PI_LO,
                scalar2=None, op0=ALU.is_lt))
            m5 = P.op("vector", lambda e, s, tf=tf, cf=cf: e.scalar_tensor_tensor(
                out=tf[:, 0:512], in0=cf[:, 0:512], scalar=TWO_PI_LO,
                in1=tf[:, 0:512], op0=ALU.mult, op1=ALU.add))
            k3 = P.op("scalar", lambda e, s, tf=tf: e.activation(
                out=tf[:, 0:512], in_=tf[:, 0:512], func=AF.Sin),
                waits=(m5,))
            k4 = P.op("vector", lambda e, s, ot=ot, psum=psum, tf=tf: e.scalar_tensor_tensor(
                out=tf[:, 0:512], in0=tf[:, 0:512], scalar=DC[:, 64 + ot:65 + ot],
                in1=psum, op0=ALU.mult, op1=ALU.add), waits=(k3,))
            hq = P.op("vector", lambda e, s, ot=ot, tf=tf: e.tensor_scalar(
                out=A_ht[:, ot, :], in0=tf[:, 0:512],
                scalar1=B.SMALL[:, ot:ot + 1], scalar2=0.0, op0=ALU.add,
                op1=ALU.max), waits=(k4, B.tok_l2small))
            fp[("hq", ot % 2)] = hq
            fp[("tffree", ot % 2)] = hq
        h_done = hq
        for ktg in range(8):
            lw2 = P.dma("gpsimd", A_w2[:],
                        B.wh["L2_W2"][512 * ktg:512 * (ktg + 1), :]
                        .rearrange("(g p) c -> p g c", p=128),
                        sem="d_w" + str(2 + ktg % 2),
                        waits=(fp.get("w2used"), h_done))
            mm = None
            for oc in range(NDT):
                psum = (B.PSA if oc < 4 else B.PSB)[:, 512 * (oc % 4):512 * (oc % 4) + 512]
                for k4_ in range(4):
                    kt = 4 * ktg + k4_
                    mm = P.op("tensor", lambda e, s, kt=kt, k4_=k4_, oc=oc, psum=psum, lw2=lw2, ktg=ktg: e.matmul(
                        out=psum, lhsT=A_w2[:, k4_, 128 * oc:128 * (oc + 1)],
                        rhs=A_ht[:, kt, :], start=(kt == 0), stop=(kt == 31)),
                        waits=(lw2, h_done, fp.get(("yev", oc % 4))) if (k4_ == 0 and ktg == 0) else ((lw2,) if k4_ == 0 else ()),
                        inc="t" if (k4_ == 3) else None)
            fp["w2used"] = mm
        for oc in range(NDT):
            psum = (B.PSA if oc < 4 else B.PSB)[:, 512 * (oc % 4):512 * (oc % 4) + 512]
            tq = slice(t0, t0 + 512)
            g1 = P.op("scalar", lambda e, s, oc=oc, psum=psum: e.activation(
                out=A_gf[:, 0:512], in_=psum, func=AF.Silu,
                bias=B.SMALL[:, 128 + oc:129 + oc]),
                waits=(mm, B.tok_l2small, ("v", P.cnt["v"])))
            g2 = P.op("vector", lambda e, s, oc=oc, tq=tq: e.tensor_tensor(
                out=A_gf[:, 0:512], in0=A_gf[:, 0:512], in1=B.ST[:, oc, tq],
                op=ALU.mult), waits=(g1, B.tok_s2))
            mm_done = P.op("vector", lambda e, s, oc=oc, tq=tq: e.tensor_tensor(
                out=B.XT[:, oc, tq], in0=B.XT[:, oc, tq], in1=A_gf[:, 0:512],
                op=ALU.add))
            fp[("yev", oc % 4)] = mm_done
    B.tok_x_final = mm_done
    ar.release()


def emit_fp4_quant(P, B, tmp, src, dst, n, waits=()):
    nb = n // 16
    U = tmp["U"][:, 0:n]
    R1 = tmp["R1"][:, 0:n]
    R2 = tmp["R2"][:, 0:n]
    CF = tmp["CF"][:, 0:n]
    CU = tmp["CU"][:, 0:n]
    BM = tmp["BM"][:, 0:nb]
    SC = tmp["SC"][:, 0:nb]
    SCH = tmp["SCH"][:, 0:nb]
    rs_bcast = BM[:, :, None].broadcast_to([128, nb, 16])
    sch_bcast = SCH[:, :, None].broadcast_to([128, nb, 16])
    src3 = src.rearrange("p (nb b) -> p nb b", b=16)
    dst3 = dst.rearrange("p (nb b) -> p nb b", b=16)
    U3 = U.rearrange("p (nb b) -> p nb b", b=16)
    R23 = R2.rearrange("p (nb b) -> p nb b", b=16)
    waits = tuple(waits) + (("v", P.cnt["v"]),)
    P.op("vector", lambda e, s: e.tensor_reduce(
        out=BM[:, :, None], in_=src3, axis=AX.X, op=ALU.max,
        apply_absolute_value=True), waits=waits)
    P.op("vector", lambda e, s: e.tensor_scalar(
        out=SC, in0=BM, scalar1=1.0 / 6.0, scalar2=1e-12, op0=ALU.mult,
        op1=ALU.max))
    P.op("vector", lambda e, s: e.reciprocal(out=BM, in_=SC))
    P.op("vector", lambda e, s: e.tensor_scalar(
        out=SCH, in0=SC, scalar1=0.5, scalar2=None, op0=ALU.mult))
    tabs = P.op("scalar", lambda e, s: e.activation(out=U, in_=src, func=AF.Abs),
                waits=waits)
    P.op("vector", lambda e, s: e.tensor_tensor(
        out=U3, in0=U3, in1=rs_bcast, op=ALU.mult), waits=(tabs,))
    P.op("vector", lambda e, s: e.tensor_scalar(
        out=R2, in0=U, scalar1=0.25, scalar2=None, op0=ALU.is_gt))
    for thr, w in ((0.75, 1.0), (1.25, 1.0), (1.75, 1.0), (2.5, 2.0),
                   (3.5, 2.0), (5.0, 4.0)):
        P.op("vector", lambda e, s, thr=thr: e.tensor_scalar(
            out=CF, in0=U, scalar1=thr, scalar2=None, op0=ALU.is_gt))
        if w == 1.0:
            P.op("vector", lambda e, s: e.tensor_tensor(
                out=R2, in0=R2, in1=CF, op=ALU.add))
        else:
            P.op("vector", lambda e, s, w=w: e.scalar_tensor_tensor(
                out=R2, in0=CF, scalar=w, in1=R2, op0=ALU.mult, op1=ALU.add))
    P.op("vector", lambda e, s: e.tensor_scalar(
        out=CU, in0=src, scalar1=0.0, scalar2=None, op0=ALU.is_lt))
    P.op("vector", lambda e, s: e.tensor_scalar(
        out=R1, in0=R2, scalar1=-1.0, scalar2=None, op0=ALU.mult))
    P.op("vector", lambda e, s: e.copy_predicated(R2, CU, R1))
    return P.op("vector", lambda e, s: e.tensor_tensor(
        out=dst3, in0=R23, in1=sch_bcast, op=ALU.mult))


def mk_fp4_tmps(ar, n=1408):
    return {
        "U": ar.alloc(2 * n).bitcast(F32),
        "R1": ar.alloc(2 * n).bitcast(F32),
        "R2": ar.alloc(2 * n).bitcast(F32),
        "CF": ar.alloc(2 * n).bitcast(F32),
        "CU": ar.alloc((n + 1) // 2 * 2).bitcast(U8)[:, 0:n],
        "BM": ar.alloc(2 * (n // 16)).bitcast(F32),
        "SC": ar.alloc(2 * (n // 16)).bitcast(F32),
        "SCH": ar.alloc(2 * (n // 16)).bitcast(F32),
    }


def emit_block3(P, B):
    P.scope = "b3:attn"
    """L3: rmsnorm -> gated full-causal attention -> Wout ; FFN: fp4 SwiGLU."""
    ar = B.ar
    ar.mark()
    SCALE = 0.125
    ones_bf = cbf(B, "ones_col")
    ones_row = cbf(B, "ones_row", rows=(0, 1))

    tok_nw = P.dma("sync", B.SMALL[:, 0:8], B.wh["L3_normw_col"], sem="d_small",
                   waits=(B.tok_xn,))

    A_invB = ar.alloc(T)
    A_invC = ar.alloc(32).bitcast(F32)
    ar.mark()
    SQ = ar.alloc3(NDT, 512)
    R_ss = ar.rowf(512)
    R_inv = ar.rowf(512)
    RB_i = ar.rowb(512)
    stats = B.PSA[0:1, 0:512]
    inv_done = None
    invc_tok = None
    for cc in range(4):
        tsl = slice(512 * cc, 512 * (cc + 1))
        sq_toks = []
        for dt_ in range(NDT):
            sq_toks.append(P.op("vector",
                lambda e, s, dt_=dt_, tsl=tsl: e.tensor_tensor(
                    out=SQ[:, dt_, :], in0=B.XN[:, dt_, tsl],
                    in1=B.XN[:, dt_, tsl], op=ALU.mult), waits=(B.tok_xn,)))
        for dt_ in range(NDT):
            mm_sq = P.op("tensor",
                lambda e, s, dt_=dt_: e.matmul(
                    out=stats, lhsT=ones_bf, rhs=SQ[:, dt_, :],
                    start=(dt_ == 0), stop=(dt_ == NDT - 1)),
                waits=(sq_toks[dt_], inv_done, B.tok_xn) if dt_ == 0 else (sq_toks[dt_],),
                inc="t" if dt_ == NDT - 1 else None)
        t1 = P.op("scalar", lambda e, s: e.mul(out=R_ss, in_=stats, mul=1.0 / 1024),
                  waits=(mm_sq,))
        P.op("scalar", lambda e, s: e.activation(
            out=R_ss, in_=R_ss, func=AF.Ln, bias=cvf(B, "rms_eps", (0, 1))))
        t3 = P.op("scalar", lambda e, s: e.activation(
            out=R_inv, in_=R_ss, func=AF.Exp, scale=-0.5))
        t4 = P.op("vector", lambda e, s: e.tensor_copy(out=RB_i, in_=R_inv),
                  waits=(t3, invc_tok))
        mm_i = P.op("tensor", lambda e, s: e.matmul(
            out=B.PSB[:, 0:512], lhsT=ones_row, rhs=RB_i, start=True, stop=True),
            waits=(t4,), inc="t")
        inv_done = P.op("scalar", lambda e, s, tsl=tsl: e.copy(
            out=A_invB[:, tsl], in_=B.PSB[:, 0:512]), waits=(mm_i,))
        for c4 in range(4):
            mm_c = P.op("tensor", lambda e, s, c4=c4: e.transpose(
                out=B.PSB[:, 512 + 1024 * (c4 % 2):512 + 1024 * (c4 % 2) + 256].bitcast(BF16)[:, 0:1],
                in_=RB_i[:, 128 * c4:128 * (c4 + 1)], identity=cbf(B, "eye")[0:1, 0:1]),
                waits=(t4, invc_tok), inc="t")
            invc_tok = P.op("scalar", lambda e, s, cc=cc, c4=c4: e.copy(
                out=A_invC[:, 4 * cc + c4:4 * cc + c4 + 1],
                in_=B.PSB[:, 512 + 1024 * (c4 % 2):512 + 1024 * (c4 % 2) + 256].bitcast(BF16)[:, 0:1]),
                waits=(mm_c,))
    ar.release()

    A_gfm = ar.alloc(T)
    A_wg = ar.alloc3(NDT, 16)
    w_g = P.dma("gpsimd", A_wg[:], B.wh["L3_Wg"].rearrange("(it p) c -> p it c", p=128),
                waits=(B.tok_xn, inv_done), sem="d_w1")
    t_wg = None
    for it in range(NDT):
        t_wg = P.op("vector", lambda e, s, it=it: e.tensor_scalar(
            out=A_wg[:, it, :], in0=A_wg[:, it, :],
            scalar1=B.SMALL[:, it:it + 1], scalar2=None, op0=ALU.mult),
            waits=(w_g, tok_nw))
    gfm_done = None
    for ch in range(4):
        tsl = slice(512 * ch, 512 * (ch + 1))
        for it in range(NDT):
            mm = P.op("tensor", lambda e, s, it=it, tsl=tsl: e.matmul(
                out=B.PSB[0:16, 1024:1536], lhsT=A_wg[:, it, :],
                rhs=B.XN[:, it, tsl], start=(it == 0), stop=(it == NDT - 1)),
                waits=(t_wg, gfm_done) if it == 0 else (),
                inc="t" if it == NDT - 1 else None)
        t1 = P.op("vector", lambda e, s, tsl=tsl: e.tensor_tensor(
            out=A_gfm[0:16, tsl], in0=B.PSB[0:16, 1024:1536],
            in1=A_invB[0:16, tsl], op=ALU.mult), waits=(mm, inv_done))
        gfm_done = P.op("scalar", lambda e, s, tsl=tsl: e.activation(
            out=A_gfm[0:16, tsl], in_=A_gfm[0:16, tsl], func=AF.Sigmoid),
            waits=(t1,))

    A_qw = ar.alloc3(NDT, 256)
    A_kw = ar.alloc3(NDT, 256)
    A_vw = ar.alloc3(NDT, 256)
    A_ow = ar.alloc3(2, 1024)
    A_qT = ar.alloc3(2, T)
    A_kT = ar.alloc3(2, T)
    A_at = ar.alloc3(2, T)
    A_vx = ar.alloc4(NTT, 4, 65)
    A_pt = ar.alloc3(4, 512)
    A_gr = ar.rowb(512)
    A_rec = ar.alloc(512)
    R_rec = ar.rowf(512)
    RB_rec = ar.rowb(512)

    prev = {}
    p_done = None
    for hg in range(4):
        c0 = hg * 256
        wsem = f"d_w{hg % 2}"
        first = invc_tok if hg == 0 else None
        w_q = P.dma("gpsimd", A_qw[:], wslice_src(B.wh["L3_Wq"], c0, c0 + 256),
                    waits=(prev.get("q_done"), B.tok_xn, inv_done), sem=wsem)
        w_k = P.dma("gpsimd", A_kw[:], wslice_src(B.wh["L3_Wkv"], c0, c0 + 256),
                    waits=(prev.get("k_done"), first), sem=wsem)
        w_v = P.dma("gpsimd", A_vw[:], wslice_src(B.wh["L3_Wkv"], 1024 + c0, 1024 + c0 + 256),
                    waits=(prev.get("v_done"), first), sem=wsem)
        w_o = P.dma("gpsimd", A_ow[:],
                    B.wh["L3_Wout"][c0:c0 + 256, :].rearrange("(it p) c -> p it c", p=128),
                    waits=(p_done, first), sem=wsem)
        nw_done = None
        for it in range(NDT):
            for wv, wt in ((A_qw, w_q), (A_kw, w_k), (A_vw, w_v)):
                nw_done = P.op("vector", lambda e, s, it=it, wv=wv: e.tensor_scalar(
                    out=wv[:, it, :], in0=wv[:, it, :],
                    scalar1=B.SMALL[:, it:it + 1], scalar2=None, op0=ALU.mult),
                    waits=(wt, tok_nw))

        def proj_fm3(wview, dst, scale=None):
            last = None
            for oc in range(2):
                for ch in range(4):
                    psum = B.PSA[:, 512 * ch:512 * (ch + 1)]
                    for it in range(NDT):
                        mm = P.op("tensor", lambda e, s, oc=oc, ch=ch, it=it, wview=wview, psum=psum: e.matmul(
                            out=psum, lhsT=wview[:, it, 128 * oc:128 * (oc + 1)],
                            rhs=B.XN[:, it, 512 * ch:512 * (ch + 1)],
                            start=(it == 0), stop=(it == NDT - 1)),
                            waits=(nw_done, inv_done, prev.get(("evac", ch))) if it == 0 else (),
                            inc="t" if it == NDT - 1 else None)
                    if scale is not None:
                        ev = P.op("vector", lambda e, s, oc=oc, ch=ch, dst=dst, psum=psum, scale=scale: e.scalar_tensor_tensor(
                            out=dst[:, oc, 512 * ch:512 * (ch + 1)], in0=psum,
                            scalar=scale, in1=A_invB[:, 512 * ch:512 * (ch + 1)],
                            op0=ALU.mult, op1=ALU.mult), waits=(mm,))
                    else:
                        ev = P.op("vector", lambda e, s, oc=oc, ch=ch, dst=dst, psum=psum: e.tensor_tensor(
                            out=dst[:, oc, 512 * ch:512 * (ch + 1)], in0=psum,
                            in1=A_invB[:, 512 * ch:512 * (ch + 1)], op=ALU.mult),
                            waits=(mm,))
                    prev[("evac", ch)] = ev
                    last = ev
            return last

        q_done = proj_fm3(A_qw, A_qT, scale=SCALE)
        k_done = proj_fm3(A_kw, A_kT)

        t_ones = P.op("vector", lambda e, s: e.memset(A_vx[:, :, :, 64:65], 1.0),
                      waits=(prev.get("at_done"),))
        v_done = None
        for tt in range(NTT):
            psum = B.PSB[:, 1024 + 512 * (tt % 2):1024 + 512 * (tt % 2) + 256]
            for it in range(NDT):
                mm = P.op("tensor", lambda e, s, tt=tt, it=it, psum=psum: e.matmul(
                    out=psum, lhsT=B.XN[:, it, 128 * tt:128 * (tt + 1)],
                    rhs=A_vw[:, it, :], start=(it == 0), stop=(it == NDT - 1)),
                    waits=(nw_done, gfm_done, prev.get(("vevac", tt % 2)), prev.get("pv_evac")) if it == 0 else (),
                    inc="t" if it == NDT - 1 else None)
            ev = P.op("vector", lambda e, s, tt=tt, psum=psum: e.tensor_scalar(
                out=A_vx[:, tt, :, 0:64],
                in0=psum.rearrange("p (a b) -> p a b", b=64),
                scalar1=A_invC[:, tt:tt + 1], scalar2=None, op0=ALU.mult),
                waits=(mm, t_ones, invc_tok))
            prev[("vevac", tt % 2)] = ev
            v_done = ev

        attn_done = prev.get("at_done")
        for qt in range(NTT):
            nkt = qt + 1
            G = (nkt + 3) // 4
            qo = 128 * (qt % 4)
            pvs = [B.PSB[0:65, 512 * hs + qo:512 * hs + qo + 128] for hs in range(4)]
            g_dma = P.dma("gpsimd", A_gr.rearrange("o (h c) -> o h c", c=128),
                          A_gfm[4 * hg:4 * hg + 4, 128 * qt:128 * (qt + 1)],
                          sem="d_gr",
                          waits=(gfm_done, prev.get("gr_used")))
            last_pv = None
            t_exp_prev = prev.get("psa_free")
            for g in range(G):
                n = min(4, nkt - 4 * g)
                for hs in range(4):
                    rb = (hs % 2) * 64
                    for ki in range(n):
                        kt = 4 * g + ki
                        mm_s = P.op("tensor", lambda e, s, hs=hs, kt=kt, qt=qt, rb=rb, ki=ki: e.matmul(
                            out=B.PSA[:, 512 * hs + 128 * ki:512 * hs + 128 * (ki + 1)],
                            lhsT=A_kT[rb:rb + 64, hs // 2, 128 * kt:128 * (kt + 1)],
                            rhs=A_qT[rb:rb + 64, hs // 2, 128 * qt:128 * (qt + 1)],
                            start=True, stop=True),
                            waits=(q_done, k_done, t_exp_prev) if (hs == 0 and ki == 0) else (),
                            inc="t" if (hs == 3 and ki == n - 1) else None)
                t_exp = P.op("scalar", lambda e, s, n=n: e.activation(
                    out=A_pt[:, :, 0:128 * n], func=AF.Exp,
                    in_=B.PSA[:].rearrange("p (h c) -> p h c", c=512)[:, :, 0:128 * n]),
                    waits=(mm_s, ("v", P.cnt["v"]),))
                t_exp_prev = t_exp
                tok_after_exp = t_exp
                if 4 * g <= qt < 4 * (g + 1):
                    ki_d = qt - 4 * g
                    tok_after_exp = P.op("vector", lambda e, s, ki_d=ki_d: e.tensor_tensor(
                        out=A_pt[:, :, 128 * ki_d:128 * (ki_d + 1)],
                        in0=A_pt[:, :, 128 * ki_d:128 * (ki_d + 1)],
                        in1=cbf(B, "triuIncl")[:, None, :].broadcast_to([128, 4, 128]),
                        op=ALU.mult), waits=(t_exp,))
                for hs in range(4):
                    for ki in range(n):
                        kt = 4 * g + ki
                        last_pv = P.op("tensor", lambda e, s, hs=hs, kt=kt, ki=ki, g=g, n=n, G=G, pv=pvs[hs]: e.matmul(
                            out=pv,
                            lhsT=A_vx[:, kt, hs, :],
                            rhs=A_pt[:, hs, 128 * ki:128 * (ki + 1)],
                            start=(g == 0 and ki == 0), stop=(g == G - 1 and ki == n - 1)),
                            waits=(tok_after_exp, v_done, prev.get("pv_evac")) if (hs == 0 and ki == 0) else (),
                            inc="t" if (hs == 3 and ki == n - 1) else None)
            sums_v = B.PSB[64:65, 0:2048].rearrange("o (h c) -> o h c", c=512)[:, :, qo:qo + 128]
            t_rc = P.op("vector", lambda e, s, sums_v=sums_v: e.reciprocal(
                out=R_rec.rearrange("o (h c) -> o h c", c=128), in_=sums_v),
                waits=(last_pv,))
            t_rg = P.op("vector", lambda e, s: e.tensor_tensor(
                out=RB_rec, in0=R_rec, in1=A_gr, op=ALU.mult), waits=(g_dma, t_rc))
            prev["gr_used"] = t_rg
            mm_rb = P.op("tensor", lambda e, s: e.matmul(
                out=B.PSA[0:64, 0:512], lhsT=ones_row[:, 0:64],
                rhs=RB_rec, start=True, stop=True), waits=(t_rg, t_exp_prev), inc="t")
            t_rcb = P.op("scalar", lambda e, s: e.copy(out=A_rec[0:64, :],
                                                       in_=B.PSA[0:64, 0:512]),
                         waits=(mm_rb, attn_done))
            prev["psa_free"] = t_rcb
            for par in range(2):
                ev = P.op("vector", lambda e, s, par=par, qt=qt, qo=qo: e.tensor_tensor(
                    out=A_at[64 * par:64 * par + 64, :, 128 * qt:128 * (qt + 1)],
                    in0=B.PSB[0:64, 0:2048].rearrange("p (h c) -> p h c", c=512)[:, par::2, qo:qo + 128],
                    in1=A_rec[0:64, :].rearrange("p (h c) -> p h c", c=128)[:, par::2, :],
                    op=ALU.mult), waits=(t_rcb,))
                prev["pv_evac"] = ev
                attn_done = ev
        prev["at_done"] = attn_done

        for oc in range(NDT):
            for ch in range(4):
                psum = B.PSA[:, 512 * ch:512 * (ch + 1)]
                for it2 in range(2):
                    mm = P.op("tensor", lambda e, s, oc=oc, ch=ch, it2=it2, psum=psum: e.matmul(
                        out=psum, lhsT=A_ow[:, it2, 128 * oc:128 * (oc + 1)],
                        rhs=A_at[:, it2, 512 * ch:512 * (ch + 1)],
                        start=(it2 == 0), stop=(it2 == 1)),
                        waits=(w_o, attn_done, prev.get(("evac", ch)), prev.get("psa_free")) if it2 == 0 else (),
                        inc="t" if it2 == 1 else None)
                if hg == 0:
                    ev = P.op("scalar", lambda e, s, oc=oc, ch=ch, psum=psum: e.copy(
                        out=B.ST[:, oc, 512 * ch:512 * (ch + 1)], in_=psum),
                        waits=(mm,))
                else:
                    ev = P.op("vector", lambda e, s, oc=oc, ch=ch, psum=psum: e.tensor_tensor(
                        out=B.ST[:, oc, 512 * ch:512 * (ch + 1)],
                        in0=B.ST[:, oc, 512 * ch:512 * (ch + 1)], in1=psum,
                        op=ALU.add), waits=(mm,))
                prev[("evac", ch)] = ev
                p_done = ev
        prev["q_done"] = q_done
        prev["k_done"] = k_done
        prev["v_done"] = v_done
    B.tok_s3 = p_done
    ar.release()
    if L3_PARTS >= 1:
        emit_block3_ffn(P, B)
    else:
        B.tok_x_final = p_done


L3_PARTS = 9


def emit_block3_ffn(P, B):
    ar = B.ar
    nc = B.nc
    if not hasattr(B, "wq_g"):
        B.wq_g = nc.dram_tensor("wq_g", [1024, 2816], BF16, kind="Internal").ap()
        B.wq_u = nc.dram_tensor("wq_u", [1024, 2816], BF16, kind="Internal").ap()
        B.wq_d = nc.dram_tensor("wq_d", [2816, 1024], BF16, kind="Internal").ap()
    eye_bf = cbf(B, "eye")

    P.scope = "b3:wq"
    ar.mark()
    tmps = mk_fp4_tmps(ar, 1408)
    WN = ar.alloc(5632).bitcast(F32)
    WQo = ar.alloc(2816)
    WTs = ar.alloc(2816)
    pq = {}
    wq_done = None
    P.wait("tensor", ("s", P.cnt["s"]), ("v", P.cnt["v"]))
    for (nat, scr, nrt, ncol) in (("L3_wgate_nat", B.wq_g, 22, 1024),
                                  ("L3_wup_nat", B.wq_u, 22, 1024),
                                  ("L3_wdown_nat", B.wq_d, 8, 2816)):
        ncs = ncol // 128
        for c in range(ncs):
            ld = P.dma("sync", WN[:, 0:128 * nrt].rearrange("p (r c) -> p r c", c=128),
                       B.wh[nat][:, 128 * c:128 * (c + 1)]
                       .rearrange("(r p) c -> p r c", p=128),
                       sem="d_sw2", waits=(pq.get("wn_used"), B.tok_s3))
            qd = None
            for q0 in range(0, 128 * nrt, 1408):
                qw = min(1408, 128 * nrt - q0)
                qd = emit_fp4_quant(P, B, tmps, WN[:, q0:q0 + qw],
                                    WQo[:, q0:q0 + qw], qw, waits=(ld,))
            pq["wn_used"] = qd
            tev = None
            for r in range(nrt):
                mm = P.op("tensor", lambda e, s, r=r: e.transpose(
                    out=B.PSA[:, 512 * (r % 4):512 * (r % 4) + 256].bitcast(BF16)[:, 0:128],
                    in_=WQo[:, 128 * r:128 * (r + 1)], identity=eye_bf),
                    waits=(qd, pq.get(("trev", r % 4))), inc="t")
                tev = P.op("vector", lambda e, s, r=r: e.tensor_copy(
                    out=WTs[:, 128 * r:128 * (r + 1)],
                    in_=B.PSA[:, 512 * (r % 4):512 * (r % 4) + 256].bitcast(BF16)[:, 0:128]),
                    waits=(mm, pq.get("wts_used")))
                pq[("trev", r % 4)] = tev
            wq_done = P.dma("sync", scr[128 * c:128 * (c + 1), :],
                            WTs[:, 0:128 * nrt], sem="d_sw3", waits=(tev,))
            pq["wts_used"] = wq_done
    ar.release()

    if L3_PARTS < 2:
        B.tok_x_final = wq_done
        return
    P.scope = "b3:xq"
    ar.mark()
    tmps = mk_fp4_tmps(ar, 1408)
    XNAT = ar.alloc3(8, 1024)
    pq2 = {}
    xq_done = None
    P.wait("tensor", ("s", P.cnt["s"]), ("v", P.cnt["v"]))
    for half in range(2):
        tb = 8 * half
        for tt in range(8):
            for dt_ in range(NDT):
                mm = P.op("tensor", lambda e, s, tt=tt, dt_=dt_, tb=tb: e.transpose(
                    out=B.PSA[:, 512 * (dt_ % 4):512 * (dt_ % 4) + 256].bitcast(BF16)[:, 0:128],
                    in_=B.XN[:, dt_, 128 * (tb + tt):128 * (tb + tt) + 128],
                    identity=eye_bf),
                    waits=(pq2.get(("tev", dt_ % 4)), xq_done), inc="t")
                ev = P.op("vector", lambda e, s, tt=tt, dt_=dt_: e.tensor_copy(
                    out=XNAT[:, tt, 128 * dt_:128 * (dt_ + 1)],
                    in_=B.PSA[:, 512 * (dt_ % 4):512 * (dt_ % 4) + 256].bitcast(BF16)[:, 0:128]),
                    waits=(mm,))
                pq2[("tev", dt_ % 4)] = ev
        for tt in range(8):
            qd = emit_fp4_quant(P, B, tmps, XNAT[:, tt, :], XNAT[:, tt, :], 1024)
            for dt_ in range(NDT):
                mm = P.op("tensor", lambda e, s, tt=tt, dt_=dt_: e.transpose(
                    out=B.PSB[:, 512 * (dt_ % 4):512 * (dt_ % 4) + 256].bitcast(BF16)[:, 0:128],
                    in_=XNAT[:, tt, 128 * dt_:128 * (dt_ + 1)],
                    identity=eye_bf),
                    waits=(qd, pq2.get(("tev2", dt_ % 4))), inc="t")
                ev = P.op("vector", lambda e, s, tt=tt, dt_=dt_, tb=tb: e.tensor_copy(
                    out=B.XN[:, dt_, 128 * (tb + tt):128 * (tb + tt) + 128],
                    in_=B.PSB[:, 512 * (dt_ % 4):512 * (dt_ % 4) + 256].bitcast(BF16)[:, 0:128]),
                    waits=(mm,))
                pq2[("tev2", dt_ % 4)] = ev
                xq_done = ev
    ar.release()

    if L3_PARTS < 3:
        B.tok_x_final = xq_done
        return
    P.scope = "b3:mm"
    ar.mark()
    A_wsl = ar.alloc4(2, NDT, 512)
    A_dk = [ar.alloc3(1, 1024), ar.alloc3(1, 1024)]
    A_h = ar.alloc3(4, 2816)
    A_sil = ar.alloc(512)
    A_gf = ar.alloc(1024).bitcast(F32)
    mm_done = None
    fp = {}
    for tch in range(4):
        t0 = 512 * tch
        for ici, ic in enumerate(range(0, 2816, 512)):
            icw = min(512, 2816 - ic)
            lg = P.dma("sync", A_wsl[:, 0, :, 0:icw],
                       B.wq_g.rearrange("(it p) i -> p it i", p=128)[:, :, ic:ic + icw],
                       sem="d_sw0", waits=(fp.get("wused"), wq_done, xq_done))
            lu = P.dma("sync", A_wsl[:, 1, :, 0:icw],
                       B.wq_u.rearrange("(it p) i -> p it i", p=128)[:, :, ic:ic + icw],
                       sem="d_sw1", waits=(fp.get("wused"), xq_done))
            mmu = None
            for tt in range(4):
                gps = B.PSA[:, 1024 * (tt % 2):1024 * (tt % 2) + 512]
                ups = B.PSA[:, 1024 * (tt % 2) + 512:1024 * (tt % 2) + 1024]
                for it in range(NDT):
                    mmg = P.op("tensor", lambda e, s, it=it, tt=tt, icw=icw, t0=t0, gps=gps: e.matmul(
                        out=gps[:, 0:icw], lhsT=B.XN[:, it, t0 + 128 * tt:t0 + 128 * (tt + 1)],
                        rhs=A_wsl[:, 0, it, 0:icw], start=(it == 0), stop=(it == NDT - 1)),
                        waits=(lg, xq_done, fp.get(("hev", tt % 2)),
                               fp.get("tchdone") if (ici == 0 and tt == 0) else None) if it == 0 else (),
                        inc="t" if it == NDT - 1 else None)
                for it in range(NDT):
                    mmu = P.op("tensor", lambda e, s, it=it, tt=tt, icw=icw, t0=t0, ups=ups: e.matmul(
                        out=ups[:, 0:icw], lhsT=B.XN[:, it, t0 + 128 * tt:t0 + 128 * (tt + 1)],
                        rhs=A_wsl[:, 1, it, 0:icw], start=(it == 0), stop=(it == NDT - 1)),
                        waits=(lu,) if it == 0 else (),
                        inc="t" if it == NDT - 1 else None)
                t_sil = P.op("scalar", lambda e, s, icw=icw, gps=gps: e.activation(
                    out=A_sil[:, 0:icw], in_=gps[:, 0:icw], func=AF.Silu),
                    waits=(mmg, ("v", P.cnt["v"])))
                hev = P.op("vector", lambda e, s, tt=tt, ic=ic, icw=icw, ups=ups: e.tensor_tensor(
                    out=A_h[:, tt, ic:ic + icw], in0=ups[:, 0:icw],
                    in1=A_sil[:, 0:icw], op=ALU.mult), waits=(mmu, t_sil))
                fp[("hev", tt % 2)] = hev
                fp[("hevt", tt)] = hev
            fp["wused"] = mmu
        if L3_PARTS < 4:
            mm_done = fp[("hevt", 3)]
            fp["tchdone"] = mm_done
            continue
        ar.mark()
        tmps = mk_fp4_tmps(ar, 704)
        A_hqt = ar.alloc3(22, 256)

        def quant_tt(tt):
            tok = None
            for c0 in range(0, 2816, 704):
                cw = min(704, 2816 - c0)
                tok = emit_fp4_quant(P, B, tmps, A_h[:, tt, c0:c0 + cw],
                                     A_h[:, tt, c0:c0 + cw], cw,
                                     waits=(fp.get(("hevt", tt)),))
            return tok

        def down_half(hf, hq_tok):
            tevs = []
            for kt in range(22):
                for t2 in range(2):
                    mm = P.op("tensor", lambda e, s, kt=kt, t2=t2, hf=hf: e.transpose(
                        out=B.PSA[:, 512 * (kt % 4):512 * (kt % 4) + 256].bitcast(BF16)[:, 128 * t2:128 * (t2 + 1)],
                        in_=A_h[:, 2 * hf + t2, 128 * kt:128 * (kt + 1)],
                        identity=eye_bf),
                        waits=(hq_tok, fp.get(("hqtev", kt % 4)), fp.get("yevac")) if t2 == 0 else (),
                        inc="t" if t2 == 1 else None)
                tev = P.op("vector", lambda e, s, kt=kt: e.tensor_copy(
                    out=A_hqt[:, kt, :],
                    in_=B.PSA[:, 512 * (kt % 4):512 * (kt % 4) + 256].bitcast(BF16)[:, 0:256]),
                    waits=(mm, fp.get("hqt_used")))
                fp[("hqtev", kt % 4)] = tev
                tevs.append(tev)
            mmd = None
            for kt in range(22):
                ldk = P.dma("sync", A_dk[kt % 2][:],
                            B.wq_d.rearrange("(kt p) o -> p kt o", p=128)[:, kt:kt + 1, :],
                            sem=("d_sw2", "d_sw3")[kt % 2],
                            waits=(fp.get(("dused", kt % 2)), wq_done))
                for oh in range(2):
                    for o2 in range(4):
                        ot = 4 * oh + o2
                        psum = (B.PSA if oh == 0 else B.PSB)[:, 512 * o2:512 * o2 + 256]
                        mmd = P.op("tensor", lambda e, s, kt=kt, ot=ot, psum=psum: e.matmul(
                            out=psum, lhsT=A_dk[kt % 2][:, 0, 128 * ot:128 * (ot + 1)],
                            rhs=A_hqt[:, kt, :], start=(kt == 0), stop=(kt == 21)),
                            waits=((tevs[kt], ldk, tevs[-1] if kt == 0 else None,
                                    fp.get("yevac") if kt == 0 else None)
                                   if (oh == 0 and o2 == 0) else ()),
                            inc="t" if (oh == 1 and o2 == 3) else None)
                fp[("dused", kt % 2)] = mmd
            last = None
            for oh in range(2):
                for o2 in range(4):
                    ot = 4 * oh + o2
                    psum = (B.PSA if oh == 0 else B.PSB)[:, 512 * o2:512 * o2 + 256]
                    tq = slice(t0 + 256 * hf, t0 + 256 * hf + 256)
                    t1 = P.op("scalar", lambda e, s, psum=psum: e.activation(
                        out=A_gf[:, 0:256], in_=psum, func=AF.Silu),
                        waits=(mmd, ("v", P.cnt["v"]), fp.get("gfree")))
                    P.op("vector", lambda e, s, ot=ot, tq=tq: e.tensor_tensor(
                        out=A_gf[:, 0:256], in0=A_gf[:, 0:256], in1=B.ST[:, ot, tq],
                        op=ALU.mult), waits=(t1, B.tok_s3))
                    last = P.op("vector", lambda e, s, ot=ot, tq=tq: e.tensor_tensor(
                        out=B.XT[:, ot, tq], in0=B.XT[:, ot, tq], in1=A_gf[:, 0:256],
                        op=ALU.add))
                    fp["gfree"] = last
            fp["yevac"] = last
            fp["hqt_used"] = mmd
            return last

        q01 = quant_tt(0)
        q01 = quant_tt(1)
        if L3_PARTS < 5:
            mm_done = q01
            fp["tchdone"] = mm_done
            ar.release()
            continue
        mm_done = down_half(0, q01)
        q23 = quant_tt(2)
        q23 = quant_tt(3)
        mm_done = down_half(1, q23)
        ar.release()
        fp["tchdone"] = mm_done
    B.tok_x_final = mm_done
    ar.release()


EMITTERS = {0: emit_block0, 1: emit_block1, 2: emit_block2, 3: emit_block3}


def kernel(**inputs):
    dev, xs = host_prep(inputs)
    weight_names = {k: v.shape for k, v in dev.items() if k != "CONSTS"}
    nc = build_program([0, 1, 2, 3], weight_names=weight_names)
    in_maps = []
    for b in range(8):
        m = {"x_in": xs[b]}
        m.update(dev)
        in_maps.append(m)
    res = run_bass_kernel_spmd(nc, in_maps, core_ids=list(range(8)))
    out = np.stack([np.asarray(r["x_out"]).reshape(D, T).T for r in res.results])
    return out.astype(np.float32)


# version 03
# speedup vs baseline: 1.0622x; 1.0021x over previous
import math
import contextlib
import numpy as np
import ml_dtypes
import concourse.bass as bass
import concourse.mybir as mybir
from concourse.bass_utils import run_bass_kernel_spmd

F32 = mybir.dt.float32
BF16 = mybir.dt.bfloat16
I32 = mybir.dt.int32
U8 = mybir.dt.uint8
AF = mybir.ActivationFunctionType
ALU = mybir.AluOpType
AX = mybir.AxisListType

BF = ml_dtypes.bfloat16

D = 1024
T = 2048
NDT = 8
NTT = 16
H = 16
DH = 64

ENGINES = ("tensor", "vector", "scalar", "gpsimd", "sync")


def _coalesce(tokens):
    best = {}
    for t in tokens:
        if t is None:
            continue
        sn, sv = t
        if sv > best.get(sn, -1):
            best[sn] = sv
    return tuple(best.items())


class Prog:
    AUTO_ENG = {"vector": "v", "scalar": "s", "gpsimd": "g"}

    def __init__(self, nc):
        self.nc = nc
        self.ops = {e: [] for e in ENGINES}
        self.cnt = {}
        self.scope = None

    def use_sems(self, *names):
        for n in names:
            self.cnt.setdefault(n, 0)

    def op(self, eng, fn, waits=(), inc=None, inc_n=1, dma=False):
        auto = (not dma) and eng in self.AUTO_ENG
        if auto:
            sem = self.AUTO_ENG[eng]
            assert inc in (None, sem)
            self.cnt.setdefault(sem, 0)
            prev = (sem, self.cnt[sem]) if self.cnt[sem] > 0 else None
            self.cnt[sem] += 1
            tok = (sem, self.cnt[sem])
            waits = tuple(waits) + ((prev,) if prev else ())
            inc, inc_n = sem, 1
        else:
            tok = None
            if inc is not None:
                self.cnt.setdefault(inc, 0)
        waits = _coalesce(self._bump(waits))
        if not auto and inc is not None:
            self.cnt[inc] += inc_n
            tok = (inc, self.cnt[inc])

        def closure(e, sems, fn=fn, waits=waits, tok=tok, inc=inc, inc_n=inc_n):
            for (sn, sv) in waits:
                e.wait_ge(sems[sn], sv)
            instr = fn(e, sems)
            if tok is not None:
                if instr is None:
                    instr = e.nop()
                instr.then_inc(sems[inc], inc_n)
            return instr

        self.ops[eng].append((self.scope, closure))
        return tok

    def _bump(self, tokens):
        out = []
        for t in tokens:
            if t is None:
                continue
            sn, sv = t
            if sn.startswith("d_"):
                sv = self.cnt.get(sn, sv)
            out.append((sn, sv))
        return out

    def wait(self, eng, *tokens):
        toks = _coalesce(self._bump(tokens))
        if not toks:
            return

        def closure(e, sems, toks=toks):
            for (sn, sv) in toks:
                e.wait_ge(sems[sn], sv)

        self.ops[eng].append((self.scope, closure))

    def dma(self, queue, out, in_, waits=(), sem=None, **kw):
        if sem is None:
            sem = "d_" + queue
        return self.op(queue, lambda e, s: e.dma_start(out=out, in_=in_, **kw),
                       waits=waits, inc=sem, inc_n=16, dma=True)


class Arena:

    def __init__(self, B):
        self.B = B
        self.off = 0
        self.stack = []
        self.total = B.AR.shape[1]

    def mark(self):
        self.stack.append(self.off)

    def release(self):
        self.off = self.stack.pop()

    def alloc(self, cols):
        o = (self.off + 3) & ~3
        self.off = o + cols
        assert self.off <= self.total, f"arena overflow {self.off} > {self.total}"
        return self.B.AR[:, o:o + cols]

    def alloc3(self, a, b):
        ap = self.alloc(a * b)
        return ap.rearrange("p (a b) -> p a b", b=b)

    def alloc4(self, a, b, c):
        ap = self.alloc(a * b * c)
        return ap.rearrange("p (a b c) -> p a b c", b=b, c=c)

    def rowf(self, n):
        ap = self.alloc(2 * n)
        return ap.bitcast(F32)[0:1, 0:n]

    def rowb(self, n):
        ap = self.alloc(n)
        return ap[0:1, :]


def _alibi_slopes(num_heads):
    closest_pow2 = 2 ** math.floor(math.log2(num_heads))
    base = 2.0 ** (-2.0 ** (-(math.log2(closest_pow2) - 3)))
    powers = np.arange(1, closest_pow2 + 1, dtype=np.float32)
    slopes = base ** powers
    if num_heads != closest_pow2:
        start = 2.0 ** (-(2.0 ** (-(math.log2(closest_pow2) - 3)) - 1))
        extra = np.linspace(start, base, num_heads - closest_pow2, dtype=np.float32)
        slopes = np.concatenate([slopes, extra])
    return slopes.astype(np.float32)


CONST_SPECS = {}
CV_VALS = {}


def build_consts():
    cols = []
    specs = {}

    def add(name, arr):
        specs[name] = (sum(c.shape[1] for c in cols), arr.shape[1])
        cols.append(arr.astype(np.float32))

    i = np.arange(128)[:, None]
    jq = np.arange(128)[None, :]

    add("eye", np.eye(128, dtype=np.float32))
    add("ones_col", np.ones((128, 1), dtype=np.float32))
    onesrow = np.zeros((128, 128), dtype=np.float32)
    onesrow[0, :] = 1.0
    add("ones_row", onesrow)
    s2t = np.zeros((128, 128), dtype=np.float32)
    s2t[0, 0:64] = 1.0
    s2t[1, 64:128] = 1.0
    add("sel2", s2t)
    s16 = np.zeros((128, 128), dtype=np.float32)
    for j in range(8):
        s16[j, 16 * j:16 * (j + 1)] = 1.0
    add("sel16", s16)
    s6 = np.zeros((128, 768), dtype=np.float32)
    for j in range(6):
        s6[j, 128 * j:128 * (j + 1)] = 1.0
    add("sel6", s6)
    add("triuIncl", (i <= jq).astype(np.float32))
    for lvl in range(4):
        blk = 2 ** (lvl + 1)
        start = (i // blk) * blk
        Mm = ((jq >= start) & (jq <= i)).astype(np.float32) / (i - start + 1)
        add(f"haarM_T_{lvl}", np.ascontiguousarray(Mm.T))
    R = np.zeros((128, 128), dtype=np.float32)
    for b0 in range(0, 128, 64):
        for ii in range(32):
            R[b0 + ii, b0 + 32 + ii] = -1.0
            R[b0 + 32 + ii, b0 + ii] = 1.0
    add("rope_rotT", np.ascontiguousarray(R.T))
    GLOBAL_CON_END = sum(c.shape[1] for c in cols)

    CV_VALS.clear()
    cv_list = [("ln_eps", 1e-5), ("rms_eps", float(np.finfo(np.float32).eps)),
               ("tiny", 1e-12), ("one", 1.0), ("half", 0.5), ("zero", 0.0),
               ("linattn_eps", 1e-6)]
    for idx, (nm, val) in enumerate(cv_list):
        CV_VALS[nm] = idx
    add("cvals", np.tile(np.array([[v for _, v in cv_list]], dtype=np.float32),
                         (128, 1)))

    slopes = _alibi_slopes(H)
    for h in range(H):
        tiles = []
        for off in (-2, -1, 0):
            rel = (i + 128 * off) - jq
            keep = (rel <= 0) & (rel > -256)
            tiles.append(np.where(keep, np.exp(slopes[h] * rel.astype(np.float64)),
                                  0.0).astype(np.float32))
        add(f"l0_strip_{h}", np.concatenate(tiles, axis=1))

    tiles = []
    for off in (-1, 0):
        rel = (i + 128 * off) - jq
        keep = (rel <= 0) & (rel > -64)
        tiles.append(keep.astype(np.float32))
    add("l1_lstrip", np.concatenate(tiles, axis=1))

    gamma = 1.0 - np.exp2(-5.0 - np.arange(H, dtype=np.float64))
    u = np.arange(128)
    for h in range(H):
        dm = np.where(i <= jq, float(gamma[h] ** -128.0), 0.0)
        add(f"l2_decmv_{h}", dm.astype(np.float32))
    for hp in range(8):
        tile = np.zeros((128, 128), dtype=np.float64)
        tile[:64, :] = gamma[2 * hp] ** u[None, :] / 8.0
        tile[64:, :] = gamma[2 * hp + 1] ** u[None, :] / 8.0
        add(f"l2_qg_{hp}", tile.astype(np.float32))
        tile2 = np.zeros((128, 128), dtype=np.float64)
        tile2[:64, :] = gamma[2 * hp] ** (128 - u[None, :])
        tile2[64:, :] = gamma[2 * hp + 1] ** (128 - u[None, :])
        add(f"l2_kg_{hp}", tile2.astype(np.float32))
    g128 = np.zeros((128, 8), dtype=np.float32)
    for hp in range(8):
        g128[:64, hp] = gamma[2 * hp] ** 128
        g128[64:, hp] = gamma[2 * hp + 1] ** 128
    add("l2_g128", g128)

    inv_freq = 1.0 / (10000.0 ** (np.arange(0, DH, 2, dtype=np.float32) / DH))
    t = np.arange(T, dtype=np.float32)
    freqs = np.einsum("i,j->ij", t, inv_freq)
    emb = np.concatenate([freqs, freqs], axis=1)
    add("rope_cos", np.ascontiguousarray(np.tile(np.cos(emb).T, (2, 1))))
    add("rope_sin", np.ascontiguousarray(np.tile(np.sin(emb).T, (2, 1))))

    CONST_SPECS.clear()
    CONST_SPECS.update(specs)
    arr = np.ascontiguousarray(np.concatenate(cols, axis=1))
    return arr, GLOBAL_CON_END


CONSTS_ARR, GLOBAL_CON_END = build_consts()
CONST_SHAPE = CONSTS_ARR.shape


def _bcol(v):
    k = v.shape[0]
    return np.ascontiguousarray(v.reshape(k // 128, 128).T)


def host_prep(inputs):
    g = {k: np.asarray(v, dtype=np.float32) for k, v in inputs.items()}
    d = {}
    tr = lambda a: np.ascontiguousarray(a.T)

    d["L0_qkvT"] = tr(g["L0_attn_qkv_w"])
    d["L0_qkv_b_col"] = _bcol(g["L0_attn_qkv_b"])
    d["L0_vb_row"] = np.ascontiguousarray(g["L0_attn_qkv_b"][2048:].reshape(1, 1024))
    d["L0_projT"] = tr(g["L0_attn_proj_w"])
    d["L0_proj_b_col"] = _bcol(g["L0_attn_proj_b"])
    d["L0_WrT"] = tr(g["L0_ffn_W_r"])
    d["L0_WkT"] = tr(g["L0_ffn_W_k"])
    d["L0_WvT"] = tr(g["L0_ffn_W_v"])
    d["L0_mur_col"] = _bcol(g["L0_ffn_mu_r"])
    d["L0_muk_col"] = _bcol(g["L0_ffn_mu_k"])

    for w in ("Wq", "Wk", "Wv", "Wkl", "Wvl", "Wo"):
        d[f"L1_{w}T"] = tr(g[f"L1_attn_{w}"])
    d["L1_WgT"] = tr(g["L1_attn_Wg"])
    d["L1_bg_col"] = _bcol(g["L1_attn_bg"])
    d["L1_WgoT_col"] = _bcol(g["L1_attn_Wgo"].reshape(1024))
    d["L1_bgo"] = np.ascontiguousarray(g["L1_attn_bgo"].reshape(1, 1))
    d["L1_bo_col"] = _bcol(g["L1_attn_bo"])
    d["L1_haar_scale"] = np.ascontiguousarray(g["L1_attn_haar_scale"].reshape(1, 4))
    bd = np.zeros((4, 2, 128, 128), dtype=np.float32)
    for lvl in range(4):
        for idx, nm in enumerate(("haar_Wk", "haar_Wv")):
            wt = g[f"L1_attn_{nm}"][lvl].T
            bd[lvl, idx, :64, :64] = wt
            bd[lvl, idx, 64:, 64:] = wt
    d["L1_haarBD"] = np.ascontiguousarray(bd.reshape(8, 128, 128))

    d["L1_gateT"] = tr(g["L1_ffn_gate_w"])
    d["L1_gate_b_bc"] = np.ascontiguousarray(
        np.tile(g["L1_ffn_gate_b"].reshape(1, 6), (128, 1)))
    for w in ("ew1", "ew2", "ew3"):
        d[f"L1_{w}T"] = np.ascontiguousarray(g[f"L1_ffn_{w}"].transpose(0, 2, 1))
    for w in ("fc1", "fc2", "fc3"):
        d[f"L1_{w}T"] = tr(g[f"L1_ffn_{w}"])

    for w in ("wq", "wk", "wv", "wg", "wo"):
        d[f"L2_{w}T"] = tr(g[f"L2_attn_{w}"])
    d["L2_W1"] = np.ascontiguousarray(g["L2_ffn_W1"])
    d["L2_b1_col"] = _bcol(g["L2_ffn_b1"])
    d["L2_theta_col"] = _bcol(g["L2_ffn_quantum_phases"])
    d["L2_wr_col"] = _bcol(g["L2_ffn_quantum_weights_real"])
    d["L2_wi_col"] = _bcol(g["L2_ffn_quantum_weights_imag"])
    d["L2_W2"] = np.ascontiguousarray(g["L2_ffn_W2"])
    d["L2_b2_col"] = _bcol(g["L2_ffn_b2"])

    d["L3_normw_col"] = _bcol(g["L3_attn_norm_w"])
    d["L3_Wq"] = np.ascontiguousarray(g["L3_attn_Wq"])
    d["L3_Wkv"] = np.ascontiguousarray(g["L3_attn_Wkv"])
    d["L3_Wout"] = np.ascontiguousarray(g["L3_attn_Wout"])
    d["L3_Wg"] = np.ascontiguousarray(g["L3_attn_Wg"])
    d["L3_wgateT"] = tr(g["L3_ffn_w_gate"])
    d["L3_wupT"] = tr(g["L3_ffn_w_up"])
    d["L3_wdownT"] = tr(g["L3_ffn_w_down"])
    d["L3_wgate_nat"] = np.ascontiguousarray(g["L3_ffn_w_gate"])
    d["L3_wup_nat"] = np.ascontiguousarray(g["L3_ffn_w_up"])
    d["L3_wdown_nat"] = np.ascontiguousarray(g["L3_ffn_w_down"])

    d["CONSTS"] = CONSTS_ARR

    x = np.asarray(inputs["L0_attn_x"], dtype=np.float32)
    xs = [np.ascontiguousarray(x[b].T) for b in range(8)]
    return d, xs


class Bufs:
    pass


def build_program(blocks, taps=(), weight_names=None):
    nc = bass.Bass("TRN2", target_bir_lowering=False, debug=False)
    P = Prog(nc)
    es = contextlib.ExitStack()

    x_in = nc.dram_tensor("x_in", [D, T], F32, kind="ExternalInput").ap()
    consts_h = nc.dram_tensor("CONSTS", list(CONST_SHAPE), F32, kind="ExternalInput").ap()
    wh = {}
    for name, shape in (weight_names or {}).items():
        wh[name] = nc.dram_tensor(name, list(shape), F32, kind="ExternalInput").ap()
    x_out = nc.dram_tensor("x_out", [D, T], F32, kind="ExternalOutput").ap()
    tap_out = {}
    for name, shape, dt_, getter in taps:
        tap_out[name] = nc.dram_tensor("tap_" + name, list(shape), dt_,
                                       kind="ExternalOutput").ap()

    with es:
        B = Bufs()
        B.nc = nc
        B.wh = wh
        B.consts_h = consts_h
        sb = lambda n, sh, dt_: es.enter_context(nc.sbuf_tensor(n, sh, dt_))
        B.XT = sb("XT", [128, NDT, T], F32)
        B.XN = sb("XN", [128, NDT, T], BF16)
        B.ST = sb("ST", [128, NDT, T], BF16)
        B.AR = sb("AR", [128, 36864], BF16)
        B.SMALL = sb("SMALL", [128, 256], F32)
        B.CON = sb("CON", [128, GLOBAL_CON_END], BF16)
        B.CONF = sb("CONF", [128, 16], F32)
        B.PSA = es.enter_context(nc.psum_tensor("PSA", [128, 2048], F32))
        B.PSB = es.enter_context(nc.psum_tensor("PSB", [128, 2048], F32))
        B.tok_x_final = None
        B.ar = Arena(B)

        P.use_sems("t", "v", "s", "g", "d_sync", "d_gpsimd", "d_scalar", "d_out",
                   "d_w0", "d_w1", "d_w2", "d_w3")

        emit_main(P, B, blocks, x_in, x_out, tap_out, taps)

        sems = {n: es.enter_context(nc.semaphore(n)) for n in sorted(P.cnt.keys())}
        block = es.enter_context(nc.Block())
        for eng in ENGINES:
            deco = getattr(block, eng)
            opslist = P.ops[eng]

            def make_section(opslist):
                def section(e):
                    cur = None
                    sid = None
                    for label, closure in opslist:
                        if label != cur:
                            if cur is not None:
                                nc.leave_named_scope(cur, sid, False)
                            sid = None
                            if label is not None:
                                sid, _ = nc.enter_named_scope(label, False)
                            cur = label
                        closure(e, sems)
                    if cur is not None:
                        nc.leave_named_scope(cur, sid, False)
                return section
            deco(make_section(opslist))

    return nc


def cvf(B, name, rows=(0, 128)):
    idx = CV_VALS[name]
    return B.CONF[rows[0]:rows[1], idx:idx + 1]


def cbf(B, name, rows=None):
    off, n = CONST_SPECS[name]
    assert off < GLOBAL_CON_END
    if rows is None:
        return B.CON[:, off:off + n]
    return B.CON[rows[0]:rows[1], off:off + n]


def chbm(B, name):
    off, n = CONST_SPECS[name]
    return B.consts_h[:, off:off + n]


def emit_main(P, B, blocks, x_in, x_out, tap_out, taps):
    xsrc = x_in.rearrange("(dt p) t -> p dt t", p=128)
    tok_x = P.dma("sync", B.XT[:], xsrc, sem="d_boot")
    ctok = P.dma("gpsimd", B.CON[:], B.consts_h[:, 0:GLOBAL_CON_END])
    o_cv, n_cv = CONST_SPECS["cvals"]
    ctok2 = P.dma("sync", B.CONF[:, 0:n_cv], B.consts_h[:, o_cv:o_cv + n_cv], sem="d_boot")
    for e in ("tensor", "vector", "scalar", "gpsimd"):
        P.wait(e, tok_x, ctok, ctok2)

    for b in blocks:
        P.scope = f"ln{b}"
        emit_layernorm(P, B)
        P.scope = f"b{b}"
        EMITTERS[b](P, B)
    P.scope = None

    for name, shape, dt_, getter in taps:
        ap, waits = getter(B)
        P.dma("sync", tap_out[name], ap, waits=waits, sem="d_out")

    done = P.dma("sync", x_out.rearrange("(dt p) t -> p dt t", p=128), B.XT[:],
                 waits=(B.tok_x_final,) if B.tok_x_final else (), sem="d_out")
    P.wait("sync", done)


def emit_layernorm(P, B):
    ar = B.ar
    ar.mark()
    ones_bf = cbf(B, "ones_col")
    ones_row = cbf(B, "ones_row", rows=(0, 1))
    SQ = ar.alloc3(NDT, 512)
    AB = ar.alloc(512)
    BB = ar.alloc(512)
    R_m = ar.rowf(512)
    R_ss = ar.rowf(512)
    R_t = ar.rowf(512)
    R_inv = ar.rowf(512)
    RB_a = ar.rowb(512)
    RB_b = ar.rowb(512)
    stats = B.PSA[0:1, 0:512]
    bcA = B.PSB[:, 0:512]
    bcB = B.PSB[:, 512:1024]
    xfin = B.tok_x_final
    prev_ab = None
    prev_bb = None
    prev_xn = None
    for cc in range(4):
        tsl = slice(512 * cc, 512 * (cc + 1))
        cast_toks = []
        for dt_ in range(NDT):
            cast_toks.append(P.op("scalar",
                lambda e, s, dt_=dt_, tsl=tsl: e.copy(out=B.XN[:, dt_, tsl],
                                                      in_=B.XT[:, dt_, tsl]),
                waits=(xfin,) if (cc == 0 and dt_ == 0) else ()))
        for dt_ in range(NDT):
            w = [cast_toks[dt_]]
            if dt_ == 0:
                w += [prev_ab, prev_bb, xfin]
            mm_mean = P.op("tensor",
                lambda e, s, dt_=dt_, tsl=tsl: e.matmul(
                    out=stats, lhsT=ones_bf, rhs=B.XN[:, dt_, tsl],
                    start=(dt_ == 0), stop=(dt_ == NDT - 1)),
                waits=w, inc="t" if dt_ == NDT - 1 else None)
        t_m = P.op("scalar", lambda e, s: e.mul(out=R_m, in_=stats, mul=1.0 / 1024),
                   waits=(mm_mean,))
        sq_toks = []
        for dt_ in range(NDT):
            sq_toks.append(P.op("vector",
                lambda e, s, dt_=dt_, tsl=tsl: e.tensor_tensor(
                    out=SQ[:, dt_, :], in0=B.XN[:, dt_, tsl],
                    in1=B.XN[:, dt_, tsl], op=ALU.mult),
                waits=(cast_toks[dt_],)))
        for dt_ in range(NDT):
            mm_sq = P.op("tensor",
                lambda e, s, dt_=dt_: e.matmul(
                    out=stats, lhsT=ones_bf, rhs=SQ[:, dt_, :],
                    start=(dt_ == 0), stop=(dt_ == NDT - 1)),
                waits=(sq_toks[dt_], t_m) if dt_ == 0 else (sq_toks[dt_],),
                inc="t" if dt_ == NDT - 1 else None)
        t_v1 = P.op("scalar", lambda e, s: e.mul(out=R_ss, in_=stats, mul=1.0 / 1024),
                    waits=(mm_sq,))
        t_v2 = P.op("vector", lambda e, s: e.tensor_tensor(
            out=R_t, in0=R_m, in1=R_m, op=ALU.mult), waits=(t_m,))
        t_v3 = P.op("vector", lambda e, s: e.tensor_tensor(
            out=R_ss, in0=R_ss, in1=R_t, op=ALU.subtract), waits=(t_v1, t_v2))
        t_v4 = P.op("scalar", lambda e, s: e.activation(
            out=R_t, in_=R_ss, func=AF.Ln, bias=cvf(B, "ln_eps", (0, 1))),
            waits=(t_v3,))
        t_v5 = P.op("scalar", lambda e, s: e.activation(
            out=R_inv, in_=R_t, func=AF.Exp, scale=-0.5), waits=(t_v4,))
        t_a = P.op("vector", lambda e, s: e.tensor_copy(out=RB_a, in_=R_inv),
                   waits=(t_v5,))
        t_b = P.op("vector", lambda e, s: e.scalar_tensor_tensor(
            out=RB_b, in0=R_m, scalar=-1.0, in1=R_inv, op0=ALU.mult, op1=ALU.mult),
            waits=(t_v5,))
        mm_a = P.op("tensor", lambda e, s: e.matmul(
            out=bcA, lhsT=ones_row, rhs=RB_a, start=True, stop=True),
            waits=(t_a,), inc="t")
        mm_b = P.op("tensor", lambda e, s: e.matmul(
            out=bcB, lhsT=ones_row, rhs=RB_b, start=True, stop=True),
            waits=(t_b,), inc="t")
        t_ab = P.op("scalar", lambda e, s: e.copy(out=AB, in_=bcA),
                    waits=(mm_a, mm_sq))
        t_bb = P.op("scalar", lambda e, s: e.copy(out=BB, in_=bcB),
                    waits=(mm_b,))
        for dt_ in range(NDT):
            P.op("vector", lambda e, s, dt_=dt_, tsl=tsl: e.tensor_tensor(
                out=SQ[:, dt_, :], in0=B.XN[:, dt_, tsl], in1=AB, op=ALU.mult),
                waits=(t_ab,) if dt_ == 0 else ())
            prev_xn = P.op("vector", lambda e, s, dt_=dt_, tsl=tsl: e.tensor_tensor(
                out=B.XN[:, dt_, tsl], in0=SQ[:, dt_, :], in1=BB, op=ALU.add),
                waits=(t_bb,) if dt_ == 0 else ())
        prev_ab, prev_bb = t_ab, t_bb
    B.tok_xn = prev_xn
    P.wait("tensor", B.tok_xn)
    ar.release()


def wslice_src(hap, c0, c1):
    return hap[:, c0:c1].rearrange("(it p) c -> p it c", p=128)


def emit_block0(P, B):
    P.scope = "b0:attn"
    ar = B.ar
    ar.mark()
    SCALE = 0.125

    tok_b = P.dma("sync", B.SMALL[:, 0:24], B.wh["L0_qkv_b_col"], sem="d_small")
    tok_pb = P.dma("sync", B.SMALL[:, 24:32], B.wh["L0_proj_b_col"], sem="d_small")
    tok_mu = P.dma("sync", B.SMALL[:, 32:40], B.wh["L0_mur_col"], sem="d_small")
    tok_mk = P.dma("sync", B.SMALL[:, 40:48], B.wh["L0_muk_col"], sem="d_small")
    t_qb = P.op("vector", lambda e, s: e.tensor_scalar(
        out=B.SMALL[:, 48:56], in0=B.SMALL[:, 0:8], scalar1=SCALE, scalar2=None,
        op0=ALU.mult), waits=(tok_b,))

    A_qw = ar.alloc3(NDT, 256)
    A_kw = ar.alloc3(NDT, 256)
    A_vw = ar.alloc3(NDT, 256)
    A_pw = ar.alloc3(2, 1024)
    A_qT = ar.alloc3(2, T)
    A_kT = ar.alloc3(2, T)
    A_at = ar.alloc3(2, T)
    A_vx = ar.alloc4(NTT, 4, 65)
    A_pt = ar.alloc3(4, 384)
    A_al = ar.alloc3(4, 384)
    A_vb = ar.alloc(256)
    A_rec = ar.alloc(512)
    R_rec = ar.rowf(512)
    RB_rec = ar.rowb(512)

    prev = {}
    p_done = None
    for hg in range(4):
        qc0 = hg * 256
        kc0 = 1024 + hg * 256
        vc0 = 2048 + hg * 256
        wsem = f"d_w{hg % 2}"
        ln0 = B.tok_xn if hg == 0 else None
        w_q = P.dma("gpsimd", A_qw[:], wslice_src(B.wh["L0_qkvT"], qc0, qc0 + 256),
                    waits=(prev.get("q_done"), ln0), sem=wsem)
        w_k = P.dma("gpsimd", A_kw[:], wslice_src(B.wh["L0_qkvT"], kc0, kc0 + 256),
                    waits=(prev.get("k_done"), ln0), sem=wsem)
        w_v = P.dma("gpsimd", A_vw[:], wslice_src(B.wh["L0_qkvT"], vc0, vc0 + 256),
                    waits=(prev.get("v_done"), ln0), sem=wsem)
        w_p = P.dma("gpsimd", A_pw[:],
                    B.wh["L0_projT"][hg * 256:(hg + 1) * 256, :]
                    .rearrange("(it p) c -> p it c", p=128),
                    waits=(p_done, ln0), sem=wsem)
        w_vb = P.dma("gpsimd", A_vb[0:1, :],
                     B.wh["L0_vb_row"][0:1, hg * 256:(hg + 1) * 256],
                     waits=(prev.get("vb_done"), ln0), sem=wsem)
        al_tok = None
        for hs in range(4):
            al_tok = P.dma("gpsimd", A_al[:, hs, :], chbm(B, f"l0_strip_{4 * hg + hs}"),
                           waits=(prev.get("al_done"), ln0), sem=wsem)

        def proj_fm(wview, dst, bias_cols, extra_waits, scale=None):
            last = None
            for oc in range(2):
                for ch in range(4):
                    psum = B.PSA[:, 512 * ch:512 * (ch + 1)]
                    for it in range(NDT):
                        mm = P.op("tensor", lambda e, s, oc=oc, ch=ch, it=it, wview=wview, psum=psum: e.matmul(
                            out=psum, lhsT=wview[:, it, 128 * oc:128 * (oc + 1)],
                            rhs=B.XN[:, it, 512 * ch:512 * (ch + 1)],
                            start=(it == 0), stop=(it == NDT - 1)),
                            waits=(tuple(extra_waits) + (B.tok_xn, prev.get(("evac", ch)),)) if it == 0 else (),
                            inc="t" if it == NDT - 1 else None)
                    ev = P.op("scalar", lambda e, s, oc=oc, ch=ch, dst=dst, psum=psum, bias_cols=bias_cols, scale=scale: e.activation(
                        out=dst[:, oc, 512 * ch:512 * (ch + 1)], in_=psum,
                        func=AF.Identity, bias=bias_cols[:, oc:oc + 1],
                        **({"scale": scale} if scale is not None else {})),
                        waits=(mm,))
                    prev[("evac", ch)] = ev
                    last = ev
            return last

        q_done = proj_fm(A_qw, A_qT, B.SMALL[:, 48 + 2 * hg:50 + 2 * hg],
                         (w_q, t_qb), scale=SCALE)
        k_done = proj_fm(A_kw, A_kT, B.SMALL[:, 8 + 2 * hg:10 + 2 * hg], (w_k, tok_b))

        mm_vb = P.op("tensor", lambda e, s: e.matmul(
            out=B.PSB[:, 0:256], lhsT=cbf(B, "ones_row", rows=(0, 1)),
            rhs=A_vb[0:1, :], start=True, stop=True),
            waits=(w_vb, prev.get(("pv", 0))), inc="t")
        t_vbb = P.op("scalar", lambda e, s: e.copy(out=A_vb, in_=B.PSB[:, 0:256]),
                     waits=(mm_vb,))
        t_ones = P.op("vector", lambda e, s: e.memset(A_vx[:, :, :, 64:65], 1.0),
                      waits=(prev.get("at_done"),))
        v_done = None
        for tt in range(NTT):
            psum = B.PSB[:, 1024 + 512 * (tt % 2):1024 + 512 * (tt % 2) + 256]
            for it in range(NDT):
                mm = P.op("tensor", lambda e, s, tt=tt, it=it, psum=psum: e.matmul(
                    out=psum, lhsT=B.XN[:, it, 128 * tt:128 * (tt + 1)],
                    rhs=A_vw[:, it, :], start=(it == 0), stop=(it == NDT - 1)),
                    waits=(w_v, B.tok_xn, prev.get(("vevac", tt % 2))) if it == 0 else (),
                    inc="t" if it == NDT - 1 else None)
            ev = P.op("vector", lambda e, s, tt=tt, psum=psum: e.scalar_tensor_tensor(
                out=A_vx[:, tt, :, 0:64],
                in0=psum.rearrange("p (a b) -> p a b", b=64), scalar=1.0,
                in1=A_vb.rearrange("p (a b) -> p a b", b=64),
                op0=ALU.mult, op1=ALU.add),
                waits=(mm, t_vbb, t_ones))
            prev[("vevac", tt % 2)] = ev
            v_done = ev

        attn_done = None
        for qt in range(NTT):
            kts = [kt for kt in (qt - 2, qt - 1, qt) if kt >= 0]
            for hs in range(4):
                rb = (hs % 2) * 64
                for kt in kts:
                    slot = 3 - (qt - kt)
                    mm_s = P.op("tensor", lambda e, s, hs=hs, kt=kt, qt=qt, rb=rb, slot=slot: e.matmul(
                        out=B.PSA[:, 512 * hs + 128 * slot:512 * hs + 128 * (slot + 1)],
                        lhsT=A_kT[rb:rb + 64, hs // 2, 128 * kt:128 * (kt + 1)],
                        rhs=A_qT[rb:rb + 64, hs // 2, 128 * qt:128 * (qt + 1)],
                        start=True, stop=True),
                        waits=(q_done, k_done, prev.get("psa_free")) if (hs == 0 and kt == kts[0]) else (),
                        inc="t" if (hs == 3 and kt == kts[-1]) else None)
            t_exp = P.op("scalar", lambda e, s: e.activation(
                out=A_pt[:], func=AF.Exp,
                in_=B.PSA[:].rearrange("p (h c) -> p h c", c=512)[:, :, 128:512]),
                waits=(mm_s,))
            t_msk = P.op("vector", lambda e, s: e.tensor_tensor(
                out=A_pt[:], in0=A_pt[:], in1=A_al[:], op=ALU.mult),
                waits=(t_exp, al_tok))
            pvbank = B.PSB[:, 512 * (qt % 2):512 * (qt % 2) + 512]
            for hs in range(4):
                for ki, kt in enumerate(kts):
                    slot = 3 - (qt - kt)
                    mm_pv = P.op("tensor", lambda e, s, hs=hs, kt=kt, slot=slot, pvbank=pvbank, ki=ki, nk=len(kts): e.matmul(
                        out=pvbank[0:65, 128 * hs:128 * (hs + 1)],
                        lhsT=A_vx[:, kt, hs, :],
                        rhs=A_pt[:, hs, 128 * (slot - 1):128 * slot],
                        start=(ki == 0), stop=(ki == nk - 1)),
                        waits=(t_msk, v_done, prev.get(("pv", qt % 2))) if (hs == 0 and ki == 0) else (),
                        inc="t" if (hs == 3 and ki == len(kts) - 1) else None)
            t_rc = P.op("vector", lambda e, s, pvbank=pvbank: e.reciprocal(
                out=R_rec, in_=pvbank[64:65, :]), waits=(mm_pv,))
            t_rb = P.op("vector", lambda e, s: e.tensor_copy(out=RB_rec, in_=R_rec),
                        waits=(t_rc,))
            mm_rb = P.op("tensor", lambda e, s: e.matmul(
                out=B.PSA[0:64, 0:512], lhsT=cbf(B, "ones_row", rows=(0, 1))[:, 0:64],
                rhs=RB_rec, start=True, stop=True), waits=(t_rb, t_exp), inc="t")
            t_rcb = P.op("scalar", lambda e, s: e.copy(out=A_rec[0:64, :],
                                                       in_=B.PSA[0:64, 0:512]),
                         waits=(mm_rb, attn_done))
            prev["psa_free"] = t_rcb
            for par in range(2):
                ev = P.op("vector", lambda e, s, par=par, qt=qt, pvbank=pvbank: e.tensor_tensor(
                    out=A_at[64 * par:64 * par + 64, :, 128 * qt:128 * (qt + 1)],
                    in0=pvbank[0:64, :].rearrange("p (h c) -> p h c", c=128)[:, par::2, :],
                    in1=A_rec[0:64, :].rearrange("p (h c) -> p h c", c=128)[:, par::2, :],
                    op=ALU.mult), waits=(t_rcb,))
                prev[("pv", qt % 2)] = ev
                attn_done = ev
        prev["at_done"] = attn_done

        for oc in range(NDT):
            for ch in range(4):
                psum = B.PSA[:, 512 * ch:512 * (ch + 1)]
                for it2 in range(2):
                    mm = P.op("tensor", lambda e, s, oc=oc, ch=ch, it2=it2, psum=psum: e.matmul(
                        out=psum, lhsT=A_pw[:, it2, 128 * oc:128 * (oc + 1)],
                        rhs=A_at[:, it2, 512 * ch:512 * (ch + 1)],
                        start=(it2 == 0), stop=(it2 == 1)),
                        waits=(w_p, attn_done, prev.get(("evac", ch)), prev.get("psa_free")) if it2 == 0 else (),
                        inc="t" if it2 == 1 else None)
                if hg == 0:
                    ev = P.op("scalar", lambda e, s, oc=oc, ch=ch, psum=psum: e.activation(
                        out=B.ST[:, oc, 512 * ch:512 * (ch + 1)], in_=psum,
                        func=AF.Identity, bias=B.SMALL[:, 24 + oc:25 + oc]),
                        waits=(mm, tok_pb))
                else:
                    ev = P.op("vector", lambda e, s, oc=oc, ch=ch, psum=psum: e.tensor_tensor(
                        out=B.ST[:, oc, 512 * ch:512 * (ch + 1)],
                        in0=B.ST[:, oc, 512 * ch:512 * (ch + 1)], in1=psum,
                        op=ALU.add), waits=(mm,))
                prev[("evac", ch)] = ev
                p_done = ev

        prev["q_done"] = q_done
        prev["k_done"] = k_done
        prev["v_done"] = v_done
        prev["vb_done"] = t_vbb
        prev["al_done"] = t_msk
    B.tok_s0 = p_done
    ar.release()

    P.scope = "b0:ffn"
    ar.mark()
    A_w1 = ar.alloc3(NDT, 1024)
    A_w2 = ar.alloc3(NDT, 1024)
    A_xr = ar.alloc3(NDT, 512)
    A_rr = ar.alloc3(NDT, 512)
    A_ka = ar.alloc3(NDT, 512)
    A_t1 = ar.alloc3(NDT, 512)
    A_kaf = ar.alloc(1024).bitcast(F32)
    A_vof = ar.alloc(1024).bitcast(F32)
    mur = B.SMALL[:, 32:40]
    muk = B.SMALL[:, 40:48]

    wload = {}
    for ch in range(4):
        wload[("fevac", ch)] = prev.get(("evac", ch))
    ffn_done = None
    barrier = B.tok_s0
    for qtr in range(4):
        q0 = 512 * qtr
        tq = slice(q0, q0 + 512)
        w_r = P.dma("gpsimd", A_w1[:], wslice_src(B.wh["L0_WrT"], 0, 1024),
                    waits=(wload.get("w1free"), B.tok_s0), sem="d_w2")
        w_k = P.dma("gpsimd", A_w2[:], wslice_src(B.wh["L0_WkT"], 0, 1024),
                    waits=(wload.get("w2free"), B.tok_s0), sem="d_w3")

        def tshift(dst, mu, waits):
            toks = []
            for dt_ in range(NDT):
                if qtr == 0:
                    P.op("vector", lambda e, s, dt_=dt_: e.tensor_tensor(
                        out=A_t1[:, dt_, 1:512], in0=B.XN[:, dt_, 1:512],
                        in1=B.XN[:, dt_, 0:511], op=ALU.subtract), waits=waits)
                    P.op("vector", lambda e, s, dt_=dt_, mu=mu, dst=dst: e.scalar_tensor_tensor(
                        out=dst[:, dt_, 1:512], in0=A_t1[:, dt_, 1:512],
                        scalar=mu[:, dt_:dt_ + 1], in1=B.XN[:, dt_, 0:511],
                        op0=ALU.mult, op1=ALU.add))
                    toks.append(P.op("vector", lambda e, s, dt_=dt_, mu=mu, dst=dst: e.tensor_scalar(
                        out=dst[:, dt_, 0:1], in0=B.XN[:, dt_, 0:1],
                        scalar1=mu[:, dt_:dt_ + 1], scalar2=None, op0=ALU.mult)))
                else:
                    P.op("vector", lambda e, s, dt_=dt_, tq=tq, q0=q0: e.tensor_tensor(
                        out=A_t1[:, dt_, :], in0=B.XN[:, dt_, tq],
                        in1=B.XN[:, dt_, q0 - 1:q0 + 511], op=ALU.subtract),
                        waits=waits)
                    toks.append(P.op("vector", lambda e, s, dt_=dt_, mu=mu, dst=dst, q0=q0: e.scalar_tensor_tensor(
                        out=dst[:, dt_, :], in0=A_t1[:, dt_, :],
                        scalar=mu[:, dt_:dt_ + 1], in1=B.XN[:, dt_, q0 - 1:q0 + 511],
                        op0=ALU.mult, op1=ALU.add)))
            return toks[-1]

        def ffn_proj(wview, wtok, src, srctok, evac):
            last = None
            for oc in range(NDT):
                psum = B.PSA[:, 512 * (oc % 4):512 * (oc % 4 + 1)]
                for it in range(NDT):
                    mm = P.op("tensor", lambda e, s, oc=oc, it=it, psum=psum, wview=wview, src=src: e.matmul(
                        out=psum, lhsT=wview[:, it, 128 * oc:128 * (oc + 1)],
                        rhs=src[:, it, :], start=(it == 0), stop=(it == NDT - 1)),
                        waits=(wtok, srctok, wload.get(("fevac", oc % 4))) if it == 0 else (),
                        inc="t" if it == NDT - 1 else None)
                ev = evac(oc, psum, mm)
                wload[("fevac", oc % 4)] = ev
                last = ev
            return last

        t_xr = tshift(A_xr, mur, (tok_mu, barrier))
        r_done = ffn_proj(A_w1, w_r, A_xr, t_xr,
                          lambda oc, psum, mm: P.op("scalar",
                              lambda e, s, oc=oc, psum=psum: e.activation(
                                  out=A_rr[:, oc, :], in_=psum, func=AF.Sigmoid),
                              waits=(mm,)))
        w_v = P.dma("gpsimd", A_w1[:], wslice_src(B.wh["L0_WvT"], 0, 1024),
                    waits=(r_done,), sem="d_w2")
        t_xk = tshift(A_xr, muk, (tok_mk, r_done))

        def ka_evac(oc, psum, mm):
            P.op("vector", lambda e, s, oc=oc, psum=psum: e.tensor_scalar(
                out=A_kaf, in0=psum, scalar1=0.0, scalar2=None,
                op0=ALU.max), waits=(mm,))
            return P.op("vector", lambda e, s, oc=oc: e.tensor_tensor(
                out=A_ka[:, oc, :], in0=A_kaf, in1=A_kaf, op=ALU.mult))

        ka_done = ffn_proj(A_w2, w_k, A_xr, t_xk, ka_evac)

        def vout_evac(oc, psum, mm):
            P.op("vector", lambda e, s, oc=oc, psum=psum: e.tensor_tensor(
                out=A_vof, in0=psum, in1=A_rr[:, oc, :], op=ALU.mult),
                waits=(mm,))
            P.op("scalar", lambda e, s, oc=oc: e.activation(
                out=A_vof, in_=A_vof, func=AF.Silu),
                waits=(("v", P.cnt["v"]),))
            P.op("vector", lambda e, s, oc=oc, tq=tq: e.tensor_tensor(
                out=A_vof, in0=A_vof, in1=B.ST[:, oc, tq],
                op=ALU.mult), waits=(B.tok_s0, ("s", P.cnt["s"])))
            return P.op("vector", lambda e, s, oc=oc, tq=tq: e.tensor_tensor(
                out=B.XT[:, oc, tq], in0=B.XT[:, oc, tq], in1=A_vof,
                op=ALU.add))

        ffn_done = ffn_proj(A_w1, w_v, A_ka, ka_done, vout_evac)
        wload["w1free"] = ffn_done
        wload["w2free"] = ka_done
        barrier = ffn_done
    B.tok_x_final = ffn_done
    ar.release()


L1_PARTS = 9
L1_HP = 8
L1_NOLOC = False
L1_NOGLOB = False
L1_NOPASS = False
L1_SEC = 9
L1_NOPHI = False


def emit_block1(P, B):
    emit_block1_attn(P, B)
    if L1_PARTS >= 2:
        emit_block1_ffn(P, B)
    else:
        B.tok_x_final = B.tok_s1


def emit_block1_attn(P, B):
    ar = B.ar
    ar.mark()
    nc = B.nc
    B.l1_diff = nc.dram_tensor("l1_diff", [D, T], BF16, kind="Internal").ap()
    B.l1_glob = nc.dram_tensor("l1_glob", [D, T], BF16, kind="Internal").ap()
    ones_row = cbf(B, "ones_row", rows=(0, 1))
    eye = cbf(B, "eye")
    triu = cbf(B, "triuIncl")

    tok_bg = P.dma("sync", B.SMALL[:, 0:8], B.wh["L1_bg_col"], sem="d_small",
                   waits=(B.tok_xn,))
    tok_bo = P.dma("sync", B.SMALL[:, 8:16], B.wh["L1_bo_col"], sem="d_small")
    tok_go = P.dma("sync", B.SMALL[:, 16:24], B.wh["L1_WgoT_col"], sem="d_small")
    tok_gb = P.dma("sync", B.SMALL[:, 40:46], B.wh["L1_gate_b_bc"], sem="d_small")
    tok_bgo = P.dma("sync", B.SMALL[0:1, 25:26], B.wh["L1_bgo"], sem="d_small")
    tok_hs = P.dma("sync", B.SMALL[0:1, 28:32], B.wh["L1_haar_scale"],
                   sem="d_small")

    RB_w = ar.rowb(4)
    A_w4 = ar.alloc(8).bitcast(F32)
    t_e = P.op("scalar", lambda e, s: e.activation(
        out=B.SMALL[0:1, 32:36], in_=B.SMALL[0:1, 28:32], func=AF.Exp),
        waits=(tok_hs,))
    P.op("vector", lambda e, s: e.tensor_reduce(
        out=B.SMALL[0:1, 36:37], in_=B.SMALL[0:1, 32:36], axis=AX.X,
        op=ALU.add), waits=(t_e,))
    P.op("vector", lambda e, s: e.reciprocal(
        out=B.SMALL[0:1, 37:38], in_=B.SMALL[0:1, 36:37]))
    P.op("vector", lambda e, s: e.tensor_scalar(
        out=B.SMALL[0:1, 32:36], in0=B.SMALL[0:1, 32:36],
        scalar1=B.SMALL[0:1, 37:38], scalar2=None, op0=ALU.mult))
    t_wb = P.op("vector", lambda e, s: e.tensor_copy(
        out=RB_w, in_=B.SMALL[0:1, 32:36]))
    mm_w = P.op("tensor", lambda e, s: e.matmul(
        out=B.PSA[:, 0:4], lhsT=ones_row, rhs=RB_w, start=True, stop=True),
        waits=(t_wb, B.tok_xn), inc="t")
    t_w4 = P.op("scalar", lambda e, s: e.copy(out=A_w4, in_=B.PSA[:, 0:4]),
                waits=(mm_w,))

    A_bd = ar.alloc3(8, 128)
    tok_bd = P.dma("gpsimd", A_bd[:],
                   B.wh["L1_haarBD"].rearrange("a p c -> p a c"),
                   waits=(B.tok_xn,), sem="d_w3")
    bd_folded = None
    for lvl in range(4):
        bd_folded = P.op("vector", lambda e, s, lvl=lvl: e.tensor_scalar(
            out=A_bd[:, 2 * lvl + 1, :], in0=A_bd[:, 2 * lvl + 1, :],
            scalar1=A_w4[:, lvl:lvl + 1], scalar2=None, op0=ALU.mult),
            waits=(tok_bd, t_w4))

    A_wa = ar.alloc3(NDT, 128)
    A_wb = ar.alloc3(NDT, 128)
    A_qT = ar.alloc(T)
    A_qpT = ar.alloc(T)
    A_kT = ar.alloc(T)
    A_kpT = ar.alloc(T)
    A_klT = ar.alloc(T)
    A_vT = ar.alloc(T)
    A_kpN = ar.alloc4(2, NTT, 64)
    A_vN = ar.alloc4(2, NTT, 65)
    A_vlN = ar.alloc4(2, NTT, 65)
    A_locT = ar.alloc(T)
    A_gT = ar.alloc(T)
    A_phit = ar.alloc(512)
    A_t2k = ar.alloc(512)
    A_t2v = ar.alloc(512)
    A_kpLN = ar.alloc4(4, 2, 64)
    A_mvN = ar.alloc4(4, 2, 65)
    A_kpLT = ar.alloc3(4, 128)
    A_am = ar.alloc3(2, 512)
    A_ls = ar.alloc(256)

    A_stF = ar.alloc(650).bitcast(F32)
    A_stB = ar.alloc(328)
    A_gtmp = ar.alloc3(2, 512)
    A_rc = ar.alloc(128)
    A_rcl = ar.alloc(512)
    A_rc0 = ar.alloc(128)
    R_n = ar.rowf(640)
    RB_n = ar.rowb(640)

    tok_ls = P.dma("gpsimd", A_ls[:], chbm(B, "l1_lstrip"),
                   waits=(B.tok_xn,), sem="d_w2")
    fp = {("pevac", 0): t_w4}
    diff_done = None
    glob_dma = None
    for hp in range(L1_HP):
        P.scope = f"b1:proj.{hp}"
        csl0 = slice(128 * hp, 128 * (hp + 1))

        def proj(wname, wbuf, wsem, evac_fn, extra_w=()):
            wtok = P.dma("gpsimd", wbuf[:], wslice_src(B.wh[wname], 128 * hp,
                                                       128 * (hp + 1)),
                         waits=(fp.get(("wfree", wsem)),) + tuple(extra_w),
                         sem=wsem)
            last = None
            for ch in range(4):
                psum = B.PSA[:, 512 * ch:512 * (ch + 1)]
                for it in range(NDT):
                    mm = P.op("tensor", lambda e, s, ch=ch, it=it, wbuf=wbuf, psum=psum: e.matmul(
                        out=psum, lhsT=wbuf[:, it, :],
                        rhs=B.XN[:, it, 512 * ch:512 * (ch + 1)],
                        start=(it == 0), stop=(it == NDT - 1)),
                        waits=(wtok, B.tok_xn, fp.get(("pevac", ch))) if it == 0 else (),
                        inc="t" if it == NDT - 1 else None)
                last = evac_fn(ch, psum, mm)
                fp[("pevac", ch)] = last
            fp[("wfree", wsem)] = last
            return last

        def phi_chain(psum, mm, dstphi, csl, after=None):
            t1 = P.op("vector", lambda e, s, psum=psum: e.tensor_scalar(
                out=A_phit[:], in0=psum, scalar1=0.0, scalar2=None,
                op0=ALU.min), waits=(mm, after, fp.get("phit")))
            t2 = P.op("scalar", lambda e, s, csl=csl, dstphi=dstphi: e.activation(
                out=dstphi[:, csl], in_=A_phit[:], func=AF.Exp), waits=(t1,))
            t3 = P.op("vector", lambda e, s, psum=psum: e.tensor_scalar(
                out=A_phit[:], in0=psum, scalar1=0.0, scalar2=None,
                op0=ALU.max), waits=(t2,))
            t4 = P.op("vector", lambda e, s, csl=csl, dstphi=dstphi: e.tensor_tensor(
                out=dstphi[:, csl], in0=dstphi[:, csl], in1=A_phit[:],
                op=ALU.add), waits=(t2,))
            fp["phit"] = t4
            return t4

        def ev_scale(dst, scale):
            def f(ch, psum, mm):
                return P.op("scalar", lambda e, s, ch=ch, psum=psum, scale=scale: e.activation(
                    out=dst[:, 512 * ch:512 * (ch + 1)], in_=psum,
                    func=AF.Identity, scale=scale), waits=(mm,))
            return f

        def ev_scale_phi(dstq, dstphi):
            def f(ch, psum, mm):
                csl = slice(512 * ch, 512 * (ch + 1))
                t0 = P.op("scalar", lambda e, s, psum=psum, csl=csl: e.activation(
                    out=dstq[:, csl], in_=psum, func=AF.Identity, scale=0.125),
                    waits=(mm,))
                return phi_chain(psum, mm, dstphi, csl, after=t0)
            return f

        def ev_copy_phi(dstraw, dstphi):
            def f(ch, psum, mm):
                csl = slice(512 * ch, 512 * (ch + 1))
                t0 = P.op("scalar", lambda e, s, psum=psum, csl=csl: e.copy(
                    out=dstraw[:, csl], in_=psum), waits=(mm,))
                return phi_chain(psum, mm, dstphi, csl, after=t0)
            return f

        def ev_copy(dst):
            def f(ch, psum, mm):
                return P.op("scalar", lambda e, s, ch=ch, psum=psum: e.copy(
                    out=dst[:, 512 * ch:512 * (ch + 1)], in_=psum), waits=(mm,))
            return f

        if L1_NOPHI:
            q_done = proj("L1_WqT", A_wa, "d_w0", ev_scale(A_qT, 0.125),
                          extra_w=(fp.get("qT_free"), fp.get("qpT_free")))
            qp_done = q_done
            k_done = proj("L1_WkT", A_wb, "d_w1", ev_copy(A_kT),
                          extra_w=(fp.get("kT_free"), fp.get("kpT_free")))
        else:
            q_done = proj("L1_WqT", A_wa, "d_w0", ev_scale_phi(A_qT, A_qpT),
                          extra_w=(fp.get("qT_free"), fp.get("qpT_free")))
            qp_done = q_done
            k_done = proj("L1_WkT", A_wb, "d_w1", ev_copy_phi(A_kT, A_kpT),
                          extra_w=(fp.get("kT_free"), fp.get("kpT_free")))
        kl_done = proj("L1_WklT", A_wa, "d_w0", ev_copy(A_klT),
                       extra_w=(fp.get("klT_free"),))
        v_done = proj("L1_WvT", A_wb, "d_w1", ev_copy(A_vT),
                      extra_w=(fp.get("vT_free"),))

        P.scope = f"b1:tr.{hp}"
        if L1_SEC < 2:
            dtoks = [v_done]
            dma_d = P.dma("sync", B.l1_diff[csl0, :], A_locT[:],
                          waits=(v_done,), sem="d_st0")
            dma_g = P.dma("sync", B.l1_glob[csl0, :], A_gT[:],
                          waits=(v_done,), sem="d_st1")
            for ch in range(4):
                fp[("pevac", ch)] = v_done
            fp["locT_free"] = dma_d
            fp["gT_free"] = dma_g
            diff_done = dma_d
            glob_dma = dma_g
            continue
        t_ones = P.op("vector", lambda e, s: e.memset(A_vN[:, :, :, 64:65], 1.0),
                      waits=(fp.get("vN_free"),))
        t_ones2 = P.op("vector", lambda e, s: e.memset(A_vlN[:, :, :, 64:65], 1.0),
                       waits=(fp.get("vlN_free"),))
        tr_done = {}
        kpn_done = None
        for c in range(NTT):
            for si, (src, wsrc) in enumerate(((A_kpT, k_done), (A_vT, v_done))):
                bsel = 2 * si + (c % 2)
                bank = B.PSB[:, 512 * bsel:512 * bsel + 64].bitcast(BF16)
                mm_t = P.op("tensor", lambda e, s, src=src, c=c, bank=bank: e.transpose(
                    out=bank[:, 0:128], in_=src[:, 128 * c:128 * (c + 1)],
                    identity=eye),
                    waits=(wsrc, tr_done.get(("ev", bsel)),
                           fp.get("psb_free")), inc="t")
                if si == 0:
                    ev = P.op("vector", lambda e, s, c=c, bank=bank: e.tensor_copy(
                        out=A_kpN[:, :, c, :],
                        in_=bank[:, 0:128].rearrange("p (a b) -> p a b", b=64)),
                        waits=(mm_t,))
                else:
                    ev = P.op("vector", lambda e, s, c=c, bank=bank: e.tensor_copy(
                        out=A_vN[:, :, c, 0:64],
                        in_=bank[:, 0:128].rearrange("p (a b) -> p a b", b=64)),
                        waits=(mm_t, t_ones))
                tr_done[("ev", bsel)] = ev
                kpn_done = ev

        if L1_SEC < 3:
            dtoks = [kpn_done]
            dma_d = P.dma("sync", B.l1_diff[csl0, :], A_locT[:],
                          waits=(kpn_done,), sem="d_st0")
            dma_g = P.dma("sync", B.l1_glob[csl0, :], A_gT[:],
                          waits=(kpn_done,), sem="d_st1")
            for ch in range(4):
                fp[("pevac", ch)] = kpn_done
            fp["psb_free"] = kpn_done
            fp["locT_free"] = dma_d
            fp["gT_free"] = dma_g
            diff_done = dma_d
            glob_dma = dma_g
            continue
        w_vl = P.dma("gpsimd", A_wb[:], wslice_src(B.wh["L1_WvlT"], 128 * hp,
                                                   128 * (hp + 1)),
                     waits=(v_done,), sem="d_w1")
        vln_done = None
        for c in range(NTT):
            psum = B.PSB[:, 1024 + 512 * (c % 2):1024 + 512 * (c % 2) + 128]
            for it in range(NDT):
                mm = P.op("tensor", lambda e, s, c=c, it=it, psum=psum: e.matmul(
                    out=psum, lhsT=B.XN[:, it, 128 * c:128 * (c + 1)],
                    rhs=A_wb[:, it, :], start=(it == 0), stop=(it == NDT - 1)),
                    waits=((w_vl, B.tok_xn, fp.get(("vlev", c % 2)))
                           + ((kpn_done,) if c < 2 else ())) if it == 0 else (),
                    inc="t" if it == NDT - 1 else None)
            ev = P.op("vector", lambda e, s, c=c, psum=psum: e.tensor_copy(
                out=A_vlN[:, :, c, 0:64],
                in_=psum.rearrange("p (a b) -> p a b", b=64)),
                waits=(mm, t_ones2))
            fp[("vlev", c % 2)] = ev
            vln_done = ev
        fp[("wfree", "d_w1")] = vln_done

        P.scope = f"b1:loc.{hp}"
        loc_done = vln_done
        for qt in range(0 if L1_NOLOC else NTT):
            sbank = B.PSA[:, 512 * (qt % 2):512 * (qt % 2) + 512]
            pvb = B.PSA[:, 1024 + 512 * (qt % 2):1024 + 512 * (qt % 2) + 128]
            kts = [kt for kt in (qt - 1, qt) if kt >= 0]
            c0 = 128 * (2 - len(kts))
            for h2 in (0, 64):
                mm_s = None
                for ki, kt in enumerate(kts):
                    sl2 = 2 - len(kts) + ki
                    mm_s = P.op("tensor", lambda e, s, h2=h2, kt=kt, qt=qt, sl2=sl2, sbank=sbank: e.matmul(
                        out=sbank[:, 128 * sl2:128 * (sl2 + 1)],
                        lhsT=A_klT[h2:h2 + 64, 128 * kt:128 * (kt + 1)],
                        rhs=A_qT[h2:h2 + 64, 128 * qt:128 * (qt + 1)],
                        start=True, stop=True),
                        waits=(q_done, kl_done, fp.get(("locs", qt % 2))) if ki == 0 else (),
                        inc="t" if ki == len(kts) - 1 else None)
                t_exp = P.op("scalar", lambda e, s, sbank=sbank, c0=c0: e.activation(
                    out=A_am[:, 0, c0:256], in_=sbank[:, c0:256],
                    func=AF.Exp), waits=(mm_s, fp.get("am0_free")))
                t_msk = P.op("vector", lambda e, s, c0=c0: e.tensor_tensor(
                    out=A_am[:, 0, c0:256], in0=A_am[:, 0, c0:256],
                    in1=A_ls[:, c0:256], op=ALU.mult),
                    waits=(t_exp, tok_ls))
                mm_pv = None
                for ki, kt in enumerate(kts):
                    mm_pv = P.op("tensor", lambda e, s, h2=h2, kt=kt, ki=ki, nk=len(kts), pvb=pvb: e.matmul(
                        out=pvb[0:65, :], lhsT=A_vlN[:, h2 // 64, kt, :],
                        rhs=A_am[:, 0, 128 * (2 - nk + ki):128 * (2 - nk + ki) + 128],
                        start=(ki == 0), stop=(ki == nk - 1)),
                        waits=(t_msk, vln_done, fp.get(("locpv", qt % 2))) if ki == 0 else (),
                        inc="t" if ki == len(kts) - 1 else None)
                t_rc = P.op("vector", lambda e, s, pvb=pvb: e.reciprocal(
                    out=R_n[0:1, 0:128], in_=pvb[64:65, :]), waits=(mm_pv,))
                t_rb = P.op("vector", lambda e, s: e.tensor_copy(
                    out=RB_n[0:1, 0:128], in_=R_n[0:1, 0:128]))
                mm_rb = P.op("tensor", lambda e, s, sbank=sbank: e.matmul(
                    out=sbank[0:64, 256:384], lhsT=ones_row[:, 0:64],
                    rhs=RB_n[0:1, 0:128], start=True, stop=True),
                    waits=(t_rb, t_msk), inc="t")
                t_rcb = P.op("scalar", lambda e, s, sbank=sbank: e.copy(
                    out=A_rc[0:64, :], in_=sbank[0:64, 256:384]),
                    waits=(mm_rb, fp.get("rc_free")))
                ev = P.op("vector", lambda e, s, h2=h2, qt=qt, pvb=pvb: e.tensor_tensor(
                    out=A_locT[h2:h2 + 64, 128 * qt:128 * (qt + 1)],
                    in0=pvb[0:64, :], in1=A_rc[0:64, :], op=ALU.mult),
                    waits=(t_rcb, fp.get("locT_free")))
                fp[("locs", qt % 2)] = t_rcb
                fp[("locpv", qt % 2)] = ev
                fp["rc_free"] = ev
                fp["am0_free"] = mm_pv
                loc_done = ev

        P.scope = f"b1:glob.{hp}"
        st_z1 = P.op("vector", lambda e, s: e.memset(A_stF[:, :], 0.0),
                     waits=(fp.get("st_free"),))
        st_z2 = P.op("vector", lambda e, s: e.memset(A_stB[:, 0:325], 0.0))
        glob_done = s4 = mm_pl = mm_a0 = mm_p0 = kplt_done = mm_al = st_z2
        for c in range(0 if L1_NOGLOB else NTT):
            ctile = slice(128 * c, 128 * (c + 1))
            lw0 = (loc_done,) if c == 0 else ()
            mm = None
            for lvl in range(4):
                mm = P.op("tensor", lambda e, s, lvl=lvl, ctile=ctile: e.matmul(
                    out=B.PSA[:, 128 * lvl:128 * (lvl + 1)],
                    lhsT=A_kT[:, ctile], rhs=A_bd[:, 2 * lvl, :],
                    start=True, stop=True),
                    waits=(k_done, bd_folded, fp.get("t2k_free")) + lw0 if lvl == 0 else (),
                    inc="t" if lvl == 3 else None)
            ev_t2k = P.op("scalar", lambda e, s: e.copy(out=A_t2k[:], in_=B.PSA[:, 0:512]),
                          waits=(mm,))
            mm = None
            for lvl in range(4):
                mm = P.op("tensor", lambda e, s, lvl=lvl: e.matmul(
                    out=B.PSA[:, 512 + 128 * lvl:512 + 128 * (lvl + 1)],
                    lhsT=cbf(B, f"haarM_T_{lvl}"), rhs=A_t2k[:, 128 * lvl:128 * (lvl + 1)],
                    start=True, stop=True),
                    waits=(ev_t2k, fp.get("mk_free")) + lw0 if lvl == 0 else (),
                    inc="t" if lvl == 3 else None)
            p1 = P.op("vector", lambda e, s: e.tensor_scalar(
                out=A_phit[:], in0=B.PSA[:, 512:1024], scalar1=0.0,
                scalar2=None, op0=ALU.min), waits=(mm, fp.get("phit")))
            p2 = P.op("scalar", lambda e, s: e.activation(
                out=A_kpLN[:].rearrange("p a b c -> p (a b c)"), in_=A_phit[:],
                func=AF.Exp), waits=(p1, fp.get("kpLN_free"), fp.get("kpLN_free2")))
            p3 = P.op("vector", lambda e, s: e.tensor_scalar(
                out=A_phit[:], in0=B.PSA[:, 512:1024], scalar1=0.0,
                scalar2=None, op0=ALU.max), waits=(p2,))
            p4 = P.op("vector", lambda e, s: e.tensor_tensor(
                out=A_kpLN[:].rearrange("p a b c -> p (a b c)"),
                in0=A_kpLN[:].rearrange("p a b c -> p (a b c)"),
                in1=A_phit[:], op=ALU.add), waits=(p2,))
            fp["phit"] = p4
            mm = None
            for lvl in range(4):
                mm = P.op("tensor", lambda e, s, lvl=lvl, ctile=ctile: e.matmul(
                    out=B.PSA[:, 1024 + 128 * lvl:1024 + 128 * (lvl + 1)],
                    lhsT=A_vT[:, ctile], rhs=A_bd[:, 2 * lvl + 1, :],
                    start=True, stop=True),
                    waits=(v_done, bd_folded, fp.get("t2v_free")) + lw0 if lvl == 0 else (),
                    inc="t" if lvl == 3 else None)
            ev_t2v = P.op("scalar", lambda e, s: e.copy(out=A_t2v[:], in_=B.PSA[:, 1024:1536]),
                          waits=(mm,))
            t_mvo = P.op("vector", lambda e, s: e.memset(A_mvN[:, :, :, 64:65], 1.0),
                         waits=(fp.get("mvN_free"),))
            mm = None
            for lvl in range(4):
                mm = P.op("tensor", lambda e, s, lvl=lvl: e.matmul(
                    out=B.PSA[:, 1536 + 128 * lvl:1536 + 128 * (lvl + 1)],
                    lhsT=cbf(B, f"haarM_T_{lvl}"), rhs=A_t2v[:, 128 * lvl:128 * (lvl + 1)],
                    start=True, stop=True),
                    waits=(ev_t2v, fp.get("mv_free")) + lw0 if lvl == 0 else (),
                    inc="t" if lvl == 3 else None)
            ev_mv = P.op("vector", lambda e, s: e.tensor_copy(
                out=A_mvN[:, :, :, 0:64],
                in_=B.PSA[:, 1536:2048].rearrange("p (a b c) -> p a b c", b=2, c=64)),
                waits=(mm, t_mvo))
            trtok = {}
            evt = None
            for j in range(8):
                lvl, h2i = j // 2, j % 2
                bank = B.PSA[:, 1024 * (j % 2):1024 * (j % 2) + 64].bitcast(BF16)
                mm_t = P.op("tensor", lambda e, s, lvl=lvl, h2i=h2i, bank=bank: e.transpose(
                    out=bank[0:64, 0:128], in_=A_kpLN[:, lvl, h2i, :],
                    identity=eye),
                    waits=(p4, trtok.get(("ev", j % 2)),
                           (ev_t2k if j % 2 == 0 else ev_t2v)), inc="t")
                evt = P.op("scalar", lambda e, s, lvl=lvl, h2i=h2i, bank=bank: e.copy(
                    out=A_kpLT[64 * h2i:64 * h2i + 64, lvl, :],
                    in_=bank[0:64, 0:128]),
                    waits=(mm_t, fp.get("kpLT_free")))
                trtok[("ev", j % 2)] = evt
            kplt_done = evt
            fp["t2k_free"] = trtok[("ev", 0)]
            fp["t2v_free"] = trtok[("ev", 1)]

            if L1_NOPASS:
                glob_done = s4 = kplt_done
                continue
            for h2 in (0, 1):
                rb = 64 * h2
                g0 = (kpn_done, vln_done) if (c == 0 and h2 == 0) else ()
                mm_a0 = P.op("tensor", lambda e, s, rb=rb, ctile=ctile: e.matmul(
                    out=B.PSB[:, 512:640], lhsT=A_kpT[rb:rb + 64, ctile],
                    rhs=A_qpT[rb:rb + 64, ctile], start=True, stop=True),
                    waits=(k_done, qp_done, fp.get("a0_free")) + g0, inc="t")
                mm_al = None
                for lvl in range(4):
                    mm_al = P.op("tensor", lambda e, s, rb=rb, lvl=lvl, ctile=ctile: e.matmul(
                        out=B.PSB[:, 128 * lvl:128 * (lvl + 1)],
                        lhsT=A_kpLT[rb:rb + 64, lvl, :],
                        rhs=A_qpT[rb:rb + 64, ctile], start=True, stop=True),
                        waits=(kplt_done, qp_done, fp.get("al_free")) + g0 if lvl == 0 else (),
                        inc="t" if lvl == 3 else None)
                t_m0 = P.op("vector", lambda e, s: e.tensor_tensor(
                    out=A_am[:, 0, 0:128], in0=B.PSB[:, 512:640], in1=triu,
                    op=ALU.mult), waits=(mm_a0, fp.get("am0p_free")))
                t_ml = P.op("vector", lambda e, s: e.tensor_tensor(
                    out=A_am[:, 1, :].rearrange("p (a b) -> p a b", b=128),
                    in0=B.PSB[:, 0:512].rearrange("p (a b) -> p a b", b=128),
                    in1=triu[:, None, :].broadcast_to([128, 4, 128]),
                    op=ALU.mult), waits=(mm_al, fp.get("aml_free")))
                mm_s0 = P.op("tensor", lambda e, s, h2=h2, c=c: e.matmul(
                    out=B.PSB[0:64, 1536 + 272:1536 + 337],
                    lhsT=A_kpN[:, h2, c, :], rhs=A_vN[:, h2, c, :],
                    start=True, stop=True),
                    waits=(kpn_done, fp.get("s0_free")) + g0, inc="t")
                mm_sl = None
                for lvl in range(4):
                    mm_sl = P.op("tensor", lambda e, s, h2=h2, lvl=lvl: e.matmul(
                        out=B.PSB[0:64, 1536 + 66 * lvl:1536 + 66 * lvl + 65],
                        lhsT=A_kpLN[:, lvl, h2, :], rhs=A_mvN[:, lvl, h2, :],
                        start=True, stop=True),
                        waits=(p4, ev_mv, fp.get("sl_free")) + g0 if lvl == 0 else (),
                        inc="t" if lvl == 3 else None)
                mm_p0 = P.op("tensor", lambda e, s, h2=h2, c=c: e.matmul(
                    out=B.PSB[0:65, 640:768], lhsT=A_vN[:, h2, c, :],
                    rhs=A_am[:, 0, 0:128], start=True, stop=(c == 0)),
                    waits=(t_m0, kpn_done, fp.get("pv0_free")),
                    inc="t" if c == 0 else None)
                if c > 0:
                    mm_p0 = P.op("tensor", lambda e, s, rb=rb, ctile=ctile: e.matmul(
                        out=B.PSB[0:65, 640:768],
                        lhsT=A_stB[rb:rb + 64, 0:65],
                        rhs=A_qpT[rb:rb + 64, ctile], start=False, stop=True),
                        waits=(fp.get("stB_tok"),), inc="t")
                mm_pl = None
                for lvl in range(4):
                    mm_pl = P.op("tensor", lambda e, s, h2=h2, lvl=lvl, c=c: e.matmul(
                        out=B.PSB[0:65, 1024 + 128 * lvl:1024 + 128 * (lvl + 1)],
                        lhsT=A_mvN[:, lvl, h2, :],
                        rhs=A_am[:, 1, 128 * lvl:128 * (lvl + 1)],
                        start=True, stop=(c == 0)),
                        waits=(t_ml, ev_mv, fp.get("pvl_free")) if lvl == 0 else (),
                        inc="t" if (c == 0 and lvl == 3) else None)
                    if c > 0:
                        mm_pl = P.op("tensor", lambda e, s, rb=rb, lvl=lvl, ctile=ctile: e.matmul(
                            out=B.PSB[0:65, 1024 + 128 * lvl:1024 + 128 * (lvl + 1)],
                            lhsT=A_stB[rb:rb + 64, 65 * (lvl + 1):65 * (lvl + 2)],
                            rhs=A_qpT[rb:rb + 64, ctile], start=False,
                            stop=True),
                            waits=(fp.get("stB_tok"),) if lvl == 0 else (),
                            inc="t" if lvl == 3 else None)
                t_n0 = P.op("vector", lambda e, s: e.tensor_scalar(
                    out=R_n[0:1, 512:640], in0=B.PSB[64:65, 640:768],
                    scalar1=cvf(B, "linattn_eps", (0, 1)), scalar2=None,
                    op0=ALU.max), waits=(mm_p0,))
                t_nl = P.op("vector", lambda e, s: e.tensor_scalar(
                    out=R_n[0:1, 0:512], in0=B.PSB[64:65, 1024:1536],
                    scalar1=cvf(B, "linattn_eps", (0, 1)), scalar2=None,
                    op0=ALU.max), waits=(mm_pl,))
                P.op("vector", lambda e, s: e.reciprocal(
                    out=R_n[0:1, 0:640], in_=R_n[0:1, 0:640]))
                t_rb2 = P.op("vector", lambda e, s: e.tensor_copy(
                    out=RB_n[0:1, 0:640], in_=R_n[0:1, 0:640]))
                mm_bl = P.op("tensor", lambda e, s: e.matmul(
                    out=B.PSA[0:64, 512:1024], lhsT=ones_row[:, 0:64],
                    rhs=RB_n[0:1, 0:512], start=True, stop=True),
                    waits=(t_rb2, fp.get("bl_free")), inc="t")
                mm_b0 = P.op("tensor", lambda e, s: e.matmul(
                    out=B.PSB[0:64, 768:896], lhsT=ones_row[:, 0:64],
                    rhs=RB_n[0:1, 512:640], start=True, stop=True),
                    waits=(t_rb2,), inc="t")
                rc_l = P.op("scalar", lambda e, s: e.copy(
                    out=A_rcl[0:64, :], in_=B.PSA[0:64, 512:1024]),
                    waits=(mm_bl, fp.get("rcl_free")))
                rc_0 = P.op("scalar", lambda e, s: e.copy(
                    out=A_rc0[0:64, :], in_=B.PSB[0:64, 768:896]),
                    waits=(mm_b0, fp.get("rc0_free")))
                g1 = P.op("vector", lambda e, s: e.tensor_tensor(
                    out=A_gtmp[0:64, 1, :], in0=B.PSB[0:64, 1024:1536],
                    in1=A_rcl[0:64, :], op=ALU.mult),
                    waits=(rc_l, fp.get("gtmp_free")))
                g2 = P.op("vector", lambda e, s: e.tensor_tensor(
                    out=A_gtmp[0:64, 0, 0:256], in0=A_gtmp[0:64, 1, 0:256],
                    in1=A_gtmp[0:64, 1, 256:512], op=ALU.add))
                g3 = P.op("vector", lambda e, s: e.tensor_tensor(
                    out=A_gtmp[0:64, 0, 256:384], in0=A_gtmp[0:64, 0, 0:128],
                    in1=A_gtmp[0:64, 0, 128:256], op=ALU.add))
                g4 = P.op("vector", lambda e, s: e.tensor_tensor(
                    out=A_gtmp[0:64, 0, 384:512], in0=B.PSB[0:64, 640:768],
                    in1=A_rc0[0:64, :], op=ALU.mult), waits=(rc_0,))
                g5 = P.op("vector", lambda e, s, rb=rb, ctile=ctile: e.tensor_tensor(
                    out=A_gT[rb:rb + 64, ctile], in0=A_gtmp[0:64, 0, 256:384],
                    in1=A_gtmp[0:64, 0, 384:512], op=ALU.add),
                    waits=(fp.get("gT_free"),))
                s1 = P.op("vector", lambda e, s, rb=rb: e.tensor_tensor(
                    out=A_stF[rb:rb + 64, 260:325], in0=A_stF[rb:rb + 64, 260:325],
                    in1=B.PSB[0:64, 1536 + 272:1536 + 337], op=ALU.add),
                    waits=(mm_s0, st_z1))
                s2 = P.op("vector", lambda e, s, rb=rb: e.tensor_tensor(
                    out=A_stF[rb:rb + 64, 0:260].rearrange("p (a b) -> p a b", b=65),
                    in0=A_stF[rb:rb + 64, 0:260].rearrange("p (a b) -> p a b", b=65),
                    in1=B.PSB[0:64, 1536:1536 + 264].rearrange(
                        "p (a b) -> p a b", b=66)[:, :, 0:65], op=ALU.add),
                    waits=(mm_sl,))
                s3 = P.op("vector", lambda e, s, rb=rb: e.tensor_copy(
                    out=A_stB[rb:rb + 64, 260:325], in_=A_stF[rb:rb + 64, 260:325]),
                    waits=(mm_p0, st_z2))
                s4 = P.op("vector", lambda e, s, rb=rb: e.tensor_copy(
                    out=A_stB[rb:rb + 64, 0:260], in_=A_stF[rb:rb + 64, 0:260]),
                    waits=(mm_pl,))
                fp["gtmp_free"] = g5
                fp["bl_free"] = rc_l
                fp["rcl_free"] = g1
                fp["rc0_free"] = g4
                fp["pvl_free"] = g1
                fp["pv0_free"] = g4
                fp["a0_free"] = g4
                fp["al_free"] = t_ml
                fp["am0p_free"] = mm_p0
                fp["aml_free"] = mm_pl
                fp["mk_free"] = rc_l
                fp["mv_free"] = g4
                fp["s0_free"] = s2
                fp["sl_free"] = s2
                if h2 == 1:
                    fp["stB_tok"] = s4
                    fp["kpLN_free2"] = mm_sl
                    fp["mvN_free"] = mm_pl
                    fp["kpLT_free"] = mm_al
                glob_done = g5
        fp["st_free"] = s4
        fp["kpLN_free"] = kplt_done
        fp["psb_free"] = s4
        for ch in range(4):
            fp[("pevac", ch)] = s4

        P.scope = f"b1:st.{hp}"
        dtoks = []
        for cc in range(4):
            csl = slice(512 * cc, 512 * (cc + 1))
            dtoks.append(P.op("vector", lambda e, s, csl=csl: e.tensor_tensor(
                out=A_locT[:, csl], in0=A_locT[:, csl], in1=A_gT[:, csl],
                op=ALU.subtract), waits=(loc_done, glob_done)))
        dma_d = P.dma("sync", B.l1_diff[csl0, :], A_locT[:], waits=tuple(dtoks),
                      sem="d_st0")
        dma_g = P.dma("sync", B.l1_glob[csl0, :], A_gT[:], waits=(glob_done,) + tuple(dtoks),
                      sem="d_st1")
        fp["qT_free"] = loc_done
        fp["qpT_free"] = mm_pl
        fp["kT_free"] = fp.get("t2k_free", loc_done)
        fp["kpT_free"] = mm_a0
        fp["klT_free"] = loc_done
        fp["vT_free"] = fp.get("t2v_free", loc_done)
        fp["vN_free"] = mm_p0
        fp["vlN_free"] = loc_done
        fp["locT_free"] = dma_d
        fp["gT_free"] = dma_g
        diff_done = dma_d
        glob_dma = dma_g

    B.l1_diff_done = diff_done
    B.l1_glob_done = glob_dma
    ar.release()
    if L1_PARTS >= 1:
        emit_block1_gate(P, B)
    else:
        B.tok_s1 = diff_done


def emit_block1_gate(P, B):
    P.scope = "b1:gate"
    """gate_h = silu([x, diff] @ Wg.T + bg); alpha = sigmoid(gate_h @ Wgo.T +
    bgo) (scalar per token); mixed = glob + alpha*diff; s1 = mixed @ Wo.T + bo
    -> ST. diff/glob tiles streamed back from Internal DRAM."""
    ar = B.ar
    ar.mark()
    ones_row = cbf(B, "ones_row", rows=(0, 1))
    A_wg = ar.alloc3(NTT, 1024)
    A_wo = [ar.alloc3(NDT, 128), ar.alloc3(NDT, 128)]
    A_df = ar.alloc3(NDT, 512)
    A_gl = ar.alloc3(NDT, 512)
    A_gh = ar.alloc3(NDT, 512)
    A_alB = ar.alloc(512)
    A_wgo = ar.alloc(8)
    R_al = ar.rowf(512)
    RB_al = ar.rowb(512)

    wg_tok = P.dma("gpsimd", A_wg[:],
                   B.wh["L1_WgT"].rearrange("(it p) c -> p it c", p=128),
                   waits=(B.tok_xn, B.l1_diff_done, B.l1_glob_done),
                   sem="d_w2")
    t_wgo = P.op("vector", lambda e, s: e.tensor_copy(
        out=A_wgo[:], in_=B.SMALL[:, 16:24]),
        waits=(("d_small", P.cnt["d_small"]),))
    fp = {}
    s1_done = None
    for tcol in range(4):
        tsl = slice(512 * tcol, 512 * (tcol + 1))
        d_df = P.dma("sync", A_df[:],
                     B.l1_diff.rearrange("(dt p) t -> p dt t", p=128)[:, :, tsl],
                     waits=(B.l1_diff_done, fp.get("df_free")), sem="d_st0")
        d_gl = P.dma("sync", A_gl[:],
                     B.l1_glob.rearrange("(dt p) t -> p dt t", p=128)[:, :, tsl],
                     waits=(B.l1_glob_done, fp.get("gl_free")), sem="d_st1")
        mm_al = None
        evs = []
        for od in range(8):
            psum = B.PSA[:, 512 * (od % 2):512 * (od % 2) + 512]
            for it in range(16):
                rhs_is_x = it < 8
                mm = P.op("tensor", lambda e, s, od=od, it=it, psum=psum, tsl=tsl, rhs_is_x=rhs_is_x: e.matmul(
                    out=psum, lhsT=A_wg[:, it, 128 * od:128 * (od + 1)],
                    rhs=(B.XN[:, it, tsl] if rhs_is_x else A_df[:, it - 8, :]),
                    start=(it == 0), stop=(it == 15)),
                    waits=(wg_tok, d_df, B.tok_xn, fp.get(("gev", od % 2))) if it == 0 else (),
                    inc="t" if it == 15 else None)
            ev = P.op("scalar", lambda e, s, od=od, psum=psum: e.activation(
                out=A_gh[:, od, :], in_=psum, func=AF.Silu,
                bias=B.SMALL[:, od:od + 1]),
                waits=(mm, ("d_small", P.cnt["d_small"]), fp.get(("ghf", od))))
            fp[("gev", od % 2)] = ev
            evs.append(ev)
        apb = B.PSB[0:1, 512 * tcol:512 * tcol + 512]
        for od in range(8):
            mm_al = P.op("tensor", lambda e, s, od=od, apb=apb: e.matmul(
                out=apb, lhsT=A_wgo[:, od:od + 1], rhs=A_gh[:, od, :],
                start=(od == 0), stop=(od == 7)),
                waits=(evs[od], t_wgo), inc="t" if od == 7 else None)
            fp[("ghf", od)] = mm_al
        t_al = P.op("scalar", lambda e, s, tcol=tcol: e.activation(
            out=R_al, in_=B.PSB[0:1, 512 * tcol:512 * tcol + 512],
            func=AF.Sigmoid, bias=B.SMALL[0:1, 25:26]), waits=(mm_al,))
        t_ab = P.op("vector", lambda e, s: e.tensor_copy(out=RB_al, in_=R_al),
                    waits=(t_al,))
        mm_ab = P.op("tensor", lambda e, s: e.matmul(
            out=B.PSA[:, 1024:1536], lhsT=ones_row, rhs=RB_al,
            start=True, stop=True),
            waits=(t_ab, fp.get("ab_free"), fp.get(("sev", 0))), inc="t")
        t_alB = P.op("scalar", lambda e, s: e.copy(out=A_alB[:], in_=B.PSA[:, 1024:1536]),
                     waits=(mm_ab, fp.get("alB_free")))
        mix_done = []
        for dt_ in range(NDT):
            P.op("vector", lambda e, s, dt_=dt_: e.tensor_tensor(
                out=A_df[:, dt_, :], in0=A_df[:, dt_, :], in1=A_alB[:],
                op=ALU.mult), waits=(t_alB,) if dt_ == 0 else ())
            mix_done.append(P.op("vector", lambda e, s, dt_=dt_: e.tensor_tensor(
                out=A_df[:, dt_, :], in0=A_df[:, dt_, :], in1=A_gl[:, dt_, :],
                op=ALU.add), waits=(d_gl,) if dt_ == 0 else ()))
        fp["gl_free"] = mix_done[-1]
        for od in range(NDT):
            wo_tok = P.dma("gpsimd", A_wo[od % 2][:],
                           wslice_src(B.wh["L1_WoT"], 128 * od, 128 * (od + 1)),
                           waits=(fp.get(("wo_free", od % 2)), B.l1_glob_done),
                           sem="d_w" + str(od % 2))
            psum = B.PSA[:, 1024 + 512 * (od % 2):1024 + 512 * (od % 2) + 512]
            for it in range(NDT):
                mm = P.op("tensor", lambda e, s, od=od, it=it, psum=psum: e.matmul(
                    out=psum, lhsT=A_wo[od % 2][:, it, :], rhs=A_df[:, it, :],
                    start=(it == 0), stop=(it == NDT - 1)),
                    waits=(wo_tok, mix_done[-1], t_alB, fp.get(("sev", od % 2))) if it == 0 else (),
                    inc="t" if it == NDT - 1 else None)
            sev = P.op("scalar", lambda e, s, od=od, psum=psum, tsl=tsl: e.activation(
                out=B.ST[:, od, tsl], in_=psum, func=AF.Identity,
                bias=B.SMALL[:, 8 + od:9 + od]), waits=(mm,))
            fp[("sev", od % 2)] = sev
            fp[("wo_free", od % 2)] = mm
            s1_done = sev
        fp["df_free"] = mm
        fp["ab_free"] = t_alB
        fp["alB_free"] = mm
    B.tok_s1 = s1_done
    ar.release()


def esl_src(hap, ei, c0, c1):
    return hap[ei, :, c0:c1].rearrange("(a p) c -> p a c", p=128)


def emit_block1_ffn(P, B):
    P.scope = "b1:route"
    """Dense top-2-of-6 MoE (expert SwiGLU-like: silu(x@ew1.T)*(x@ew3.T) @
    ew2.T, weighted by normalized top-2 sigmoid router probs) + shared
    expert, + glue x += s*silu(f). Router math kept in fp32 (routing
    comparisons must not be perturbed by bf16); ge weights folded into h
    before the down-projection so experts accumulate in one SBUF fp32 y."""
    ar = B.ar
    ar.mark()
    eye = cbf(B, "eye")
    A_gw = ar.alloc3(NDT, 6)
    P_nat = ar.alloc(192).bitcast(F32).rearrange("p (a b) -> p a b", b=6)
    CF = ar.alloc(192).bitcast(F32).rearrange("p (a b) -> p a b", b=6)
    M1 = ar.alloc(32).bitcast(F32)
    M2 = ar.alloc(32).bitcast(F32)
    DEN = ar.alloc(32).bitcast(F32)
    GE_bf = ar.alloc(96).rearrange("p (a b) -> p a b", b=6)
    A_geT = ar.alloc(T)
    A_geB = ar.alloc(1024)
    A_h = ar.alloc3(NDT, 1024)
    A_y = ar.alloc(16384).bitcast(F32).rearrange("p (a b) -> p a b", b=1024)
    A_we1 = [ar.alloc3(NDT, 128), ar.alloc3(NDT, 128)]
    A_we3 = [ar.alloc3(NDT, 128), ar.alloc3(NDT, 128)]
    A_w2 = [ar.alloc3(NDT, 128), ar.alloc3(NDT, 128)]
    A_sil = ar.alloc(512)
    A_gf = ar.alloc(512)

    g_tok = P.dma("gpsimd", A_gw[:],
                  B.wh["L1_gateT"].rearrange("(a p) c -> p a c", p=128),
                  waits=(B.tok_xn, B.tok_s1), sem="d_w3")
    fp = {}
    rt_toks = []
    for tt in range(NTT):
        psum = B.PSB[:, 512 * (tt % 2):512 * (tt % 2) + 6]
        for it in range(NDT):
            mm = P.op("tensor", lambda e, s, tt=tt, it=it, psum=psum: e.matmul(
                out=psum, lhsT=B.XN[:, it, 128 * tt:128 * (tt + 1)],
                rhs=A_gw[:, it, :], start=(it == 0), stop=(it == NDT - 1)),
                waits=(g_tok, B.tok_xn, B.tok_s1, fp.get(("lg", tt % 2))) if it == 0 else (),
                inc="t" if it == NDT - 1 else None)
        t1 = P.op("vector", lambda e, s, tt=tt, psum=psum: e.tensor_tensor(
            out=P_nat[:, tt, :], in0=psum, in1=B.SMALL[:, 40:46], op=ALU.add),
            waits=(mm, ("d_small", P.cnt["d_small"])))
        t2 = P.op("scalar", lambda e, s, tt=tt: e.activation(
            out=P_nat[:, tt, :], in_=P_nat[:, tt, :], func=AF.Sigmoid),
            waits=(t1,))
        fp[("lg", tt % 2)] = t2
        c1 = P.op("vector", lambda e, s, tt=tt: e.tensor_reduce(
            out=M1[:, tt:tt + 1], in_=P_nat[:, tt, :], axis=AX.X, op=ALU.max),
            waits=(t2,))
        P.op("vector", lambda e, s, tt=tt: e.tensor_scalar(
            out=CF[:, tt, :], in0=P_nat[:, tt, :], scalar1=M1[:, tt:tt + 1],
            scalar2=None, op0=ALU.is_equal))
        P.op("vector", lambda e, s, tt=tt: e.scalar_tensor_tensor(
            out=CF[:, tt, :], in0=CF[:, tt, :], scalar=-1e30,
            in1=P_nat[:, tt, :], op0=ALU.mult, op1=ALU.add))
        P.op("vector", lambda e, s, tt=tt: e.tensor_reduce(
            out=M2[:, tt:tt + 1], in_=CF[:, tt, :], axis=AX.X, op=ALU.max))
        P.op("vector", lambda e, s, tt=tt: e.tensor_scalar(
            out=CF[:, tt, :], in0=P_nat[:, tt, :], scalar1=M2[:, tt:tt + 1],
            scalar2=None, op0=ALU.is_ge))
        P.op("vector", lambda e, s, tt=tt: e.tensor_scalar(
            out=DEN[:, tt:tt + 1], in0=M1[:, tt:tt + 1],
            scalar1=M2[:, tt:tt + 1], scalar2=1e-8, op0=ALU.add, op1=ALU.add))
        P.op("vector", lambda e, s, tt=tt: e.reciprocal(
            out=DEN[:, tt:tt + 1], in_=DEN[:, tt:tt + 1]))
        P.op("vector", lambda e, s, tt=tt: e.tensor_tensor(
            out=CF[:, tt, :], in0=CF[:, tt, :], in1=P_nat[:, tt, :],
            op=ALU.mult))
        rt_toks.append(P.op("vector", lambda e, s, tt=tt: e.tensor_scalar(
            out=GE_bf[:, tt, :], in0=CF[:, tt, :], scalar1=DEN[:, tt:tt + 1],
            scalar2=None, op0=ALU.mult)))
    geT_done = None
    for tt in range(NTT):
        bank = B.PSB[:, 512 * (tt % 2) + 256:512 * (tt % 2) + 320].bitcast(BF16)
        mm_t = P.op("tensor", lambda e, s, tt=tt, bank=bank: e.transpose(
            out=bank[0:6, 0:128], in_=GE_bf[:, tt, :], identity=eye),
            waits=(rt_toks[-1], fp.get(("get", tt % 2))), inc="t")
        geT_done = P.op("scalar", lambda e, s, tt=tt, bank=bank: e.copy(
            out=A_geT[0:6, 128 * tt:128 * (tt + 1)], in_=bank[0:6, 0:128]),
            waits=(mm_t,))
        fp[("get", tt % 2)] = geT_done

    WN = {"w1": ("L1_ew1T", "L1_fc1T"), "w3": ("L1_ew3T", "L1_fc2T"),
          "w2": ("L1_ew2T", "L1_fc3T")}

    def wsrc(kind, ei, c0, c1):
        en, sn = WN[kind]
        if ei == 6:
            return wslice_src(B.wh[sn], c0, c1)
        return esl_src(B.wh[en], ei, c0, c1)

    x_done = None
    for tch in range(2):
        P.scope = f"b1:moe.{tch}"
        t0 = 1024 * tch
        for ei in range(7):
            shared = ei == 6
            geb_ev = []
            if not shared:
                for hh in range(2):
                    gps = B.PSB[:, 1024 + 512 * hh:1024 + 512 * (hh + 1)]
                    mm_g = P.op("tensor", lambda e, s, ei=ei, hh=hh, t0=t0, gps=gps: e.matmul(
                        out=gps, lhsT=cbf(B, "sel6")[0:6, 128 * ei:128 * (ei + 1)],
                        rhs=A_geT[0:6, t0 + 512 * hh:t0 + 512 * (hh + 1)],
                        start=True, stop=True),
                        waits=(geT_done, fp.get(("gebf", hh))), inc="t")
                    ev = P.op("scalar", lambda e, s, hh=hh, gps=gps: e.copy(
                        out=A_geB[:, 512 * hh:512 * (hh + 1)], in_=gps),
                        waits=(mm_g, fp.get(("gebu", hh))))
                    geb_ev.append(ev)
                    fp[("gebf", hh)] = ev
            h_done = None
            for it in range(NDT):
                bi = it % 2
                w1t = P.dma("gpsimd", A_we1[bi][:], wsrc("w1", ei, 128 * it, 128 * (it + 1)),
                            waits=(fp.get(("w1f", bi)), B.tok_s1),
                            sem=("d_w0", "d_w4")[bi])
                w3t = P.dma("gpsimd", A_we3[bi][:], wsrc("w3", ei, 128 * it, 128 * (it + 1)),
                            waits=(fp.get(("w3f", bi)), B.tok_s1),
                            sem=("d_w1", "d_w5")[bi])
                for hh in range(2):
                    ps1 = B.PSA[:, 512 * hh:512 * (hh + 1)]
                    ps3 = B.PSA[:, 1024 + 512 * hh:1024 + 512 * (hh + 1)]
                    for kt in range(NDT):
                        mm1 = P.op("tensor", lambda e, s, bi=bi, kt=kt, hh=hh, t0=t0, ps1=ps1: e.matmul(
                            out=ps1, lhsT=A_we1[bi][:, kt, :],
                            rhs=B.XN[:, kt, t0 + 512 * hh:t0 + 512 * (hh + 1)],
                            start=(kt == 0), stop=(kt == NDT - 1)),
                            waits=(w1t, B.tok_s1, fp.get(("h1f", hh))) if kt == 0 else (),
                            inc="t" if kt == NDT - 1 else None)
                    for kt in range(NDT):
                        mm3 = P.op("tensor", lambda e, s, bi=bi, kt=kt, hh=hh, t0=t0, ps3=ps3: e.matmul(
                            out=ps3, lhsT=A_we3[bi][:, kt, :],
                            rhs=B.XN[:, kt, t0 + 512 * hh:t0 + 512 * (hh + 1)],
                            start=(kt == 0), stop=(kt == NDT - 1)),
                            waits=(w3t, fp.get(("h3f", hh))) if kt == 0 else (),
                            inc="t" if kt == NDT - 1 else None)
                    sil = P.op("scalar", lambda e, s, ps1=ps1: e.activation(
                        out=A_sil[:], in_=ps1, func=AF.Silu),
                        waits=(mm1, fp.get("silf")))
                    hv = P.op("vector", lambda e, s, it=it, hh=hh, ps3=ps3: e.tensor_tensor(
                        out=A_h[:, it, 512 * hh:512 * (hh + 1)], in0=A_sil[:],
                        in1=ps3, op=ALU.mult),
                        waits=(sil, mm3, fp.get(("hfree", it, hh))))
                    fp[("h1f", hh)] = hv
                    fp[("h3f", hh)] = hv
                    fp["silf"] = hv
                    if not shared:
                        hv = P.op("vector", lambda e, s, it=it, hh=hh: e.tensor_tensor(
                            out=A_h[:, it, 512 * hh:512 * (hh + 1)],
                            in0=A_h[:, it, 512 * hh:512 * (hh + 1)],
                            in1=A_geB[:, 512 * hh:512 * (hh + 1)],
                            op=ALU.mult), waits=(geb_ev[hh],))
                    h_done = hv
                fp[("w1f", bi)] = mm1
                fp[("w3f", bi)] = mm3
            for dt_ in range(NDT):
                bi = dt_ % 2
                w2t = P.dma("gpsimd", A_w2[bi][:], wsrc("w2", ei, 128 * dt_, 128 * (dt_ + 1)),
                            waits=(fp.get(("w2f", bi)), B.tok_s1),
                            sem=("d_w2", "d_w6")[bi])
                for hh in range(2):
                    psy = B.PSB[:, 512 * hh:512 * (hh + 1)]
                    for it in range(NDT):
                        mm = P.op("tensor", lambda e, s, bi=bi, it=it, hh=hh, psy=psy: e.matmul(
                            out=psy, lhsT=A_w2[bi][:, it, :],
                            rhs=A_h[:, it, 512 * hh:512 * (hh + 1)],
                            start=(it == 0), stop=(it == NDT - 1)),
                            waits=(w2t, h_done, geT_done, fp.get(("yev", hh))) if it == 0 else (),
                            inc="t" if it == NDT - 1 else None)
                    if ei == 0:
                        yev = P.op("scalar", lambda e, s, dt_=dt_, hh=hh, psy=psy: e.copy(
                            out=A_y[:, dt_, 512 * hh:512 * (hh + 1)], in_=psy),
                            waits=(mm, fp.get(("yw", dt_, hh))))
                    else:
                        yev = P.op("vector", lambda e, s, dt_=dt_, hh=hh, psy=psy: e.tensor_tensor(
                            out=A_y[:, dt_, 512 * hh:512 * (hh + 1)],
                            in0=A_y[:, dt_, 512 * hh:512 * (hh + 1)], in1=psy,
                            op=ALU.add), waits=(mm,))
                    fp[("yev", hh)] = yev
                    fp[("yv", dt_, hh)] = yev
                fp[("w2f", bi)] = mm
            for it in range(NDT):
                for hh in range(2):
                    fp[("hfree", it, hh)] = mm
        for dt_ in range(NDT):
            for hh in range(2):
                tq = slice(t0 + 512 * hh, t0 + 512 * (hh + 1))
                g1 = P.op("scalar", lambda e, s, dt_=dt_, hh=hh: e.activation(
                    out=A_gf[:], in_=A_y[:, dt_, 512 * hh:512 * (hh + 1)],
                    func=AF.Silu),
                    waits=(fp[("yv", dt_, hh)], fp.get("gff")))
                g2 = P.op("vector", lambda e, s, dt_=dt_, tq=tq: e.tensor_tensor(
                    out=A_gf[:], in0=A_gf[:], in1=B.ST[:, dt_, tq],
                    op=ALU.mult), waits=(g1, B.tok_s1))
                x_done = P.op("vector", lambda e, s, dt_=dt_, tq=tq: e.tensor_tensor(
                    out=B.XT[:, dt_, tq], in0=B.XT[:, dt_, tq], in1=A_gf[:],
                    op=ALU.add))
                fp["gff"] = x_done
                fp[("yw", dt_, hh)] = x_done
    B.tok_x_final = x_done
    ar.release()


def emit_block2(P, B):
    P.scope = "b2:attn"
    """L2: RoPE retention (chunked recurrence) * silu(gate) @ wo ; quantum FFN."""
    ar = B.ar
    ar.mark()
    gamma = [1.0 - 2.0 ** (-5.0 - h) for h in range(H)]

    tok_b1 = P.dma("sync", B.SMALL[:, 0:32], B.wh["L2_b1_col"], sem="d_small",
                   waits=(B.tok_xn,))
    tok_th = P.dma("sync", B.SMALL[:, 32:64], B.wh["L2_theta_col"], sem="d_small")
    tok_wr = P.dma("sync", B.SMALL[:, 64:96], B.wh["L2_wr_col"], sem="d_small")
    tok_wi = P.dma("sync", B.SMALL[:, 96:128], B.wh["L2_wi_col"], sem="d_small")
    tok_b2 = P.dma("sync", B.SMALL[:, 128:136], B.wh["L2_b2_col"], sem="d_small")
    B.tok_l2small = tok_b2

    A_qw = ar.alloc3(NDT, 256)
    A_kw = ar.alloc3(NDT, 256)
    A_vw = ar.alloc3(NDT, 256)
    A_gw = ar.alloc3(NDT, 256)
    A_ow = ar.alloc3(2, 1024)
    A_qT = ar.alloc3(2, T)
    A_kT = ar.alloc3(2, T)
    A_gT = ar.alloc3(2, T)
    A_vx = ar.alloc4(NTT, 4, 64)
    A_kn = ar.alloc(256)
    A_pm = ar.alloc3(4, 128)
    A_S = ar.alloc(256).bitcast(F32)
    A_Sb = ar.alloc(256)
    A_rt = ar.alloc(512).bitcast(F32)
    A_cos = ar.alloc(T)
    A_sin = ar.alloc(T)
    A_qg = ar.alloc3(2, 128)
    A_kg = ar.alloc3(2, 128)
    A_dmv = ar.alloc3(4, 128)
    tok_cos = P.dma("gpsimd", A_cos[:], chbm(B, "rope_cos"), sem="d_gr",
                    waits=(B.tok_xn,))
    tok_sin = P.dma("gpsimd", A_sin[:], chbm(B, "rope_sin"), sem="d_gr")

    prev = {}
    p_done = None
    for hg in range(4):
        c0 = hg * 256
        wsem = f"d_w{hg % 2}"
        w_q = P.dma("gpsimd", A_qw[:], wslice_src(B.wh["L2_wqT"], c0, c0 + 256),
                    waits=(prev.get("q_done"), B.tok_xn), sem=wsem)
        w_k = P.dma("gpsimd", A_kw[:], wslice_src(B.wh["L2_wkT"], c0, c0 + 256),
                    waits=(prev.get("k_done"),), sem=wsem)
        w_v = P.dma("gpsimd", A_vw[:], wslice_src(B.wh["L2_wvT"], c0, c0 + 256),
                    waits=(prev.get("v_done"),), sem=wsem)
        w_g = P.dma("gpsimd", A_gw[:], wslice_src(B.wh["L2_wgT"], c0, c0 + 256),
                    waits=(prev.get("g_done"),), sem=wsem)
        w_o = P.dma("gpsimd", A_ow[:],
                    B.wh["L2_woT"][c0:c0 + 256, :].rearrange("(it p) c -> p it c", p=128),
                    waits=(p_done,), sem=wsem)
        tok_con = None
        for oc in range(2):
            tok_con = P.dma("gpsimd", A_qg[:, oc, :], chbm(B, f"l2_qg_{2 * hg + oc}"),
                            sem="d_gr", waits=(prev.get("con_used"),))
            tok_con = P.dma("gpsimd", A_kg[:, oc, :], chbm(B, f"l2_kg_{2 * hg + oc}"),
                            sem="d_gr")
        for hs in range(4):
            h_ = 4 * hg + 2 * (hs // 2) + (hs % 2)
            tok_con = P.dma("gpsimd", A_dmv[:, hs, :], chbm(B, f"l2_decmv_{h_}"),
                            sem="d_gr")

        def proj2(wview, dst, wtok, act=None):
            last = None
            for oc in range(2):
                for ch in range(4):
                    psum = B.PSA[:, 512 * ch:512 * (ch + 1)]
                    for it in range(NDT):
                        mm = P.op("tensor", lambda e, s, oc=oc, ch=ch, it=it, wview=wview, psum=psum: e.matmul(
                            out=psum, lhsT=wview[:, it, 128 * oc:128 * (oc + 1)],
                            rhs=B.XN[:, it, 512 * ch:512 * (ch + 1)],
                            start=(it == 0), stop=(it == NDT - 1)),
                            waits=(wtok, B.tok_xn, prev.get(("evac", ch))) if it == 0 else (),
                            inc="t" if it == NDT - 1 else None)
                    if act is not None:
                        ev = P.op("scalar", lambda e, s, oc=oc, ch=ch, dst=dst, psum=psum, act=act: e.activation(
                            out=dst[:, oc, 512 * ch:512 * (ch + 1)], in_=psum, func=act),
                            waits=(mm,))
                    else:
                        ev = P.op("scalar", lambda e, s, oc=oc, ch=ch, dst=dst, psum=psum: e.copy(
                            out=dst[:, oc, 512 * ch:512 * (ch + 1)], in_=psum),
                            waits=(mm,))
                    prev[("evac", ch)] = ev
                    last = ev
            return last

        q_done = proj2(A_qw, A_qT, w_q)
        k_done = proj2(A_kw, A_kT, w_k)
        g_done = proj2(A_gw, A_gT, w_g, act=AF.Silu)

        v_done = None
        for tt in range(NTT):
            psum = B.PSB[:, 1024 + 512 * (tt % 2):1024 + 512 * (tt % 2) + 256]
            for it in range(NDT):
                mm = P.op("tensor", lambda e, s, tt=tt, it=it, psum=psum: e.matmul(
                    out=psum, lhsT=B.XN[:, it, 128 * tt:128 * (tt + 1)],
                    rhs=A_vw[:, it, :], start=(it == 0), stop=(it == NDT - 1)),
                    waits=(w_v, B.tok_xn, prev.get(("vevac", tt % 2)), prev.get("ret_done")) if it == 0 else (),
                    inc="t" if it == NDT - 1 else None)
            ev = P.op("vector", lambda e, s, tt=tt, psum=psum: e.tensor_copy(
                out=A_vx[:, tt, :, :],
                in_=psum.rearrange("p (a b) -> p a b", b=64)), waits=(mm,))
            prev[("vevac", tt % 2)] = ev
            v_done = ev

        rope_done = {}
        for (srcT, src_done, tag) in ((A_qT, q_done, "q"), (A_kT, k_done, "k")):
            lastt = None
            for oc in range(2):
                for ch in range(8):
                    tsl = slice(256 * ch, 256 * (ch + 1))
                    mm = P.op("tensor", lambda e, s, oc=oc, tsl=tsl, srcT=srcT: e.matmul(
                        out=B.PSA[:, 1536:1792], lhsT=cbf(B, "rope_rotT"),
                        rhs=srcT[:, oc, tsl], start=True, stop=True),
                        waits=(src_done, g_done, prev.get("rope_ps")), inc="t")
                    t1 = P.op("vector", lambda e, s, tsl=tsl: e.tensor_tensor(
                        out=A_rt[:, 0:256], in0=B.PSA[:, 1536:1792],
                        in1=A_sin[:, tsl], op=ALU.mult), waits=(mm, tok_sin))
                    prev["rope_ps"] = t1
                    P.op("vector", lambda e, s, oc=oc, tsl=tsl, srcT=srcT: e.tensor_tensor(
                        out=srcT[:, oc, tsl], in0=srcT[:, oc, tsl],
                        in1=A_cos[:, tsl], op=ALU.mult), waits=(tok_cos,))
                    lastt = P.op("vector", lambda e, s, oc=oc, tsl=tsl, srcT=srcT: e.tensor_tensor(
                        out=srcT[:, oc, tsl], in0=srcT[:, oc, tsl],
                        in1=A_rt[:, 0:256], op=ALU.add))
            rope_done[tag] = lastt

        fold_done = {}
        for (srcT, ctab, dep, tag) in ((A_qT, A_qg, rope_done["q"], "q"),
                                       (A_kT, A_kg, rope_done["k"], "k")):
            lastt = None
            for oc in range(2):
                cc_ = ctab[:, oc, :]
                lastt = P.op("vector", lambda e, s, oc=oc, srcT=srcT, cc_=cc_: e.tensor_tensor(
                    out=srcT[:, oc, :].rearrange("p (c t) -> p c t", t=128),
                    in0=srcT[:, oc, :].rearrange("p (c t) -> p c t", t=128),
                    in1=cc_[:, None, :].broadcast_to([128, 16, 128]),
                    op=ALU.mult), waits=(dep, tok_con))
            fold_done[tag] = lastt

        t_s0 = P.op("vector", lambda e, s: e.memset(A_S[:, 0:128], 0.0),
                    waits=(prev.get("ret_done"),))
        t_s0b = P.op("vector", lambda e, s: e.memset(A_Sb[:, 0:256], 0.0))

        ret_done = None
        sttok = {}
        for c in range(NTT):
            csl = slice(128 * c, 128 * (c + 1))
            for hs in range(4):
                rb = 64 * (hs % 2)
                oc = hs // 2
                h = 4 * hg + 2 * oc + (hs % 2)
                bank = B.PSA[:, 512 * hs:512 * (hs + 1)]
                obank = B.PSB[0:64, 512 * hs + 128 * (c % 4):512 * hs + 128 * (c % 4) + 128]
                Sf = A_S[rb:rb + 64, 64 * oc:64 * oc + 64]
                Sb = A_Sb[rb:rb + 64, 64 * oc:64 * oc + 64]
                mm_a = P.op("tensor", lambda e, s, hs=hs, rb=rb, oc=oc, csl=csl, bank=bank: e.matmul(
                    out=bank[:, 0:128], lhsT=A_kT[rb:rb + 64, oc, csl],
                    rhs=A_qT[rb:rb + 64, oc, csl], start=True, stop=True),
                    waits=(fold_done["q"], fold_done["k"],
                           sttok.get(("stt", hs)), t_s0b, v_done), inc="t")
                t_pm = P.op("vector", lambda e, s, hs=hs, bank=bank: e.tensor_tensor(
                    out=A_pm[:, hs, :], in0=bank[:, 0:128],
                    in1=A_dmv[:, hs, :], op=ALU.mult), waits=(mm_a, tok_con))
                mm_i1 = P.op("tensor", lambda e, s, hs=hs, c=c, obank=obank: e.matmul(
                    out=obank, lhsT=A_vx[:, c, hs, :], rhs=A_pm[:, hs, :],
                    start=True, stop=False),
                    waits=(t_pm, sttok.get(("oev", hs))), inc="t")
                mm_i2 = P.op("tensor", lambda e, s, hs=hs, rb=rb, oc=oc, csl=csl, obank=obank, Sb=Sb: e.matmul(
                    out=obank, lhsT=Sb, rhs=A_qT[rb:rb + 64, oc, csl],
                    start=False, stop=True), inc="t")
                mm_t = P.op("tensor", lambda e, s, hs=hs, rb=rb, oc=oc, csl=csl, bank=bank: e.transpose(
                    out=bank[:, 256:288].bitcast(BF16)[:, 0:64],
                    in_=A_kT[rb:rb + 64, oc, csl], identity=cbf(B, "eye")[rb:rb + 64, rb:rb + 64]),
                    waits=(fold_done["k"], sttok.get(("knev", hs))), inc="t")
                knev = P.op("vector", lambda e, s, hs=hs, bank=bank: e.tensor_copy(
                    out=A_kn[:, 64 * hs:64 * hs + 64],
                    in_=bank[:, 256:288].bitcast(BF16)[:, 0:64]), waits=(mm_t,))
                sttok[("knev", hs)] = knev
                mm_s = P.op("tensor", lambda e, s, hs=hs, c=c, bank=bank: e.matmul(
                    out=bank[0:64, 384:448], lhsT=A_kn[:, 64 * hs:64 * hs + 64],
                    rhs=A_vx[:, c, hs, :], start=True, stop=True),
                    waits=(knev, v_done), inc="t")
                oev = P.op("vector", lambda e, s, hs=hs, rb=rb, oc=oc, csl=csl, obank=obank: e.tensor_tensor(
                    out=A_kT[rb:rb + 64, oc, csl], in0=obank,
                    in1=A_gT[rb:rb + 64, oc, csl], op=ALU.mult),
                    waits=(mm_i2, g_done))
                sttok[("oev", hs)] = oev
                stt_t = P.op("vector", lambda e, s, h=h, bank=bank, Sf=Sf: e.scalar_tensor_tensor(
                    out=Sf, in0=Sf, scalar=float(gamma[h] ** 128),
                    in1=bank[0:64, 384:448], op0=ALU.mult, op1=ALU.add),
                    waits=(mm_s, t_s0))
                sttok[("stt", hs)] = P.op("vector", lambda e, s, Sf=Sf, Sb=Sb: e.tensor_copy(
                    out=Sb, in_=Sf))
                ret_done = sttok[("stt", hs)]
                prev[("retoev", hs)] = oev
        last_oev = prev[("retoev", 3)]
        prev["ret_done"] = ret_done

        for oc2 in range(NDT):
            for ch in range(4):
                psum = B.PSA[:, 512 * ch:512 * (ch + 1)]
                for it2 in range(2):
                    mm = P.op("tensor", lambda e, s, oc2=oc2, ch=ch, it2=it2, psum=psum: e.matmul(
                        out=psum, lhsT=A_ow[:, it2, 128 * oc2:128 * (oc2 + 1)],
                        rhs=A_kT[:, it2, 512 * ch:512 * (ch + 1)],
                        start=(it2 == 0), stop=(it2 == 1)),
                        waits=(w_o, last_oev, prev.get(("evac", ch)), sttok.get(("stt", 3))) if it2 == 0 else (),
                        inc="t" if it2 == 1 else None)
                if hg == 0:
                    ev = P.op("scalar", lambda e, s, oc2=oc2, ch=ch, psum=psum: e.copy(
                        out=B.ST[:, oc2, 512 * ch:512 * (ch + 1)], in_=psum),
                        waits=(mm,))
                else:
                    ev = P.op("vector", lambda e, s, oc2=oc2, ch=ch, psum=psum: e.tensor_tensor(
                        out=B.ST[:, oc2, 512 * ch:512 * (ch + 1)],
                        in0=B.ST[:, oc2, 512 * ch:512 * (ch + 1)], in1=psum,
                        op=ALU.add), waits=(mm,))
                prev[("evac", ch)] = ev
                p_done = ev
        prev["q_done"] = q_done
        prev["k_done"] = k_done
        prev["v_done"] = v_done
        prev["g_done"] = g_done
        prev["con_used"] = ret_done
    B.tok_s2 = p_done
    ar.release()
    emit_block2_ffn(P, B)


def emit_block2_ffn(P, B):
    P.scope = "b2:ffn"
    """hidden = xn@W1+b1; quantum transform; relu; y = act@W2 + b2; glue.

    sin() on ScalarE needs args in [-pi,pi]. Range reduction is done with
    strict compare-and-shift passes (thresholds +-PI_LO just below pi, shift
    TWO_PI_LO just below 2pi) so reduced values stay strictly inside
    [-pi,pi] under device fp32, sim fp32 and sim fp64 arithmetic alike
    (float->int cast rounding differs between device and sim, so casts are
    avoided entirely). theta_b = theta+0.1*b1 is per-feature (|theta| <=
    ~22 here, 6 passes cover |x|<=40.8 = 6.5 sigma of the theta init
    randn*2pi); after that the per-element argument theta_b_red + 0.1*h has
    |arg| <= pi + 0.1*|h|max << 3pi, so ONE shift pass suffices in the hot
    loop (|0.1*h| <= ~0.6 for layernormed x and W1 ~ randn/sqrt(d))."""
    ar = B.ar
    ar.mark()
    PI_LO = 3.1415925025939941
    TWO_PI_LO = 6.2831850051879883
    DC = ar.alloc(256).bitcast(F32)

    def _reduce_passes(dst, tmp, npass):
        tok = None
        for _ in range(npass):
            P.op("vector", lambda e, s, dst=dst, tmp=tmp: e.tensor_scalar(
                out=tmp, in0=dst, scalar1=PI_LO, scalar2=None, op0=ALU.is_gt))
            P.op("vector", lambda e, s, dst=dst, tmp=tmp: e.scalar_tensor_tensor(
                out=dst, in0=tmp, scalar=-TWO_PI_LO, in1=dst,
                op0=ALU.mult, op1=ALU.add))
            P.op("vector", lambda e, s, dst=dst, tmp=tmp: e.tensor_scalar(
                out=tmp, in0=dst, scalar1=-PI_LO, scalar2=None, op0=ALU.is_lt))
            tok = P.op("vector", lambda e, s, dst=dst, tmp=tmp: e.scalar_tensor_tensor(
                out=dst, in0=tmp, scalar=TWO_PI_LO, in1=dst,
                op0=ALU.mult, op1=ALU.add))
        return tok

    P.op("vector", lambda e, s: e.scalar_tensor_tensor(
        out=DC[:, 0:32], in0=B.SMALL[:, 0:32], scalar=0.1,
        in1=B.SMALL[:, 32:64], op0=ALU.mult, op1=ALU.add),
        waits=(B.tok_s2, ("d_small", P.cnt["d_small"])))
    _reduce_passes(DC[:, 0:32], DC[:, 96:128], 6)
    P.op("vector", lambda e, s: e.tensor_copy(
        out=DC[:, 32:64], in_=B.SMALL[:, 32:64]))
    _reduce_passes(DC[:, 32:64], DC[:, 96:128], 6)
    P.op("vector", lambda e, s: e.tensor_scalar(
        out=DC[:, 96:128], in0=DC[:, 32:64], scalar1=math.pi / 2.0,
        scalar2=None, op0=ALU.add))
    P.op("vector", lambda e, s: e.tensor_scalar(
        out=DC[:, 64:96], in0=DC[:, 96:128], scalar1=PI_LO, scalar2=None,
        op0=ALU.is_gt))
    t_c = P.op("vector", lambda e, s: e.scalar_tensor_tensor(
        out=DC[:, 96:128], in0=DC[:, 64:96], scalar=-TWO_PI_LO,
        in1=DC[:, 96:128], op0=ALU.mult, op1=ALU.add))
    t_sin = P.op("scalar", lambda e, s: e.activation(
        out=DC[:, 32:64], in_=DC[:, 32:64], func=AF.Sin), waits=(t_c,))
    t_cos = P.op("scalar", lambda e, s: e.activation(
        out=DC[:, 96:128], in_=DC[:, 96:128], func=AF.Sin))
    P.op("vector", lambda e, s: e.tensor_tensor(
        out=DC[:, 64:96], in0=DC[:, 96:128], in1=B.SMALL[:, 64:96],
        op=ALU.mult), waits=(t_cos,))
    P.op("vector", lambda e, s: e.tensor_tensor(
        out=DC[:, 96:128], in0=DC[:, 32:64], in1=B.SMALL[:, 96:128],
        op=ALU.mult), waits=(t_sin,))
    P.op("vector", lambda e, s: e.tensor_tensor(
        out=DC[:, 64:96], in0=DC[:, 64:96], in1=DC[:, 96:128], op=ALU.add))
    P.op("vector", lambda e, s: e.tensor_tensor(
        out=DC[:, 64:96], in0=DC[:, 64:96], in1=B.SMALL[:, 96:128],
        op=ALU.mult))
    tC1 = P.op("vector", lambda e, s: e.tensor_scalar(
        out=DC[:, 64:96], in0=DC[:, 64:96], scalar1=0.1, scalar2=None,
        op0=ALU.mult))

    A_w1 = ar.alloc3(NDT, 128)
    A_w2 = ar.alloc3(4, 1024)
    A_ht = ar.alloc3(32, 512)
    A_tf = [ar.alloc(1024).bitcast(F32), ar.alloc(1024).bitcast(F32)]
    A_cf = [ar.alloc(1024).bitcast(F32), ar.alloc(1024).bitcast(F32)]
    A_gf = ar.alloc(1024).bitcast(F32)
    fp = {}
    mm_done = None
    for tch in range(4):
        t0 = 512 * tch
        for ot in range(32):
            lw = P.dma("gpsimd", A_w1[:],
                       B.wh["L2_W1"].rearrange("(it p) c -> p it c", p=128)[:, :, 128 * ot:128 * (ot + 1)],
                       sem="d_w" + str(2 + ot % 2),
                       waits=(fp.get("w1used"), B.tok_s2))
            psum = B.PSA[:, 512 * (ot % 2):512 * (ot % 2) + 512]
            for it in range(NDT):
                mm = P.op("tensor", lambda e, s, it=it, ot=ot, t0=t0, psum=psum: e.matmul(
                    out=psum, lhsT=A_w1[:, it, :], rhs=B.XN[:, it, t0:t0 + 512],
                    start=(it == 0), stop=(it == NDT - 1)),
                    waits=(lw, B.tok_xn, fp.get(("hq", ot % 2)),
                           fp.get(("yev", ot % 2))) if it == 0 else (),
                    inc="t" if it == NDT - 1 else None)
            fp["w1used"] = mm
            tf = A_tf[ot % 2]
            cf = A_cf[ot % 2]
            m1 = P.op("scalar", lambda e, s, ot=ot, psum=psum, tf=tf: e.activation(
                out=tf[:, 0:512], in_=psum, func=AF.Identity, scale=0.1,
                bias=DC[:, ot:ot + 1]),
                waits=(mm, tC1, fp.get(("tffree", ot % 2))))
            m2 = P.op("vector", lambda e, s, tf=tf, cf=cf: e.tensor_scalar(
                out=cf[:, 0:512], in0=tf[:, 0:512], scalar1=PI_LO,
                scalar2=None, op0=ALU.is_gt), waits=(m1,))
            P.op("vector", lambda e, s, tf=tf, cf=cf: e.scalar_tensor_tensor(
                out=tf[:, 0:512], in0=cf[:, 0:512], scalar=-TWO_PI_LO,
                in1=tf[:, 0:512], op0=ALU.mult, op1=ALU.add))
            P.op("vector", lambda e, s, tf=tf, cf=cf: e.tensor_scalar(
                out=cf[:, 0:512], in0=tf[:, 0:512], scalar1=-PI_LO,
                scalar2=None, op0=ALU.is_lt))
            m5 = P.op("vector", lambda e, s, tf=tf, cf=cf: e.scalar_tensor_tensor(
                out=tf[:, 0:512], in0=cf[:, 0:512], scalar=TWO_PI_LO,
                in1=tf[:, 0:512], op0=ALU.mult, op1=ALU.add))
            k3 = P.op("scalar", lambda e, s, tf=tf: e.activation(
                out=tf[:, 0:512], in_=tf[:, 0:512], func=AF.Sin),
                waits=(m5,))
            k4 = P.op("vector", lambda e, s, ot=ot, psum=psum, tf=tf: e.scalar_tensor_tensor(
                out=tf[:, 0:512], in0=tf[:, 0:512], scalar=DC[:, 64 + ot:65 + ot],
                in1=psum, op0=ALU.mult, op1=ALU.add), waits=(k3,))
            hq = P.op("vector", lambda e, s, ot=ot, tf=tf: e.tensor_scalar(
                out=A_ht[:, ot, :], in0=tf[:, 0:512],
                scalar1=B.SMALL[:, ot:ot + 1], scalar2=0.0, op0=ALU.add,
                op1=ALU.max), waits=(k4, B.tok_l2small))
            fp[("hq", ot % 2)] = hq
            fp[("tffree", ot % 2)] = hq
        h_done = hq
        for ktg in range(8):
            lw2 = P.dma("gpsimd", A_w2[:],
                        B.wh["L2_W2"][512 * ktg:512 * (ktg + 1), :]
                        .rearrange("(g p) c -> p g c", p=128),
                        sem="d_w" + str(2 + ktg % 2),
                        waits=(fp.get("w2used"), h_done))
            mm = None
            for oc in range(NDT):
                psum = (B.PSA if oc < 4 else B.PSB)[:, 512 * (oc % 4):512 * (oc % 4) + 512]
                for k4_ in range(4):
                    kt = 4 * ktg + k4_
                    mm = P.op("tensor", lambda e, s, kt=kt, k4_=k4_, oc=oc, psum=psum, lw2=lw2, ktg=ktg: e.matmul(
                        out=psum, lhsT=A_w2[:, k4_, 128 * oc:128 * (oc + 1)],
                        rhs=A_ht[:, kt, :], start=(kt == 0), stop=(kt == 31)),
                        waits=(lw2, h_done, fp.get(("yev", oc % 4))) if (k4_ == 0 and ktg == 0) else ((lw2,) if k4_ == 0 else ()),
                        inc="t" if (k4_ == 3) else None)
            fp["w2used"] = mm
        for oc in range(NDT):
            psum = (B.PSA if oc < 4 else B.PSB)[:, 512 * (oc % 4):512 * (oc % 4) + 512]
            tq = slice(t0, t0 + 512)
            g1 = P.op("scalar", lambda e, s, oc=oc, psum=psum: e.activation(
                out=A_gf[:, 0:512], in_=psum, func=AF.Silu,
                bias=B.SMALL[:, 128 + oc:129 + oc]),
                waits=(mm, B.tok_l2small, ("v", P.cnt["v"])))
            g2 = P.op("vector", lambda e, s, oc=oc, tq=tq: e.tensor_tensor(
                out=A_gf[:, 0:512], in0=A_gf[:, 0:512], in1=B.ST[:, oc, tq],
                op=ALU.mult), waits=(g1, B.tok_s2))
            mm_done = P.op("vector", lambda e, s, oc=oc, tq=tq: e.tensor_tensor(
                out=B.XT[:, oc, tq], in0=B.XT[:, oc, tq], in1=A_gf[:, 0:512],
                op=ALU.add))
            fp[("yev", oc % 4)] = mm_done
    B.tok_x_final = mm_done
    ar.release()


def emit_fp4_quant(P, B, tmp, src, dst, n, waits=()):
    nb = n // 16
    U = tmp["U"][:, 0:n]
    R1 = tmp["R1"][:, 0:n]
    R2 = tmp["R2"][:, 0:n]
    CF = tmp["CF"][:, 0:n]
    CU = tmp["CU"][:, 0:n]
    BM = tmp["BM"][:, 0:nb]
    SC = tmp["SC"][:, 0:nb]
    SCH = tmp["SCH"][:, 0:nb]
    rs_bcast = BM[:, :, None].broadcast_to([128, nb, 16])
    sch_bcast = SCH[:, :, None].broadcast_to([128, nb, 16])
    src3 = src.rearrange("p (nb b) -> p nb b", b=16)
    dst3 = dst.rearrange("p (nb b) -> p nb b", b=16)
    U3 = U.rearrange("p (nb b) -> p nb b", b=16)
    R23 = R2.rearrange("p (nb b) -> p nb b", b=16)
    waits = tuple(waits) + (("v", P.cnt["v"]),)
    P.op("vector", lambda e, s: e.tensor_reduce(
        out=BM[:, :, None], in_=src3, axis=AX.X, op=ALU.max,
        apply_absolute_value=True), waits=waits)
    P.op("vector", lambda e, s: e.tensor_scalar(
        out=SC, in0=BM, scalar1=1.0 / 6.0, scalar2=1e-12, op0=ALU.mult,
        op1=ALU.max))
    P.op("vector", lambda e, s: e.reciprocal(out=BM, in_=SC))
    P.op("vector", lambda e, s: e.tensor_scalar(
        out=SCH, in0=SC, scalar1=0.5, scalar2=None, op0=ALU.mult))
    tabs = P.op("scalar", lambda e, s: e.activation(out=U, in_=src, func=AF.Abs),
                waits=waits)
    P.op("vector", lambda e, s: e.tensor_tensor(
        out=U3, in0=U3, in1=rs_bcast, op=ALU.mult), waits=(tabs,))
    P.op("vector", lambda e, s: e.tensor_scalar(
        out=R2, in0=U, scalar1=0.25, scalar2=None, op0=ALU.is_gt))
    for thr, w in ((0.75, 1.0), (1.25, 1.0), (1.75, 1.0), (2.5, 2.0),
                   (3.5, 2.0), (5.0, 4.0)):
        P.op("vector", lambda e, s, thr=thr: e.tensor_scalar(
            out=CF, in0=U, scalar1=thr, scalar2=None, op0=ALU.is_gt))
        if w == 1.0:
            P.op("vector", lambda e, s: e.tensor_tensor(
                out=R2, in0=R2, in1=CF, op=ALU.add))
        else:
            P.op("vector", lambda e, s, w=w: e.scalar_tensor_tensor(
                out=R2, in0=CF, scalar=w, in1=R2, op0=ALU.mult, op1=ALU.add))
    P.op("vector", lambda e, s: e.tensor_scalar(
        out=CU, in0=src, scalar1=0.0, scalar2=None, op0=ALU.is_lt))
    P.op("vector", lambda e, s: e.tensor_scalar(
        out=R1, in0=R2, scalar1=-1.0, scalar2=None, op0=ALU.mult))
    P.op("vector", lambda e, s: e.copy_predicated(R2, CU, R1))
    return P.op("vector", lambda e, s: e.tensor_tensor(
        out=dst3, in0=R23, in1=sch_bcast, op=ALU.mult))


def mk_fp4_tmps(ar, n=1408):
    return {
        "U": ar.alloc(2 * n).bitcast(F32),
        "R1": ar.alloc(2 * n).bitcast(F32),
        "R2": ar.alloc(2 * n).bitcast(F32),
        "CF": ar.alloc(2 * n).bitcast(F32),
        "CU": ar.alloc((n + 1) // 2 * 2).bitcast(U8)[:, 0:n],
        "BM": ar.alloc(2 * (n // 16)).bitcast(F32),
        "SC": ar.alloc(2 * (n // 16)).bitcast(F32),
        "SCH": ar.alloc(2 * (n // 16)).bitcast(F32),
    }


def emit_block3(P, B):
    P.scope = "b3:attn"
    """L3: rmsnorm -> gated full-causal attention -> Wout ; FFN: fp4 SwiGLU."""
    ar = B.ar
    ar.mark()
    SCALE = 0.125
    ones_bf = cbf(B, "ones_col")
    ones_row = cbf(B, "ones_row", rows=(0, 1))

    tok_nw = P.dma("sync", B.SMALL[:, 0:8], B.wh["L3_normw_col"], sem="d_small",
                   waits=(B.tok_xn,))

    A_invB = ar.alloc(T)
    A_invC = ar.alloc(32).bitcast(F32)
    ar.mark()
    SQ = ar.alloc3(NDT, 512)
    R_ss = ar.rowf(512)
    R_inv = ar.rowf(512)
    RB_i = ar.rowb(512)
    stats = B.PSA[0:1, 0:512]
    inv_done = None
    invc_tok = None
    for cc in range(4):
        tsl = slice(512 * cc, 512 * (cc + 1))
        sq_toks = []
        for dt_ in range(NDT):
            sq_toks.append(P.op("vector",
                lambda e, s, dt_=dt_, tsl=tsl: e.tensor_tensor(
                    out=SQ[:, dt_, :], in0=B.XN[:, dt_, tsl],
                    in1=B.XN[:, dt_, tsl], op=ALU.mult), waits=(B.tok_xn,)))
        for dt_ in range(NDT):
            mm_sq = P.op("tensor",
                lambda e, s, dt_=dt_: e.matmul(
                    out=stats, lhsT=ones_bf, rhs=SQ[:, dt_, :],
                    start=(dt_ == 0), stop=(dt_ == NDT - 1)),
                waits=(sq_toks[dt_], inv_done, B.tok_xn) if dt_ == 0 else (sq_toks[dt_],),
                inc="t" if dt_ == NDT - 1 else None)
        t1 = P.op("scalar", lambda e, s: e.mul(out=R_ss, in_=stats, mul=1.0 / 1024),
                  waits=(mm_sq,))
        P.op("scalar", lambda e, s: e.activation(
            out=R_ss, in_=R_ss, func=AF.Ln, bias=cvf(B, "rms_eps", (0, 1))))
        t3 = P.op("scalar", lambda e, s: e.activation(
            out=R_inv, in_=R_ss, func=AF.Exp, scale=-0.5))
        t4 = P.op("vector", lambda e, s: e.tensor_copy(out=RB_i, in_=R_inv),
                  waits=(t3, invc_tok))
        mm_i = P.op("tensor", lambda e, s: e.matmul(
            out=B.PSB[:, 0:512], lhsT=ones_row, rhs=RB_i, start=True, stop=True),
            waits=(t4,), inc="t")
        inv_done = P.op("scalar", lambda e, s, tsl=tsl: e.copy(
            out=A_invB[:, tsl], in_=B.PSB[:, 0:512]), waits=(mm_i,))
        for c4 in range(4):
            mm_c = P.op("tensor", lambda e, s, c4=c4: e.transpose(
                out=B.PSB[:, 512 + 1024 * (c4 % 2):512 + 1024 * (c4 % 2) + 256].bitcast(BF16)[:, 0:1],
                in_=RB_i[:, 128 * c4:128 * (c4 + 1)], identity=cbf(B, "eye")[0:1, 0:1]),
                waits=(t4, invc_tok), inc="t")
            invc_tok = P.op("scalar", lambda e, s, cc=cc, c4=c4: e.copy(
                out=A_invC[:, 4 * cc + c4:4 * cc + c4 + 1],
                in_=B.PSB[:, 512 + 1024 * (c4 % 2):512 + 1024 * (c4 % 2) + 256].bitcast(BF16)[:, 0:1]),
                waits=(mm_c,))
    ar.release()

    A_gfm = ar.alloc(T)
    A_wg = ar.alloc3(NDT, 16)
    w_g = P.dma("gpsimd", A_wg[:], B.wh["L3_Wg"].rearrange("(it p) c -> p it c", p=128),
                waits=(B.tok_xn, inv_done), sem="d_w1")
    t_wg = None
    for it in range(NDT):
        t_wg = P.op("vector", lambda e, s, it=it: e.tensor_scalar(
            out=A_wg[:, it, :], in0=A_wg[:, it, :],
            scalar1=B.SMALL[:, it:it + 1], scalar2=None, op0=ALU.mult),
            waits=(w_g, tok_nw))
    gfm_done = None
    for ch in range(4):
        tsl = slice(512 * ch, 512 * (ch + 1))
        for it in range(NDT):
            mm = P.op("tensor", lambda e, s, it=it, tsl=tsl: e.matmul(
                out=B.PSB[0:16, 1024:1536], lhsT=A_wg[:, it, :],
                rhs=B.XN[:, it, tsl], start=(it == 0), stop=(it == NDT - 1)),
                waits=(t_wg, gfm_done) if it == 0 else (),
                inc="t" if it == NDT - 1 else None)
        t1 = P.op("vector", lambda e, s, tsl=tsl: e.tensor_tensor(
            out=A_gfm[0:16, tsl], in0=B.PSB[0:16, 1024:1536],
            in1=A_invB[0:16, tsl], op=ALU.mult), waits=(mm, inv_done))
        gfm_done = P.op("scalar", lambda e, s, tsl=tsl: e.activation(
            out=A_gfm[0:16, tsl], in_=A_gfm[0:16, tsl], func=AF.Sigmoid),
            waits=(t1,))

    A_qw = ar.alloc3(NDT, 256)
    A_kw = ar.alloc3(NDT, 256)
    A_vw = ar.alloc3(NDT, 256)
    A_ow = ar.alloc3(2, 1024)
    A_qT = ar.alloc3(2, T)
    A_kT = ar.alloc3(2, T)
    A_at = ar.alloc3(2, T)
    A_vx = ar.alloc4(NTT, 4, 65)
    A_pt = ar.alloc3(4, 512)
    A_gr = ar.rowb(512)
    A_rec = ar.alloc(512)
    R_rec = ar.rowf(512)
    RB_rec = ar.rowb(512)

    prev = {}
    p_done = None
    for hg in range(4):
        c0 = hg * 256
        wsem = f"d_w{hg % 2}"
        first = invc_tok if hg == 0 else None
        w_q = P.dma("gpsimd", A_qw[:], wslice_src(B.wh["L3_Wq"], c0, c0 + 256),
                    waits=(prev.get("q_done"), B.tok_xn, inv_done), sem=wsem)
        w_k = P.dma("gpsimd", A_kw[:], wslice_src(B.wh["L3_Wkv"], c0, c0 + 256),
                    waits=(prev.get("k_done"), first), sem=wsem)
        w_v = P.dma("gpsimd", A_vw[:], wslice_src(B.wh["L3_Wkv"], 1024 + c0, 1024 + c0 + 256),
                    waits=(prev.get("v_done"), first), sem=wsem)
        w_o = P.dma("gpsimd", A_ow[:],
                    B.wh["L3_Wout"][c0:c0 + 256, :].rearrange("(it p) c -> p it c", p=128),
                    waits=(p_done, first), sem=wsem)
        nw_done = None
        for it in range(NDT):
            for wv, wt in ((A_qw, w_q), (A_kw, w_k), (A_vw, w_v)):
                nw_done = P.op("vector", lambda e, s, it=it, wv=wv: e.tensor_scalar(
                    out=wv[:, it, :], in0=wv[:, it, :],
                    scalar1=B.SMALL[:, it:it + 1], scalar2=None, op0=ALU.mult),
                    waits=(wt, tok_nw))

        def proj_fm3(wview, dst, scale=None):
            last = None
            for oc in range(2):
                for ch in range(4):
                    psum = B.PSA[:, 512 * ch:512 * (ch + 1)]
                    for it in range(NDT):
                        mm = P.op("tensor", lambda e, s, oc=oc, ch=ch, it=it, wview=wview, psum=psum: e.matmul(
                            out=psum, lhsT=wview[:, it, 128 * oc:128 * (oc + 1)],
                            rhs=B.XN[:, it, 512 * ch:512 * (ch + 1)],
                            start=(it == 0), stop=(it == NDT - 1)),
                            waits=(nw_done, inv_done, prev.get(("evac", ch))) if it == 0 else (),
                            inc="t" if it == NDT - 1 else None)
                    if scale is not None:
                        ev = P.op("vector", lambda e, s, oc=oc, ch=ch, dst=dst, psum=psum, scale=scale: e.scalar_tensor_tensor(
                            out=dst[:, oc, 512 * ch:512 * (ch + 1)], in0=psum,
                            scalar=scale, in1=A_invB[:, 512 * ch:512 * (ch + 1)],
                            op0=ALU.mult, op1=ALU.mult), waits=(mm,))
                    else:
                        ev = P.op("vector", lambda e, s, oc=oc, ch=ch, dst=dst, psum=psum: e.tensor_tensor(
                            out=dst[:, oc, 512 * ch:512 * (ch + 1)], in0=psum,
                            in1=A_invB[:, 512 * ch:512 * (ch + 1)], op=ALU.mult),
                            waits=(mm,))
                    prev[("evac", ch)] = ev
                    last = ev
            return last

        q_done = proj_fm3(A_qw, A_qT, scale=SCALE)
        k_done = proj_fm3(A_kw, A_kT)

        t_ones = P.op("vector", lambda e, s: e.memset(A_vx[:, :, :, 64:65], 1.0),
                      waits=(prev.get("at_done"),))
        v_done = None
        for tt in range(NTT):
            psum = B.PSB[:, 1024 + 512 * (tt % 2):1024 + 512 * (tt % 2) + 256]
            for it in range(NDT):
                mm = P.op("tensor", lambda e, s, tt=tt, it=it, psum=psum: e.matmul(
                    out=psum, lhsT=B.XN[:, it, 128 * tt:128 * (tt + 1)],
                    rhs=A_vw[:, it, :], start=(it == 0), stop=(it == NDT - 1)),
                    waits=(nw_done, gfm_done, prev.get(("vevac", tt % 2)), prev.get("pv_evac")) if it == 0 else (),
                    inc="t" if it == NDT - 1 else None)
            ev = P.op("vector", lambda e, s, tt=tt, psum=psum: e.tensor_scalar(
                out=A_vx[:, tt, :, 0:64],
                in0=psum.rearrange("p (a b) -> p a b", b=64),
                scalar1=A_invC[:, tt:tt + 1], scalar2=None, op0=ALU.mult),
                waits=(mm, t_ones, invc_tok))
            prev[("vevac", tt % 2)] = ev
            v_done = ev

        attn_done = prev.get("at_done")
        for qt in range(NTT):
            nkt = qt + 1
            G = (nkt + 3) // 4
            qo = 128 * (qt % 4)
            pvs = [B.PSB[0:65, 512 * hs + qo:512 * hs + qo + 128] for hs in range(4)]
            g_dma = P.dma("gpsimd", A_gr.rearrange("o (h c) -> o h c", c=128),
                          A_gfm[4 * hg:4 * hg + 4, 128 * qt:128 * (qt + 1)],
                          sem="d_gr",
                          waits=(gfm_done, prev.get("gr_used")))
            last_pv = None
            t_exp_prev = prev.get("psa_free")
            for g in range(G):
                n = min(4, nkt - 4 * g)
                for hs in range(4):
                    rb = (hs % 2) * 64
                    for ki in range(n):
                        kt = 4 * g + ki
                        mm_s = P.op("tensor", lambda e, s, hs=hs, kt=kt, qt=qt, rb=rb, ki=ki: e.matmul(
                            out=B.PSA[:, 512 * hs + 128 * ki:512 * hs + 128 * (ki + 1)],
                            lhsT=A_kT[rb:rb + 64, hs // 2, 128 * kt:128 * (kt + 1)],
                            rhs=A_qT[rb:rb + 64, hs // 2, 128 * qt:128 * (qt + 1)],
                            start=True, stop=True),
                            waits=(q_done, k_done, t_exp_prev) if (hs == 0 and ki == 0) else (),
                            inc="t" if (hs == 3 and ki == n - 1) else None)
                t_exp = P.op("scalar", lambda e, s, n=n: e.activation(
                    out=A_pt[:, :, 0:128 * n], func=AF.Exp,
                    in_=B.PSA[:].rearrange("p (h c) -> p h c", c=512)[:, :, 0:128 * n]),
                    waits=(mm_s, ("v", P.cnt["v"]),))
                t_exp_prev = t_exp
                tok_after_exp = t_exp
                if 4 * g <= qt < 4 * (g + 1):
                    ki_d = qt - 4 * g
                    tok_after_exp = P.op("vector", lambda e, s, ki_d=ki_d: e.tensor_tensor(
                        out=A_pt[:, :, 128 * ki_d:128 * (ki_d + 1)],
                        in0=A_pt[:, :, 128 * ki_d:128 * (ki_d + 1)],
                        in1=cbf(B, "triuIncl")[:, None, :].broadcast_to([128, 4, 128]),
                        op=ALU.mult), waits=(t_exp,))
                for hs in range(4):
                    for ki in range(n):
                        kt = 4 * g + ki
                        last_pv = P.op("tensor", lambda e, s, hs=hs, kt=kt, ki=ki, g=g, n=n, G=G, pv=pvs[hs]: e.matmul(
                            out=pv,
                            lhsT=A_vx[:, kt, hs, :],
                            rhs=A_pt[:, hs, 128 * ki:128 * (ki + 1)],
                            start=(g == 0 and ki == 0), stop=(g == G - 1 and ki == n - 1)),
                            waits=(tok_after_exp, v_done, prev.get("pv_evac")) if (hs == 0 and ki == 0) else (),
                            inc="t" if (hs == 3 and ki == n - 1) else None)
            sums_v = B.PSB[64:65, 0:2048].rearrange("o (h c) -> o h c", c=512)[:, :, qo:qo + 128]
            t_rc = P.op("vector", lambda e, s, sums_v=sums_v: e.reciprocal(
                out=R_rec.rearrange("o (h c) -> o h c", c=128), in_=sums_v),
                waits=(last_pv,))
            t_rg = P.op("vector", lambda e, s: e.tensor_tensor(
                out=RB_rec, in0=R_rec, in1=A_gr, op=ALU.mult), waits=(g_dma, t_rc))
            prev["gr_used"] = t_rg
            mm_rb = P.op("tensor", lambda e, s: e.matmul(
                out=B.PSA[0:64, 0:512], lhsT=ones_row[:, 0:64],
                rhs=RB_rec, start=True, stop=True), waits=(t_rg, t_exp_prev), inc="t")
            t_rcb = P.op("scalar", lambda e, s: e.copy(out=A_rec[0:64, :],
                                                       in_=B.PSA[0:64, 0:512]),
                         waits=(mm_rb, attn_done))
            prev["psa_free"] = t_rcb
            for par in range(2):
                ev = P.op("vector", lambda e, s, par=par, qt=qt, qo=qo: e.tensor_tensor(
                    out=A_at[64 * par:64 * par + 64, :, 128 * qt:128 * (qt + 1)],
                    in0=B.PSB[0:64, 0:2048].rearrange("p (h c) -> p h c", c=512)[:, par::2, qo:qo + 128],
                    in1=A_rec[0:64, :].rearrange("p (h c) -> p h c", c=128)[:, par::2, :],
                    op=ALU.mult), waits=(t_rcb,))
                prev["pv_evac"] = ev
                attn_done = ev
        prev["at_done"] = attn_done

        for oc in range(NDT):
            for ch in range(4):
                psum = B.PSA[:, 512 * ch:512 * (ch + 1)]
                for it2 in range(2):
                    mm = P.op("tensor", lambda e, s, oc=oc, ch=ch, it2=it2, psum=psum: e.matmul(
                        out=psum, lhsT=A_ow[:, it2, 128 * oc:128 * (oc + 1)],
                        rhs=A_at[:, it2, 512 * ch:512 * (ch + 1)],
                        start=(it2 == 0), stop=(it2 == 1)),
                        waits=(w_o, attn_done, prev.get(("evac", ch)), prev.get("psa_free")) if it2 == 0 else (),
                        inc="t" if it2 == 1 else None)
                if hg == 0:
                    ev = P.op("scalar", lambda e, s, oc=oc, ch=ch, psum=psum: e.copy(
                        out=B.ST[:, oc, 512 * ch:512 * (ch + 1)], in_=psum),
                        waits=(mm,))
                else:
                    ev = P.op("vector", lambda e, s, oc=oc, ch=ch, psum=psum: e.tensor_tensor(
                        out=B.ST[:, oc, 512 * ch:512 * (ch + 1)],
                        in0=B.ST[:, oc, 512 * ch:512 * (ch + 1)], in1=psum,
                        op=ALU.add), waits=(mm,))
                prev[("evac", ch)] = ev
                p_done = ev
        prev["q_done"] = q_done
        prev["k_done"] = k_done
        prev["v_done"] = v_done
    B.tok_s3 = p_done
    ar.release()
    if L3_PARTS >= 1:
        emit_block3_ffn(P, B)
    else:
        B.tok_x_final = p_done


L3_PARTS = 9


def emit_block3_ffn(P, B):
    ar = B.ar
    nc = B.nc
    if not hasattr(B, "wq_g"):
        B.wq_g = nc.dram_tensor("wq_g", [1024, 2816], BF16, kind="Internal").ap()
        B.wq_u = nc.dram_tensor("wq_u", [1024, 2816], BF16, kind="Internal").ap()
        B.wq_d = nc.dram_tensor("wq_d", [2816, 1024], BF16, kind="Internal").ap()
    eye_bf = cbf(B, "eye")

    P.scope = "b3:wq"
    ar.mark()
    tmps = mk_fp4_tmps(ar, 1408)
    WN = ar.alloc(5632).bitcast(F32)
    WQo = ar.alloc(2816)
    WTs = ar.alloc(2816)
    pq = {}
    wq_done = None
    P.wait("tensor", ("s", P.cnt["s"]), ("v", P.cnt["v"]))
    for (nat, scr, nrt, ncol) in (("L3_wgate_nat", B.wq_g, 22, 1024),
                                  ("L3_wup_nat", B.wq_u, 22, 1024),
                                  ("L3_wdown_nat", B.wq_d, 8, 2816)):
        ncs = ncol // 128
        for c in range(ncs):
            ld = P.dma("sync", WN[:, 0:128 * nrt].rearrange("p (r c) -> p r c", c=128),
                       B.wh[nat][:, 128 * c:128 * (c + 1)]
                       .rearrange("(r p) c -> p r c", p=128),
                       sem="d_sw2", waits=(pq.get("wn_used"), B.tok_s3))
            qd = None
            for q0 in range(0, 128 * nrt, 1408):
                qw = min(1408, 128 * nrt - q0)
                qd = emit_fp4_quant(P, B, tmps, WN[:, q0:q0 + qw],
                                    WQo[:, q0:q0 + qw], qw, waits=(ld,))
            pq["wn_used"] = qd
            tev = None
            for r in range(nrt):
                mm = P.op("tensor", lambda e, s, r=r: e.transpose(
                    out=B.PSA[:, 512 * (r % 4):512 * (r % 4) + 256].bitcast(BF16)[:, 0:128],
                    in_=WQo[:, 128 * r:128 * (r + 1)], identity=eye_bf),
                    waits=(qd, pq.get(("trev", r % 4))), inc="t")
                tev = P.op("vector", lambda e, s, r=r: e.tensor_copy(
                    out=WTs[:, 128 * r:128 * (r + 1)],
                    in_=B.PSA[:, 512 * (r % 4):512 * (r % 4) + 256].bitcast(BF16)[:, 0:128]),
                    waits=(mm, pq.get("wts_used")))
                pq[("trev", r % 4)] = tev
            wq_done = P.dma("sync", scr[128 * c:128 * (c + 1), :],
                            WTs[:, 0:128 * nrt], sem="d_sw3", waits=(tev,))
            pq["wts_used"] = wq_done
    ar.release()

    if L3_PARTS < 2:
        B.tok_x_final = wq_done
        return
    P.scope = "b3:xq"
    ar.mark()
    tmps = mk_fp4_tmps(ar, 1408)
    XNAT = ar.alloc3(8, 1024)
    pq2 = {}
    xq_done = None
    P.wait("tensor", ("s", P.cnt["s"]), ("v", P.cnt["v"]))
    for half in range(2):
        tb = 8 * half
        for tt in range(8):
            for dt_ in range(NDT):
                mm = P.op("tensor", lambda e, s, tt=tt, dt_=dt_, tb=tb: e.transpose(
                    out=B.PSA[:, 512 * (dt_ % 4):512 * (dt_ % 4) + 256].bitcast(BF16)[:, 0:128],
                    in_=B.XN[:, dt_, 128 * (tb + tt):128 * (tb + tt) + 128],
                    identity=eye_bf),
                    waits=(pq2.get(("tev", dt_ % 4)), xq_done), inc="t")
                ev = P.op("vector", lambda e, s, tt=tt, dt_=dt_: e.tensor_copy(
                    out=XNAT[:, tt, 128 * dt_:128 * (dt_ + 1)],
                    in_=B.PSA[:, 512 * (dt_ % 4):512 * (dt_ % 4) + 256].bitcast(BF16)[:, 0:128]),
                    waits=(mm,))
                pq2[("tev", dt_ % 4)] = ev
        for tt in range(8):
            qd = emit_fp4_quant(P, B, tmps, XNAT[:, tt, :], XNAT[:, tt, :], 1024)
            for dt_ in range(NDT):
                mm = P.op("tensor", lambda e, s, tt=tt, dt_=dt_: e.transpose(
                    out=B.PSB[:, 512 * (dt_ % 4):512 * (dt_ % 4) + 256].bitcast(BF16)[:, 0:128],
                    in_=XNAT[:, tt, 128 * dt_:128 * (dt_ + 1)],
                    identity=eye_bf),
                    waits=(qd, pq2.get(("tev2", dt_ % 4))), inc="t")
                ev = P.op("vector", lambda e, s, tt=tt, dt_=dt_, tb=tb: e.tensor_copy(
                    out=B.XN[:, dt_, 128 * (tb + tt):128 * (tb + tt) + 128],
                    in_=B.PSB[:, 512 * (dt_ % 4):512 * (dt_ % 4) + 256].bitcast(BF16)[:, 0:128]),
                    waits=(mm,))
                pq2[("tev2", dt_ % 4)] = ev
                xq_done = ev
    ar.release()

    if L3_PARTS < 3:
        B.tok_x_final = xq_done
        return
    P.scope = "b3:mm"
    ar.mark()
    A_wsl = ar.alloc4(2, NDT, 512)
    A_dk = [ar.alloc3(1, 1024), ar.alloc3(1, 1024)]
    A_h = ar.alloc3(4, 2816)
    A_sil = ar.alloc(512)
    A_gf = ar.alloc(1024).bitcast(F32)
    mm_done = None
    fp = {}
    for tch in range(4):
        t0 = 512 * tch
        for ici, ic in enumerate(range(0, 2816, 512)):
            icw = min(512, 2816 - ic)
            lg = P.dma("sync", A_wsl[:, 0, :, 0:icw],
                       B.wq_g.rearrange("(it p) i -> p it i", p=128)[:, :, ic:ic + icw],
                       sem="d_sw0", waits=(fp.get("wused"), wq_done, xq_done))
            lu = P.dma("sync", A_wsl[:, 1, :, 0:icw],
                       B.wq_u.rearrange("(it p) i -> p it i", p=128)[:, :, ic:ic + icw],
                       sem="d_sw1", waits=(fp.get("wused"), xq_done))
            mmu = None
            for tt in range(4):
                gps = B.PSA[:, 1024 * (tt % 2):1024 * (tt % 2) + 512]
                ups = B.PSA[:, 1024 * (tt % 2) + 512:1024 * (tt % 2) + 1024]
                for it in range(NDT):
                    mmg = P.op("tensor", lambda e, s, it=it, tt=tt, icw=icw, t0=t0, gps=gps: e.matmul(
                        out=gps[:, 0:icw], lhsT=B.XN[:, it, t0 + 128 * tt:t0 + 128 * (tt + 1)],
                        rhs=A_wsl[:, 0, it, 0:icw], start=(it == 0), stop=(it == NDT - 1)),
                        waits=(lg, xq_done, fp.get(("hev", tt % 2)),
                               fp.get("tchdone") if (ici == 0 and tt == 0) else None) if it == 0 else (),
                        inc="t" if it == NDT - 1 else None)
                for it in range(NDT):
                    mmu = P.op("tensor", lambda e, s, it=it, tt=tt, icw=icw, t0=t0, ups=ups: e.matmul(
                        out=ups[:, 0:icw], lhsT=B.XN[:, it, t0 + 128 * tt:t0 + 128 * (tt + 1)],
                        rhs=A_wsl[:, 1, it, 0:icw], start=(it == 0), stop=(it == NDT - 1)),
                        waits=(lu,) if it == 0 else (),
                        inc="t" if it == NDT - 1 else None)
                t_sil = P.op("scalar", lambda e, s, icw=icw, gps=gps: e.activation(
                    out=A_sil[:, 0:icw], in_=gps[:, 0:icw], func=AF.Silu),
                    waits=(mmg, ("v", P.cnt["v"])))
                hev = P.op("vector", lambda e, s, tt=tt, ic=ic, icw=icw, ups=ups: e.tensor_tensor(
                    out=A_h[:, tt, ic:ic + icw], in0=ups[:, 0:icw],
                    in1=A_sil[:, 0:icw], op=ALU.mult), waits=(mmu, t_sil))
                fp[("hev", tt % 2)] = hev
                fp[("hevt", tt)] = hev
            fp["wused"] = mmu
        if L3_PARTS < 4:
            mm_done = fp[("hevt", 3)]
            fp["tchdone"] = mm_done
            continue
        ar.mark()
        tmps = mk_fp4_tmps(ar, 704)

        def quant_tt(tt):
            tok = None
            for c0 in range(0, 2816, 704):
                cw = min(704, 2816 - c0)
                tok = emit_fp4_quant(P, B, tmps, A_h[:, tt, c0:c0 + cw],
                                     A_h[:, tt, c0:c0 + cw], cw,
                                     waits=(fp.get(("hevt", tt)),))
            return tok

        for tt in range(4):
            hq_tok = quant_tt(tt)
        ar.release()
        if L3_PARTS < 5:
            mm_done = hq_tok
            fp["tchdone"] = mm_done
            continue
        ar.mark()
        A_hqt = ar.alloc3(22, 512)
        tevs = []
        for kt in range(22):
            for t2 in range(4):
                mm = P.op("tensor", lambda e, s, kt=kt, t2=t2: e.transpose(
                    out=B.PSA[:, 512 * (kt % 4):512 * (kt % 4) + 256].bitcast(BF16)[:, 128 * t2:128 * (t2 + 1)],
                    in_=A_h[:, t2, 128 * kt:128 * (kt + 1)],
                    identity=eye_bf),
                    waits=(hq_tok, fp.get(("hqtev", kt % 4)), fp.get("yevac")) if t2 == 0 else (),
                    inc="t" if t2 == 3 else None)
            tev = P.op("vector", lambda e, s, kt=kt: e.tensor_copy(
                out=A_hqt[:, kt, :],
                in_=B.PSA[:, 512 * (kt % 4):512 * (kt % 4) + 256].bitcast(BF16)[:, 0:512]),
                waits=(mm, fp.get("hqt_used")))
            fp[("hqtev", kt % 4)] = tev
            tevs.append(tev)
        mmd = None
        for kt in range(22):
            ldk = P.dma("sync", A_dk[kt % 2][:],
                        B.wq_d.rearrange("(kt p) o -> p kt o", p=128)[:, kt:kt + 1, :],
                        sem=("d_sw2", "d_sw3")[kt % 2],
                        waits=(fp.get(("dused", kt % 2)), wq_done))
            for ot in range(8):
                psum = (B.PSA if ot < 4 else B.PSB)[:, 512 * (ot % 4):512 * (ot % 4) + 512]
                mmd = P.op("tensor", lambda e, s, kt=kt, ot=ot, psum=psum: e.matmul(
                    out=psum, lhsT=A_dk[kt % 2][:, 0, 128 * ot:128 * (ot + 1)],
                    rhs=A_hqt[:, kt, :], start=(kt == 0), stop=(kt == 21)),
                    waits=((tevs[kt], ldk, tevs[-1] if kt == 0 else None,
                            fp.get("yevac") if kt == 0 else None)
                           if ot == 0 else ()),
                    inc="t" if ot == 7 else None)
            fp[("dused", kt % 2)] = mmd
        for ot in range(8):
            psum = (B.PSA if ot < 4 else B.PSB)[:, 512 * (ot % 4):512 * (ot % 4) + 512]
            tq = slice(t0, t0 + 512)
            t1 = P.op("scalar", lambda e, s, psum=psum: e.activation(
                out=A_gf[:, 0:512], in_=psum, func=AF.Silu),
                waits=(mmd, ("v", P.cnt["v"]), fp.get("gfree")))
            P.op("vector", lambda e, s, ot=ot, tq=tq: e.tensor_tensor(
                out=A_gf[:, 0:512], in0=A_gf[:, 0:512], in1=B.ST[:, ot, tq],
                op=ALU.mult), waits=(t1, B.tok_s3))
            mm_done = P.op("vector", lambda e, s, ot=ot, tq=tq: e.tensor_tensor(
                out=B.XT[:, ot, tq], in0=B.XT[:, ot, tq], in1=A_gf[:, 0:512],
                op=ALU.add))
            fp["gfree"] = mm_done
        fp["yevac"] = mm_done
        fp["hqt_used"] = mmd
        ar.release()
        fp["tchdone"] = mm_done
    B.tok_x_final = mm_done
    ar.release()


EMITTERS = {0: emit_block0, 1: emit_block1, 2: emit_block2, 3: emit_block3}


def kernel(**inputs):
    dev, xs = host_prep(inputs)
    weight_names = {k: v.shape for k, v in dev.items() if k != "CONSTS"}
    nc = build_program([0, 1, 2, 3], weight_names=weight_names)
    in_maps = []
    for b in range(8):
        m = {"x_in": xs[b]}
        m.update(dev)
        in_maps.append(m)
    res = run_bass_kernel_spmd(nc, in_maps, core_ids=list(range(8)))
    out = np.stack([np.asarray(r["x_out"]).reshape(D, T).T for r in res.results])
    return out.astype(np.float32)
